# version 03
# baseline (speedup 1.0000x reference)
import numpy as np
import ml_dtypes
from contextlib import ExitStack

import concourse.bass as bass
import concourse.mybir as mybir
from concourse.bass_utils import run_bass_kernel_spmd

F32 = mybir.dt.float32
BF16 = mybir.dt.bfloat16
AF = mybir.ActivationFunctionType
ALU = mybir.AluOpType
AX = mybir.AxisListType
BF = ml_dtypes.bfloat16

NC = 8
T = 8192
TC = T // NC
D = 2048
H = 16
DK = 128
CH = 128
FW = TC + 3
EPS = 1e-5
RDK = float(DK) ** -0.5

_WRAP = {
    "memset", "memzero", "tensor_copy", "tensor_tensor", "tensor_add", "tensor_sub",
    "tensor_mul", "tensor_max", "tensor_scalar", "tensor_single_scalar",
    "tensor_scalar_add", "tensor_scalar_sub", "tensor_scalar_mul", "tensor_scalar_max",
    "tensor_scalar_min", "tensor_relu", "tensor_reduce", "reduce_max", "reduce_sum",
    "scalar_tensor_tensor", "tensor_tensor_scan", "select", "copy_predicated",
    "transpose", "reciprocal", "stream_shuffle", "bn_stats", "bn_aggr", "max",
    "max_index", "pool", "pool_avg", "pool_max", "match_replace",
    "activation", "sign", "square", "sqrt", "mul", "copy", "add", "iota",
    "affine_select",
}


class Sem:
    __slots__ = ("h", "n", "np_")

    def __init__(self, h):
        self.h = h
        self.n = 0


def inc(instr, sem, k=1):
    instr.then_inc(sem.h, k)
    sem.n += k


class W:

    def __init__(self, eng, sem):
        self._e = eng
        self.s = sem

    @property
    def n(self):
        return self.s.n

    def wait(self, sem, val):
        self._e.wait_ge(sem.h, val)

    def __getattr__(self, name):
        attr = getattr(self._e, name)
        if name in _WRAP:
            def f(*a, **k):
                self._e.wait_ge(self.s.h, self.s.n)
                ins = attr(*a, **k)
                inc(ins, self.s, 1)
                return ins
            return f
        return attr


class Ctx:
    def __init__(self):
        self.nc = bass.Bass("TRN2", target_bir_lowering=False)
        self.es = ExitStack()
        self.block = None
        self.bar = None
        self.barn = 0
        self.vv = None
        self.ss = None
        self.gg = None
        self.cur = None
        self.prefix = ""

    def sem(self, name):
        return Sem(self.es.enter_context(self.nc.semaphore(self.prefix + name)))

    def sbuf(self, name, shape, dt, es=None):
        tgt = es or self.cur or self.es
        return tgt.enter_context(self.nc.sbuf_tensor(self.prefix + name, shape, dt))

    def psum(self, name, shape, dt=F32):
        tgt = self.cur or self.es
        return tgt.enter_context(self.nc.psum_tensor(self.prefix + name, shape, dt))

    def din(self, name, shape, dt):
        return self.nc.dram_tensor(name, shape, dt, kind="ExternalInput").ap()

    def dout(self, name, shape, dt):
        return self.nc.dram_tensor(name, shape, dt, kind="ExternalOutput").ap()

    def dint(self, name, shape, dt):
        return self.nc.dram_tensor(name, shape, dt, kind="Internal").ap()

    def start(self):
        if self.block is not None:
            return self.block
        self.bar = self.sem("bar")
        self.vv = self.sem("vv")
        self.ss = self.sem("ss")
        self.gg = self.sem("gg")
        self.block = self.es.enter_context(self.nc.Block())
        return self.block

    def vsec(self, fn):
        @self.block.vector
        def _(v):
            fn(W(v, self.vv))

    def ssec(self, fn):
        @self.block.scalar
        def _(s):
            fn(W(s, self.ss))

    def gsec(self, fn):
        @self.block.gpsimd
        def _(g):
            fn(W(g, self.gg))

    def tsec(self, fn):
        @self.block.tensor
        def _(t):
            fn(t)

    def sysec(self, fn):
        @self.block.sync
        def _(sy):
            fn(sy)

    def barrier(self):
        self.barn += 1
        tgt = 5 * self.barn
        bar = self.bar

        def arr(e):
            e._e.wait_ge(e.s.h, e.s.n)
            e._e.sem_inc(bar.h, 1)
            e._e.wait_ge(bar.h, tgt)

        self.vsec(arr)
        self.ssec(arr)
        self.gsec(arr)

        @self.block.tensor
        def _(t):
            t.sem_inc(bar.h, 1)
            t.wait_ge(bar.h, tgt)

        @self.block.sync
        def _(sy):
            sy.sem_inc(bar.h, 1)
            sy.wait_ge(bar.h, tgt)
        bar.n = tgt


def drams_l1(ctx, inner):
    mk = ctx.dint if inner else ctx.dout
    d = {}
    d["xm"] = ctx.din("xm", [TC, D], F32)
    d["xh"] = ctx.din("xh", [3, D], F32)
    d["Wqkvg"] = ctx.din("Wqkvg", [4, D, D], F32)
    d["Wab"] = ctx.din("Wab", [128, 40 * 16], F32)
    d["bab"] = ctx.din("bab", [40, 1], F32)
    d["convw"] = ctx.din("convw", [128, 192], F32)
    d["convb"] = ctx.din("convb", [128, 48], F32)
    d["iden"] = ctx.din("iden", [128, 128], F32)
    d["xnfm"] = mk("xnfm", [D, TC], BF16)
    d["proj"] = [mk(nm, [D, TC], BF16) for nm in ("q_d", "k_d", "v_d", "g_d")]
    d["smallL"] = mk("smallL", [40, TC], F32)
    return d


def build_l1():
    ctx = Ctx()
    dr = drams_l1(ctx, False)
    phase_l1(ctx, dr)
    return ctx


def phase_l1(ctx, dr):
    xm, xh, Wqkvg, Wab, bab = dr["xm"], dr["xh"], dr["Wqkvg"], dr["Wab"], dr["bab"]
    convw, convb, iden = dr["convw"], dr["convb"], dr["iden"]
    xnfm_d, proj_d, smallL_d = dr["xnfm"], dr["proj"], dr["smallL"]
    es_ph = ExitStack()
    oc, op = ctx.cur, ctx.prefix
    ctx.cur, ctx.prefix = es_ph, "L1_"

    grps = [(0, 512), (512, 512), (1024, 3)]

    if True:
        xt = [ctx.sbuf(f"xt{i}", [128, D], F32) for i in range(3)]
        lnc = ctx.sbuf("lnc", [128, D], F32)
        col = ctx.sbuf("col", [128, 12], F32)
        konst = ctx.sbuf("konst", [128, 2], F32)
        xnbf = [ctx.sbuf(f"xnbf{c}", [128, FW], BF16) for c in range(16)]
        idn = ctx.sbuf("idn", [128, 128], F32)
        wabs = ctx.sbuf("wabs", [128, 40 * 16], F32)
        babs = ctx.sbuf("babs", [40, 1], F32)
        smalls = ctx.sbuf("smalls", [40, FW], F32)
        cw = ctx.sbuf("cw", [128, 192], F32)
        cb = ctx.sbuf("cb", [128, 48], F32)
        qpre = [ctx.sbuf(f"qpre{i}", [128, FW], F32) for i in range(2)]
        cvo = [ctx.sbuf(f"cvo{i}", [128, TC], F32) for i in range(4)]
        obf = [ctx.sbuf(f"obf{i}", [128, TC], BF16) for i in range(2)]
        okbf = ctx.sbuf("okbf", [128, TC], BF16)
        gbf = [ctx.sbuf(f"gbf{i}", [128, FW], BF16) for i in range(2)]

        pA = [ctx.psum(f"pA{i}", [128, 512], F32) for i in range(2)]
        pB = [ctx.psum(f"pB{i}", [128, 512], F32) for i in range(2)]
        pC = [ctx.psum(f"pC{i}", [128, 512], F32) for i in range(2)]
        pT = [ctx.psum(f"pT{i}", [128, 512], F32) for i in range(2)]

        dC = ctx.sem("dC")
        dXT = [ctx.sem(f"dXT{i}") for i in range(3)]
        dW = ctx.sem("dW")
        dS = ctx.sem("dS")
        tT = ctx.sem("tT")
        tG = ctx.sem("tG")
        dOBq = ctx.sem("dOBq")
        dOBv = ctx.sem("dOBv")
        dGB = [ctx.sem("dGB0"), ctx.sem("dGB1")]
        dOK = ctx.sem("dOK")

        b = ctx.start()

        es_fm = ExitStack()
        xnfm = [ctx.sbuf(f"xnfm{c}", [128, FW], F32, es=es_fm) for c in range(16)]

        def p0(sy):
            inc(sy.dma_start(out=idn[:], in_=iden[:, :]), dC, 16)
            inc(sy.dma_start(out=babs[:], in_=bab[:, :]), dC, 16)
            inc(sy.dma_start(out=cw[:], in_=convw[:, :]), dC, 16)
            inc(sy.dma_start(out=cb[:], in_=convb[:, :]), dC, 16)
            inc(sy.dma_start(out=wabs[:], in_=Wab[:, :]), dC, 16)
        ctx.sysec(p0)
        const_loads = dC.n

        nrows = [3] + [128] * 8
        coff = [0] + [3 + 128 * i for i in range(8)]

        def p1_loads(sy):
            for i in range(9):
                if i >= 3:
                    sy.wait_ge(tT.h, 16 * (i - 2))
                if i == 0:
                    inc(sy.dma_start(out=xt[0][0:3, :], in_=xh[:, :]), dXT[0], 16)
                else:
                    inc(sy.dma_start(out=xt[i % 3][:, :], in_=xm[(i - 1) * 128:i * 128, :]),
                        dXT[i % 3], 16)
        ctx.sysec(p1_loads)

        vv0 = ctx.vv.n
        ss0 = ctx.ss.n
        vmark = {}

        def rstd_end_fwd(i):
            return ss0 + (2 * (i + 1) if i <= 2 else 24 + 18 * (i - 3))

        def p1_v(v):
            v.memset(konst[:, 0:1], EPS)
            for i in range(9):
                r = nrows[i]
                tile = xt[i % 3]
                o = 6 * (i % 2)
                v.wait(dXT[i % 3], 16 * (i // 3 + 1))
                v.reduce_sum(col[0:r, o:o + 1], tile[0:r, :], axis=AX.X)
                v.tensor_scalar(col[0:r, o + 1:o + 2], col[0:r, o:o + 1], 1.0 / D, None, ALU.mult)
                v.tensor_scalar(lnc[0:r, :], tile[0:r, :], col[0:r, o + 1:o + 2], None, ALU.subtract)
                v.tensor_mul(tile[0:r, :], lnc[0:r, :], lnc[0:r, :])
                v.reduce_sum(col[0:r, o + 2:o + 3], tile[0:r, :], axis=AX.X)
                vmark[("sumsq", i)] = ctx.vv.n
                v.wait(ctx.ss, rstd_end_fwd(i))
                v.tensor_scalar(tile[0:r, :], lnc[0:r, :], col[0:r, o + 4:o + 5], None, ALU.mult)
                vmark[("xn", i)] = ctx.vv.n
        ctx.vsec(p1_v)

        se_mark = {}

        def rstd_end(i):
            return ss0 + (2 * (i + 1) if i <= 2 else 24 + 18 * (i - 3))

        def p1_s(s):
            s.wait(dC, const_loads)

            def rstd(i):
                r = nrows[i]
                o = 6 * (i % 2)
                s.wait(ctx.vv, vmark[("sumsq", i)])
                s.activation(col[0:r, o + 3:o + 4], col[0:r, o + 2:o + 3], AF.Ln,
                             bias=konst[0:r, 0:1], scale=1.0 / D)
                s.activation(col[0:r, o + 4:o + 5], col[0:r, o + 3:o + 4], AF.Exp, scale=-0.5)
                assert ctx.ss.n == rstd_end(i), (i, ctx.ss.n)

            def evacs(i):
                r = nrows[i]
                for c in range(16):
                    gx = i * 16 + c
                    par = gx % 2
                    s.wait(tT, gx + 1)
                    s.copy(xnfm[c][:, coff[i]:coff[i] + r], pT[par][0:128, 0:r])
                    se_mark[gx] = ctx.ss.n

            for i in range(3):
                rstd(i)
            for i in range(3, 9):
                evacs(i - 3)
                rstd(i)
            for i in range(6, 9):
                evacs(i)
        ctx.ssec(p1_s)
        assert ctx.ss.n == ss0 + 18 + 144

        def p1_t(t):
            t.wait_ge(dC.h, const_loads)
            for i in range(9):
                r = nrows[i]
                tile = xt[i % 3]
                t.wait_ge(ctx.vv.h, vmark[("xn", i)])
                for c in range(16):
                    gx = i * 16 + c
                    par = gx % 2
                    if gx >= 2:
                        t.wait_ge(ctx.ss.h, se_mark[gx - 2])
                    inc(t.transpose(pT[par][0:128, 0:r], tile[0:r, c * 128:(c + 1) * 128],
                                    idn[0:r, 0:r]), tT, 1)
        ctx.tsec(p1_t)

        ctx.barrier()

        tG0 = tG.n
        ss1 = ctx.ss.n

        def p2_t(t):
            t.wait_ge(dC.h, const_loads)
            for gi, (go, gn) in enumerate(grps):
                dst = [pA[0], pB[0], pC[0]][gi]
                last = None
                for c in range(16):
                    last = t.matmul(dst[0:40, 0:gn], wabs[:, c * 40:(c + 1) * 40],
                                    xnfm[c][:, go:go + gn], start=(c == 0), stop=(c == 15))
                inc(last, tG, 1)
        ctx.tsec(p2_t)

        def p2_s(s):
            s.wait(dC, const_loads)
            for gi, (go, gn) in enumerate(grps):
                src = [pA[0], pB[0], pC[0]][gi]
                s.wait(tG, tG0 + gi + 1)
                s.activation(smalls[0:40, go:go + gn], src[0:40, 0:gn], AF.Identity,
                             bias=babs[0:40, 0:1])
        ctx.ssec(p2_s)

        def p2_st(sy):
            sy.wait_ge(ctx.ss.h, ss1 + 3)
            inc(sy.dma_start(out=smallL_d[:, :], in_=smalls[0:40, 3:FW]), dS, 16)
        ctx.sysec(p2_st)

        vv1 = ctx.vv.n

        def p3_v(v):
            for c in range(16):
                v.tensor_copy(xnbf[c][:], xnfm[c][:])
        ctx.vsec(p3_v)

        def p3_st(sy):
            for c in range(16):
                sy.wait_ge(ctx.vv.h, vv1 + c + 1)
                inc(sy.dma_start(out=xnfm_d[c * 128:(c + 1) * 128, :], in_=xnbf[c][:, 3:FW]), dS, 16)
        ctx.sysec(p3_st)

        ctx.barrier()
        es_fm.close()

        wbuf = [ctx.sbuf(f"wbuf{c}", [128, D], BF16) for c in range(16)]

        tG1 = tG.n
        dW0 = dW.n
        ss2 = ctx.ss.n
        vv2 = ctx.vv.n

        def s_evac_n(gx, gi):
            if gx < 48:
                return ss2 + 4 * gx + gi + 1
            return ss2 + 192 + 3 * (gx - 48) + gi + 1

        def s_silu_n(gx):
            return ss2 + 4 * gx + 4

        def v_conv_n(gx):
            ksc = min(max(gx - 16, 0), 16)
            return vv2 + 4 * gx + ksc + 4

        def v_kscale_n(j):
            return v_conv_n(16 + j) + 1

        def p4_g(g):
            for m in range(4):
                if m >= 1:
                    g.wait_ge(tG.h, tG1 + 48 * m)
                for c in range(16):
                    inc(g.dma_start(out=wbuf[c][:], in_=Wqkvg[m, c * 128:(c + 1) * 128, :]), dW, 16)
        ctx.gsec(p4_g)

        def p4_t(t):
            for m in range(4):
                t.wait_ge(dW.h, dW0 + 256 * (m + 1))
                for j in range(16):
                    gx = m * 16 + j
                    if gx >= 2:
                        t.wait_ge(ctx.ss.h, s_evac_n(gx - 2, 2))
                    par = gx % 2
                    for gi, (go, gn) in enumerate(grps):
                        dst = [pA[par], pB[par], pC[par]][gi]
                        last = None
                        for c in range(16):
                            last = t.matmul(dst[0:128, 0:gn], wbuf[c][:, j * 128:(j + 1) * 128],
                                            xnbf[c][:, go:go + gn], start=(c == 0), stop=(c == 15))
                        inc(last, tG, 1)
        ctx.tsec(p4_t)

        def p4_s(s):
            for m in range(4):
                for j in range(16):
                    gx = m * 16 + j
                    par = gx % 2
                    if m < 3 and gx >= 2:
                        s.wait(ctx.vv, v_conv_n(gx - 2))
                    if m == 3 and j >= 2:
                        s.wait(dGB[j % 2], 16 * ((j - j % 2) // 2))
                    for gi, (go, gn) in enumerate(grps):
                        src = [pA[par], pB[par], pC[par]][gi]
                        s.wait(tG, tG1 + 3 * gx + gi + 1)
                        if m < 3:
                            s.activation(qpre[par][:, go:go + gn], src[0:128, 0:gn], AF.Identity)
                        else:
                            s.activation(gbf[j % 2][:, go:go + gn], src[0:128, 0:gn], AF.Sigmoid)
                    if m < 3:
                        s.wait(ctx.vv, v_conv_n(gx))
                        if m == 0 and j >= 2:
                            s.wait(dOBq, 16 * j)
                        elif m == 1:
                            if j < 2:
                                s.wait(dOBq, 16 * 16)
                            else:
                                s.wait(ctx.vv, v_kscale_n(j - 2))
                        elif m == 2:
                            if j < 2:
                                s.wait(ctx.vv, v_kscale_n(15))
                            else:
                                s.wait(dOBv, 16 * j)
                        s.activation(obf[par][:], cvo[2 * par + 1][:], AF.Silu,
                                     bias=cb[0:128, m * 16 + j:m * 16 + j + 1])
        ctx.ssec(p4_s)
        assert ctx.ss.n == ss2 + 240, ctx.ss.n - ss2

        def p4_v(v):
            v.wait(dC, const_loads)
            for m in range(3):
                for j in range(16):
                    gx = m * 16 + j
                    par = gx % 2
                    v.wait(ctx.ss, s_evac_n(gx, 2))
                    src = qpre[par]
                    ping, pong = cvo[2 * par], cvo[2 * par + 1]
                    v.tensor_scalar(ping[:], src[:, 0:TC],
                                    cw[:, m * 64 + 0 * 16 + j:m * 64 + 0 * 16 + j + 1], None, ALU.mult)
                    cur, oth = ping, pong
                    for jj in (1, 2, 3):
                        v.scalar_tensor_tensor(oth[:], src[:, jj:jj + TC],
                                               cw[:, m * 64 + jj * 16 + j:m * 64 + jj * 16 + j + 1],
                                               cur[:], ALU.mult, ALU.add)
                        cur, oth = oth, cur
                    assert ctx.vv.n == v_conv_n(gx), (gx, ctx.vv.n - vv2)
                    if m == 1:
                        v.wait(ctx.ss, s_silu_n(gx))
                        if j >= 1:
                            v.wait(dOK, 16 * j)
                        v.tensor_scalar(okbf[:], obf[par][:], RDK, None, ALU.mult)
        ctx.vsec(p4_v)

        def p4_st(sy):
            for m in range(4):
                for j in range(16):
                    gx = m * 16 + j
                    par = gx % 2
                    if m == 3:
                        sy.wait_ge(ctx.ss.h, s_evac_n(gx, 2))
                        inc(sy.dma_start(out=proj_d[3][j * 128:(j + 1) * 128, :],
                                         in_=gbf[j % 2][:, 3:FW]), dGB[j % 2], 16)
                    elif m == 1:
                        sy.wait_ge(ctx.vv.h, v_kscale_n(j))
                        inc(sy.dma_start(out=proj_d[1][j * 128:(j + 1) * 128, :],
                                         in_=okbf[:]), dOK, 16)
                    else:
                        sy.wait_ge(ctx.ss.h, s_silu_n(gx))
                        sem = dOBq if m == 0 else dOBv
                        inc(sy.dma_start(out=proj_d[m][j * 128:(j + 1) * 128, :],
                                         in_=obf[par][:]), sem, 16)
        ctx.sysec(p4_st)

        def fin(sy):
            for sem in (dS, dOBq, dOBv, dGB[0], dGB[1], dOK):
                sy.wait_ge(sem.h, sem.n)
        ctx.sysec(fin)

    es_ph.close()
    ctx.cur, ctx.prefix = oc, op
    return ctx


def _l1_inputs(inp):
    x = np.asarray(inp["L0_attn_x"], np.float32)[0]
    Wqkvg = np.ascontiguousarray(np.stack([inp["L0_attn_Wq"], inp["L0_attn_Wk"],
                                           inp["L0_attn_Wv"], inp["L0_attn_Wg"]]).astype(np.float32))
    Wab = np.concatenate(
        [inp["L0_attn_Wa"], inp["L0_attn_Wb"], np.asarray(inp["L0_ffn_gate"]).T], axis=1).astype(np.float32)
    Wab = np.ascontiguousarray(Wab.reshape(16, 128, 40).transpose(1, 0, 2).reshape(128, 640))
    bab = np.ascontiguousarray(np.concatenate(
        [inp["L0_attn_ba"], inp["L0_attn_bb"], np.zeros(8, np.float32)]).astype(np.float32)[:, None])
    convw = np.stack([inp["L0_attn_qcw"], inp["L0_attn_kcw"],
                      inp["L0_attn_vcw"]]).astype(np.float32)
    convw = np.ascontiguousarray(convw.reshape(3, 16, 128, 4).transpose(2, 0, 3, 1).reshape(128, 192))
    convb = np.stack([inp["L0_attn_qcb"], inp["L0_attn_kcb"],
                      inp["L0_attn_vcb"]]).astype(np.float32)
    convb = np.ascontiguousarray(convb.reshape(3, 16, 128).transpose(2, 0, 1).reshape(128, 48))
    iden = np.eye(128, dtype=np.float32)
    maps = []
    for c in range(NC):
        s = c * TC
        xhc = x[s - 3:s] if c > 0 else np.zeros((3, D), np.float32)
        maps.append({
            "xm": np.ascontiguousarray(x[s:s + TC]),
            "xh": np.ascontiguousarray(xhc),
            "Wqkvg": Wqkvg, "Wab": Wab, "bab": bab,
            "convw": convw, "convb": convb, "iden": iden,
        })
    return maps


def run_l1(inp, **kw):
    ctx = build_l1()
    maps = _l1_inputs(inp)
    res = run_bass_kernel_spmd(ctx.nc, maps, core_ids=list(range(NC)), **kw)
    return res.results


class Prog:
    def __init__(self, ctx):
        self.ctx = ctx
        self.q = {"t": [], "v": [], "s": [], "g": [], "y": []}
        self.marks = {}
        self.tt = ctx.sem("tt")
        self.tt_p = self.tt.n
        self.vv_p = ctx.vv.n
        self.ss_p = ctx.ss.n
        self.gg_p = ctx.gg.n

    def _resolve(self, after):
        out = []
        for a in after or ():
            if isinstance(a, tuple):
                out.append(a)
            else:
                out.append(self.marks[a])
        return out

    def t(self, emit, after=(), mark=None):
        w = self._resolve(after)
        self.q["t"].append((emit, w, self.tt))
        self.tt_p += 1
        if mark:
            self.marks[mark] = (self.tt, self.tt_p)

    def v(self, emit, after=(), mark=None):
        w = self._resolve(after)
        self.q["v"].append((emit, w, None))
        self.vv_p += 1
        if mark:
            self.marks[mark] = (self.ctx.vv, self.vv_p)

    def s(self, emit, after=(), mark=None):
        w = self._resolve(after)
        self.q["s"].append((emit, w, None))
        self.ss_p += 1
        if mark:
            self.marks[mark] = (self.ctx.ss, self.ss_p)

    def g(self, emit, after=(), mark=None):
        w = self._resolve(after)
        self.q["g"].append((emit, w, None))
        self.gg_p += 1
        if mark:
            self.marks[mark] = (self.ctx.gg, self.gg_p)

    def y(self, emit, dma_sem, after=(), mark=None):
        w = self._resolve(after)
        self.q["y"].append((emit, w, dma_sem))
        if not hasattr(dma_sem, "np_"):
            dma_sem.np_ = dma_sem.n
        dma_sem.np_ += 16
        if mark:
            self.marks[mark] = (dma_sem, dma_sem.np_)

    def gd(self, emit, dma_sem, after=(), mark=None):
        w = self._resolve(after)
        self.q["g"].append(("dma", emit, w, dma_sem))
        if not hasattr(dma_sem, "np_"):
            dma_sem.np_ = dma_sem.n
        dma_sem.np_ += 16
        if mark:
            self.marks[mark] = (dma_sem, dma_sem.np_)

    def mark_now(self, name, sem, val):
        self.marks[name] = (sem, val)

    def emit(self):
        ctx = self.ctx

        def run_t(t):
            for emit, waits, sem in self.q["t"]:
                for (s, v) in waits:
                    t.wait_ge(s.h, v)
                ins = emit(t)
                inc(ins, sem, 1)

        def run_w(w, ops):
            for entry in ops:
                if entry[0] == "dma":
                    _, emit, waits, dsem = entry
                    for (s, v) in waits:
                        w.wait(s, v)
                    ins = emit(w)
                    inc(ins, dsem, 16)
                else:
                    emit, waits, _ = entry
                    for (s, v) in waits:
                        w.wait(s, v)
                    emit(w)

        def run_y(sy):
            for emit, waits, dsem in self.q["y"]:
                for (s, v) in waits:
                    sy.wait_ge(s.h, v)
                ins = emit(sy)
                inc(ins, dsem, 16)

        ctx.tsec(run_t)
        ctx.vsec(lambda v: run_w(v, self.q["v"]))
        ctx.ssec(lambda s: run_w(s, self.q["s"]))
        if self.q["g"]:
            ctx.gsec(lambda g: run_w(g, self.q["g"]))
        ctx.sysec(run_y)
        assert ctx.vv.n == self.vv_p and ctx.ss.n == self.ss_p and ctx.gg.n == self.gg_p


def drams_l2(ctx, inner):
    mk = ctx.dint if inner else ctx.din
    d = {}
    for nm in ("qx", "kx", "vx", "gx"):
        d[nm] = mk(nm, [D, TC], BF16)
    d["smx"] = mk("smx", [32, TC], F32)
    d["lnw"] = ctx.din("lnw", [1, DK], F32)
    d["lnb"] = ctx.din("lnb", [1, DK], F32)
    d["masks"] = ctx.din("masks", [3, CH, CH], F32)
    d["idenf"] = ctx.din("idenf", [128, 128], F32)
    d["idenb"] = ctx.din("idenb", [128, 128], BF16)
    d["scr"] = ctx.dint("scr_l2", [3, 64, CH], F32)
    d["og"] = (ctx.dint if inner else ctx.dout)("og_d", [T, 256], BF16)
    return d


def build_l2():
    ctx = Ctx()
    dr = drams_l2(ctx, False)
    phase_l2(ctx, dr)
    return ctx


def phase_l2(ctx, dr):
    qx, kx, vx, gx2 = dr["qx"], dr["kx"], dr["vx"], dr["gx"]
    smx = dr["smx"]
    lnw_d, lnb_d = dr["lnw"], dr["lnb"]
    masks_d, idenf_d, idenb_d = dr["masks"], dr["idenf"], dr["idenb"]
    scr_d = dr["scr"]
    og_d = dr["og"]
    es_ph = ExitStack()
    oc, op = ctx.cur, ctx.prefix
    ctx.cur, ctx.prefix = es_ph, "L2_"

    NCH = 64
    GRP = 8

    if True:
        Kfm = [ctx.sbuf(f"Kfm{h}", [128, T], BF16) for h in range(2)]
        Qfm = [ctx.sbuf(f"Qfm{h}", [128, T], BF16) for h in range(2)]
        Ktm = [ctx.sbuf(f"Ktm{i}", [128, 128], BF16) for i in range(2)]
        Vtm = [ctx.sbuf(f"Vtm{i}", [128, 128], BF16) for i in range(2)]
        Gtm = [ctx.sbuf(f"Gtm{i}", [128, 128], BF16) for i in range(2)]
        ogb = [ctx.sbuf(f"ogb{i}", [128, 128], BF16) for i in range(2)]
        FRN = [ctx.sbuf(f"FRN{i}", [1, GRP * CH], F32) for i in range(2)]
        FRP = [ctx.sbuf(f"FRP{i}", [1, GRP * CH], F32) for i in range(2)]
        FRC = [ctx.sbuf(f"FRC{i}", [1, GRP * CH], F32) for i in range(2)]
        Alg = ctx.sbuf("Alg", [64, CH], F32)
        Blg = ctx.sbuf("Blg", [64, CH], F32)
        asig = ctx.sbuf("asig", [64, CH], F32)
        la = ctx.sbuf("la", [64, CH], F32)
        bsig = ctx.sbuf("bsig", [64, CH], F32)
        lb = ctx.sbuf("lb", [64, CH], F32)
        CUM = ctx.sbuf("CUM", [64, CH + 2], F32)
        cplnb = ctx.sbuf("cplnb", [64, CH], F32)
        negc = ctx.sbuf("negc", [64, CH], F32)
        Ccol = ctx.sbuf("Ccol", [128, 64], F32)
        CPcol = ctx.sbuf("CPcol", [128, 64], F32)
        bcol = ctx.sbuf("bcol", [128, 64], F32)
        LBcol = ctx.sbuf("LBcol", [128, 64], F32)
        cplnbcol = ctx.sbuf("cplnbcol", [128, 64], F32)
        negccol = ctx.sbuf("negccol", [128, 64], F32)
        ccrow = ctx.sbuf("ccrow", [1, 64], F32)
        CBsb = ctx.sbuf("CBsb", [128, 64], F32)
        ECcol = ctx.sbuf("ECcol", [128, 64], F32)
        ecp = ctx.sbuf("ecp", [128, 64], F32)
        bec = ctx.sbuf("bec", [128, 64], F32)
        cbmc = ctx.sbuf("cbmc", [128, 64], F32)
        KSC = ctx.sbuf("KSC", [128, 64], F32)
        ACB = ctx.sbuf("ACB", [128, 64], F32)
        ones1 = ctx.sbuf("ones1", [1, 128], F32)
        msk = [ctx.sbuf(f"msk{i}", [128, 128], F32) for i in range(3)]
        idnf = ctx.sbuf("idnf", [128, 128], F32)
        idnb = ctx.sbuf("idnb", [128, 128], BF16)
        lnwr = ctx.sbuf("lnwr", [1, DK], F32)
        lnbr = ctx.sbuf("lnbr", [1, DK], F32)
        LNWbc = ctx.sbuf("LNWbc", [128, DK], F32)
        LNBbc = ctx.sbuf("LNBbc", [128, DK], F32)
        konst = ctx.sbuf("konst", [128, 1], F32)
        eN = ctx.sbuf("eN", [128, 128], F32)
        eNT = ctx.sbuf("eNT", [128, 128], F32)
        eP = ctx.sbuf("eP", [128, 128], F32)
        kkSL = ctx.sbuf("kkSL", [128, 128], F32)
        kkSU = ctx.sbuf("kkSU", [128, 128], F32)
        qkUI = ctx.sbuf("qkUI", [128, 128], F32)
        Nbf = ctx.sbuf("Nbf", [128, 128], BF16)
        NTbf = ctx.sbuf("NTbf", [128, 128], BF16)
        Mqkbf = ctx.sbuf("Mqkbf", [128, 128], BF16)
        NP = [ctx.sbuf(f"NP{l}", [128, 256], BF16) for l in range(1, 7)]
        Xb = [ctx.sbuf(f"Xb{i}", [128, 256], BF16) for i in range(2)]
        Wfm = ctx.sbuf("Wfm", [128, 128], BF16)
        Rbf = ctx.sbuf("Rbf", [128, 128], BF16)
        Kpbf = ctx.sbuf("Kpbf", [128, 128], BF16)
        O1 = ctx.sbuf("O1", [128, 128], F32)
        Of = ctx.sbuf("Of", [128, 128], F32)
        Oc = ctx.sbuf("Oc", [128, 128], F32)
        gsq = ctx.sbuf("gsq", [128, 128], F32)
        gapp = ctx.sbuf("gapp", [128, 128], F32)
        gcol = ctx.sbuf("gcol", [128, 8], F32)
        Zf = ctx.sbuf("Zf", [128, 128], F32)
        Zbf = ctx.sbuf("Zbf", [128, 128], BF16)

        pKDQ = ctx.psum("pKDQ", [128, 512], F32)
        pQS = ctx.psum("pQS", [128, 512], F32)
        pSOL = ctx.psum("pSOL", [128, 512], F32)
        pWZ = ctx.psum("pWZ", [128, 512], F32)
        pQZ = ctx.psum("pQZ", [128, 512], F32)
        pMK = ctx.psum("pMK", [128, 512], F32)
        pKP = ctx.psum("pKP", [128, 512], F32)
        pTB = ctx.psum("pTB", [128, 1024], BF16)
        pST = pSOL

        dC = ctx.sem("dC")
        dKQ = ctx.sem("dKQ")
        dAB = ctx.sem("dAB")
        dBNw = ctx.sem("dBNw")
        dFR = [ctx.sem("dFR0"), ctx.sem("dFR1")]
        dKT = [ctx.sem("dKT0"), ctx.sem("dKT1")]
        dVT = [ctx.sem("dVT0"), ctx.sem("dVT1")]
        dGT = [ctx.sem("dGT0"), ctx.sem("dGT1")]
        dOG = [ctx.sem("dOG0"), ctx.sem("dOG1")]

        ctx.start()
        P = Prog(ctx)

        P.y(lambda sy: sy.dma_start(out=msk[0][:], in_=masks_d[0]), dC)
        P.y(lambda sy: sy.dma_start(out=msk[1][:], in_=masks_d[1]), dC)
        P.y(lambda sy: sy.dma_start(out=msk[2][:], in_=masks_d[2]), dC)
        P.y(lambda sy: sy.dma_start(out=idnf[:], in_=idenf_d[:, :]), dC)
        P.y(lambda sy: sy.dma_start(out=idnb[:], in_=idenb_d[:, :]), dC)
        P.y(lambda sy: sy.dma_start(out=lnwr[:], in_=lnw_d[:, :]), dC)
        P.y(lambda sy: sy.dma_start(out=lnbr[:], in_=lnb_d[:, :]), dC)
        kx4 = kx.rearrange("(r dd) t -> dd r t", dd=256)
        qx4 = qx.rearrange("(r dd) t -> dd r t", dd=256)
        for h in range(2):
            P.y(lambda sy, h=h: sy.dma_start(out=Kfm[h][:], in_=kx4[h * 128:(h + 1) * 128]), dKQ)
            P.y(lambda sy, h=h: sy.dma_start(out=Qfm[h][:], in_=qx4[h * 128:(h + 1) * 128]), dKQ)
        P.mark_now("dC_all", dC, dC.np_)
        P.mark_now("dKQ_all", dKQ, dKQ.np_)

        P.v(lambda v: v.memset(ones1[:], 1.0), mark="ones")
        P.v(lambda v: v.memset(konst[:], EPS), mark="konst")
        P.t(lambda t: t.matmul(pST[:, 0:DK], ones1[:], lnwr[:], start=True, stop=True),
            after=["ones", "dC_all"], mark="t_lnw")
        P.s(lambda s: s.copy(LNWbc[:], pST[:, 0:DK]), after=["t_lnw"], mark="s_lnw")
        P.t(lambda t: t.matmul(pST[:, 0:DK], ones1[:], lnbr[:], start=True, stop=True),
            after=["s_lnw"], mark="t_lnb")
        P.s(lambda s: s.copy(LNBbc[:], pST[:, 0:DK]), after=["t_lnb"], mark="s_lnb")

        def setup(h, guard):
            hp = f"h{h}"
            smx4 = smx.rearrange("(r q) (lc t) -> q r lc t", q=4, t=CH)
            P.y(lambda sy, h=h: sy.dma_start(out=Alg[:], in_=smx4[h]), dAB, after=guard)
            P.y(lambda sy, h=h: sy.dma_start(out=Blg[:], in_=smx4[2 + h]), dAB)
            P.mark_now(f"{hp}_ab", dAB, dAB.np_)
            P.s(lambda s: s.activation(asig[:], Alg[:], AF.Sigmoid), after=[f"{hp}_ab"] + guard)
            P.s(lambda s: s.activation(la[:], asig[:], AF.Ln), mark=f"{hp}_la")
            P.s(lambda s: s.activation(bsig[:], Blg[:], AF.Sigmoid), mark=f"{hp}_bsig")
            P.s(lambda s: s.activation(lb[:], bsig[:], AF.Ln), mark=f"{hp}_lb")
            P.v(lambda v: v.memset(CUM[:, 0:1], 0.0), after=guard)
            P.v(lambda v: v.tensor_tensor_scan(CUM[:, 1:CH + 1], la[:], la[:], 0.0,
                                               ALU.add, ALU.bypass), after=[f"{hp}_lb"],
                mark=f"{hp}_cum")
            P.v(lambda v: v.tensor_add(cplnb[:], CUM[:, 0:CH], lb[:]), mark=f"{hp}_cplnb")
            P.v(lambda v: v.tensor_scalar(negc[:], CUM[:, 1:CH + 1], -1.0, None, ALU.mult),
                mark=f"{hp}_negc")
            P.y(lambda sy: sy.dma_start(out=scr_d[0], in_=cplnb[:]), dBNw,
                after=[f"{hp}_cplnb"] + guard)
            P.y(lambda sy: sy.dma_start(out=scr_d[1], in_=negc[:]), dBNw, after=[f"{hp}_negc"])
            P.y(lambda sy: sy.dma_start(out=scr_d[2], in_=CUM[:, 1:CH + 1]), dBNw)
            P.mark_now(f"{hp}_bnw", dBNw, dBNw.np_)
            P.t(lambda t: t.transpose(pST[0:128, 0:64], CUM[:, 1:CH + 1], idnf[0:64, 0:64]),
                after=[f"{hp}_negc", "dC_all", "s_lnb"], mark=f"{hp}_tc")
            P.s(lambda s: s.copy(Ccol[:], pST[0:128, 0:64]), after=[f"{hp}_tc"], mark=f"{hp}_ccol")
            P.t(lambda t: t.transpose(pST[0:128, 0:64], CUM[:, 0:CH], idnf[0:64, 0:64]),
                after=[f"{hp}_ccol"], mark=f"{hp}_tcp")
            P.s(lambda s: s.copy(CPcol[:], pST[0:128, 0:64]), after=[f"{hp}_tcp"], mark=f"{hp}_cpcol")
            P.t(lambda t: t.transpose(pST[0:128, 0:64], bsig[:], idnf[0:64, 0:64]),
                after=[f"{hp}_cpcol"], mark=f"{hp}_tb")
            P.s(lambda s: s.copy(bcol[:], pST[0:128, 0:64]), after=[f"{hp}_tb"], mark=f"{hp}_bcol")
            P.t(lambda t: t.transpose(pST[0:1, 0:64], CUM[:, CH:CH + 1], idnf[0:64, 0:64]),
                after=[f"{hp}_bcol"], mark=f"{hp}_tcc")
            P.s(lambda s: s.copy(ccrow[:], pST[0:1, 0:64]), after=[f"{hp}_tcc"], mark=f"{hp}_ccrow")
            P.t(lambda t: t.matmul(pST[0:128, 0:64], ones1[:], ccrow[:], start=True, stop=True),
                after=[f"{hp}_ccrow", "ones"], mark=f"{hp}_tcb")
            P.s(lambda s: s.copy(CBsb[:], pST[0:128, 0:64]), after=[f"{hp}_tcb"], mark=f"{hp}_cb")
            P.s(lambda s: s.activation(LBcol[:], bcol[:], AF.Ln), after=[f"{hp}_bcol"],
                mark=f"{hp}_lbcol")
            P.v(lambda v: v.tensor_add(cplnbcol[:], CPcol[:], LBcol[:]),
                after=[f"{hp}_lbcol", f"{hp}_cpcol"], mark=f"{hp}_cpc")
            P.v(lambda v: v.tensor_scalar(negccol[:], Ccol[:], -1.0, None, ALU.mult),
                after=[f"{hp}_ccol"], mark=f"{hp}_ngc")
            P.s(lambda s: s.activation(ECcol[:], Ccol[:], AF.Exp), mark=f"{hp}_ec")
            P.s(lambda s: s.activation(ecp[:], CPcol[:], AF.Exp), mark=f"{hp}_ecp")
            P.v(lambda v: v.tensor_mul(bec[:], bcol[:], ecp[:]), after=[f"{hp}_ecp", f"{hp}_bcol"],
                mark=f"{hp}_bec")
            P.v(lambda v: v.tensor_sub(cbmc[:], CBsb[:], Ccol[:]), after=[f"{hp}_cb"],
                mark=f"{hp}_cbmc")
            P.s(lambda s: s.activation(KSC[:], cbmc[:], AF.Exp), after=[f"{hp}_cbmc"], mark=f"{hp}_ksc")
            P.s(lambda s: s.activation(ACB[:], CBsb[:], AF.Exp), mark=f"{hp}_acb")
            P.v(lambda v: v.memset(Zf[:], 0.0), after=guard)
            P.v(lambda v: v.memset(Zbf[:], 0.0), mark=f"{hp}_zbf_init")

        def refill(h, g, gg):
            par2 = gg % 2
            rs = slice(g * GRP, (g + 1) * GRP)
            guard = [f"h{h}_bnw"]
            if gg >= 2:
                guard.append(f"g{(gg - 2) * GRP + GRP - 1}_dp")
            P.y(lambda sy: sy.dma_start(out=FRP[par2][:],
                                        in_=scr_d[0:1, rs].rearrange("x a b -> x (a b)")),
                dFR[par2], after=guard)
            P.y(lambda sy: sy.dma_start(out=FRN[par2][:],
                                        in_=scr_d[1:2, rs].rearrange("x a b -> x (a b)")),
                dFR[par2])
            P.y(lambda sy: sy.dma_start(out=FRC[par2][:],
                                        in_=scr_d[2:3, rs].rearrange("x a b -> x (a b)")),
                dFR[par2])
            P.mark_now(f"fr{gg}", dFR[par2], dFR[par2].np_)

        def chunk(h, c, gc):
            hp = f"h{h}"
            p = f"g{gc}"
            pm = f"g{gc - 1}"
            csl = slice(c * CH, (c + 1) * CH)
            KfmH, QfmH = Kfm[h], Qfm[h]
            par = gc % 2
            gg = gc // GRP
            rsl = slice((c % GRP) * CH, (c % GRP + 1) * CH)

            nload = (gc - par) // 2 + 1
            rr, lc = c // 8, c % 8
            rows = slice(rr * 256 + h * 128, rr * 256 + (h + 1) * 128)
            lsl = slice(lc * 128, (lc + 1) * 128)
            ld_guard = [f"g{gc - 2}_kp"] if gc >= 2 else []
            P.y(lambda sy, rows=rows, lsl=lsl: sy.dma_start(
                out=Ktm[par][:], in_=kx[rows, lsl], transpose=True), dKT[par], after=ld_guard)
            ld_guard = [f"g{gc - 2}_x0b"] if gc >= 2 else []
            P.y(lambda sy, rows=rows, lsl=lsl: sy.dma_start(
                out=Vtm[par][:], in_=vx[rows, lsl], transpose=True), dVT[par], after=ld_guard)
            ld_guard = [f"g{gc - 2}_og"] if gc >= 2 else []
            P.y(lambda sy, rows=rows, lsl=lsl: sy.dma_start(
                out=Gtm[par][:], in_=gx2[rows, lsl], transpose=True), dGT[par], after=ld_guard)

            guard1 = ["dKQ_all", f"fr{gg}"]
            if gc > 0:
                guard1 += [f"{pm}_qkui"]
            P.t(lambda t: t.matmul(pKDQ[:, 0:128], KfmH[:, csl], KfmH[:, csl],
                                   start=True, stop=True), after=guard1, mark=f"{p}_kk")
            P.t(lambda t: t.matmul(pKDQ[:, 128:256], ones1[:], FRN[gg % 2][0:1, rsl],
                                   start=True, stop=True), mark=f"{p}_dn")
            P.t(lambda t: t.matmul(pKDQ[:, 256:384], ones1[:], FRP[gg % 2][0:1, rsl],
                                   start=True, stop=True), mark=f"{p}_dnt")
            P.t(lambda t: t.matmul(pKDQ[:, 384:512], ones1[:], FRC[gg % 2][0:1, rsl],
                                   start=True, stop=True), mark=f"{p}_dp")
            P.t(lambda t: t.matmul(pQS[:, 0:128], KfmH[:, csl], QfmH[:, csl],
                                   start=True, stop=True), mark=f"{p}_qk")

            g_eN = [f"{p}_dp", f"{hp}_cpc"] + ([f"{pm}_nbf"] if gc > 0 else [])
            P.v(lambda v: v.tensor_scalar(eN[:], pKDQ[:, 128:256], cplnbcol[:, c:c + 1],
                                          0.0, ALU.add, ALU.min), after=g_eN, mark=f"{p}_ena")
            P.s(lambda s: s.activation(eN[:], eN[:], AF.Exp), after=[f"{p}_ena"], mark=f"{p}_en")
            P.v(lambda v: v.tensor_scalar(eNT[:], pKDQ[:, 256:384], negccol[:, c:c + 1],
                                          0.0, ALU.add, ALU.min),
                after=[f"{p}_dp", f"{hp}_ngc"] + ([f"{pm}_ntbf"] if gc > 0 else []),
                mark=f"{p}_enta")
            P.s(lambda s: s.activation(eNT[:], eNT[:], AF.Exp), after=[f"{p}_enta"],
                mark=f"{p}_ent")
            P.v(lambda v: v.tensor_scalar(eP[:], pKDQ[:, 384:512], negccol[:, c:c + 1],
                                          0.0, ALU.add, ALU.min),
                after=[f"{p}_dp"] + ([f"{pm}_mqkbf"] if gc > 0 else []), mark=f"{p}_epa")
            P.s(lambda s: s.activation(eP[:], eP[:], AF.Exp), after=[f"{p}_epa"], mark=f"{p}_ep")
            g_sl = [f"{p}_dp", "dC_all"] + ([f"{pm}_nbf"] if gc > 0 else [])
            P.v(lambda v: v.tensor_mul(kkSL[:], pKDQ[:, 0:128], msk[0][:]), after=g_sl,
                mark=f"{p}_kksl")
            P.v(lambda v: v.tensor_mul(kkSU[:], pKDQ[:, 0:128], msk[1][:]),
                after=([f"{pm}_ntbf"] if gc > 0 else []), mark=f"{p}_kksu")
            P.v(lambda v: v.tensor_mul(qkUI[:], pQS[:, 0:128], msk[2][:]),
                after=[f"{p}_qk"] + ([f"{pm}_mqkbf"] if gc > 0 else []), mark=f"{p}_qkui")
            g_nb = [f"{p}_en"]
            if gc > 0:
                g_nb.append(f"{pm}_tsol7")
            P.v(lambda v: v.tensor_mul(Nbf[:], eN[:], kkSL[:]), after=g_nb, mark=f"{p}_nbf")
            P.v(lambda v: v.tensor_mul(NTbf[:], eNT[:], kkSU[:]), after=[f"{p}_ent"],
                mark=f"{p}_ntbf")
            g_mq = [f"{p}_ep"]
            if gc > 0:
                g_mq.append(f"{pm}_tmqkr")
            P.v(lambda v: v.tensor_mul(Mqkbf[:], eP[:], qkUI[:]), after=g_mq, mark=f"{p}_mqkbf")

            for l in range(1, 7):
                if l == 1:
                    lA = NTbf[:]
                    lB = Nbf[:]
                else:
                    lA = NP[l - 2][:, 128:256]
                    lB = NP[l - 2][:, 0:128]
                g_sq = [f"{p}_ntbf"] if l == 1 else [f"{p}_sqe{l - 1}"]
                if gc > 0:
                    g_sq.append(f"{pm}_tsol{7 - l}")
                P.t(lambda t, lA=lA, lB=lB: t.matmul(pQS[:, 128:256], lA, lB,
                                                     start=True, stop=True),
                    after=g_sq, mark=f"{p}_sqa{l}")
                P.t(lambda t, lA=lA, lB=lB: t.matmul(pQS[:, 256:384], lB, lA,
                                                     start=True, stop=True), mark=f"{p}_sqb{l}")
                P.s(lambda s, l=l: s.copy(NP[l - 1][:], pQS[:, 128:384]), after=[f"{p}_sqb{l}"],
                    mark=f"{p}_sqe{l}")

            g_x0 = [f"{hp}_bec", (dKT[par], 16 * nload)]
            if gc >= 1:
                g_x0.append(f"{pm}_tsol7")
            P.v(lambda v: v.tensor_scalar(Xb[0][:, 0:128], Ktm[par][:], bec[:, c:c + 1],
                                          None, ALU.mult), after=g_x0, mark=f"{p}_x0a")
            P.v(lambda v: v.tensor_scalar(Xb[0][:, 128:256], Vtm[par][:], bcol[:, c:c + 1],
                                          None, ALU.mult),
                after=[f"{hp}_bcol", (dVT[par], 16 * nload)], mark=f"{p}_x0b")
            P.v(lambda v: v.tensor_scalar(Kpbf[:], Ktm[par][:], KSC[:, c:c + 1], None, ALU.mult),
                after=[f"{hp}_ksc"] + ([f"{pm}_tkpr"] if gc > 0 else []), mark=f"{p}_kp")

            xp = 0
            for j in range(1, 8):
                if j < 7:
                    g_s = [f"{p}_sqe{7 - j}", f"{p}_x0b" if j == 1 else f"{p}_x{j - 1}"]
                else:
                    g_s = [f"{p}_x{j - 1}"]
                if j == 1 and gc > 0:
                    g_s.append(f"{pm}_x7")
                if j < 7:
                    lhs = NP[6 - j]
                    P.t(lambda t, lhs=lhs, xp=xp: t.matmul(pSOL[:, 0:256], lhs[:, 128:256],
                                                           Xb[xp][:], start=True, stop=True),
                        after=g_s, mark=f"{p}_tsol{j}")
                else:
                    P.t(lambda t, xp=xp: t.matmul(pSOL[:, 0:256], NTbf[:], Xb[xp][:],
                                                  start=True, stop=True),
                        after=g_s, mark=f"{p}_tsol{j}")
                g_v = [f"{p}_tsol{j}"]
                if j == 2 and gc > 0:
                    g_v.append(f"{pm}_twtr")
                if j < 7:
                    P.v(lambda v, xp=xp: v.tensor_add(Xb[1 - xp][:], Xb[xp][:], pSOL[:, 0:256]),
                        after=g_v, mark=f"{p}_x{j}")
                else:
                    P.v(lambda v, xp=xp: v.tensor_sub(Xb[1 - xp][:], Xb[xp][:], pSOL[:, 0:256]),
                        after=g_v, mark=f"{p}_x{j}")
                xp = 1 - xp
            xf = xp

            P.t(lambda t, xf=xf: t.transpose(pTB[:, 0:128], Xb[xf][:, 0:128], idnb[:]),
                after=[f"{p}_x7"] + ([f"{pm}_twz"] if gc > 0 else []), mark=f"{p}_twtr")
            P.s(lambda s: s.copy(Wfm[:], pTB[:, 0:128]), after=[f"{p}_twtr"], mark=f"{p}_wfm")

            g_wz = [f"{p}_wfm", f"{hp}_zbf_init" if c == 0 else f"{pm}_zbf"]
            if gc > 0:
                g_wz.append(f"{pm}_r")
            P.t(lambda t: t.matmul(pWZ[:, 0:128], Wfm[:], Zbf[:], start=True, stop=True),
                after=g_wz, mark=f"{p}_twz")
            g_qz = [f"{pm}_o1"] if gc > 0 else []
            P.t(lambda t: t.matmul(pQZ[:, 0:128], QfmH[:, csl], Zbf[:], start=True, stop=True),
                after=g_qz, mark=f"{p}_tqz")
            P.v(lambda v, xf=xf: v.tensor_sub(Rbf[:], Xb[xf][:, 128:256], pWZ[:, 0:128]),
                after=[f"{p}_twz"], mark=f"{p}_r")
            P.t(lambda t: t.matmul(pMK[:, 0:128], Mqkbf[:], Rbf[:], start=True, stop=True),
                after=[f"{p}_r", f"{p}_mqkbf"] + ([f"{pm}_oadd"] if gc > 0 else []),
                mark=f"{p}_tmqkr")
            P.t(lambda t: t.matmul(pKP[:, 0:128], Kpbf[:], Rbf[:], start=True, stop=True),
                after=[f"{p}_kp"] + ([f"{pm}_zstt"] if gc > 0 else []), mark=f"{p}_tkpr")
            g_o1 = [f"{p}_tqz", f"{hp}_ec"]
            if gc > 0:
                g_o1.append(f"{pm}_oadd")
            P.s(lambda s: s.activation(O1[:], pQZ[:, 0:128], AF.Identity,
                                       scale=ECcol[:, c:c + 1]), after=g_o1, mark=f"{p}_o1")
            P.v(lambda v: v.tensor_add(Of[:], O1[:], pMK[:, 0:128]),
                after=[f"{p}_o1", f"{p}_tmqkr"], mark=f"{p}_oadd")
            P.v(lambda v: v.scalar_tensor_tensor(Zf[:], Zf[:], ACB[:, c:c + 1], pKP[:, 0:128],
                                                 ALU.mult, ALU.add),
                after=[f"{p}_tkpr", f"{hp}_acb"], mark=f"{p}_zstt")
            P.v(lambda v: v.tensor_copy(Zbf[:], Zf[:]), mark=f"{p}_zbf")

            P.v(lambda v: v.reduce_sum(gcol[:, 0:1], Of[:], axis=AX.X), after=[f"{p}_oadd"])
            P.v(lambda v: v.tensor_scalar(gcol[:, 1:2], gcol[:, 0:1], 1.0 / DK, None, ALU.mult))
            P.v(lambda v: v.tensor_scalar(Oc[:], Of[:], gcol[:, 1:2], None, ALU.subtract))
            P.v(lambda v: v.tensor_mul(gsq[:], Oc[:], Oc[:]))
            P.v(lambda v: v.reduce_sum(gcol[:, 2:3], gsq[:], axis=AX.X), mark=f"{p}_gvar")
            P.s(lambda s: s.activation(gcol[:, 3:4], gcol[:, 2:3], AF.Ln,
                                       bias=konst[:, 0:1], scale=1.0 / DK),
                after=[f"{p}_gvar", "konst"])
            P.s(lambda s: s.activation(gcol[:, 4:5], gcol[:, 3:4], AF.Exp, scale=-0.5),
                mark=f"{p}_grstd")
            P.v(lambda v: v.scalar_tensor_tensor(gapp[:], Oc[:], gcol[:, 4:5], LNWbc[:],
                                                 ALU.mult, ALU.mult),
                after=[f"{p}_grstd", "s_lnw"])
            P.v(lambda v: v.tensor_add(gapp[:], gapp[:], LNBbc[:]), after=["s_lnb"])
            g_og = [(dGT[par], 16 * nload)]
            if gc >= 2:
                g_og.append((dOG[par], 16 * ((gc - par) // 2)))
            P.v(lambda v: v.tensor_mul(ogb[par][:], gapp[:], Gtm[par][:]), after=g_og,
                mark=f"{p}_og")
            P.y(lambda sy: sy.dma_start(out=og_d[c * CH:(c + 1) * CH, h * 128:(h + 1) * 128],
                                        in_=ogb[par][:]), dOG[par], after=[f"{p}_og"])

        NG = NCH // GRP
        setup(0, ["dC_all"])
        for c in range(NCH):
            if c % GRP == 0:
                refill(0, c // GRP, c // GRP)
            chunk(0, c, c)
        gl = NCH - 1
        g1 = [f"g{gl}_dp", f"g{gl}_tqz", f"g{gl}_x0a", f"g{gl}_zstt",
              f"g{gl}_o1", f"g{gl}_kp", f"g{gl}_grstd", f"g{gl}_og"]
        setup(1, g1)
        for c in range(NCH):
            if c % GRP == 0:
                refill(1, c // GRP, NG + c // GRP)
            chunk(1, c, NCH + c)

        P.emit()

        def fin(sy):
            for sem in (dOG[0], dOG[1]):
                sy.wait_ge(sem.h, sem.n)
        ctx.sysec(fin)

    es_ph.close()
    ctx.cur, ctx.prefix = oc, op
    return ctx


def _l2_inputs(stage, inp):
    masks = np.stack([
        np.tril(np.ones((CH, CH), np.float32), -1),
        np.triu(np.ones((CH, CH), np.float32), 1),
        np.triu(np.ones((CH, CH), np.float32), 0),
    ])
    idenf = np.eye(128, dtype=np.float32)
    idenb = np.eye(128, dtype=np.float32).astype(BF)
    lnw = np.asarray(inp["L0_attn_ln_w"], np.float32)[None, :]
    lnb = np.asarray(inp["L0_attn_ln_b"], np.float32)[None, :]

    def a2a(full, c):
        return np.ascontiguousarray(np.concatenate(
            [full[2 * c * 128:(2 * c + 2) * 128, r * TC:(r + 1) * TC] for r in range(NC)],
            axis=0))

    maps = []
    for c in range(NC):
        h0, h1 = 2 * c, 2 * c + 1
        smx = np.ascontiguousarray(np.concatenate(
            [np.stack([stage["alphaL"][h0, r * TC:(r + 1) * TC],
                       stage["alphaL"][h1, r * TC:(r + 1) * TC],
                       stage["betaL"][h0, r * TC:(r + 1) * TC],
                       stage["betaL"][h1, r * TC:(r + 1) * TC]]) for r in range(NC)],
            axis=0).astype(np.float32))
        maps.append({
            "qx": a2a(stage["q_full"], c),
            "kx": a2a(stage["k_full"], c),
            "vx": a2a(stage["v_full"], c),
            "gx": a2a(stage["g_full"], c),
            "smx": smx, "lnw": lnw, "lnb": lnb,
            "masks": masks, "idenf": idenf, "idenb": idenb,
        })
    return maps


def run_l2(stage, inp, **kw):
    ctx = build_l2()
    maps = _l2_inputs(stage, inp)
    res = run_bass_kernel_spmd(ctx.nc, maps, core_ids=list(range(NC)), **kw)
    return res.results


def drams_l3(ctx, inner):
    d = {}
    d["ogx"] = (ctx.dint if inner else ctx.din)("ogx", [T, 256], BF16)
    d["Wo"] = ctx.din("Wo", [D, D], F32)
    d["ebias"] = ctx.din("ebias", [1, 8], F32)
    d["e8d"] = ctx.din("e8d", [8, 8 * 128], BF16)
    d["onesd"] = ctx.din("onesd", [1, 128], F32)
    d["w13"] = ctx.din("w13", [18, D, 1024], F32)
    d["w2t"] = ctx.din("w2t", [9, 1024, D], F32)
    d["s_spill"] = ctx.dint("s_spill", [TC, D], F32)
    d["x1"] = (ctx.dint if inner else ctx.dout)("x1", [TC, D], F32)
    return d


def build_l3():
    ctx = Ctx()
    dr = drams_l3(ctx, False)
    dr["xnfm"] = ctx.din("xnfm", [D, TC], BF16)
    dr["gateL"] = ctx.din("gateL", [8, TC], F32)
    dr["iden"] = ctx.din("iden", [128, 128], F32)
    dr["xm"] = ctx.din("xm", [TC, D], F32)
    phase_l3(ctx, dr)
    return ctx


def phase_l3(ctx, dr):
    ogx_d = dr["ogx"]
    Wo_d, xnfm_d, gateL_d = dr["Wo"], dr["xnfm"], dr["gateL"]
    ebias_d, e8_d, ones_d, iden_d = dr["ebias"], dr["e8d"], dr["onesd"], dr["iden"]
    w13_d, w2_d, xm_d, s_d, x1_d = dr["w13"], dr["w2t"], dr["xm"], dr["s_spill"], dr["x1"]
    es_ph = ExitStack()
    oc, op = ctx.cur, ctx.prefix
    ctx.cur, ctx.prefix = es_ph, "L3_"

    NE = 9

    if True:
        xnfm = [ctx.sbuf(f"xnfm{c}", [128, TC], BF16) for c in range(16)]
        stile = [ctx.sbuf(f"stile{i}", [128, 512], F32) for i in range(2)]
        gl = ctx.sbuf("gl", [8, TC], F32)
        sselfm = ctx.sbuf("sselfm", [8, TC], BF16)
        ones1 = ctx.sbuf("ones1", [1, 128], F32)
        e8s = ctx.sbuf("e8s", [8, 8 * 128], BF16)
        ebs = ctx.sbuf("ebs", [1, 8], F32)
        bbc = ctx.sbuf("bbc", [128, 8], F32)
        idn = ctx.sbuf("idn", [128, 128], F32)
        rT = ctx.sbuf("rT", [128, 8], F32)
        rB = ctx.sbuf("rB", [128, 8], F32)
        rE = ctx.sbuf("rE", [128, 8], F32)
        rQ = ctx.sbuf("rQ", [128, 8], F32)
        rS = ctx.sbuf("rS", [128, 8], F32)
        rcol = ctx.sbuf("rcol", [128, 4], F32)
        sval = ctx.sbuf("sval", [128, 8], F32)
        sz = ctx.sbuf("sz", [128, 512], BF16)
        sbc = [ctx.sbuf(f"sbc{i}", [128, 512], F32) for i in range(2)]
        xres = [ctx.sbuf(f"xres{i}", [128, 512], F32) for i in range(2)]
        sres = [ctx.sbuf(f"sres{i}", [128, 512], F32) for i in range(2)]
        gsil = ctx.sbuf("gsil", [128, 512], F32)
        gmul = ctx.sbuf("gmul", [128, 512], F32)
        x1t = [ctx.sbuf(f"x1t{i}", [128, 512], F32) for i in range(2)]
        es_a = ExitStack()
        ogfm = [ctx.sbuf(f"ogfm{c}", [128, TC], BF16, es=es_a) for c in range(16)]
        wob = [ctx.sbuf(f"wob{c}", [128, D], BF16, es=es_a) for c in range(16)]

        pZ = [ctx.psum(f"pZ{i}", [128, 512], F32) for i in range(2)]
        pW2 = [ctx.psum(f"pW2{i}", [128, 512], F32) for i in range(4)]
        pMISC = ctx.psum("pMISC", [128, 512], F32)
        pBC = ctx.psum("pBC", [128, 512], F32)

        dOG = ctx.sem("dOG")
        dWO = ctx.sem("dWO")
        dXN = ctx.sem("dXN")
        dGL = ctx.sem("dGL")
        dSP = ctx.sem("dSP")
        dW1 = [ctx.sem("dW1a"), ctx.sem("dW1b")]
        dW2 = ctx.sem("dW2")
        dXR = [ctx.sem("dXR0"), ctx.sem("dXR1")]
        dSR = [ctx.sem("dSR0"), ctx.sem("dSR1")]
        dX1 = ctx.sem("dX1")
        dX1.np_ = 0

        ctx.start()
        P = Prog(ctx)

        for c in range(16):
            for j in range(8):
                rr, hh = c // 2, c % 2
                P.y(lambda sy, c=c, j=j, rr=rr, hh=hh: sy.dma_start(
                    out=ogfm[c][:, j * 128:(j + 1) * 128],
                    in_=ogx_d[rr * TC + j * 128:rr * TC + (j + 1) * 128,
                              hh * 128:(hh + 1) * 128], transpose=True), dOG)
            P.y(lambda sy, c=c: sy.dma_start(out=xnfm[c][:], in_=xnfm_d[c * 128:(c + 1) * 128, :]), dXN)
            P.gd(lambda g, c=c: g.dma_start(out=wob[c][:], in_=Wo_d[c * 128:(c + 1) * 128, :]), dWO)
        P.mark_now("og_all", dOG, dOG.np_)
        P.mark_now("xn_all", dXN, dXN.np_)
        P.mark_now("wo_all", dWO, dWO.np_)

        P.y(lambda sy: sy.dma_start(out=gl[:], in_=gateL_d[:, :]), dGL)
        P.y(lambda sy: sy.dma_start(out=ones1[:], in_=ones_d[:, :]), dGL)
        P.y(lambda sy: sy.dma_start(out=e8s[:], in_=e8_d[:, :]), dGL)
        P.y(lambda sy: sy.dma_start(out=ebs[:], in_=ebias_d[:, :]), dGL)
        P.y(lambda sy: sy.dma_start(out=idn[:], in_=iden_d[:, :]), dGL)
        P.mark_now("gl_all", dGL, dGL.np_)

        for j in range(8):
            for dg in range(4):
                gi = j * 4 + dg
                par = gi % 2
                guard = ["og_all", "wo_all"]
                if gi >= 2:
                    guard.append(f"sA{gi - 2}")
                for kc in range(16):
                    P.t(lambda t, kc=kc, j=j, dg=dg, par=par: t.matmul(
                        pZ[par][:, 0:512], ogfm[kc][:, j * 128:(j + 1) * 128],
                        wob[kc][:, dg * 512:(dg + 1) * 512],
                        start=(kc == 0), stop=(kc == 15)),
                        after=guard if kc == 0 else (), mark=f"tA{gi}" if kc == 15 else None)
                g_e = [f"tA{gi}"]
                if gi >= 2:
                    g_e.append((dSP, 16 * gi))
                P.s(lambda s, par=par: s.copy(stile[par][:], pZ[par][:, 0:512]),
                    after=g_e, mark=f"sA{gi}")
                P.y(lambda sy, j=j, dg=dg, par=par: sy.dma_start(
                    out=s_d[j * 128:(j + 1) * 128, dg * 512:(dg + 1) * 512], in_=stile[par][:]),
                    dSP, after=[f"sA{gi}"])
        P.mark_now("sp_all", dSP, dSP.np_)

        P.t(lambda t: t.matmul(pBC[0:128, 0:8], ones1[:], ebs[:], start=True, stop=True),
            after=["gl_all"], mark="t_bbc")
        P.s(lambda s: s.copy(bbc[:], pBC[0:128, 0:8]), after=["t_bbc"], mark="s_bbc")
        for j in range(8):
            g_tr = ["gl_all"]
            if j >= 1:
                g_tr.append(f"r_ss{j - 1}")
            P.t(lambda t, j=j: t.transpose(pMISC[0:128, 0:8], gl[0:8, j * 128:(j + 1) * 128],
                                           idn[0:8, 0:8]), after=g_tr, mark=f"r_tr{j}")
            g_st = [f"r_tr{j}"]
            if j >= 1:
                g_st.append(f"r_sv{j - 1}")
            P.s(lambda s: s.activation(rT[:], pMISC[0:128, 0:8], AF.Sigmoid),
                after=g_st, mark=f"r_st{j}")
            P.v(lambda v: v.tensor_add(rB[:], rT[:], bbc[:]), after=[f"r_st{j}", "s_bbc"])
            P.v(lambda v: v.reduce_max(rcol[:, 0:1], rB[:], axis=AX.X))
            P.v(lambda v: v.tensor_scalar(rE[:], rB[:], rcol[:, 0:1], None, ALU.is_equal))
            P.v(lambda v: v.tensor_tensor_scan(rQ[:], rE[:], rE[:], 0.0, ALU.add, ALU.bypass))
            P.v(lambda v: v.tensor_scalar(rQ[:], rQ[:], 1.0, None, ALU.is_equal))
            P.v(lambda v: v.tensor_mul(rS[:], rE[:], rQ[:]))
            P.v(lambda v: v.scalar_tensor_tensor(rB[:], rS[:], -1e30, rB[:], ALU.mult, ALU.add))
            P.v(lambda v: v.reduce_max(rcol[:, 1:2], rB[:], axis=AX.X))
            P.v(lambda v: v.tensor_scalar(rE[:], rB[:], rcol[:, 1:2], None, ALU.is_equal))
            P.v(lambda v: v.tensor_tensor_scan(rQ[:], rE[:], rE[:], 0.0, ALU.add, ALU.bypass))
            P.v(lambda v: v.tensor_scalar(rQ[:], rQ[:], 1.0, None, ALU.is_equal))
            P.v(lambda v: v.tensor_mul(rE[:], rE[:], rQ[:]))
            P.v(lambda v: v.tensor_add(rS[:], rS[:], rE[:]))
            P.v(lambda v: v.tensor_mul(sval[:], rT[:], rS[:]), mark=f"r_sv{j}")
            g_tv = [f"r_sv{j}", f"r_st{j}"]
            P.t(lambda t, j=j: t.transpose(pMISC[0:8, 0:128], sval[:], idn[:]),
                after=g_tv, mark=f"r_tv{j}")
            P.s(lambda s, j=j: s.copy(sselfm[0:8, j * 128:(j + 1) * 128], pMISC[0:8, 0:128]),
                after=[f"r_tv{j}"], mark=f"r_ss{j}")

        es_a.close()
        wm = [[ctx.sbuf(f"wm0_{c}", [128, 1024], BF16) for c in range(16)]]
        hti = [[ctx.sbuf(f"ht{e}_{hc}", [128, 512], BF16) for hc in range(8)] for e in range(NE)]
        xe = [ctx.sbuf(f"xe{c}", [128, 512], BF16) for c in range(16)]
        w2b = [ctx.sbuf(f"w2b{hc}", [128, 512], BF16) for hc in range(8)]
        _hsz = [ctx.sbuf(f"hsz{hc}", [128, 512], BF16) for hc in range(8)]
        hsz = [_hsz for _ in range(NE)]
        widx = 0

        def load_mat(eidx, m):
            nonlocal widx
            guard = ["wo_all", "tA31"]
            if widx >= 1:
                guard.append(f"zmm_done{widx - 1}")
            for c in range(16):
                P.gd(lambda g, eidx=eidx, m=m, c=c: g.dma_start(
                    out=wm[0][c][:], in_=w13_d[2 * eidx + m, c * 128:(c + 1) * 128, :]),
                    dW1[0], after=(guard if c == 0 else ()))
            P.mark_now(f"w{widx}", dW1[0], dW1[0].np_)
            widx += 1

        for grp in range(2):
            gsl = slice(grp * 512, (grp + 1) * 512)
            for eidx in range(NE):
                ge = grp * NE + eidx
                epar = ge % 2
                if eidx < 8:
                    g_b = ["r_ss7"]
                    if ge >= 2:
                        g_b.append(f"xe_done{ge - 2}")
                    P.t(lambda t, eidx=eidx, gsl=gsl: t.matmul(
                        pBC[:, 0:512], e8s[:, eidx * 128:(eidx + 1) * 128], sselfm[0:8, gsl],
                        start=True, stop=True), after=g_b, mark=f"t_bc{ge}")
                    P.s(lambda s, epar=epar: s.copy(sbc[epar][:], pBC[:, 0:512]),
                        after=[f"t_bc{ge}"], mark=f"s_bc{ge}")
                    for c in range(16):
                        g_xe = [f"s_bc{ge}", "xn_all"]
                        if ge >= 1:
                            g_xe.append(f"zmm_done{2 * ge - 1}")
                        P.v(lambda v, c=c, epar=epar, gsl=gsl: v.tensor_mul(
                            xe[c][:], xnfm[c][:, gsl], sbc[epar][:]),
                            after=g_xe if c == 0 else ())
                    P.mark_now(f"xe_done{ge}", ctx.vv, P.vv_p)
                else:
                    P.mark_now(f"xe_done{ge}", ctx.vv, P.vv_p)
                load_mat(eidx, 0)
                w1i = widx - 1
                for hc in range(8):
                    par = hc % 2
                    g_m = [f"w{w1i}"]
                    if eidx < 8:
                        g_m.append(f"xe_done{ge}")
                    if hc >= 2:
                        g_m.append(f"sz{ge}_{hc - 2}")
                    elif ge >= 1:
                        g_m.append(f"h{ge - 1}_{6 + hc}")
                    for c in range(16):
                        P.t(lambda t, c=c, hc=hc, par=par, eidx=eidx, gsl=gsl:
                            t.matmul(pZ[par][:, 0:512],
                                     wm[0][c][:, hc * 128:(hc + 1) * 128],
                                     (xe[c][:] if eidx < 8 else xnfm[c][:, gsl]),
                                     start=(c == 0), stop=(c == 15)),
                            after=g_m if c == 0 else (),
                            mark=f"z1mm{ge}_{hc}" if c == 15 else None)
                    g_sz = [f"z1mm{ge}_{hc}"]
                    if hc >= 1:
                        g_sz.append(f"h{ge}_{hc - 1}_free" if False else f"sz{ge}_{hc - 1}")
                    P.s(lambda s, par=par, hc=hc, eidx=eidx: s.activation(
                        hsz[eidx][hc][:], pZ[par][:, 0:512], AF.Silu),
                        after=g_sz, mark=f"sz{ge}_{hc}")
                P.mark_now(f"zmm_done{w1i}", P.tt, P.tt_p)
                load_mat(eidx, 1)
                w3i = widx - 1
                for hc in range(8):
                    par = hc % 2
                    g_m = [f"w{w3i}", f"sz{ge}_{hc}"]
                    if hc >= 2:
                        g_m.append(f"h{ge}_{hc - 2}")
                    else:
                        g_m.append(f"sz{ge}_{6 + hc}")
                    for c in range(16):
                        P.t(lambda t, c=c, hc=hc, par=par, eidx=eidx, gsl=gsl:
                            t.matmul(pZ[par][:, 0:512],
                                     wm[0][c][:, hc * 128:(hc + 1) * 128],
                                     (xe[c][:] if eidx < 8 else xnfm[c][:, gsl]),
                                     start=(c == 0), stop=(c == 15)),
                            after=g_m if c == 0 else (),
                            mark=f"z3mm{ge}_{hc}" if c == 15 else None)
                    g_h = [f"z3mm{ge}_{hc}"]
                    if grp == 1:
                        g_h.append(f"w2r0_{eidx}_{hc}")
                    P.v(lambda v, hc=hc, par=par, eidx=eidx: v.tensor_mul(
                        hti[eidx][hc][:], pZ[par][:, 0:512], hsz[eidx][hc][:]),
                        after=g_h, mark=f"h{ge}_{hc}")
                P.mark_now(f"zmm_done{w3i}", P.tt, P.tt_p)

            for dg in range(4):
                for eidx in range(NE):
                    for hc in range(8):
                        g_l = []
                        if hc == 0:
                            g_l = [f"h{grp * NE + eidx}_7"]
                            if grp > 0 or dg > 0 or eidx > 0:
                                g_l.append(f"w2mmprev")
                        P.gd(lambda g, eidx=eidx, dg=dg, hc=hc: g.dma_start(
                            out=w2b[hc][:], in_=w2bf_slice(w2_d, eidx, hc, dg)), dW2,
                            after=g_l)
                    P.mark_now(f"w2l{grp}_{dg}_{eidx}", dW2, dW2.np_)
                    for t128 in range(4):
                        g_mm = [f"w2l{grp}_{dg}_{eidx}"]
                        if eidx == 0 and hc == 7:
                            pass
                        if eidx == 0:
                            g_mm.append(f"h{grp * NE + NE - 1}_7")
                            if grp > 0 or dg > 0:
                                g_mm.append(f"csil{grp * 4 + dg - 1}_{t128}")
                        for hc in range(8):
                            P.t(lambda t, t128=t128, hc=hc, eidx=eidx:
                                t.matmul(pW2[t128][:, 0:512],
                                         hti[eidx][hc][:, t128 * 128:(t128 + 1) * 128],
                                         w2b[hc][:],
                                         start=(eidx == 0 and hc == 0),
                                         stop=(eidx == NE - 1 and hc == 7)),
                                after=g_mm if hc == 0 else (),
                                mark=(f"w2f{grp}_{dg}_{t128}" if (eidx == NE - 1 and hc == 7)
                                      else None))
                    P.mark_now(f"w2mmprev", P.tt, P.tt_p)
                    if grp == 0:
                        for hc in range(8):
                            P.mark_now(f"w2r0_{eidx}_{hc}", P.tt, P.tt_p)
                gdi = grp * 4 + dg
                for t128 in range(4):
                    j = grp * 4 + t128
                    par = t128 % 2
                    g_xl = []
                    if gdi >= 1:
                        g_xl.append(f"cmb{gdi - 1}_{t128}")
                    elif t128 >= 2:
                        g_xl.append(f"cmb0_{t128 - 2}")
                    P.y(lambda sy, j=j, dg=dg, par=par: sy.dma_start(
                        out=xres[par][:], in_=xm_d[j * 128:(j + 1) * 128, dg * 512:(dg + 1) * 512]),
                        dXR[par], after=g_xl, mark=f"dx{gdi}_{t128}")
                    P.y(lambda sy, j=j, dg=dg, par=par: sy.dma_start(
                        out=sres[par][:], in_=s_d[j * 128:(j + 1) * 128, dg * 512:(dg + 1) * 512]),
                        dSR[par], after=["sp_all"] + g_xl, mark=f"ds{gdi}_{t128}")
                    g_sil = [f"w2f{grp}_{dg}_{t128}"]
                    if t128 >= 1:
                        g_sil.append(f"cmbv{gdi}_{t128 - 1}")
                    elif gdi >= 1:
                        g_sil.append(f"cmbv{gdi - 1}_3")
                    P.s(lambda s, t128=t128: s.activation(gsil[:], pW2[t128][:, 0:512], AF.Silu),
                        after=g_sil, mark=f"csil{gdi}_{t128}")
                    g_cv = [f"csil{gdi}_{t128}", (dSR[par], dSR[par].np_)]
                    P.v(lambda v, par=par: v.tensor_mul(gmul[:], gsil[:], sres[par][:]),
                        after=g_cv, mark=f"cmbv{gdi}_{t128}")
                    g_cmb = [(dXR[par], dXR[par].np_)]
                    if dX1.np_ > 0:
                        g_cmb.append((dX1, dX1.np_))
                    P.v(lambda v, par=par: v.tensor_add(x1t[par][:], gmul[:], xres[par][:]),
                        after=g_cmb, mark=f"cmb{gdi}_{t128}")
                    P.y(lambda sy, j=j, dg=dg, par=par: sy.dma_start(
                        out=x1_d[j * 128:(j + 1) * 128, dg * 512:(dg + 1) * 512],
                        in_=x1t[par][:]), dX1, after=[f"cmb{gdi}_{t128}"])

        P.emit()

        def fin(sy):
            sy.wait_ge(dX1.h, dX1.n)
        ctx.sysec(fin)

    es_ph.close()
    ctx.cur, ctx.prefix = oc, op
    return ctx


def w2bf_slice(w2_d, eidx, hc, dg):
    return w2_d[eidx, hc * 128:(hc + 1) * 128, dg * 512:(dg + 1) * 512]


def _l3_inputs(stage, inp):
    w13 = []
    for e in range(8):
        w13.append(np.asarray(inp["L0_ffn_w1"][e]).T)
        w13.append(np.asarray(inp["L0_ffn_w3"][e]).T)
    w13.append(np.asarray(inp["L0_ffn_shared_w1"]).T)
    w13.append(np.asarray(inp["L0_ffn_shared_w3"]).T)
    w13 = np.ascontiguousarray(np.stack(w13).astype(np.float32))
    w2t = np.ascontiguousarray(np.stack(
        [np.asarray(inp["L0_ffn_w2"][e]).T for e in range(8)]
        + [np.asarray(inp["L0_ffn_shared_w2"]).T]).astype(np.float32))
    e8 = np.zeros((8, 8 * 128), dtype=BF)
    for e in range(8):
        e8[e, e * 128:(e + 1) * 128] = 1
    onesd = np.ones((1, 128), np.float32)
    iden = np.eye(128, dtype=np.float32)
    ebias = np.ascontiguousarray(np.asarray(inp["L0_ffn_expert_bias"], np.float32)[None, :])
    Wo = np.ascontiguousarray(np.asarray(inp["L0_attn_Wo"], np.float32))
    x = np.asarray(inp["L0_attn_x"], np.float32)[0]
    og_tm = stage["og_fm"].T
    maps = []
    for c in range(NC):
        sl = slice(c * TC, (c + 1) * TC)
        ogx = np.ascontiguousarray(np.concatenate(
            [og_tm[sl, 2 * r * 128:(2 * r + 2) * 128] for r in range(NC)], axis=0))
        maps.append({
            "ogx": ogx,
            "Wo": Wo,
            "xnfm": np.ascontiguousarray(stage["xnfm"][:, sl]),
            "gateL": np.ascontiguousarray(stage["gateL"][:, sl]),
            "ebias": ebias, "e8d": e8, "onesd": onesd, "iden": iden,
            "w13": w13, "w2t": w2t,
            "xm": np.ascontiguousarray(x[sl]),
        })
    return maps


def run_l3(stage, inp, **kw):
    ctx = build_l3()
    maps = _l3_inputs(stage, inp)
    res = run_bass_kernel_spmd(ctx.nc, maps, core_ids=list(range(NC)), **kw)
    return res.results


def drams_l4(ctx, inner):
    mk = ctx.dint if inner else ctx.dout
    d = {}
    d["Wqkv"] = ctx.din("Wqkv", [3, D, D], F32)
    d["bcols"] = ctx.din("bcols", [128, 48], F32)
    d["gatew"] = ctx.din("gatew", [128, 8 * 16], F32)
    d["cosd"] = ctx.din("cosd", [128, TC], F32)
    d["ssind"] = ctx.din("ssind", [128, TC], F32)
    d["xn1fm"] = mk("xn1fm", [D, TC], BF16)
    d["q1_d"] = mk("q1_d", [D, TC], BF16)
    d["k1_d"] = mk("k1_d", [D, TC], BF16)
    d["v1tm"] = mk("v1tm", [TC, D], BF16)
    d["gate1L"] = mk("gate1L", [8, TC], F32)
    return d


def build_l4():
    ctx = Ctx()
    dr = drams_l4(ctx, False)
    dr["x1"] = ctx.din("x1", [TC, D], F32)
    dr["iden"] = ctx.din("iden", [128, 128], F32)
    phase_l4(ctx, dr)
    return ctx


def phase_l4(ctx, dr):
    x1_d, Wqkv_d, bcols_d, gatew_d = dr["x1"], dr["Wqkv"], dr["bcols"], dr["gatew"]
    cos_d, ssin_d, iden_d = dr["cosd"], dr["ssind"], dr["iden"]
    xnfm1_d, q1_d, k1_d, v1tm_d, gate1L_d = (dr["xn1fm"], dr["q1_d"], dr["k1_d"],
                                             dr["v1tm"], dr["gate1L"])
    es_ph = ExitStack()
    oc, op = ctx.cur, ctx.prefix
    ctx.cur, ctx.prefix = es_ph, "L4_"

    proj_d = [q1_d, k1_d, None]

    if True:
        xnbf = [ctx.sbuf(f"xnbf{c}", [128, TC], BF16) for c in range(16)]
        idn = ctx.sbuf("idn", [128, 128], F32)
        gws = ctx.sbuf("gws", [128, 8 * 16], F32)
        bcs = ctx.sbuf("bcs", [128, 48], F32)
        glt = ctx.sbuf("glt", [8, TC], F32)
        lnc = ctx.sbuf("lnc", [128, D], F32)
        col = ctx.sbuf("col", [128, 12], F32)
        konst = ctx.sbuf("konst", [128, 2], F32)
        cosb = ctx.sbuf("cosb", [128, TC], F32)
        ssinb = ctx.sbuf("ssinb", [128, TC], F32)
        qpre = [ctx.sbuf(f"qpre{i}", [128, TC], F32) for i in range(2)]
        shuf = ctx.sbuf("shuf", [128, TC], F32)
        rt1 = ctx.sbuf("rt1", [128, TC], F32)
        rt2s = ctx.sbuf("rt2s", [128, TC], F32)
        obf = [ctx.sbuf(f"obf{i}", [128, TC], BF16) for i in range(2)]
        vtb = [ctx.sbuf(f"vtb{i}", [128, 128], BF16) for i in range(2)]
        es_t = ExitStack()
        xt = [ctx.sbuf(f"xt{i}", [128, D], F32, es=es_t) for i in range(3)]
        xnfm = [ctx.sbuf(f"xnfm{c}", [128, TC], F32, es=es_t) for c in range(16)]

        pA = [ctx.psum(f"pA{i}", [128, 512], F32) for i in range(2)]
        pB = [ctx.psum(f"pB{i}", [128, 512], F32) for i in range(2)]
        pT = [ctx.psum(f"pT{i}", [128, 512], F32) for i in range(2)]

        dC = ctx.sem("dC")
        dXT = [ctx.sem(f"dXT{i}") for i in range(3)]
        dW = ctx.sem("dW")
        dS = ctx.sem("dS")
        dQ = [ctx.sem("dQ0"), ctx.sem("dQ1")]
        dQV = [ctx.sem("dQV0"), ctx.sem("dQV1")]

        ctx.start()
        P = Prog(ctx)

        P.y(lambda sy: sy.dma_start(out=idn[:], in_=iden_d[:, :]), dC)
        P.y(lambda sy: sy.dma_start(out=gws[:], in_=gatew_d[:, :]), dC)
        P.y(lambda sy: sy.dma_start(out=bcs[:], in_=bcols_d[:, :]), dC)
        P.y(lambda sy: sy.dma_start(out=cosb[:], in_=cos_d[:, :]), dC)
        P.y(lambda sy: sy.dma_start(out=ssinb[:], in_=ssin_d[:, :]), dC)
        P.mark_now("dC_all", dC, dC.np_)
        P.v(lambda v: v.memset(konst[:, 0:1], EPS), mark="konst")

        for i in range(3):
            P.y(lambda sy, i=i: sy.dma_start(out=xt[i % 3][:], in_=x1_d[i * 128:(i + 1) * 128, :]),
                dXT[i % 3], mark=f"xl{i}")
        for i in range(8):
            o = 6 * (i % 2)
            tile = xt[i % 3]
            P.v(lambda v, i=i, o=o, tile=tile: v.reduce_sum(col[:, o:o + 1], tile[:], axis=AX.X),
                after=[(dXT[i % 3], 16 * (i // 3 + 1))] + ([f"xn{i - 2}"] if i >= 2 else []))
            P.v(lambda v, o=o: v.tensor_scalar(col[:, o + 1:o + 2], col[:, o:o + 1], 1.0 / D,
                                               None, ALU.mult))
            P.v(lambda v, o=o, tile=tile: v.tensor_scalar(lnc[:], tile[:], col[:, o + 1:o + 2],
                                                          None, ALU.subtract))
            P.v(lambda v, tile=tile: v.tensor_mul(tile[:], lnc[:], lnc[:]))
            P.v(lambda v, o=o, tile=tile: v.reduce_sum(col[:, o + 2:o + 3], tile[:], axis=AX.X),
                mark=f"sumsq{i}")
            P.s(lambda s, o=o: s.activation(col[:, o + 3:o + 4], col[:, o + 2:o + 3], AF.Ln,
                                            bias=konst[:, 0:1], scale=1.0 / D),
                after=[f"sumsq{i}", "konst"])
            P.s(lambda s, o=o: s.activation(col[:, o + 4:o + 5], col[:, o + 3:o + 4], AF.Exp,
                                            scale=-0.5), mark=f"rstd{i}")
            P.v(lambda v, o=o, tile=tile: v.tensor_scalar(tile[:], lnc[:], col[:, o + 4:o + 5],
                                                          None, ALU.mult),
                after=[f"rstd{i}"], mark=f"xn{i}")
            for c in range(16):
                gx = i * 16 + c
                par = gx % 2
                g_t = [f"xn{i}", "dC_all"]
                if gx >= 2:
                    g_t.append(f"te{gx - 2}")
                P.t(lambda t, i=i, c=c, par=par, tile=tile: t.transpose(
                    pT[par][0:128, 0:128], tile[:, c * 128:(c + 1) * 128], idn[:]),
                    after=g_t, mark=(f"tT{i}" if c == 15 else f"tt{gx}"))
                P.s(lambda s, i=i, c=c, par=par: s.copy(xnfm[c][:, i * 128:(i + 1) * 128],
                                                        pT[par][0:128, 0:128]),
                    after=[f"tT{i}" if c == 15 else f"tt{gx}"], mark=f"te{gx}")
            if i + 3 < 8:
                P.y(lambda sy, i=i: sy.dma_start(out=xt[(i + 3) % 3][:],
                                                 in_=x1_d[(i + 3) * 128:(i + 4) * 128, :]),
                    dXT[(i + 3) % 3], after=[f"tT{i}"], mark=f"xl{i + 3}")
        P.mark_now("ln_done", ctx.ss, P.ss_p)

        for gi in range(2):
            last = None
            for c in range(16):
                P.t(lambda t, gi=gi, c=c: t.matmul(pA[gi][0:8, 0:512], gws[:, c * 8:(c + 1) * 8],
                                                   xnfm[c][:, gi * 512:(gi + 1) * 512],
                                                   start=(c == 0), stop=(c == 15)),
                    after=["ln_done", "dC_all"] if c == 0 else (), mark=f"tg{gi}" if c == 15 else None)
            P.s(lambda s, gi=gi: s.copy(glt[0:8, gi * 512:(gi + 1) * 512], pA[gi][0:8, 0:512]),
                after=[f"tg{gi}"], mark=f"ge{gi}")
        P.y(lambda sy: sy.dma_start(out=gate1L_d[:, :], in_=glt[:, :]), dS,
            after=["ge0", "ge1"])

        for c in range(16):
            P.v(lambda v, c=c: v.tensor_copy(xnbf[c][:], xnfm[c][:]), after=["ln_done"],
                mark=f"cb{c}")
            P.y(lambda sy, c=c: sy.dma_start(out=xnfm1_d[c * 128:(c + 1) * 128, :],
                                             in_=xnbf[c][:]), dS, after=[f"cb{c}"])
        P.mark_now("cast_done", ctx.vv, P.vv_p)

        es_t.close()
        wbuf = [ctx.sbuf(f"wbuf{c}", [128, D], BF16) for c in range(16)]

        for m in range(3):
            g_w = ["cast_done", "ge0", "ge1"]
            if m >= 1:
                g_w.append(f"pm{m - 1}")
            for c in range(16):
                P.gd(lambda g, m=m, c=c: g.dma_start(out=wbuf[c][:],
                                                     in_=Wqkv_d[m, c * 128:(c + 1) * 128, :]),
                     dW, after=(g_w if c == 0 else ()))
            P.mark_now(f"wl{m}", dW, dW.np_)
            for j in range(16):
                gx = m * 16 + j
                par = gx % 2
                for gi in range(2):
                    g_mm = [f"wl{m}"]
                    if gx >= 2:
                        g_mm.append(f"ev{gx - 2}_{gi}")
                    last = None
                    for c in range(16):
                        P.t(lambda t, c=c, j=j, gi=gi, par=par: t.matmul(
                            pB[par][:, 0:512] if gi == 0 else pA[par][:, 0:512],
                            wbuf[c][:, j * 128:(j + 1) * 128],
                            xnbf[c][:, gi * 512:(gi + 1) * 512],
                            start=(c == 0), stop=(c == 15)),
                            after=g_mm if c == 0 else (),
                            mark=f"pm{gx}_{gi}" if c == 15 else None)
                    g_ev = [f"pm{gx}_{gi}"]
                    if gx >= 2:
                        g_ev.append(f"rope{gx - 2}" if (gx - 2) % 48 < 32 or True else "")
                    P.s(lambda s, gi=gi, par=par, m=m, j=j: s.activation(
                        qpre[par][:, gi * 512:(gi + 1) * 512],
                        (pB[par][:, 0:512] if gi == 0 else pA[par][:, 0:512]),
                        AF.Identity, bias=bcs[:, m * 16 + j:m * 16 + j + 1]),
                        after=g_ev, mark=f"ev{gx}_{gi}")
                if m < 2:
                    g_r = [f"ev{gx}_1", "dC_all"]
                    if gx >= 2:
                        g_r.append((dQ[par], 16 * ((gx - par) // 2)))
                    P.v(lambda v, par=par: v.stream_shuffle(shuf[:], qpre[par][:],
                                                            [i ^ 1 for i in range(32)]),
                        after=g_r)
                    P.v(lambda v, par=par: v.tensor_mul(rt1[:], qpre[par][:], cosb[:]))
                    P.v(lambda v: v.tensor_mul(rt2s[:], shuf[:], ssinb[:]))
                    P.v(lambda v, par=par: v.tensor_add(obf[par][:], rt1[:], rt2s[:]),
                        mark=f"rope{gx}")
                    P.y(lambda sy, m=m, j=j, par=par: sy.dma_start(
                        out=proj_d[m][j * 128:(j + 1) * 128, :], in_=obf[par][:]),
                        dQ[par], after=[f"rope{gx}"])
                else:
                    for t8 in range(8):
                        vgx = j * 8 + t8
                        g_t = [f"ev{gx}_1"]
                        if vgx >= 2:
                            g_t.append(f"vte{vgx - 2}")
                        P.t(lambda t, t8=t8, par=par, vp=vgx % 2: t.transpose(
                            pT[vp][0:128, 0:128], qpre[par][:, t8 * 128:(t8 + 1) * 128],
                            idn[:]), after=g_t, mark=f"vtr{vgx}")
                        g_e = [f"vtr{vgx}"]
                        if vgx >= 2:
                            g_e.append((dQV[vgx % 2], 16 * ((vgx - vgx % 2) // 2)))
                        P.s(lambda s, vp=vgx % 2: s.copy(vtb[vp][:], pT[vp][0:128, 0:128]),
                            after=g_e, mark=f"vte{vgx}")
                        P.y(lambda sy, j=j, t8=t8, vp=vgx % 2: sy.dma_start(
                            out=v1tm_d[t8 * 128:(t8 + 1) * 128, j * 128:(j + 1) * 128],
                            in_=vtb[vp][:]), dQV[vgx % 2], after=[f"vte{vgx}"])
                    P.mark_now(f"rope{gx}", ctx.ss, P.ss_p)
            P.mark_now(f"pm{m}", P.tt, P.tt_p)

        P.emit()

        def fin(sy):
            for sem in (dS, dQ[0], dQ[1], dQV[0], dQV[1]):
                sy.wait_ge(sem.h, sem.n)
        ctx.sysec(fin)

    es_ph.close()
    ctx.cur, ctx.prefix = oc, op
    return ctx


def _l4_inputs(inp, x1_full):
    Wqkv = np.ascontiguousarray(np.stack([inp["L1_attn_Wq"], inp["L1_attn_Wk"],
                                          inp["L1_attn_Wv"]]).astype(np.float32))
    bcols = np.stack([np.asarray(inp["L1_attn_bq"]), np.asarray(inp["L1_attn_bk"]),
                      np.asarray(inp["L1_attn_bv"])]).astype(np.float32)
    bcols = np.ascontiguousarray(bcols.reshape(3, 16, 128).transpose(2, 0, 1).reshape(128, 48))
    gatew = np.asarray(inp["L1_ffn_gate_w"], np.float32)
    gatew = np.ascontiguousarray(gatew.reshape(16, 128, 8).transpose(1, 0, 2).reshape(128, 128))
    iden = np.eye(128, dtype=np.float32)
    hd = 128
    jj = np.arange(hd // 2)
    inv_freq = 10000.0 ** (-2.0 * jj / hd)
    pos = np.arange(T)
    ang = pos[:, None] * inv_freq[None, :]
    sin2 = np.stack([np.sin(ang), np.sin(ang)], -1).reshape(T, hd)
    cos2 = np.stack([np.cos(ang), np.cos(ang)], -1).reshape(T, hd)
    sign = np.where(np.arange(hd) % 2 == 0, -1.0, 1.0)
    ssin = sin2 * sign[None, :]
    cosT = np.ascontiguousarray(cos2.T.astype(np.float32))
    ssinT = np.ascontiguousarray(ssin.T.astype(np.float32))
    maps = []
    for c in range(NC):
        sl = slice(c * TC, (c + 1) * TC)
        maps.append({
            "x1": np.ascontiguousarray(x1_full[sl]),
            "Wqkv": Wqkv, "bcols": bcols, "gatew": gatew,
            "cosd": np.ascontiguousarray(cosT[:, sl]),
            "ssind": np.ascontiguousarray(ssinT[:, sl]),
            "iden": iden,
        })
    return maps


def run_l4(inp, x1_full, **kw):
    ctx = build_l4()
    maps = _l4_inputs(inp, x1_full)
    res = run_bass_kernel_spmd(ctx.nc, maps, core_ids=list(range(NC)), **kw)
    return res.results


def drams_l5(ctx, inner):
    d = {}
    d["k1ag"] = (ctx.dint if inner else ctx.din)("k1ag", [NC * D, TC], BF16)
    d["v1t"] = (ctx.dint if inner else ctx.din)("v1t", [T, D], BF16)
    d["Wo1"] = ctx.din("Wo1", [D, D], F32)
    d["bo1"] = ctx.din("bo1", [1, D], F32)
    d["s1"] = (ctx.dint if inner else ctx.dout)("s1", [TC, D], F32)
    return d


def build_l5():
    ctx = Ctx()
    dr = drams_l5(ctx, False)
    dr["q1_d"] = ctx.din("q1", [D, TC], BF16)
    dr["onesd"] = ctx.din("onesd", [1, 128], F32)
    phase_l5(ctx, dr)
    return ctx


def phase_l5(ctx, dr):
    q1_d, k1ag_d, v1_d = dr["q1_d"], dr["k1ag"], dr["v1t"]
    Wo_d, bo_d, ones_d, s1_d = dr["Wo1"], dr["bo1"], dr["onesd"], dr["s1"]
    es_ph = ExitStack()
    oc, op = ctx.cur, ctx.prefix
    ctx.cur, ctx.prefix = es_ph, "L5_"

    QB = 512
    NKT = T // 128

    if True:
        qfm = [ctx.sbuf(f"qfm{c}", [128, TC], BF16) for c in range(16)]
        kh = [ctx.sbuf(f"kh{i}", [128, T], BF16) for i in range(2)]
        vh = [ctx.sbuf(f"vh{i}", [128, T], BF16) for i in range(2)]
        wos = [ctx.sbuf(f"wos{c}", [128, D], BF16) for c in range(16)]
        PTa = ctx.sbuf("PTa", [128, 2048], BF16)
        PTb = ctx.sbuf("PTb", [128, 1024], BF16)
        On = [ctx.sbuf(f"On{h}", [128, QB], BF16) for h in range(16)]
        ones1 = ctx.sbuf("ones1", [1, 128], F32)
        onesc = ctx.sbuf("onesc", [128, 1], F32)
        onescb = ctx.sbuf("onescb", [128, 1], BF16)
        qsq = ctx.sbuf("qsq", [128, QB], F32)
        kcol = ctx.sbuf("kcol", [128, 1], F32)
        kcsq = ctx.sbuf("kcsq", [128, 1], F32)
        lnkb = ctx.sbuf("lnkb", [1, 16], F32)
        qn2 = ctx.sbuf("qn2", [1, QB], F32)
        sc1 = ctx.sbuf("sc1", [1, 4], F32)
        mhl = ctx.sbuf("mhl", [1, 1], F32)
        negMc = ctx.sbuf("negMc", [128, 1], F32)
        dsb = ctx.sbuf("dsb", [1, QB], F32)
        drc = ctx.sbuf("drc", [1, QB], F32)
        drb = ctx.sbuf("drb", [128, QB], F32)
        bob = [ctx.sbuf(f"bob{dg}", [128, 512], F32) for dg in range(4)]
        bor = ctx.sbuf("bor", [1, 512], F32)
        s1t = [ctx.sbuf(f"s1t{i}", [128, 512], F32) for i in range(2)]

        pSa = ctx.psum("pSa", [128, 2048], F32)
        pSb = ctx.psum("pSb", [128, 1024], F32)
        pO = ctx.psum("pO", [128, 512], F32)
        pDEN = ctx.psum("pDEN", [128, 512], F32)

        dQ = ctx.sem("dQ")
        dK = [ctx.sem("dK0"), ctx.sem("dK1")]
        dV = [ctx.sem("dV0"), ctx.sem("dV1")]
        dWO = ctx.sem("dWO")
        dC = ctx.sem("dC")
        dBOL = ctx.sem("dBOL")
        dS1 = ctx.sem("dS1")

        ctx.start()
        P = Prog(ctx)

        for c in range(16):
            P.y(lambda sy, c=c: sy.dma_start(out=qfm[c][:], in_=q1_d[c * 128:(c + 1) * 128, :]), dQ)
            P.gd(lambda g, c=c: g.dma_start(out=wos[c][:], in_=Wo_d[c * 128:(c + 1) * 128, :]), dWO)
        P.y(lambda sy: sy.dma_start(out=ones1[:], in_=ones_d[:, :]), dC)
        P.mark_now("q_all", dQ, dQ.np_)
        P.mark_now("wo_all", dWO, dWO.np_)
        P.mark_now("c_all", dC, dC.np_)
        P.v(lambda v: v.memset(onesc[:], 1.0), mark="onesc")
        P.v(lambda v: v.memset(onescb[:], 1.0), mark="onescb")
        P.v(lambda v: v.memset(mhl[:], -0.5 * np.log(128.0)), mark="mhl")
        for dg in range(4):
            P.y(lambda sy, dg=dg: sy.dma_start(out=bor[:], in_=bo_d[0:1, dg * 512:(dg + 1) * 512]),
                dBOL, after=[f"bo{dg - 1}"] if dg else [], mark=f"bol{dg}")
            P.t(lambda t, dg=dg: t.matmul(pO[:, 0:512], ones1[:], bor[:],
                                          start=True, stop=True),
                after=["c_all", f"bol{dg}"] + ([f"bo{dg - 1}"] if dg else []), mark=f"tbo{dg}")
            P.s(lambda s, dg=dg: s.copy(bob[dg][:], pO[:, 0:512]), after=[f"tbo{dg}"],
                mark=f"bo{dg}")

        sched = []
        kt = 0
        while kt < NKT:
            n = 4 if (len(sched) % 2 == 0) else 2
            n = min(n, NKT - kt)
            sched.append((kt, n))
            kt += n

        for qb in range(2):
            qsl = slice(qb * QB, (qb + 1) * QB)
            for h in range(16):
                hi = qb * 16 + h
                par = hi % 2
                nk = (hi - par) // 2 + 1
                g_k = [f"att{hi - 2}"] if hi >= 2 else []
                P.y(lambda sy, h=h, par=par: sy.dma_start(
                    out=kh[par][:],
                    in_=k1ag_d.rearrange("(r d) t -> d r t", d=D)[h * 128:(h + 1) * 128]),
                    dK[par], after=g_k)
                P.y(lambda sy, h=h, par=par: sy.dma_start(
                    out=vh[par][:].rearrange("p (kt c) -> p kt c", c=128),
                    in_=v1_d.rearrange("(kt p) (hh c) -> hh p kt c", p=128, c=128)[h]),
                    dV[par], after=g_k)
                KH, VH = kh[par], vh[par]
                kld = (dK[par], 16 * nk)
                vld = (dV[par], 16 * nk)
                if qb == 0:
                    P.v(lambda v, KH=KH: v.reduce_max(kcol[:], KH[:], axis=AX.X,
                                                      apply_absolute_value=True),
                        after=[kld] + ([f"m_lnkb{h - 2}"] if h >= 2 else []), mark=f"kc{h}")
                    P.s(lambda s: s.square(kcsq[:], kcol[:]), after=[f"kc{h}"], mark=f"kcs{h}")
                    g_kb = [f"kcs{h}", "onesc"]
                    if hi >= 1:
                        g_kb.append(f"m_expM{hi - 1}")
                    P.t(lambda t: t.matmul(pDEN[0:1, 0:1], kcsq[:], onesc[:],
                                           start=True, stop=True), after=g_kb, mark=f"m_kb{h}")
                    P.s(lambda s, h=h: s.activation(lnkb[0:1, h:h + 1], pDEN[0:1, 0:1], AF.Ln),
                        after=[f"m_kb{h}"], mark=f"m_lnkb{h}")
                g_qs = ["q_all"]
                if hi >= 1:
                    g_qs.append(f"m_negM{hi - 1}")
                P.s(lambda s, h=h, qsl=qsl: s.square(qsq[:], qfm[h][:, qsl]), after=g_qs,
                    mark=f"m_qsq{hi}")
                g_q2 = [f"m_qsq{hi}", "onesc"]
                g_q2.append(f"m_lnkb{h}" if qb == 0 else "m_lnkb15")
                P.t(lambda t: t.matmul(pDEN[0:1, 0:QB], onesc[:], qsq[:], start=True, stop=True),
                    after=g_q2, mark=f"m_q2{hi}")
                P.s(lambda s: s.copy(qn2[:], pDEN[0:1, 0:QB]), after=[f"m_q2{hi}"], mark=f"m_qc{hi}")
                P.v(lambda v: v.reduce_max(sc1[0:1, 0:1], qn2[:], axis=AX.X), after=[f"m_qc{hi}"],
                    mark=f"m_qm{hi}")
                P.s(lambda s: s.activation(sc1[0:1, 1:2], sc1[0:1, 0:1], AF.Ln),
                    after=[f"m_qm{hi}"], mark=f"m_lnq{hi}")
                P.v(lambda v, h=h: v.tensor_add(sc1[0:1, 2:3], sc1[0:1, 1:2], lnkb[0:1, h:h + 1]),
                    after=[f"m_lnq{hi}"], mark=f"m_sum{hi}")
                P.s(lambda s: s.activation(sc1[0:1, 3:4], sc1[0:1, 2:3], AF.Exp, scale=0.5,
                                           bias=mhl[0:1, 0:1]), after=[f"m_sum{hi}", "mhl"],
                    mark=f"m_expM{hi}")
                P.t(lambda t: t.matmul(pDEN[0:128, 0:1], ones1[:], sc1[0:1, 3:4],
                                       start=True, stop=True), after=[f"m_expM{hi}"],
                    mark=f"m_bc{hi}")
                P.s(lambda s: s.activation(negMc[:], pDEN[0:128, 0:1], AF.Identity, scale=-1.0),
                    after=[f"m_bc{hi}"], mark=f"m_negM{hi}")

                for bi, (kt0, n) in enumerate(sched):
                    bank = pSa if bi % 2 == 0 else pSb
                    pt = PTa if bi % 2 == 0 else PTb
                    for i in range(n):
                        ktc = kt0 + i
                        g_mm = []
                        if i == 0:
                            g_mm = [kld, f"m_negM{hi}"]
                            if bi >= 2:
                                g_mm.append(f"ex{hi}_{bi - 2}")
                            elif hi >= 1:
                                g_mm.append(f"ex{hi - 1}_{last_bi - 2 + bi}" if False else
                                            f"ex{hi - 1}_{len(sched) - 2 + bi}")
                        P.t(lambda t, ktc=ktc, i=i, bank=bank, KH=KH, h=h, qsl=qsl: t.matmul(
                            bank[:, i * 512:(i + 1) * 512], KH[:, ktc * 128:(ktc + 1) * 128],
                            qfm[h][:, qsl], start=True, stop=True),
                            after=g_mm, mark=f"st{hi}_{bi}" if i == n - 1 else None)
                    g_ex = [f"st{hi}_{bi}"]
                    if bi >= 2:
                        g_ex.append(f"dn{hi}_{bi - 2}")
                    elif hi >= 1:
                        g_ex.append(f"dn{hi - 1}_{len(sched) - 2 + bi}")
                    P.s(lambda s, bank=bank, pt=pt, n=n: s.activation(
                        pt[:, 0:n * 512], bank[:, 0:n * 512], AF.Exp,
                        bias=negMc[:, 0:1], scale=RDK), after=g_ex, mark=f"ex{hi}_{bi}")
                    for i in range(n):
                        ktc = kt0 + i
                        g_dn = [f"ex{hi}_{bi}"]
                        if bi == 0 and i == 0:
                            g_dn.append(f"m_negM{hi}")
                            g_dn.append(vld)
                            if hi >= 1:
                                g_dn.append(f"on{hi - 1}")
                        P.t(lambda t, ktc=ktc, i=i, pt=pt, first=(bi == 0 and i == 0): t.matmul(
                            pDEN[0:1, 0:QB], onescb[:], pt[0:128, i * 512:(i + 1) * 512],
                            start=first, stop=(ktc == NKT - 1)),
                            after=g_dn + (["onescb"] if (bi == 0 and i == 0) else []),
                            mark=f"dnf{hi}" if ktc == NKT - 1 else None)
                        P.t(lambda t, ktc=ktc, i=i, pt=pt, VH=VH, first=(bi == 0 and i == 0): t.matmul(
                            pO[:, 0:QB], VH[:, ktc * 128:(ktc + 1) * 128],
                            pt[0:128, i * 512:(i + 1) * 512],
                            start=first, stop=(ktc == NKT - 1)),
                            mark=f"dn{hi}_{bi}" if i == n - 1 else None)
                P.s(lambda s: s.copy(dsb[:], pDEN[0:1, 0:QB]), after=[f"dnf{hi}", f"dn{hi}_{len(sched) - 1}"],
                    mark=f"de{hi}")
                P.v(lambda v: v.reciprocal(drc[:], dsb[:]), after=[f"de{hi}"], mark=f"dr{hi}")
                P.t(lambda t: t.matmul(pDEN[0:128, 0:QB], ones1[:], drc[:], start=True, stop=True),
                    after=[f"dr{hi}"], mark=f"drb{hi}")
                P.s(lambda s: s.copy(drb[:], pDEN[0:128, 0:QB]), after=[f"drb{hi}"], mark=f"drs{hi}")
                g_on = [f"drs{hi}", f"dn{hi}_{len(sched) - 1}"]
                if qb == 1:
                    g_on.append(f"woe{0}_{h}" if False else f"wo_done0")
                P.v(lambda v, h=h: v.tensor_mul(On[h][:], pO[:, 0:QB], drb[:]), after=g_on,
                    mark=f"on{hi}")
                P.mark_now(f"att{hi}", ctx.vv, P.vv_p)

            for t128 in range(4):
                for dg in range(4):
                    wi = t128 * 4 + dg
                    par = wi % 2
                    g_w = ["wo_all", f"on{qb * 16 + 15}"]
                    if wi >= 2:
                        g_w.append(f"woe{qb}_{wi - 2}")
                    for h in range(16):
                        P.t(lambda t, h=h, t128=t128, dg=dg, par=par: t.matmul(
                            (pSa if par == 0 else pSb)[:, 0:512],
                            On[h][:, t128 * 128:(t128 + 1) * 128],
                            wos[h][:, dg * 512:(dg + 1) * 512],
                            start=(h == 0), stop=(h == 15)),
                            after=g_w if h == 0 else (),
                            mark=f"wom{qb}_{wi}" if h == 15 else None)
                    g_e = [f"wom{qb}_{wi}", f"bo{dg}"]
                    if qb * 16 + wi >= 2:
                        g_e.append((dS1, 16 * (qb * 16 + wi)))
                    P.v(lambda v, dg=dg, par=par: v.tensor_add(s1t[par][:],
                                                               (pSa if par == 0 else pSb)[:, 0:512],
                                                               bob[dg][:]),
                        after=g_e, mark=f"woe{qb}_{wi}")
                    P.y(lambda sy, qb=qb, t128=t128, dg=dg, par=par: sy.dma_start(
                        out=s1_d[qb * 512 + t128 * 128:qb * 512 + (t128 + 1) * 128,
                                 dg * 512:(dg + 1) * 512], in_=s1t[par][:]),
                        dS1, after=[f"woe{qb}_{wi}"])
            P.mark_now(f"wo_done{qb}", P.tt, P.tt_p)

        P.emit()

        def fin(sy):
            sy.wait_ge(dS1.h, dS1.n)
        ctx.sysec(fin)

    es_ph.close()
    ctx.cur, ctx.prefix = oc, op
    return ctx


def _l5_inputs(inp, stage):
    Wo = np.ascontiguousarray(np.asarray(inp["L1_attn_Wo"], np.float32))
    bo = np.ascontiguousarray(np.asarray(inp["L1_attn_bo"], np.float32)[None, :])
    onesd = np.ones((1, 128), np.float32)
    k1ag = np.ascontiguousarray(np.concatenate(
        [stage["k1_full"][:, r * TC:(r + 1) * TC] for r in range(NC)], axis=0))
    maps = []
    for c in range(NC):
        sl = slice(c * TC, (c + 1) * TC)
        maps.append({
            "q1": np.ascontiguousarray(stage["q1_full"][:, sl]),
            "k1ag": k1ag, "v1t": stage["v1_tm"],
            "Wo1": Wo, "bo1": bo, "onesd": onesd,
        })
    return maps


def run_l5(inp, stage, **kw):
    ctx = build_l5()
    maps = _l5_inputs(inp, stage)
    res = run_bass_kernel_spmd(ctx.nc, maps, core_ids=list(range(NC)), **kw)
    return res.results


def drams_l6(ctx):
    d = {}
    d["upw"] = ctx.din("upw", [D, T], F32)
    d["dwnw"] = ctx.din("dwnw", [T, D], F32)
    d["eww"] = ctx.din("eww", [D, D], F32)
    d["gam"] = ctx.din("gam", [1, D], F32)
    d["bet"] = ctx.din("bet", [1, D], F32)
    d["outp"] = ctx.dout("outp", [TC, D], F32)
    return d


def build_l6():
    ctx = Ctx()
    dr = drams_l6(ctx)
    dr["xn1fm"] = ctx.din("xn1fm", [D, TC], BF16)
    dr["gate1L"] = ctx.din("gate1L", [8, TC], F32)
    dr["x1"] = ctx.din("x1", [TC, D], F32)
    dr["s1"] = ctx.din("s1", [TC, D], F32)
    dr["onesd"] = ctx.din("onesd", [1, 128], F32)
    dr["iden"] = ctx.din("iden", [128, 128], F32)
    phase_l6(ctx, dr)
    return ctx


def phase_l6(ctx, dr):
    xnfm1_d, gate1L_d = dr["xn1fm"], dr["gate1L"]
    up_d, down_d, ew_d = dr["upw"], dr["dwnw"], dr["eww"]
    gam_d, bet_d = dr["gam"], dr["bet"]
    x1_d, s1_d = dr["x1"], dr["s1"]
    ones_d, iden_d = dr["onesd"], dr["iden"]
    out_d = dr["outp"]
    es_ph = ExitStack()
    oc, op = ctx.cur, ctx.prefix
    ctx.cur, ctx.prefix = es_ph, "L6_"

    NIC = 64

    if True:
        xnfm = [ctx.sbuf(f"xnfm{c}", [128, TC], BF16) for c in range(16)]
        h1 = [ctx.sbuf(f"h1_{i}", [128, 512], BF16) for i in range(NIC)]
        upb = [ctx.sbuf(f"upb{i}", [128, 512], BF16) for i in range(2)]
        dwnb = [ctx.sbuf("dwnb0", [128, 4096], BF16)]
        ewb = [ctx.sbuf("ewb0", [128, D], BF16)]
        relu = ctx.sbuf("relu", [128, 512], F32)
        scl = ctx.sbuf("scl", [128, 8], F32)
        sbcast = ctx.sbuf("sbcast", [128, TC], F32)
        idn = ctx.sbuf("idn", [128, 128], F32)
        ones1 = ctx.sbuf("ones1", [1, 128], F32)
        grow = ctx.sbuf("grow", [1, 512], F32)
        gbc = [ctx.sbuf(f"gbc{dg}", [128, 512], F32) for dg in range(4)]
        bbc = [ctx.sbuf(f"bbc{dg}", [128, 512], F32) for dg in range(4)]
        ytm = [ctx.sbuf(f"ytm{i}", [128, D], F32) for i in range(4)]
        konst = ctx.sbuf("konst", [128, 2], F32)
        col = ctx.sbuf("col", [128, 8], F32)
        lnc = ctx.sbuf("lnc", [128, D], F32)
        lsq = ctx.sbuf("lsq", [128, D], F32)

        es_b = ExitStack()
        gl = ctx.sbuf("gl", [8, TC], F32, es=es_b)
        logt = ctx.sbuf("logt", [128, 8], F32, es=es_b)
        rE = ctx.sbuf("rE", [128, 8], F32, es=es_b)
        rQ = ctx.sbuf("rQ", [128, 8], F32, es=es_b)
        rM = ctx.sbuf("rM", [128, 8], F32, es=es_b)
        rcol = ctx.sbuf("rcol", [128, 4], F32, es=es_b)
        srow = ctx.sbuf("srow", [1, TC], F32, es=es_b)

        pU = [ctx.psum(f"pU{i}", [128, 512], F32) for i in range(4)]
        pY = [ctx.psum(f"pY{i}", [128, 512], F32) for i in range(2)]
        pT2 = [ctx.psum(f"pT2{i}", [128, 512], F32) for i in range(2)]

        dXN = ctx.sem("dXN")
        dGL = ctx.sem("dGL")
        dUP = [ctx.sem("dUP0"), ctx.sem("dUP1")]
        dDW = [ctx.sem("dDW0"), ctx.sem("dDW1")]
        dEW = [ctx.sem("dEW0"), ctx.sem("dEW1")]
        dGB = ctx.sem("dGB")
        dXS = [ctx.sem("dXS0"), ctx.sem("dXS1")]
        dOU = ctx.sem("dOU")
        dOU.np_ = 0

        ctx.start()
        P = Prog(ctx)

        for c in range(16):
            P.y(lambda sy, c=c: sy.dma_start(out=xnfm[c][:], in_=xnfm1_d[c * 128:(c + 1) * 128, :]), dXN)
        P.y(lambda sy: sy.dma_start(out=gl[:], in_=gate1L_d[:, :]), dGL)
        P.y(lambda sy: sy.dma_start(out=idn[:], in_=iden_d[:, :]), dGL)
        P.y(lambda sy: sy.dma_start(out=ones1[:], in_=ones_d[:, :]), dGL)
        P.mark_now("xn_all", dXN, dXN.np_)
        P.mark_now("gl_all", dGL, dGL.np_)
        P.v(lambda v: v.memset(konst[:, 0:1], EPS), mark="konst")

        for j in range(8):
            g_tr = ["gl_all"]
            if j >= 1:
                g_tr.append(f"sc_sr{j - 1}")
            P.t(lambda t, j=j: t.transpose(pT2[0][0:128, 0:8], gl[0:8, j * 128:(j + 1) * 128],
                                           idn[0:8, 0:8]), after=g_tr, mark=f"sc_tr{j}")
            P.s(lambda s: s.copy(logt[:], pT2[0][0:128, 0:8]), after=[f"sc_tr{j}"], mark=f"sc_lg{j}")
            P.v(lambda v: v.reduce_max(rcol[:, 0:1], logt[:], axis=AX.X), after=[f"sc_lg{j}"])
            P.v(lambda v: v.tensor_scalar(rE[:], logt[:], rcol[:, 0:1], None, ALU.is_equal))
            P.v(lambda v: v.tensor_tensor_scan(rQ[:], rE[:], rE[:], 0.0, ALU.add, ALU.bypass))
            P.v(lambda v: v.tensor_scalar(rQ[:], rQ[:], 1.0, None, ALU.is_equal))
            P.v(lambda v: v.tensor_mul(rE[:], rE[:], rQ[:]))
            P.v(lambda v: v.scalar_tensor_tensor(rM[:], rE[:], -1e30, logt[:], ALU.mult, ALU.add))
            P.v(lambda v: v.reduce_max(rcol[:, 1:2], rM[:], axis=AX.X))
            P.v(lambda v, j=j: v.tensor_add(scl[:, j:j + 1], rcol[:, 0:1], rcol[:, 1:2]),
                mark=f"sc_sc{j}")
            P.t(lambda t, j=j: t.transpose(pT2[0][0:1, 0:128], scl[:, j:j + 1], idn[:]),
                after=[f"sc_sc{j}", f"sc_lg{j}"], mark=f"sc_tv{j}")
            P.s(lambda s, j=j: s.copy(srow[0:1, j * 128:(j + 1) * 128], pT2[0][0:1, 0:128]),
                after=[f"sc_tv{j}"], mark=f"sc_sr{j}")
        for gi in range(2):
            P.t(lambda t, gi=gi: t.matmul(pT2[1][:, 0:512], ones1[:],
                                          srow[0:1, gi * 512:(gi + 1) * 512], start=True, stop=True),
                after=["sc_sr7"] + ([f"sc_bc{gi - 1}"] if gi else []), mark=f"sc_tb{gi}")
            P.s(lambda s, gi=gi: s.copy(sbcast[:, gi * 512:(gi + 1) * 512], pT2[1][:, 0:512]),
                after=[f"sc_tb{gi}"], mark=f"sc_bc{gi}")

        for dg in range(4):
            P.y(lambda sy, dg=dg: sy.dma_start(out=grow[:], in_=gam_d[0:1, dg * 512:(dg + 1) * 512]),
                dGB, after=[f"lb_g{dg - 1}"] if dg else [], mark=f"lb_lg{dg}")
            P.t(lambda t, dg=dg: t.matmul(pT2[1][:, 0:512], ones1[:], grow[:], start=True, stop=True),
                after=[f"lb_lg{dg}", "sc_bc1"], mark=f"lb_tg{dg}")
            P.s(lambda s, dg=dg: s.copy(gbc[dg][:], pT2[1][:, 0:512]), after=[f"lb_tg{dg}"],
                mark=f"lb_g{dg}")
        for dg in range(4):
            P.y(lambda sy, dg=dg: sy.dma_start(out=grow[:], in_=bet_d[0:1, dg * 512:(dg + 1) * 512]),
                dGB, after=[f"lb_g3"] + ([f"lb_b{dg - 1}"] if dg else []), mark=f"lb_lb{dg}")
            P.t(lambda t, dg=dg: t.matmul(pT2[1][:, 0:512], ones1[:], grow[:], start=True, stop=True),
                after=[f"lb_lb{dg}", f"lb_g{dg}" if dg == 0 else f"lb_b{dg - 1}"], mark=f"lb_tb{dg}")
            P.s(lambda s, dg=dg: s.copy(bbc[dg][:], pT2[1][:, 0:512]), after=[f"lb_tb{dg}"],
                mark=f"lb_b{dg}")

        es_b.close()
        xct = [ctx.sbuf("xct0", [128, D], F32)]
        sct = [ctx.sbuf("sct0", [128, D], F32)]

        for half in range(2):
            hsl = slice(half * 512, (half + 1) * 512)
            for icg in range(16):
                for dc in range(16):
                    di = half * 256 + icg * 16 + dc
                    par = di % 2
                    g_l = ["xn_all"]
                    if di >= 2:
                        g_l.append(f"up_mm{di - 2}")
                    P.gd(lambda g, icg=icg, dc=dc, par=par: g.dma_start(
                        out=upb[par][:],
                        in_=up_d[dc * 128:(dc + 1) * 128, icg * 512:(icg + 1) * 512]),
                        dUP[par], after=g_l)
                    for i4 in range(4):
                        g_m = [(dUP[par], dUP[par].np_)]
                        if dc == 0:
                            if icg >= 1:
                                g_m.append(f"up_ev{half}_{icg - 1}")
                            elif half == 1:
                                g_m.append(f"up_ev0_15")
                                g_m.append("dn_mm15")
                        P.t(lambda t, i4=i4, dc=dc, par=par, hsl=hsl: t.matmul(
                            pU[i4][:, 0:512], upb[par][:, i4 * 128:(i4 + 1) * 128],
                            xnfm[dc][:, hsl], start=(dc == 0), stop=(dc == 15)),
                            after=g_m if i4 == 0 else (),
                            mark=f"up_mm{di}" if i4 == 3 else None)
                for i4 in range(4):
                    ic = icg * 4 + i4
                    g_r = [f"up_mm{half * 256 + icg * 16 + 15}"]
                    if half == 1:
                        g_r.append("dn_mm15")
                    if ic >= 1:
                        g_r.append(f"up_sq{half}_{ic - 1}")
                    elif half == 1:
                        g_r.append("up_sq0_63")
                    P.v(lambda v, i4=i4: v.tensor_scalar_max(relu[:], pU[i4][:, 0:512], 0.0),
                        after=g_r, mark=f"up_rl{half}_{ic}")
                    P.s(lambda s, ic=ic: s.square(h1[ic][:], relu[:]),
                        after=[f"up_rl{half}_{ic}"], mark=f"up_sq{half}_{ic}")
                P.mark_now(f"up_ev{half}_{icg}", ctx.ss, P.ss_p)
            for c in range(16):
                P.v(lambda v, c=c, hsl=hsl: v.tensor_mul(xnfm[c][:, hsl], xnfm[c][:, hsl],
                                                         sbcast[:, hsl]),
                    after=([f"up_mm{half * 256 + 255}", "sc_bc1"] if c == 0 else ()))
            P.mark_now(f"xs_done{half}", ctx.vv, P.vv_p)
            for dcol in range(16):
                gdi = half * 16 + dcol
                par = gdi % 2
                g_ld = []
                if gdi >= 1:
                    g_ld = [f"dn_mm{gdi - 1}"]
                P.gd(lambda g, dcol=dcol: g.dma_start(
                    out=dwnb[0][:].rearrange("p (ic c) -> p ic c", c=128),
                    in_=down_d.rearrange("(ic p) d -> p ic d", p=128)[:, 0:32, dcol * 128:(dcol + 1) * 128]),
                    dDW[0], after=g_ld, mark=f"dn_ld{gdi}")
                P.gd(lambda g, dcol=dcol: g.dma_start(
                    out=ewb[0][:].rearrange("p (dc c) -> p dc c", c=128),
                    in_=ew_d.rearrange("(dc p) d -> p dc d", p=128)[:, :, dcol * 128:(dcol + 1) * 128]),
                    dEW[0], after=[f"dn_mm{gdi - 1}"] if gdi >= 1 else [],
                    mark=f"ew_ld{gdi}")
                g_mm0 = [f"dn_ld{gdi}", f"up_sq{half}_63", f"xs_done{half}"]
                if gdi >= 2:
                    g_mm0.append(f"yse{gdi - 2}_3")
                for ic in range(32):
                    P.t(lambda t, ic=ic, par=par: t.matmul(
                        pY[par][:, 0:512], dwnb[0][:, ic * 128:(ic + 1) * 128],
                        h1[ic][:], start=(ic == 0), stop=False),
                        after=g_mm0 if ic == 0 else ())
                P.mark_now(f"dn_hm{gdi}", P.tt, P.tt_p)
                P.gd(lambda g, dcol=dcol: g.dma_start(
                    out=dwnb[0][:].rearrange("p (ic c) -> p ic c", c=128),
                    in_=down_d.rearrange("(ic p) d -> p ic d", p=128)[:, 32:64, dcol * 128:(dcol + 1) * 128]),
                    dDW[0], after=[f"dn_hm{gdi}"], mark=f"dn_ld2{gdi}")
                for ic in range(32):
                    P.t(lambda t, ic=ic, par=par: t.matmul(
                        pY[par][:, 0:512], dwnb[0][:, ic * 128:(ic + 1) * 128],
                        h1[32 + ic][:], start=False, stop=False),
                        after=[f"dn_ld2{gdi}"] if ic == 0 else ())
                for dc in range(16):
                    P.t(lambda t, dc=dc, par=par, hsl=hsl: t.matmul(
                        pY[par][:, 0:512], ewb[0][:, dc * 128:(dc + 1) * 128],
                        xnfm[dc][:, hsl], start=False, stop=(dc == 15)),
                        after=[f"ew_ld{gdi}"] if dc == 0 else (),
                        mark=f"dn_mm{gdi}" if dc == 15 else None)
                for t4 in range(4):
                    g_tp = [f"dn_mm{gdi}"]
                    if gdi >= 1:
                        g_tp.append(f"ytr_e{gdi - 1}_{t4}")
                    P.s(lambda s, t4=t4, par=par: s.copy(lsq[:, t4 * 128:(t4 + 1) * 128],
                                                         pY[par][:, t4 * 128:(t4 + 1) * 128]),
                        after=g_tp, mark=f"yse{gdi}_{t4}")
                    P.t(lambda t, t4=t4, par=par: t.transpose(
                        pT2[par][:, 0:128], lsq[:, t4 * 128:(t4 + 1) * 128], idn[:]),
                        after=[f"yse{gdi}_{t4}"] + ([f"ytr_e{gdi - 2}_{t4}"] if gdi >= 2 else []),
                        mark=f"ytr{gdi}_{t4}")
                    P.s(lambda s, t4=t4, par=par, dcol=dcol: s.copy(
                        ytm[t4][:, dcol * 128:(dcol + 1) * 128], pT2[par][:, 0:128]),
                        after=[f"ytr{gdi}_{t4}"], mark=f"ytr_e{gdi}_{t4}")
            for t4 in range(4):
                tj = half * 4 + t4
                g_ln = [f"ytr_e{half * 16 + 15}_3", f"ytr{half * 16 + 15}_3", "konst"]
                if tj >= 1:
                    g_ln.append(f"cmb{tj - 1}")
                P.v(lambda v, t4=t4: v.reduce_sum(col[:, 0:1], ytm[t4][:], axis=AX.X), after=g_ln)
                P.v(lambda v: v.tensor_scalar(col[:, 1:2], col[:, 0:1], 1.0 / D, None, ALU.mult))
                P.v(lambda v, t4=t4: v.tensor_scalar(lnc[:], ytm[t4][:], col[:, 1:2], None,
                                                     ALU.subtract))
                P.v(lambda v: v.tensor_mul(lsq[:], lnc[:], lnc[:]))
                P.v(lambda v: v.reduce_sum(col[:, 2:3], lsq[:], axis=AX.X), mark=f"lvar{tj}")
                P.s(lambda s: s.activation(col[:, 3:4], col[:, 2:3], AF.Ln, bias=konst[:, 0:1],
                                           scale=1.0 / D), after=[f"lvar{tj}"])
                P.s(lambda s: s.activation(col[:, 4:5], col[:, 3:4], AF.Exp, scale=-0.5),
                    mark=f"lrstd{tj}")
                P.v(lambda v: v.tensor_scalar(lsq[:], lnc[:], col[:, 4:5], None, ALU.mult),
                    after=[f"lrstd{tj}"], mark=f"lnn{tj}")
                for dg in range(4):
                    P.v(lambda v, dg=dg: v.tensor_mul(lnc[:, dg * 512:(dg + 1) * 512],
                                                      lsq[:, dg * 512:(dg + 1) * 512], gbc[dg][:]),
                        after=[f"lnn{tj}", "lb_g3"] if dg == 0 else ())
                    P.v(lambda v, dg=dg: v.tensor_add(lnc[:, dg * 512:(dg + 1) * 512],
                                                      lnc[:, dg * 512:(dg + 1) * 512], bbc[dg][:]),
                        after=["lb_b3"] if dg == 0 else (), mark=f"lgb{tj}" if dg == 3 else None)
                P.s(lambda s: s.activation(lsq[:], lnc[:], AF.Silu), after=[f"lgb{tj}"],
                    mark=f"lsil{tj}")
                g_cl = []
                if tj >= 1:
                    g_cl.append(f"cmb{tj - 1}")
                    g_cl.append((dOU, dOU.np_))
                P.y(lambda sy, tj=tj: sy.dma_start(out=xct[0][:],
                                                   in_=x1_d[tj * 128:(tj + 1) * 128, :]),
                    dXS[0], after=g_cl)
                P.y(lambda sy, tj=tj: sy.dma_start(out=sct[0][:],
                                                   in_=s1_d[tj * 128:(tj + 1) * 128, :]),
                    dXS[0])
                g_c = [f"lsil{tj}", (dXS[0], dXS[0].np_)]
                if dOU.np_ > 0:
                    g_c.append((dOU, dOU.np_))
                P.v(lambda v: v.tensor_mul(sct[0][:], lsq[:], sct[0][:]), after=g_c)
                P.v(lambda v: v.tensor_add(xct[0][:], xct[0][:], sct[0][:]),
                    mark=f"cmb{tj}")
                P.y(lambda sy, tj=tj: sy.dma_start(
                    out=out_d[tj * 128:(tj + 1) * 128, :], in_=xct[0][:]), dOU,
                    after=[f"cmb{tj}"])

        P.emit()

        def fin(sy):
            sy.wait_ge(dOU.h, dOU.n)
        ctx.sysec(fin)

    es_ph.close()
    ctx.cur, ctx.prefix = oc, op
    return ctx


def _l6_inputs(inp, stage):
    onesd = np.ones((1, 128), np.float32)
    iden = np.eye(128, dtype=np.float32)
    up = np.ascontiguousarray(np.asarray(inp["L1_ffn_up_w"], np.float32))
    dwn = np.ascontiguousarray(np.asarray(inp["L1_ffn_down_w"], np.float32))
    ew = np.ascontiguousarray(np.asarray(inp["L1_ffn_expert_weight"], np.float32))
    gam = np.ascontiguousarray(np.asarray(inp["L1_ffn_ln_gamma"], np.float32)[None, :])
    bet = np.ascontiguousarray(np.asarray(inp["L1_ffn_ln_beta"], np.float32)[None, :])
    maps = []
    for c in range(NC):
        sl = slice(c * TC, (c + 1) * TC)
        maps.append({
            "xn1fm": np.ascontiguousarray(stage["xnfm1"][:, sl]),
            "gate1L": np.ascontiguousarray(stage["gate1L"][:, sl]),
            "upw": up, "dwnw": dwn, "eww": ew, "gam": gam, "bet": bet,
            "x1": np.ascontiguousarray(stage["x1_full"][sl]),
            "s1": np.ascontiguousarray(stage["s1_full"][sl]),
            "onesd": onesd, "iden": iden,
        })
    return maps


def run_l6(inp, stage, **kw):
    ctx = build_l6()
    maps = _l6_inputs(inp, stage)
    res = run_bass_kernel_spmd(ctx.nc, maps, core_ids=list(range(NC)), **kw)
    return res.results


def kernel(**inputs):
    inp = {k: np.asarray(v) for k, v in inputs.items()}

    res1 = run_l1(inp)
    q_full = np.concatenate([r["q_d"] for r in res1], axis=1)
    k_full = np.concatenate([r["k_d"] for r in res1], axis=1)
    v_full = np.concatenate([r["v_d"] for r in res1], axis=1)
    g_full = np.concatenate([r["g_d"] for r in res1], axis=1)
    small = np.concatenate([r["smallL"] for r in res1], axis=1)
    xnfm_full = np.concatenate([r["xnfm"] for r in res1], axis=1)
    stage2 = {
        "q_full": q_full, "k_full": k_full, "v_full": v_full, "g_full": g_full,
        "alphaL": np.ascontiguousarray(small[0:16]),
        "betaL": np.ascontiguousarray(small[16:32]),
    }
    res2 = run_l2(stage2, inp)
    og_tm = np.concatenate([r["og_d"] for r in res2], axis=1)
    stage3 = {
        "og_fm": np.ascontiguousarray(og_tm.T),
        "xnfm": xnfm_full,
        "gateL": np.ascontiguousarray(small[32:40]),
    }
    res3 = run_l3(stage3, inp)
    x1_full = np.ascontiguousarray(np.concatenate([r["x1"] for r in res3], axis=0))

    res4 = run_l4(inp, x1_full)
    q1_full = np.concatenate([r["q1_d"] for r in res4], axis=1)
    k1_full = np.ascontiguousarray(np.concatenate([r["k1_d"] for r in res4], axis=1))
    v1_tm = np.concatenate([r["v1tm"] for r in res4], axis=0)
    xnfm1_full = np.concatenate([r["xn1fm"] for r in res4], axis=1)
    gate1L_full = np.ascontiguousarray(np.concatenate([r["gate1L"] for r in res4], axis=1))
    stage5 = {
        "q1_full": q1_full,
        "k1_full": k1_full,
        "v1_tm": np.ascontiguousarray(v1_tm),
    }
    res5 = run_l5(inp, stage5)
    s1_full = np.ascontiguousarray(np.concatenate([r["s1"] for r in res5], axis=0))

    stage6 = {
        "xnfm1": xnfm1_full, "gate1L": gate1L_full,
        "x1_full": x1_full, "s1_full": s1_full,
    }
    res6 = run_l6(inp, stage6)
    out = np.concatenate([r["outp"] for r in res6], axis=0).astype(np.float32)
    return out.reshape(1, T, D)


# version 05
# speedup vs baseline: 1.0546x; 1.0546x over previous
import numpy as np
import ml_dtypes
from contextlib import ExitStack

import concourse.bass as bass
import concourse.mybir as mybir
from concourse.bass_utils import run_bass_kernel_spmd

F32 = mybir.dt.float32
BF16 = mybir.dt.bfloat16
AF = mybir.ActivationFunctionType
ALU = mybir.AluOpType
AX = mybir.AxisListType
BF = ml_dtypes.bfloat16

NC = 8
T = 8192
TC = T // NC
D = 2048
H = 16
DK = 128
CH = 128
FW = TC + 3
EPS = 1e-5
RDK = float(DK) ** -0.5

_WRAP = {
    "memset", "memzero", "tensor_copy", "tensor_tensor", "tensor_add", "tensor_sub",
    "tensor_mul", "tensor_max", "tensor_scalar", "tensor_single_scalar",
    "tensor_scalar_add", "tensor_scalar_sub", "tensor_scalar_mul", "tensor_scalar_max",
    "tensor_scalar_min", "tensor_relu", "tensor_reduce", "reduce_max", "reduce_sum",
    "scalar_tensor_tensor", "tensor_tensor_scan", "select", "copy_predicated",
    "transpose", "reciprocal", "stream_shuffle", "bn_stats", "bn_aggr", "max",
    "max_index", "pool", "pool_avg", "pool_max", "match_replace",
    "activation", "sign", "square", "sqrt", "mul", "copy", "add", "iota",
    "affine_select",
}


class Sem:
    __slots__ = ("h", "n", "np_")

    def __init__(self, h):
        self.h = h
        self.n = 0


def inc(instr, sem, k=1):
    instr.then_inc(sem.h, k)
    sem.n += k


class W:

    def __init__(self, eng, sem):
        self._e = eng
        self.s = sem

    @property
    def n(self):
        return self.s.n

    def wait(self, sem, val):
        self._e.wait_ge(sem.h, val)

    def __getattr__(self, name):
        attr = getattr(self._e, name)
        if name in _WRAP:
            def f(*a, **k):
                self._e.wait_ge(self.s.h, self.s.n)
                ins = attr(*a, **k)
                inc(ins, self.s, 1)
                return ins
            return f
        return attr


class Ctx:
    def __init__(self):
        self.nc = bass.Bass("TRN2", target_bir_lowering=False)
        self.es = ExitStack()
        self.block = None
        self.bar = None
        self.barn = 0
        self.vv = None
        self.ss = None
        self.gg = None
        self.cur = None
        self.prefix = ""

    def sem(self, name):
        return Sem(self.es.enter_context(self.nc.semaphore(self.prefix + name)))

    def sbuf(self, name, shape, dt, es=None):
        tgt = es or self.cur or self.es
        return tgt.enter_context(self.nc.sbuf_tensor(self.prefix + name, shape, dt))

    def psum(self, name, shape, dt=F32):
        tgt = self.cur or self.es
        return tgt.enter_context(self.nc.psum_tensor(self.prefix + name, shape, dt))

    def din(self, name, shape, dt):
        return self.nc.dram_tensor(name, shape, dt, kind="ExternalInput").ap()

    def dout(self, name, shape, dt):
        return self.nc.dram_tensor(name, shape, dt, kind="ExternalOutput").ap()

    def dint(self, name, shape, dt):
        return self.nc.dram_tensor(name, shape, dt, kind="Internal").ap()

    def start(self):
        if self.block is not None:
            return self.block
        self.bar = self.sem("bar")
        self.vv = self.sem("vv")
        self.ss = self.sem("ss")
        self.gg = self.sem("gg")
        self.block = self.es.enter_context(self.nc.Block())
        return self.block

    def vsec(self, fn):
        @self.block.vector
        def _(v):
            fn(W(v, self.vv))

    def ssec(self, fn):
        @self.block.scalar
        def _(s):
            fn(W(s, self.ss))

    def gsec(self, fn):
        @self.block.gpsimd
        def _(g):
            fn(W(g, self.gg))

    def tsec(self, fn):
        @self.block.tensor
        def _(t):
            fn(t)

    def sysec(self, fn):
        @self.block.sync
        def _(sy):
            fn(sy)

    def barrier(self):
        self.barn += 1
        tgt = 5 * self.barn
        bar = self.bar

        def arr(e):
            e._e.wait_ge(e.s.h, e.s.n)
            e._e.sem_inc(bar.h, 1)
            e._e.wait_ge(bar.h, tgt)

        self.vsec(arr)
        self.ssec(arr)
        self.gsec(arr)

        @self.block.tensor
        def _(t):
            t.sem_inc(bar.h, 1)
            t.wait_ge(bar.h, tgt)

        @self.block.sync
        def _(sy):
            sy.sem_inc(bar.h, 1)
            sy.wait_ge(bar.h, tgt)
        bar.n = tgt


def drams_l1(ctx, inner):
    mk = ctx.dint if inner else ctx.dout
    d = {}
    d["xm"] = ctx.din("xm", [TC, D], F32)
    d["xh"] = ctx.din("xh", [3, D], F32)
    d["Wqkvg"] = ctx.din("Wqkvg", [4, D, D], F32)
    d["Wab"] = ctx.din("Wab", [128, 40 * 16], F32)
    d["bab"] = ctx.din("bab", [40, 1], F32)
    d["convw"] = ctx.din("convw", [128, 192], F32)
    d["convb"] = ctx.din("convb", [128, 48], F32)
    d["iden"] = ctx.din("iden", [128, 128], F32)
    d["xnfm"] = mk("xnfm", [D, TC], BF16)
    d["proj"] = [mk(nm, [D, TC], BF16) for nm in ("q_d", "k_d", "v_d", "g_d")]
    d["smallL"] = mk("smallL", [40, TC], F32)
    return d


def build_l1():
    ctx = Ctx()
    dr = drams_l1(ctx, False)
    phase_l1(ctx, dr)
    return ctx


def phase_l1(ctx, dr):
    xm, xh, Wqkvg, Wab, bab = dr["xm"], dr["xh"], dr["Wqkvg"], dr["Wab"], dr["bab"]
    convw, convb, iden = dr["convw"], dr["convb"], dr["iden"]
    xnfm_d, proj_d, smallL_d = dr["xnfm"], dr["proj"], dr["smallL"]
    es_ph = ExitStack()
    oc, op = ctx.cur, ctx.prefix
    ctx.cur, ctx.prefix = es_ph, "L1_"

    grps = [(0, 512), (512, 512), (1024, 3)]

    if True:
        xt = [ctx.sbuf(f"xt{i}", [128, D], F32) for i in range(3)]
        lnc = ctx.sbuf("lnc", [128, D], F32)
        col = ctx.sbuf("col", [128, 12], F32)
        konst = ctx.sbuf("konst", [128, 2], F32)
        xnbf = [ctx.sbuf(f"xnbf{c}", [128, FW], BF16) for c in range(16)]
        idn = ctx.sbuf("idn", [128, 128], F32)
        wabs = ctx.sbuf("wabs", [128, 40 * 16], F32)
        babs = ctx.sbuf("babs", [40, 1], F32)
        smalls = ctx.sbuf("smalls", [40, FW], F32)
        cw = ctx.sbuf("cw", [128, 192], F32)
        cb = ctx.sbuf("cb", [128, 48], F32)
        qpre = [ctx.sbuf(f"qpre{i}", [128, FW], F32) for i in range(2)]
        cvo = [ctx.sbuf(f"cvo{i}", [128, TC], F32) for i in range(4)]
        obf = [ctx.sbuf(f"obf{i}", [128, TC], BF16) for i in range(2)]
        okbf = ctx.sbuf("okbf", [128, TC], BF16)
        gbf = [ctx.sbuf(f"gbf{i}", [128, FW], BF16) for i in range(2)]

        pA = [ctx.psum(f"pA{i}", [128, 512], F32) for i in range(2)]
        pB = [ctx.psum(f"pB{i}", [128, 512], F32) for i in range(2)]
        pC = [ctx.psum(f"pC{i}", [128, 512], F32) for i in range(2)]
        pT = [ctx.psum(f"pT{i}", [128, 512], F32) for i in range(2)]

        dC = ctx.sem("dC")
        dXT = [ctx.sem(f"dXT{i}") for i in range(3)]
        dW = ctx.sem("dW")
        dS = ctx.sem("dS")
        tT = ctx.sem("tT")
        tG = ctx.sem("tG")
        dOBq = ctx.sem("dOBq")
        dOBv = ctx.sem("dOBv")
        dGB = [ctx.sem("dGB0"), ctx.sem("dGB1")]
        dOK = ctx.sem("dOK")

        b = ctx.start()

        es_fm = ExitStack()
        xnfm = [ctx.sbuf(f"xnfm{c}", [128, FW], F32, es=es_fm) for c in range(16)]

        def p0(sy):
            inc(sy.dma_start(out=idn[:], in_=iden[:, :]), dC, 16)
            inc(sy.dma_start(out=babs[:], in_=bab[:, :]), dC, 16)
            inc(sy.dma_start(out=cw[:], in_=convw[:, :]), dC, 16)
            inc(sy.dma_start(out=cb[:], in_=convb[:, :]), dC, 16)
            inc(sy.dma_start(out=wabs[:], in_=Wab[:, :]), dC, 16)
        ctx.sysec(p0)
        const_loads = dC.n

        nrows = [3] + [128] * 8
        coff = [0] + [3 + 128 * i for i in range(8)]

        def p1_loads(sy):
            for i in range(9):
                if i >= 3:
                    sy.wait_ge(tT.h, 16 * (i - 2))
                if i == 0:
                    inc(sy.dma_start(out=xt[0][0:3, :], in_=xh[:, :]), dXT[0], 16)
                else:
                    inc(sy.dma_start(out=xt[i % 3][:, :], in_=xm[(i - 1) * 128:i * 128, :]),
                        dXT[i % 3], 16)
        ctx.sysec(p1_loads)

        vv0 = ctx.vv.n
        ss0 = ctx.ss.n
        vmark = {}

        def rstd_end_fwd(i):
            return ss0 + (2 * (i + 1) if i <= 2 else 24 + 18 * (i - 3))

        def p1_v(v):
            v.memset(konst[:, 0:1], EPS)
            for i in range(9):
                r = nrows[i]
                tile = xt[i % 3]
                o = 6 * (i % 2)
                v.wait(dXT[i % 3], 16 * (i // 3 + 1))
                v.reduce_sum(col[0:r, o:o + 1], tile[0:r, :], axis=AX.X)
                v.tensor_scalar(col[0:r, o + 1:o + 2], col[0:r, o:o + 1], 1.0 / D, None, ALU.mult)
                v.tensor_scalar(lnc[0:r, :], tile[0:r, :], col[0:r, o + 1:o + 2], None, ALU.subtract)
                v.tensor_mul(tile[0:r, :], lnc[0:r, :], lnc[0:r, :])
                v.reduce_sum(col[0:r, o + 2:o + 3], tile[0:r, :], axis=AX.X)
                vmark[("sumsq", i)] = ctx.vv.n
                v.wait(ctx.ss, rstd_end_fwd(i))
                v.tensor_scalar(tile[0:r, :], lnc[0:r, :], col[0:r, o + 4:o + 5], None, ALU.mult)
                vmark[("xn", i)] = ctx.vv.n
        ctx.vsec(p1_v)

        se_mark = {}

        def rstd_end(i):
            return ss0 + (2 * (i + 1) if i <= 2 else 24 + 18 * (i - 3))

        def p1_s(s):
            s.wait(dC, const_loads)

            def rstd(i):
                r = nrows[i]
                o = 6 * (i % 2)
                s.wait(ctx.vv, vmark[("sumsq", i)])
                s.activation(col[0:r, o + 3:o + 4], col[0:r, o + 2:o + 3], AF.Ln,
                             bias=konst[0:r, 0:1], scale=1.0 / D)
                s.activation(col[0:r, o + 4:o + 5], col[0:r, o + 3:o + 4], AF.Exp, scale=-0.5)
                assert ctx.ss.n == rstd_end(i), (i, ctx.ss.n)

            def evacs(i):
                r = nrows[i]
                for c in range(16):
                    gx = i * 16 + c
                    par = gx % 2
                    s.wait(tT, gx + 1)
                    s.copy(xnfm[c][:, coff[i]:coff[i] + r], pT[par][0:128, 0:r])
                    se_mark[gx] = ctx.ss.n

            for i in range(3):
                rstd(i)
            for i in range(3, 9):
                evacs(i - 3)
                rstd(i)
            for i in range(6, 9):
                evacs(i)
        ctx.ssec(p1_s)
        assert ctx.ss.n == ss0 + 18 + 144

        def p1_t(t):
            t.wait_ge(dC.h, const_loads)
            for i in range(9):
                r = nrows[i]
                tile = xt[i % 3]
                t.wait_ge(ctx.vv.h, vmark[("xn", i)])
                for c in range(16):
                    gx = i * 16 + c
                    par = gx % 2
                    if gx >= 2:
                        t.wait_ge(ctx.ss.h, se_mark[gx - 2])
                    inc(t.transpose(pT[par][0:128, 0:r], tile[0:r, c * 128:(c + 1) * 128],
                                    idn[0:r, 0:r]), tT, 1)
        ctx.tsec(p1_t)

        ctx.barrier()

        tG0 = tG.n
        ss1 = ctx.ss.n

        def p2_t(t):
            t.wait_ge(dC.h, const_loads)
            for gi, (go, gn) in enumerate(grps):
                dst = [pA[0], pB[0], pC[0]][gi]
                last = None
                for c in range(16):
                    last = t.matmul(dst[0:40, 0:gn], wabs[:, c * 40:(c + 1) * 40],
                                    xnfm[c][:, go:go + gn], start=(c == 0), stop=(c == 15))
                inc(last, tG, 1)
        ctx.tsec(p2_t)

        def p2_s(s):
            s.wait(dC, const_loads)
            for gi, (go, gn) in enumerate(grps):
                src = [pA[0], pB[0], pC[0]][gi]
                s.wait(tG, tG0 + gi + 1)
                s.activation(smalls[0:40, go:go + gn], src[0:40, 0:gn], AF.Identity,
                             bias=babs[0:40, 0:1])
        ctx.ssec(p2_s)

        def p2_st(sy):
            sy.wait_ge(ctx.ss.h, ss1 + 3)
            inc(sy.dma_start(out=smallL_d[:, :], in_=smalls[0:40, 3:FW]), dS, 16)
        ctx.sysec(p2_st)

        vv1 = ctx.vv.n

        def p3_v(v):
            for c in range(16):
                v.tensor_copy(xnbf[c][:], xnfm[c][:])
        ctx.vsec(p3_v)

        def p3_st(sy):
            for c in range(16):
                sy.wait_ge(ctx.vv.h, vv1 + c + 1)
                inc(sy.dma_start(out=xnfm_d[c * 128:(c + 1) * 128, :], in_=xnbf[c][:, 3:FW]), dS, 16)
        ctx.sysec(p3_st)

        ctx.barrier()
        es_fm.close()

        wbuf = [ctx.sbuf(f"wbuf{c}", [128, D], BF16) for c in range(16)]

        tG1 = tG.n
        dW0 = dW.n
        ss2 = ctx.ss.n
        vv2 = ctx.vv.n

        def s_evac_n(gx, gi):
            if gx < 48:
                return ss2 + 4 * gx + gi + 1
            return ss2 + 192 + 3 * (gx - 48) + gi + 1

        def s_silu_n(gx):
            return ss2 + 4 * gx + 4

        def v_conv_n(gx):
            ksc = min(max(gx - 16, 0), 16)
            return vv2 + 4 * gx + ksc + 4

        def v_kscale_n(j):
            return v_conv_n(16 + j) + 1

        def p4_g(g):
            for m in range(4):
                if m >= 1:
                    g.wait_ge(tG.h, tG1 + 48 * m)
                for c in range(16):
                    inc(g.dma_start(out=wbuf[c][:], in_=Wqkvg[m, c * 128:(c + 1) * 128, :]), dW, 16)
        ctx.gsec(p4_g)

        def p4_t(t):
            for m in range(4):
                t.wait_ge(dW.h, dW0 + 256 * (m + 1))
                for j in range(16):
                    gx = m * 16 + j
                    if gx >= 2:
                        t.wait_ge(ctx.ss.h, s_evac_n(gx - 2, 2))
                    par = gx % 2
                    for gi, (go, gn) in enumerate(grps):
                        dst = [pA[par], pB[par], pC[par]][gi]
                        last = None
                        for c in range(16):
                            last = t.matmul(dst[0:128, 0:gn], wbuf[c][:, j * 128:(j + 1) * 128],
                                            xnbf[c][:, go:go + gn], start=(c == 0), stop=(c == 15))
                        inc(last, tG, 1)
        ctx.tsec(p4_t)

        def p4_s(s):
            for m in range(4):
                for j in range(16):
                    gx = m * 16 + j
                    par = gx % 2
                    if m < 3 and gx >= 2:
                        s.wait(ctx.vv, v_conv_n(gx - 2))
                    if m == 3 and j >= 2:
                        s.wait(dGB[j % 2], 16 * ((j - j % 2) // 2))
                    for gi, (go, gn) in enumerate(grps):
                        src = [pA[par], pB[par], pC[par]][gi]
                        s.wait(tG, tG1 + 3 * gx + gi + 1)
                        if m < 3:
                            s.activation(qpre[par][:, go:go + gn], src[0:128, 0:gn], AF.Identity)
                        else:
                            s.activation(gbf[j % 2][:, go:go + gn], src[0:128, 0:gn], AF.Sigmoid)
                    if m < 3:
                        s.wait(ctx.vv, v_conv_n(gx))
                        if m == 0 and j >= 2:
                            s.wait(dOBq, 16 * j)
                        elif m == 1:
                            if j < 2:
                                s.wait(dOBq, 16 * 16)
                            else:
                                s.wait(ctx.vv, v_kscale_n(j - 2))
                        elif m == 2:
                            if j < 2:
                                s.wait(ctx.vv, v_kscale_n(15))
                            else:
                                s.wait(dOBv, 16 * j)
                        s.activation(obf[par][:], cvo[2 * par + 1][:], AF.Silu,
                                     bias=cb[0:128, m * 16 + j:m * 16 + j + 1])
        ctx.ssec(p4_s)
        assert ctx.ss.n == ss2 + 240, ctx.ss.n - ss2

        def p4_v(v):
            v.wait(dC, const_loads)
            for m in range(3):
                for j in range(16):
                    gx = m * 16 + j
                    par = gx % 2
                    v.wait(ctx.ss, s_evac_n(gx, 2))
                    src = qpre[par]
                    ping, pong = cvo[2 * par], cvo[2 * par + 1]
                    v.tensor_scalar(ping[:], src[:, 0:TC],
                                    cw[:, m * 64 + 0 * 16 + j:m * 64 + 0 * 16 + j + 1], None, ALU.mult)
                    cur, oth = ping, pong
                    for jj in (1, 2, 3):
                        v.scalar_tensor_tensor(oth[:], src[:, jj:jj + TC],
                                               cw[:, m * 64 + jj * 16 + j:m * 64 + jj * 16 + j + 1],
                                               cur[:], ALU.mult, ALU.add)
                        cur, oth = oth, cur
                    assert ctx.vv.n == v_conv_n(gx), (gx, ctx.vv.n - vv2)
                    if m == 1:
                        v.wait(ctx.ss, s_silu_n(gx))
                        if j >= 1:
                            v.wait(dOK, 16 * j)
                        v.tensor_scalar(okbf[:], obf[par][:], RDK, None, ALU.mult)
        ctx.vsec(p4_v)

        def p4_st(sy):
            for m in range(4):
                for j in range(16):
                    gx = m * 16 + j
                    par = gx % 2
                    if m == 3:
                        sy.wait_ge(ctx.ss.h, s_evac_n(gx, 2))
                        inc(sy.dma_start(out=proj_d[3][j * 128:(j + 1) * 128, :],
                                         in_=gbf[j % 2][:, 3:FW]), dGB[j % 2], 16)
                    elif m == 1:
                        sy.wait_ge(ctx.vv.h, v_kscale_n(j))
                        inc(sy.dma_start(out=proj_d[1][j * 128:(j + 1) * 128, :],
                                         in_=okbf[:]), dOK, 16)
                    else:
                        sy.wait_ge(ctx.ss.h, s_silu_n(gx))
                        sem = dOBq if m == 0 else dOBv
                        inc(sy.dma_start(out=proj_d[m][j * 128:(j + 1) * 128, :],
                                         in_=obf[par][:]), sem, 16)
        ctx.sysec(p4_st)

        def fin(sy):
            for sem in (dS, dOBq, dOBv, dGB[0], dGB[1], dOK):
                sy.wait_ge(sem.h, sem.n)
        ctx.sysec(fin)

    es_ph.close()
    ctx.cur, ctx.prefix = oc, op
    return ctx


def _l1_inputs(inp):
    x = np.asarray(inp["L0_attn_x"], np.float32)[0]
    Wqkvg = np.ascontiguousarray(np.stack([inp["L0_attn_Wq"], inp["L0_attn_Wk"],
                                           inp["L0_attn_Wv"], inp["L0_attn_Wg"]]).astype(np.float32))
    Wab = np.concatenate(
        [inp["L0_attn_Wa"], inp["L0_attn_Wb"], np.asarray(inp["L0_ffn_gate"]).T], axis=1).astype(np.float32)
    Wab = np.ascontiguousarray(Wab.reshape(16, 128, 40).transpose(1, 0, 2).reshape(128, 640))
    bab = np.ascontiguousarray(np.concatenate(
        [inp["L0_attn_ba"], inp["L0_attn_bb"], np.zeros(8, np.float32)]).astype(np.float32)[:, None])
    convw = np.stack([inp["L0_attn_qcw"], inp["L0_attn_kcw"],
                      inp["L0_attn_vcw"]]).astype(np.float32)
    convw = np.ascontiguousarray(convw.reshape(3, 16, 128, 4).transpose(2, 0, 3, 1).reshape(128, 192))
    convb = np.stack([inp["L0_attn_qcb"], inp["L0_attn_kcb"],
                      inp["L0_attn_vcb"]]).astype(np.float32)
    convb = np.ascontiguousarray(convb.reshape(3, 16, 128).transpose(2, 0, 1).reshape(128, 48))
    iden = np.eye(128, dtype=np.float32)
    maps = []
    for c in range(NC):
        s = c * TC
        xhc = x[s - 3:s] if c > 0 else np.zeros((3, D), np.float32)
        maps.append({
            "xm": np.ascontiguousarray(x[s:s + TC]),
            "xh": np.ascontiguousarray(xhc),
            "Wqkvg": Wqkvg, "Wab": Wab, "bab": bab,
            "convw": convw, "convb": convb, "iden": iden,
        })
    return maps


def run_l1(inp, **kw):
    ctx = build_l1()
    maps = _l1_inputs(inp)
    res = run_bass_kernel_spmd(ctx.nc, maps, core_ids=list(range(NC)), **kw)
    return res.results


class Prog:
    def __init__(self, ctx):
        self.ctx = ctx
        self.q = {"t": [], "v": [], "s": [], "g": [], "y": []}
        self.marks = {}
        self.tt = ctx.sem("tt")
        self.tt_p = self.tt.n
        self.vv_p = ctx.vv.n
        self.ss_p = ctx.ss.n
        self.gg_p = ctx.gg.n

    def _resolve(self, after):
        out = []
        for a in after or ():
            if isinstance(a, tuple):
                out.append(a)
            else:
                out.append(self.marks[a])
        return out

    def t(self, emit, after=(), mark=None):
        w = self._resolve(after)
        self.q["t"].append((emit, w, self.tt))
        self.tt_p += 1
        if mark:
            self.marks[mark] = (self.tt, self.tt_p)

    def v(self, emit, after=(), mark=None):
        w = self._resolve(after)
        self.q["v"].append((emit, w, None))
        self.vv_p += 1
        if mark:
            self.marks[mark] = (self.ctx.vv, self.vv_p)

    def s(self, emit, after=(), mark=None):
        w = self._resolve(after)
        self.q["s"].append((emit, w, None))
        self.ss_p += 1
        if mark:
            self.marks[mark] = (self.ctx.ss, self.ss_p)

    def g(self, emit, after=(), mark=None):
        w = self._resolve(after)
        self.q["g"].append((emit, w, None))
        self.gg_p += 1
        if mark:
            self.marks[mark] = (self.ctx.gg, self.gg_p)

    def y(self, emit, dma_sem, after=(), mark=None):
        w = self._resolve(after)
        self.q["y"].append((emit, w, dma_sem))
        if not hasattr(dma_sem, "np_"):
            dma_sem.np_ = dma_sem.n
        dma_sem.np_ += 16
        if mark:
            self.marks[mark] = (dma_sem, dma_sem.np_)

    def gd(self, emit, dma_sem, after=(), mark=None):
        w = self._resolve(after)
        self.q["g"].append(("dma", emit, w, dma_sem))
        if not hasattr(dma_sem, "np_"):
            dma_sem.np_ = dma_sem.n
        dma_sem.np_ += 16
        if mark:
            self.marks[mark] = (dma_sem, dma_sem.np_)

    def mark_now(self, name, sem, val):
        self.marks[name] = (sem, val)

    def emit(self):
        ctx = self.ctx

        def run_t(t):
            for emit, waits, sem in self.q["t"]:
                for (s, v) in waits:
                    t.wait_ge(s.h, v)
                ins = emit(t)
                inc(ins, sem, 1)

        def run_w(w, ops):
            for entry in ops:
                if entry[0] == "dma":
                    _, emit, waits, dsem = entry
                    for (s, v) in waits:
                        w.wait(s, v)
                    ins = emit(w)
                    inc(ins, dsem, 16)
                else:
                    emit, waits, _ = entry
                    for (s, v) in waits:
                        w.wait(s, v)
                    emit(w)

        def run_y(sy):
            for emit, waits, dsem in self.q["y"]:
                for (s, v) in waits:
                    sy.wait_ge(s.h, v)
                ins = emit(sy)
                inc(ins, dsem, 16)

        ctx.tsec(run_t)
        ctx.vsec(lambda v: run_w(v, self.q["v"]))
        ctx.ssec(lambda s: run_w(s, self.q["s"]))
        if self.q["g"]:
            ctx.gsec(lambda g: run_w(g, self.q["g"]))
        ctx.sysec(run_y)
        assert ctx.vv.n == self.vv_p and ctx.ss.n == self.ss_p and ctx.gg.n == self.gg_p


def drams_l2(ctx, inner):
    mk = ctx.dint if inner else ctx.din
    d = {}
    for nm in ("qx", "kx", "vx", "gx"):
        d[nm] = mk(nm, [D, TC], BF16)
    d["smx"] = mk("smx", [32, TC], F32)
    d["lnw"] = ctx.din("lnw", [1, DK], F32)
    d["lnb"] = ctx.din("lnb", [1, DK], F32)
    d["masks"] = ctx.din("masks", [3, CH, CH], F32)
    d["idenf"] = ctx.din("idenf", [128, 128], F32)
    d["idenb"] = ctx.din("idenb", [128, 128], BF16)
    d["scr"] = ctx.dint("scr_l2", [3, 64, CH], F32)
    d["og"] = (ctx.dint if inner else ctx.dout)("og_d", [T, 256], BF16)
    return d


def build_l2():
    ctx = Ctx()
    dr = drams_l2(ctx, False)
    phase_l2(ctx, dr)
    return ctx


def phase_l2(ctx, dr):
    qx, kx, vx, gx2 = dr["qx"], dr["kx"], dr["vx"], dr["gx"]
    smx = dr["smx"]
    lnw_d, lnb_d = dr["lnw"], dr["lnb"]
    masks_d, idenf_d, idenb_d = dr["masks"], dr["idenf"], dr["idenb"]
    scr_d = dr["scr"]
    og_d = dr["og"]
    es_ph = ExitStack()
    oc, op = ctx.cur, ctx.prefix
    ctx.cur, ctx.prefix = es_ph, "L2_"

    NCH = 64
    GRP = 8

    if True:
        Kfm = [ctx.sbuf(f"Kfm{h}", [128, T], BF16) for h in range(2)]
        Qfm = [ctx.sbuf(f"Qfm{h}", [128, T], BF16) for h in range(2)]
        Ktm = [ctx.sbuf(f"Ktm{i}", [128, 128], BF16) for i in range(2)]
        Vtm = [ctx.sbuf(f"Vtm{i}", [128, 128], BF16) for i in range(2)]
        Gtm = [ctx.sbuf(f"Gtm{i}", [128, 128], BF16) for i in range(2)]
        ogb = [ctx.sbuf(f"ogb{i}", [128, 128], BF16) for i in range(2)]
        FRN = [ctx.sbuf(f"FRN{i}", [1, GRP * CH], F32) for i in range(2)]
        FRP = [ctx.sbuf(f"FRP{i}", [1, GRP * CH], F32) for i in range(2)]
        FRC = [ctx.sbuf(f"FRC{i}", [1, GRP * CH], F32) for i in range(2)]
        Alg = ctx.sbuf("Alg", [64, CH], F32)
        Blg = ctx.sbuf("Blg", [64, CH], F32)
        asig = ctx.sbuf("asig", [64, CH], F32)
        la = ctx.sbuf("la", [64, CH], F32)
        bsig = ctx.sbuf("bsig", [64, CH], F32)
        lb = ctx.sbuf("lb", [64, CH], F32)
        CUM = ctx.sbuf("CUM", [64, CH + 2], F32)
        cplnb = ctx.sbuf("cplnb", [64, CH], F32)
        negc = ctx.sbuf("negc", [64, CH], F32)
        Ccol = ctx.sbuf("Ccol", [128, 64], F32)
        CPcol = ctx.sbuf("CPcol", [128, 64], F32)
        bcol = ctx.sbuf("bcol", [128, 64], F32)
        LBcol = ctx.sbuf("LBcol", [128, 64], F32)
        cplnbcol = ctx.sbuf("cplnbcol", [128, 64], F32)
        negccol = ctx.sbuf("negccol", [128, 64], F32)
        ccrow = ctx.sbuf("ccrow", [1, 64], F32)
        CBsb = ctx.sbuf("CBsb", [128, 64], F32)
        ECcol = ctx.sbuf("ECcol", [128, 64], F32)
        ecp = ctx.sbuf("ecp", [128, 64], F32)
        bec = ctx.sbuf("bec", [128, 64], F32)
        cbmc = ctx.sbuf("cbmc", [128, 64], F32)
        KSC = ctx.sbuf("KSC", [128, 64], F32)
        ACB = ctx.sbuf("ACB", [128, 64], F32)
        ones1 = ctx.sbuf("ones1", [1, 128], F32)
        msk = [ctx.sbuf(f"msk{i}", [128, 128], F32) for i in range(3)]
        idnf = ctx.sbuf("idnf", [128, 128], F32)
        idnb = ctx.sbuf("idnb", [128, 128], BF16)
        lnwr = ctx.sbuf("lnwr", [1, DK], F32)
        lnbr = ctx.sbuf("lnbr", [1, DK], F32)
        LNWbc = ctx.sbuf("LNWbc", [128, DK], F32)
        LNBbc = ctx.sbuf("LNBbc", [128, DK], F32)
        konst = ctx.sbuf("konst", [128, 1], F32)
        eN = ctx.sbuf("eN", [128, 128], F32)
        eNT = ctx.sbuf("eNT", [128, 128], F32)
        eP = ctx.sbuf("eP", [128, 128], F32)
        kkSL = ctx.sbuf("kkSL", [128, 128], F32)
        kkSU = ctx.sbuf("kkSU", [128, 128], F32)
        qkUI = ctx.sbuf("qkUI", [128, 128], F32)
        Nbf = ctx.sbuf("Nbf", [128, 128], BF16)
        NTbf = ctx.sbuf("NTbf", [128, 128], BF16)
        Mqkbf = ctx.sbuf("Mqkbf", [128, 128], BF16)
        NP = [ctx.sbuf(f"NP{l}", [128, 256], BF16) for l in range(1, 7)]
        Xb = [ctx.sbuf(f"Xb{i}", [128, 256], BF16) for i in range(2)]
        Wfm = ctx.sbuf("Wfm", [128, 128], BF16)
        Rbf = ctx.sbuf("Rbf", [128, 128], BF16)
        Kpbf = ctx.sbuf("Kpbf", [128, 128], BF16)
        O1 = ctx.sbuf("O1", [128, 128], F32)
        Of = ctx.sbuf("Of", [128, 128], F32)
        Oc = ctx.sbuf("Oc", [128, 128], F32)
        gsq = ctx.sbuf("gsq", [128, 128], F32)
        gapp = ctx.sbuf("gapp", [128, 128], F32)
        gcol = ctx.sbuf("gcol", [128, 8], F32)
        Zf = ctx.sbuf("Zf", [128, 128], F32)
        Zbf = ctx.sbuf("Zbf", [128, 128], BF16)

        pKDQ = ctx.psum("pKDQ", [128, 512], F32)
        pQS = ctx.psum("pQS", [128, 512], F32)
        pSOL = ctx.psum("pSOL", [128, 512], F32)
        pWZ = ctx.psum("pWZ", [128, 512], F32)
        pQZ = ctx.psum("pQZ", [128, 512], F32)
        pMK = ctx.psum("pMK", [128, 512], F32)
        pKP = ctx.psum("pKP", [128, 512], F32)
        pTB = ctx.psum("pTB", [128, 1024], BF16)
        pST = pSOL

        dC = ctx.sem("dC")
        dKQ = ctx.sem("dKQ")
        dAB = ctx.sem("dAB")
        dBNw = ctx.sem("dBNw")
        dFR = [ctx.sem("dFR0"), ctx.sem("dFR1")]
        dKT = [ctx.sem("dKT0"), ctx.sem("dKT1")]
        dVT = [ctx.sem("dVT0"), ctx.sem("dVT1")]
        dGT = [ctx.sem("dGT0"), ctx.sem("dGT1")]
        dOG = [ctx.sem("dOG0"), ctx.sem("dOG1")]

        ctx.start()
        P = Prog(ctx)

        P.y(lambda sy: sy.dma_start(out=msk[0][:], in_=masks_d[0]), dC)
        P.y(lambda sy: sy.dma_start(out=msk[1][:], in_=masks_d[1]), dC)
        P.y(lambda sy: sy.dma_start(out=msk[2][:], in_=masks_d[2]), dC)
        P.y(lambda sy: sy.dma_start(out=idnf[:], in_=idenf_d[:, :]), dC)
        P.y(lambda sy: sy.dma_start(out=idnb[:], in_=idenb_d[:, :]), dC)
        P.y(lambda sy: sy.dma_start(out=lnwr[:], in_=lnw_d[:, :]), dC)
        P.y(lambda sy: sy.dma_start(out=lnbr[:], in_=lnb_d[:, :]), dC)
        kx4 = kx.rearrange("(r dd) t -> dd r t", dd=256)
        qx4 = qx.rearrange("(r dd) t -> dd r t", dd=256)
        for h in range(2):
            P.y(lambda sy, h=h: sy.dma_start(out=Kfm[h][:], in_=kx4[h * 128:(h + 1) * 128]), dKQ)
            P.y(lambda sy, h=h: sy.dma_start(out=Qfm[h][:], in_=qx4[h * 128:(h + 1) * 128]), dKQ)
        P.mark_now("dC_all", dC, dC.np_)
        P.mark_now("dKQ_all", dKQ, dKQ.np_)

        P.v(lambda v: v.memset(ones1[:], 1.0), mark="ones")
        P.v(lambda v: v.memset(konst[:], EPS), mark="konst")
        P.t(lambda t: t.matmul(pST[:, 0:DK], ones1[:], lnwr[:], start=True, stop=True),
            after=["ones", "dC_all"], mark="t_lnw")
        P.s(lambda s: s.copy(LNWbc[:], pST[:, 0:DK]), after=["t_lnw"], mark="s_lnw")
        P.t(lambda t: t.matmul(pST[:, 0:DK], ones1[:], lnbr[:], start=True, stop=True),
            after=["s_lnw"], mark="t_lnb")
        P.s(lambda s: s.copy(LNBbc[:], pST[:, 0:DK]), after=["t_lnb"], mark="s_lnb")

        def setup(h, guard):
            hp = f"h{h}"
            smx4 = smx.rearrange("(r q) (lc t) -> q r lc t", q=4, t=CH)
            P.y(lambda sy, h=h: sy.dma_start(out=Alg[:], in_=smx4[h]), dAB, after=guard)
            P.y(lambda sy, h=h: sy.dma_start(out=Blg[:], in_=smx4[2 + h]), dAB)
            P.mark_now(f"{hp}_ab", dAB, dAB.np_)
            P.s(lambda s: s.activation(asig[:], Alg[:], AF.Sigmoid), after=[f"{hp}_ab"] + guard)
            P.s(lambda s: s.activation(la[:], asig[:], AF.Ln), mark=f"{hp}_la")
            P.s(lambda s: s.activation(bsig[:], Blg[:], AF.Sigmoid), mark=f"{hp}_bsig")
            P.s(lambda s: s.activation(lb[:], bsig[:], AF.Ln), mark=f"{hp}_lb")
            P.v(lambda v: v.memset(CUM[:, 0:1], 0.0), after=guard)
            P.v(lambda v: v.tensor_tensor_scan(CUM[:, 1:CH + 1], la[:], la[:], 0.0,
                                               ALU.add, ALU.bypass), after=[f"{hp}_lb"],
                mark=f"{hp}_cum")
            P.v(lambda v: v.tensor_add(cplnb[:], CUM[:, 0:CH], lb[:]), mark=f"{hp}_cplnb")
            P.v(lambda v: v.tensor_scalar(negc[:], CUM[:, 1:CH + 1], -1.0, None, ALU.mult),
                mark=f"{hp}_negc")
            P.y(lambda sy: sy.dma_start(out=scr_d[0], in_=cplnb[:]), dBNw,
                after=[f"{hp}_cplnb"] + guard)
            P.y(lambda sy: sy.dma_start(out=scr_d[1], in_=negc[:]), dBNw, after=[f"{hp}_negc"])
            P.y(lambda sy: sy.dma_start(out=scr_d[2], in_=CUM[:, 1:CH + 1]), dBNw)
            P.mark_now(f"{hp}_bnw", dBNw, dBNw.np_)
            P.t(lambda t: t.transpose(pST[0:128, 0:64], CUM[:, 1:CH + 1], idnf[0:64, 0:64]),
                after=[f"{hp}_negc", "dC_all", "s_lnb"], mark=f"{hp}_tc")
            P.s(lambda s: s.copy(Ccol[:], pST[0:128, 0:64]), after=[f"{hp}_tc"], mark=f"{hp}_ccol")
            P.t(lambda t: t.transpose(pST[0:128, 0:64], CUM[:, 0:CH], idnf[0:64, 0:64]),
                after=[f"{hp}_ccol"], mark=f"{hp}_tcp")
            P.s(lambda s: s.copy(CPcol[:], pST[0:128, 0:64]), after=[f"{hp}_tcp"], mark=f"{hp}_cpcol")
            P.t(lambda t: t.transpose(pST[0:128, 0:64], bsig[:], idnf[0:64, 0:64]),
                after=[f"{hp}_cpcol"], mark=f"{hp}_tb")
            P.s(lambda s: s.copy(bcol[:], pST[0:128, 0:64]), after=[f"{hp}_tb"], mark=f"{hp}_bcol")
            P.t(lambda t: t.transpose(pST[0:1, 0:64], CUM[:, CH:CH + 1], idnf[0:64, 0:64]),
                after=[f"{hp}_bcol"], mark=f"{hp}_tcc")
            P.s(lambda s: s.copy(ccrow[:], pST[0:1, 0:64]), after=[f"{hp}_tcc"], mark=f"{hp}_ccrow")
            P.t(lambda t: t.matmul(pST[0:128, 0:64], ones1[:], ccrow[:], start=True, stop=True),
                after=[f"{hp}_ccrow", "ones"], mark=f"{hp}_tcb")
            P.s(lambda s: s.copy(CBsb[:], pST[0:128, 0:64]), after=[f"{hp}_tcb"], mark=f"{hp}_cb")
            P.s(lambda s: s.activation(LBcol[:], bcol[:], AF.Ln), after=[f"{hp}_bcol"],
                mark=f"{hp}_lbcol")
            P.v(lambda v: v.tensor_add(cplnbcol[:], CPcol[:], LBcol[:]),
                after=[f"{hp}_lbcol", f"{hp}_cpcol"], mark=f"{hp}_cpc")
            P.v(lambda v: v.tensor_scalar(negccol[:], Ccol[:], -1.0, None, ALU.mult),
                after=[f"{hp}_ccol"], mark=f"{hp}_ngc")
            P.s(lambda s: s.activation(ECcol[:], Ccol[:], AF.Exp), mark=f"{hp}_ec")
            P.s(lambda s: s.activation(ecp[:], CPcol[:], AF.Exp), mark=f"{hp}_ecp")
            P.v(lambda v: v.tensor_mul(bec[:], bcol[:], ecp[:]), after=[f"{hp}_ecp", f"{hp}_bcol"],
                mark=f"{hp}_bec")
            P.v(lambda v: v.tensor_sub(cbmc[:], CBsb[:], Ccol[:]), after=[f"{hp}_cb"],
                mark=f"{hp}_cbmc")
            P.s(lambda s: s.activation(KSC[:], cbmc[:], AF.Exp), after=[f"{hp}_cbmc"], mark=f"{hp}_ksc")
            P.s(lambda s: s.activation(ACB[:], CBsb[:], AF.Exp), mark=f"{hp}_acb")
            P.v(lambda v: v.memset(Zf[:], 0.0), after=guard)
            P.v(lambda v: v.memset(Zbf[:], 0.0), mark=f"{hp}_zbf_init")

        def refill(h, g, gg):
            par2 = gg % 2
            rs = slice(g * GRP, (g + 1) * GRP)
            guard = [f"h{h}_bnw"]
            if gg >= 2:
                guard.append(f"g{(gg - 2) * GRP + GRP - 1}_dp")
            P.y(lambda sy: sy.dma_start(out=FRP[par2][:],
                                        in_=scr_d[0:1, rs].rearrange("x a b -> x (a b)")),
                dFR[par2], after=guard)
            P.y(lambda sy: sy.dma_start(out=FRN[par2][:],
                                        in_=scr_d[1:2, rs].rearrange("x a b -> x (a b)")),
                dFR[par2])
            P.y(lambda sy: sy.dma_start(out=FRC[par2][:],
                                        in_=scr_d[2:3, rs].rearrange("x a b -> x (a b)")),
                dFR[par2])
            P.mark_now(f"fr{gg}", dFR[par2], dFR[par2].np_)

        def chunk(h, c, gc):
            hp = f"h{h}"
            p = f"g{gc}"
            pm = f"g{gc - 1}"
            csl = slice(c * CH, (c + 1) * CH)
            KfmH, QfmH = Kfm[h], Qfm[h]
            par = gc % 2
            gg = gc // GRP
            rsl = slice((c % GRP) * CH, (c % GRP + 1) * CH)

            nload = (gc - par) // 2 + 1
            rr, lc = c // 8, c % 8
            rows = slice(rr * 256 + h * 128, rr * 256 + (h + 1) * 128)
            lsl = slice(lc * 128, (lc + 1) * 128)
            ld_guard = [f"g{gc - 2}_kp"] if gc >= 2 else []
            P.y(lambda sy, rows=rows, lsl=lsl: sy.dma_start(
                out=Ktm[par][:], in_=kx[rows, lsl], transpose=True), dKT[par], after=ld_guard)
            ld_guard = [f"g{gc - 2}_x0b"] if gc >= 2 else []
            P.y(lambda sy, rows=rows, lsl=lsl: sy.dma_start(
                out=Vtm[par][:], in_=vx[rows, lsl], transpose=True), dVT[par], after=ld_guard)
            ld_guard = [f"g{gc - 2}_og"] if gc >= 2 else []
            P.y(lambda sy, rows=rows, lsl=lsl: sy.dma_start(
                out=Gtm[par][:], in_=gx2[rows, lsl], transpose=True), dGT[par], after=ld_guard)

            guard1 = ["dKQ_all", f"fr{gg}"]
            if gc > 0:
                guard1 += [f"{pm}_qkui"]
            P.t(lambda t: t.matmul(pKDQ[:, 0:128], KfmH[:, csl], KfmH[:, csl],
                                   start=True, stop=True), after=guard1, mark=f"{p}_kk")
            P.t(lambda t: t.matmul(pKDQ[:, 128:256], ones1[:], FRN[gg % 2][0:1, rsl],
                                   start=True, stop=True), mark=f"{p}_dn")
            P.t(lambda t: t.matmul(pKDQ[:, 256:384], ones1[:], FRP[gg % 2][0:1, rsl],
                                   start=True, stop=True), mark=f"{p}_dnt")
            P.t(lambda t: t.matmul(pKDQ[:, 384:512], ones1[:], FRC[gg % 2][0:1, rsl],
                                   start=True, stop=True), mark=f"{p}_dp")
            P.t(lambda t: t.matmul(pQS[:, 0:128], KfmH[:, csl], QfmH[:, csl],
                                   start=True, stop=True), mark=f"{p}_qk")

            g_eN = [f"{p}_dp", f"{hp}_cpc"] + ([f"{pm}_nbf"] if gc > 0 else [])
            P.v(lambda v: v.tensor_scalar(eN[:], pKDQ[:, 128:256], cplnbcol[:, c:c + 1],
                                          0.0, ALU.add, ALU.min), after=g_eN, mark=f"{p}_ena")
            P.s(lambda s: s.activation(eN[:], eN[:], AF.Exp), after=[f"{p}_ena"], mark=f"{p}_en")
            P.v(lambda v: v.tensor_scalar(eNT[:], pKDQ[:, 256:384], negccol[:, c:c + 1],
                                          0.0, ALU.add, ALU.min),
                after=[f"{p}_dp", f"{hp}_ngc"] + ([f"{pm}_ntbf"] if gc > 0 else []),
                mark=f"{p}_enta")
            P.s(lambda s: s.activation(eNT[:], eNT[:], AF.Exp), after=[f"{p}_enta"],
                mark=f"{p}_ent")
            P.v(lambda v: v.tensor_scalar(eP[:], pKDQ[:, 384:512], negccol[:, c:c + 1],
                                          0.0, ALU.add, ALU.min),
                after=[f"{p}_dp"] + ([f"{pm}_mqkbf"] if gc > 0 else []), mark=f"{p}_epa")
            P.s(lambda s: s.activation(eP[:], eP[:], AF.Exp), after=[f"{p}_epa"], mark=f"{p}_ep")
            g_sl = [f"{p}_dp", "dC_all"] + ([f"{pm}_nbf"] if gc > 0 else [])
            P.v(lambda v: v.tensor_mul(kkSL[:], pKDQ[:, 0:128], msk[0][:]), after=g_sl,
                mark=f"{p}_kksl")
            P.v(lambda v: v.tensor_mul(kkSU[:], pKDQ[:, 0:128], msk[1][:]),
                after=([f"{pm}_ntbf"] if gc > 0 else []), mark=f"{p}_kksu")
            P.v(lambda v: v.tensor_mul(qkUI[:], pQS[:, 0:128], msk[2][:]),
                after=[f"{p}_qk"] + ([f"{pm}_mqkbf"] if gc > 0 else []), mark=f"{p}_qkui")
            g_nb = [f"{p}_en"]
            if gc > 0:
                g_nb.append(f"{pm}_tsol7")
            P.v(lambda v: v.tensor_mul(Nbf[:], eN[:], kkSL[:]), after=g_nb, mark=f"{p}_nbf")
            P.v(lambda v: v.tensor_mul(NTbf[:], eNT[:], kkSU[:]), after=[f"{p}_ent"],
                mark=f"{p}_ntbf")
            g_mq = [f"{p}_ep"]
            if gc > 0:
                g_mq.append(f"{pm}_tmqkr")
            P.v(lambda v: v.tensor_mul(Mqkbf[:], eP[:], qkUI[:]), after=g_mq, mark=f"{p}_mqkbf")

            for l in range(1, 7):
                if l == 1:
                    lA = NTbf[:]
                    lB = Nbf[:]
                else:
                    lA = NP[l - 2][:, 128:256]
                    lB = NP[l - 2][:, 0:128]
                g_sq = [f"{p}_ntbf"] if l == 1 else [f"{p}_sqe{l - 1}"]
                if gc > 0:
                    g_sq.append(f"{pm}_tsol{7 - l}")
                P.t(lambda t, lA=lA, lB=lB: t.matmul(pQS[:, 128:256], lA, lB,
                                                     start=True, stop=True),
                    after=g_sq, mark=f"{p}_sqa{l}")
                P.t(lambda t, lA=lA, lB=lB: t.matmul(pQS[:, 256:384], lB, lA,
                                                     start=True, stop=True), mark=f"{p}_sqb{l}")
                P.s(lambda s, l=l: s.copy(NP[l - 1][:], pQS[:, 128:384]), after=[f"{p}_sqb{l}"],
                    mark=f"{p}_sqe{l}")

            g_x0 = [f"{hp}_bec", (dKT[par], 16 * nload)]
            if gc >= 1:
                g_x0.append(f"{pm}_tsol7")
            P.v(lambda v: v.tensor_scalar(Xb[0][:, 0:128], Ktm[par][:], bec[:, c:c + 1],
                                          None, ALU.mult), after=g_x0, mark=f"{p}_x0a")
            P.v(lambda v: v.tensor_scalar(Xb[0][:, 128:256], Vtm[par][:], bcol[:, c:c + 1],
                                          None, ALU.mult),
                after=[f"{hp}_bcol", (dVT[par], 16 * nload)], mark=f"{p}_x0b")
            P.v(lambda v: v.tensor_scalar(Kpbf[:], Ktm[par][:], KSC[:, c:c + 1], None, ALU.mult),
                after=[f"{hp}_ksc"] + ([f"{pm}_tkpr"] if gc > 0 else []), mark=f"{p}_kp")

            xp = 0
            for j in range(1, 8):
                if j < 7:
                    g_s = [f"{p}_sqe{7 - j}", f"{p}_x0b" if j == 1 else f"{p}_x{j - 1}"]
                else:
                    g_s = [f"{p}_x{j - 1}"]
                if j == 1 and gc > 0:
                    g_s.append(f"{pm}_x7")
                if j < 7:
                    lhs = NP[6 - j]
                    P.t(lambda t, lhs=lhs, xp=xp: t.matmul(pSOL[:, 0:256], lhs[:, 128:256],
                                                           Xb[xp][:], start=True, stop=True),
                        after=g_s, mark=f"{p}_tsol{j}")
                else:
                    P.t(lambda t, xp=xp: t.matmul(pSOL[:, 0:256], NTbf[:], Xb[xp][:],
                                                  start=True, stop=True),
                        after=g_s, mark=f"{p}_tsol{j}")
                g_v = [f"{p}_tsol{j}"]
                if j == 2 and gc > 0:
                    g_v.append(f"{pm}_twtr")
                if j < 7:
                    P.v(lambda v, xp=xp: v.tensor_add(Xb[1 - xp][:], Xb[xp][:], pSOL[:, 0:256]),
                        after=g_v, mark=f"{p}_x{j}")
                else:
                    P.v(lambda v, xp=xp: v.tensor_sub(Xb[1 - xp][:], Xb[xp][:], pSOL[:, 0:256]),
                        after=g_v, mark=f"{p}_x{j}")
                xp = 1 - xp
            xf = xp

            P.t(lambda t, xf=xf: t.transpose(pTB[:, 0:128], Xb[xf][:, 0:128], idnb[:]),
                after=[f"{p}_x7"] + ([f"{pm}_twz"] if gc > 0 else []), mark=f"{p}_twtr")
            P.s(lambda s: s.copy(Wfm[:], pTB[:, 0:128]), after=[f"{p}_twtr"], mark=f"{p}_wfm")

            g_wz = [f"{p}_wfm", f"{hp}_zbf_init" if c == 0 else f"{pm}_zbf"]
            if gc > 0:
                g_wz.append(f"{pm}_r")
            P.t(lambda t: t.matmul(pWZ[:, 0:128], Wfm[:], Zbf[:], start=True, stop=True),
                after=g_wz, mark=f"{p}_twz")
            g_qz = [f"{pm}_o1"] if gc > 0 else []
            P.t(lambda t: t.matmul(pQZ[:, 0:128], QfmH[:, csl], Zbf[:], start=True, stop=True),
                after=g_qz, mark=f"{p}_tqz")
            P.v(lambda v, xf=xf: v.tensor_sub(Rbf[:], Xb[xf][:, 128:256], pWZ[:, 0:128]),
                after=[f"{p}_twz"], mark=f"{p}_r")
            P.t(lambda t: t.matmul(pMK[:, 0:128], Mqkbf[:], Rbf[:], start=True, stop=True),
                after=[f"{p}_r", f"{p}_mqkbf"] + ([f"{pm}_oadd"] if gc > 0 else []),
                mark=f"{p}_tmqkr")
            P.t(lambda t: t.matmul(pKP[:, 0:128], Kpbf[:], Rbf[:], start=True, stop=True),
                after=[f"{p}_kp"] + ([f"{pm}_zstt"] if gc > 0 else []), mark=f"{p}_tkpr")
            g_o1 = [f"{p}_tqz", f"{hp}_ec"]
            if gc > 0:
                g_o1.append(f"{pm}_oadd")
            P.s(lambda s: s.activation(O1[:], pQZ[:, 0:128], AF.Identity,
                                       scale=ECcol[:, c:c + 1]), after=g_o1, mark=f"{p}_o1")
            P.v(lambda v: v.tensor_add(Of[:], O1[:], pMK[:, 0:128]),
                after=[f"{p}_o1", f"{p}_tmqkr"], mark=f"{p}_oadd")
            P.v(lambda v: v.scalar_tensor_tensor(Zf[:], Zf[:], ACB[:, c:c + 1], pKP[:, 0:128],
                                                 ALU.mult, ALU.add),
                after=[f"{p}_tkpr", f"{hp}_acb"], mark=f"{p}_zstt")
            P.v(lambda v: v.tensor_copy(Zbf[:], Zf[:]), mark=f"{p}_zbf")

            P.v(lambda v: v.reduce_sum(gcol[:, 0:1], Of[:], axis=AX.X), after=[f"{p}_oadd"])
            P.v(lambda v: v.tensor_scalar(gcol[:, 1:2], gcol[:, 0:1], 1.0 / DK, None, ALU.mult))
            P.v(lambda v: v.tensor_scalar(Oc[:], Of[:], gcol[:, 1:2], None, ALU.subtract))
            P.v(lambda v: v.tensor_mul(gsq[:], Oc[:], Oc[:]))
            P.v(lambda v: v.reduce_sum(gcol[:, 2:3], gsq[:], axis=AX.X), mark=f"{p}_gvar")
            P.s(lambda s: s.activation(gcol[:, 3:4], gcol[:, 2:3], AF.Ln,
                                       bias=konst[:, 0:1], scale=1.0 / DK),
                after=[f"{p}_gvar", "konst"])
            P.s(lambda s: s.activation(gcol[:, 4:5], gcol[:, 3:4], AF.Exp, scale=-0.5),
                mark=f"{p}_grstd")
            P.v(lambda v: v.scalar_tensor_tensor(gapp[:], Oc[:], gcol[:, 4:5], LNWbc[:],
                                                 ALU.mult, ALU.mult),
                after=[f"{p}_grstd", "s_lnw"])
            P.v(lambda v: v.tensor_add(gapp[:], gapp[:], LNBbc[:]), after=["s_lnb"])
            g_og = [(dGT[par], 16 * nload)]
            if gc >= 2:
                g_og.append((dOG[par], 16 * ((gc - par) // 2)))
            P.v(lambda v: v.tensor_mul(ogb[par][:], gapp[:], Gtm[par][:]), after=g_og,
                mark=f"{p}_og")
            P.y(lambda sy: sy.dma_start(out=og_d[c * CH:(c + 1) * CH, h * 128:(h + 1) * 128],
                                        in_=ogb[par][:]), dOG[par], after=[f"{p}_og"])

        NG = NCH // GRP
        setup(0, ["dC_all"])
        for c in range(NCH):
            if c % GRP == 0:
                refill(0, c // GRP, c // GRP)
            chunk(0, c, c)
        gl = NCH - 1
        g1 = [f"g{gl}_dp", f"g{gl}_tqz", f"g{gl}_x0a", f"g{gl}_zstt",
              f"g{gl}_o1", f"g{gl}_kp", f"g{gl}_grstd", f"g{gl}_og"]
        setup(1, g1)
        for c in range(NCH):
            if c % GRP == 0:
                refill(1, c // GRP, NG + c // GRP)
            chunk(1, c, NCH + c)

        P.emit()

        def fin(sy):
            for sem in (dOG[0], dOG[1]):
                sy.wait_ge(sem.h, sem.n)
        ctx.sysec(fin)

    es_ph.close()
    ctx.cur, ctx.prefix = oc, op
    return ctx


def _l2_inputs(stage, inp):
    masks = np.stack([
        np.tril(np.ones((CH, CH), np.float32), -1),
        np.triu(np.ones((CH, CH), np.float32), 1),
        np.triu(np.ones((CH, CH), np.float32), 0),
    ])
    idenf = np.eye(128, dtype=np.float32)
    idenb = np.eye(128, dtype=np.float32).astype(BF)
    lnw = np.asarray(inp["L0_attn_ln_w"], np.float32)[None, :]
    lnb = np.asarray(inp["L0_attn_ln_b"], np.float32)[None, :]

    def a2a(full, c):
        return np.ascontiguousarray(np.concatenate(
            [full[2 * c * 128:(2 * c + 2) * 128, r * TC:(r + 1) * TC] for r in range(NC)],
            axis=0))

    maps = []
    for c in range(NC):
        h0, h1 = 2 * c, 2 * c + 1
        smx = np.ascontiguousarray(np.concatenate(
            [np.stack([stage["alphaL"][h0, r * TC:(r + 1) * TC],
                       stage["alphaL"][h1, r * TC:(r + 1) * TC],
                       stage["betaL"][h0, r * TC:(r + 1) * TC],
                       stage["betaL"][h1, r * TC:(r + 1) * TC]]) for r in range(NC)],
            axis=0).astype(np.float32))
        maps.append({
            "qx": a2a(stage["q_full"], c),
            "kx": a2a(stage["k_full"], c),
            "vx": a2a(stage["v_full"], c),
            "gx": a2a(stage["g_full"], c),
            "smx": smx, "lnw": lnw, "lnb": lnb,
            "masks": masks, "idenf": idenf, "idenb": idenb,
        })
    return maps


def run_l2(stage, inp, **kw):
    ctx = build_l2()
    maps = _l2_inputs(stage, inp)
    res = run_bass_kernel_spmd(ctx.nc, maps, core_ids=list(range(NC)), **kw)
    return res.results


def drams_l3(ctx, inner):
    d = {}
    d["ogx"] = (ctx.dint if inner else ctx.din)("ogx", [T, 256], BF16)
    d["Wo"] = ctx.din("Wo", [D, D], F32)
    d["ebias"] = ctx.din("ebias", [1, 8], F32)
    d["e8d"] = ctx.din("e8d", [8, 8 * 128], BF16)
    d["onesd"] = ctx.din("onesd", [1, 128], F32)
    d["w13"] = ctx.din("w13", [18, D, 1024], F32)
    d["w2t"] = ctx.din("w2t", [9, 1024, D], F32)
    d["s_spill"] = ctx.dint("s_spill", [TC, D], F32)
    d["x1"] = (ctx.dint if inner else ctx.dout)("x1", [TC, D], F32)
    return d


def build_l3():
    ctx = Ctx()
    dr = drams_l3(ctx, False)
    dr["xnfm"] = ctx.din("xnfm", [D, TC], BF16)
    dr["gateL"] = ctx.din("gateL", [8, TC], F32)
    dr["iden"] = ctx.din("iden", [128, 128], F32)
    dr["xm"] = ctx.din("xm", [TC, D], F32)
    phase_l3(ctx, dr)
    return ctx


def phase_l3(ctx, dr):
    ogx_d = dr["ogx"]
    Wo_d, xnfm_d, gateL_d = dr["Wo"], dr["xnfm"], dr["gateL"]
    ebias_d, e8_d, ones_d, iden_d = dr["ebias"], dr["e8d"], dr["onesd"], dr["iden"]
    w13_d, w2_d, xm_d, s_d, x1_d = dr["w13"], dr["w2t"], dr["xm"], dr["s_spill"], dr["x1"]
    es_ph = ExitStack()
    oc, op = ctx.cur, ctx.prefix
    ctx.cur, ctx.prefix = es_ph, "L3_"

    NE = 9

    if True:
        xnfm = [ctx.sbuf(f"xnfm{c}", [128, TC], BF16) for c in range(16)]
        stile = [ctx.sbuf(f"stile{i}", [128, 512], F32) for i in range(2)]
        gl = ctx.sbuf("gl", [8, TC], F32)
        sselfm = ctx.sbuf("sselfm", [8, TC], BF16)
        ones1 = ctx.sbuf("ones1", [1, 128], F32)
        e8s = ctx.sbuf("e8s", [8, 8 * 128], BF16)
        ebs = ctx.sbuf("ebs", [1, 8], F32)
        bbc = ctx.sbuf("bbc", [128, 8], F32)
        idn = ctx.sbuf("idn", [128, 128], F32)
        rT = ctx.sbuf("rT", [128, 8], F32)
        rB = ctx.sbuf("rB", [128, 8], F32)
        rE = ctx.sbuf("rE", [128, 8], F32)
        rQ = ctx.sbuf("rQ", [128, 8], F32)
        rS = ctx.sbuf("rS", [128, 8], F32)
        rcol = ctx.sbuf("rcol", [128, 4], F32)
        sval = ctx.sbuf("sval", [128, 8], F32)
        sz = ctx.sbuf("sz", [128, 512], BF16)
        sbc = [ctx.sbuf(f"sbc{i}", [128, 512], F32) for i in range(2)]
        xres = [ctx.sbuf(f"xres{i}", [128, 512], F32) for i in range(2)]
        sres = [ctx.sbuf(f"sres{i}", [128, 512], F32) for i in range(2)]
        gsil = ctx.sbuf("gsil", [128, 512], F32)
        gmul = ctx.sbuf("gmul", [128, 512], F32)
        x1t = [ctx.sbuf(f"x1t{i}", [128, 512], F32) for i in range(2)]
        es_a = ExitStack()
        ogfm = [ctx.sbuf(f"ogfm{c}", [128, TC], BF16, es=es_a) for c in range(16)]
        wob = [ctx.sbuf(f"wob{c}", [128, D], BF16, es=es_a) for c in range(16)]

        pZ = [ctx.psum(f"pZ{i}", [128, 512], F32) for i in range(2)]
        pW2 = [ctx.psum(f"pW2{i}", [128, 512], F32) for i in range(4)]
        pMISC = ctx.psum("pMISC", [128, 512], F32)
        pBC = ctx.psum("pBC", [128, 512], F32)

        dOG = ctx.sem("dOG")
        dWO = ctx.sem("dWO")
        dXN = ctx.sem("dXN")
        dGL = ctx.sem("dGL")
        dSP = ctx.sem("dSP")
        dW1 = [ctx.sem("dW1a"), ctx.sem("dW1b")]
        dW2 = ctx.sem("dW2")
        dXR = [ctx.sem("dXR0"), ctx.sem("dXR1")]
        dSR = [ctx.sem("dSR0"), ctx.sem("dSR1")]
        dX1 = ctx.sem("dX1")
        dX1.np_ = 0

        ctx.start()
        P = Prog(ctx)

        for c in range(16):
            for j in range(8):
                rr, hh = c // 2, c % 2
                P.y(lambda sy, c=c, j=j, rr=rr, hh=hh: sy.dma_start(
                    out=ogfm[c][:, j * 128:(j + 1) * 128],
                    in_=ogx_d[rr * TC + j * 128:rr * TC + (j + 1) * 128,
                              hh * 128:(hh + 1) * 128], transpose=True), dOG)
            P.y(lambda sy, c=c: sy.dma_start(out=xnfm[c][:], in_=xnfm_d[c * 128:(c + 1) * 128, :]), dXN)
            P.gd(lambda g, c=c: g.dma_start(out=wob[c][:], in_=Wo_d[c * 128:(c + 1) * 128, :]), dWO)
        P.mark_now("og_all", dOG, dOG.np_)
        P.mark_now("xn_all", dXN, dXN.np_)
        P.mark_now("wo_all", dWO, dWO.np_)

        P.y(lambda sy: sy.dma_start(out=gl[:], in_=gateL_d[:, :]), dGL)
        P.y(lambda sy: sy.dma_start(out=ones1[:], in_=ones_d[:, :]), dGL)
        P.y(lambda sy: sy.dma_start(out=e8s[:], in_=e8_d[:, :]), dGL)
        P.y(lambda sy: sy.dma_start(out=ebs[:], in_=ebias_d[:, :]), dGL)
        P.y(lambda sy: sy.dma_start(out=idn[:], in_=iden_d[:, :]), dGL)
        P.mark_now("gl_all", dGL, dGL.np_)

        for j in range(8):
            for dg in range(4):
                gi = j * 4 + dg
                par = gi % 2
                guard = ["og_all", "wo_all"]
                if gi >= 2:
                    guard.append(f"sA{gi - 2}")
                for kc in range(16):
                    P.t(lambda t, kc=kc, j=j, dg=dg, par=par: t.matmul(
                        pZ[par][:, 0:512], ogfm[kc][:, j * 128:(j + 1) * 128],
                        wob[kc][:, dg * 512:(dg + 1) * 512],
                        start=(kc == 0), stop=(kc == 15)),
                        after=guard if kc == 0 else (), mark=f"tA{gi}" if kc == 15 else None)
                g_e = [f"tA{gi}"]
                if gi >= 2:
                    g_e.append((dSP, 16 * gi))
                P.s(lambda s, par=par: s.copy(stile[par][:], pZ[par][:, 0:512]),
                    after=g_e, mark=f"sA{gi}")
                P.y(lambda sy, j=j, dg=dg, par=par: sy.dma_start(
                    out=s_d[j * 128:(j + 1) * 128, dg * 512:(dg + 1) * 512], in_=stile[par][:]),
                    dSP, after=[f"sA{gi}"])
        P.mark_now("sp_all", dSP, dSP.np_)

        P.t(lambda t: t.matmul(pBC[0:128, 0:8], ones1[:], ebs[:], start=True, stop=True),
            after=["gl_all"], mark="t_bbc")
        P.s(lambda s: s.copy(bbc[:], pBC[0:128, 0:8]), after=["t_bbc"], mark="s_bbc")
        for j in range(8):
            g_tr = ["gl_all"]
            if j >= 1:
                g_tr.append(f"r_ss{j - 1}")
            P.t(lambda t, j=j: t.transpose(pMISC[0:128, 0:8], gl[0:8, j * 128:(j + 1) * 128],
                                           idn[0:8, 0:8]), after=g_tr, mark=f"r_tr{j}")
            g_st = [f"r_tr{j}"]
            if j >= 1:
                g_st.append(f"r_sv{j - 1}")
            P.s(lambda s: s.activation(rT[:], pMISC[0:128, 0:8], AF.Sigmoid),
                after=g_st, mark=f"r_st{j}")
            P.v(lambda v: v.tensor_add(rB[:], rT[:], bbc[:]), after=[f"r_st{j}", "s_bbc"])
            P.v(lambda v: v.reduce_max(rcol[:, 0:1], rB[:], axis=AX.X))
            P.v(lambda v: v.tensor_scalar(rE[:], rB[:], rcol[:, 0:1], None, ALU.is_equal))
            P.v(lambda v: v.tensor_tensor_scan(rQ[:], rE[:], rE[:], 0.0, ALU.add, ALU.bypass))
            P.v(lambda v: v.tensor_scalar(rQ[:], rQ[:], 1.0, None, ALU.is_equal))
            P.v(lambda v: v.tensor_mul(rS[:], rE[:], rQ[:]))
            P.v(lambda v: v.scalar_tensor_tensor(rB[:], rS[:], -1e30, rB[:], ALU.mult, ALU.add))
            P.v(lambda v: v.reduce_max(rcol[:, 1:2], rB[:], axis=AX.X))
            P.v(lambda v: v.tensor_scalar(rE[:], rB[:], rcol[:, 1:2], None, ALU.is_equal))
            P.v(lambda v: v.tensor_tensor_scan(rQ[:], rE[:], rE[:], 0.0, ALU.add, ALU.bypass))
            P.v(lambda v: v.tensor_scalar(rQ[:], rQ[:], 1.0, None, ALU.is_equal))
            P.v(lambda v: v.tensor_mul(rE[:], rE[:], rQ[:]))
            P.v(lambda v: v.tensor_add(rS[:], rS[:], rE[:]))
            P.v(lambda v: v.tensor_mul(sval[:], rT[:], rS[:]), mark=f"r_sv{j}")
            g_tv = [f"r_sv{j}", f"r_st{j}"]
            P.t(lambda t, j=j: t.transpose(pMISC[0:8, 0:128], sval[:], idn[:]),
                after=g_tv, mark=f"r_tv{j}")
            P.s(lambda s, j=j: s.copy(sselfm[0:8, j * 128:(j + 1) * 128], pMISC[0:8, 0:128]),
                after=[f"r_tv{j}"], mark=f"r_ss{j}")

        es_a.close()
        wm = [[ctx.sbuf(f"wm0_{c}", [128, 1024], BF16) for c in range(16)]]
        hti = [[ctx.sbuf(f"ht{e}_{hc}", [128, 512], BF16) for hc in range(8)] for e in range(NE)]
        xe = [ctx.sbuf(f"xe{c}", [128, 512], BF16) for c in range(16)]
        w2b = [ctx.sbuf(f"w2b{hc}", [128, 512], BF16) for hc in range(8)]
        _hsz = [ctx.sbuf(f"hsz{hc}", [128, 512], BF16) for hc in range(8)]
        hsz = [_hsz for _ in range(NE)]
        widx = 0

        def load_mat(eidx, m):
            nonlocal widx
            guard = ["wo_all", "tA31"]
            if widx >= 1:
                guard.append(f"zmm_done{widx - 1}")
            for c in range(16):
                P.gd(lambda g, eidx=eidx, m=m, c=c: g.dma_start(
                    out=wm[0][c][:], in_=w13_d[2 * eidx + m, c * 128:(c + 1) * 128, :]),
                    dW1[0], after=(guard if c == 0 else ()))
            P.mark_now(f"w{widx}", dW1[0], dW1[0].np_)
            widx += 1

        for grp in range(2):
            gsl = slice(grp * 512, (grp + 1) * 512)
            for eidx in range(NE):
                ge = grp * NE + eidx
                epar = ge % 2
                if eidx < 8:
                    g_b = ["r_ss7"]
                    if ge >= 2:
                        g_b.append(f"xe_done{ge - 2}")
                    P.t(lambda t, eidx=eidx, gsl=gsl: t.matmul(
                        pBC[:, 0:512], e8s[:, eidx * 128:(eidx + 1) * 128], sselfm[0:8, gsl],
                        start=True, stop=True), after=g_b, mark=f"t_bc{ge}")
                    P.s(lambda s, epar=epar: s.copy(sbc[epar][:], pBC[:, 0:512]),
                        after=[f"t_bc{ge}"], mark=f"s_bc{ge}")
                    for c in range(16):
                        g_xe = [f"s_bc{ge}", "xn_all"]
                        if ge >= 1:
                            g_xe.append(f"zmm_done{2 * ge - 1}")
                        P.v(lambda v, c=c, epar=epar, gsl=gsl: v.tensor_mul(
                            xe[c][:], xnfm[c][:, gsl], sbc[epar][:]),
                            after=g_xe if c == 0 else ())
                    P.mark_now(f"xe_done{ge}", ctx.vv, P.vv_p)
                else:
                    P.mark_now(f"xe_done{ge}", ctx.vv, P.vv_p)
                load_mat(eidx, 0)
                w1i = widx - 1
                for hc in range(8):
                    par = hc % 2
                    g_m = [f"w{w1i}"]
                    if eidx < 8:
                        g_m.append(f"xe_done{ge}")
                    if hc >= 2:
                        g_m.append(f"sz{ge}_{hc - 2}")
                    elif ge >= 1:
                        g_m.append(f"h{ge - 1}_{6 + hc}")
                    for c in range(16):
                        P.t(lambda t, c=c, hc=hc, par=par, eidx=eidx, gsl=gsl:
                            t.matmul(pZ[par][:, 0:512],
                                     wm[0][c][:, hc * 128:(hc + 1) * 128],
                                     (xe[c][:] if eidx < 8 else xnfm[c][:, gsl]),
                                     start=(c == 0), stop=(c == 15)),
                            after=g_m if c == 0 else (),
                            mark=f"z1mm{ge}_{hc}" if c == 15 else None)
                    g_sz = [f"z1mm{ge}_{hc}"]
                    if hc >= 1:
                        g_sz.append(f"h{ge}_{hc - 1}_free" if False else f"sz{ge}_{hc - 1}")
                    P.s(lambda s, par=par, hc=hc, eidx=eidx: s.activation(
                        hsz[eidx][hc][:], pZ[par][:, 0:512], AF.Silu),
                        after=g_sz, mark=f"sz{ge}_{hc}")
                P.mark_now(f"zmm_done{w1i}", P.tt, P.tt_p)
                load_mat(eidx, 1)
                w3i = widx - 1
                for hc in range(8):
                    par = hc % 2
                    g_m = [f"w{w3i}", f"sz{ge}_{hc}"]
                    if hc >= 2:
                        g_m.append(f"h{ge}_{hc - 2}")
                    else:
                        g_m.append(f"sz{ge}_{6 + hc}")
                    for c in range(16):
                        P.t(lambda t, c=c, hc=hc, par=par, eidx=eidx, gsl=gsl:
                            t.matmul(pZ[par][:, 0:512],
                                     wm[0][c][:, hc * 128:(hc + 1) * 128],
                                     (xe[c][:] if eidx < 8 else xnfm[c][:, gsl]),
                                     start=(c == 0), stop=(c == 15)),
                            after=g_m if c == 0 else (),
                            mark=f"z3mm{ge}_{hc}" if c == 15 else None)
                    g_h = [f"z3mm{ge}_{hc}"]
                    if grp == 1:
                        g_h.append(f"w2r0_{eidx}_{hc}")
                    P.v(lambda v, hc=hc, par=par, eidx=eidx: v.tensor_mul(
                        hti[eidx][hc][:], pZ[par][:, 0:512], hsz[eidx][hc][:]),
                        after=g_h, mark=f"h{ge}_{hc}")
                P.mark_now(f"zmm_done{w3i}", P.tt, P.tt_p)

            for dg in range(4):
                for eidx in range(NE):
                    for hc in range(8):
                        g_l = []
                        if hc == 0:
                            g_l = [f"h{grp * NE + eidx}_7"]
                            if grp > 0 or dg > 0 or eidx > 0:
                                g_l.append(f"w2mmprev")
                        P.gd(lambda g, eidx=eidx, dg=dg, hc=hc: g.dma_start(
                            out=w2b[hc][:], in_=w2bf_slice(w2_d, eidx, hc, dg)), dW2,
                            after=g_l)
                    P.mark_now(f"w2l{grp}_{dg}_{eidx}", dW2, dW2.np_)
                    for t128 in range(4):
                        g_mm = [f"w2l{grp}_{dg}_{eidx}"]
                        if eidx == 0 and hc == 7:
                            pass
                        if eidx == 0:
                            g_mm.append(f"h{grp * NE + NE - 1}_7")
                            if grp > 0 or dg > 0:
                                g_mm.append(f"csil{grp * 4 + dg - 1}_{t128}")
                        for hc in range(8):
                            P.t(lambda t, t128=t128, hc=hc, eidx=eidx:
                                t.matmul(pW2[t128][:, 0:512],
                                         hti[eidx][hc][:, t128 * 128:(t128 + 1) * 128],
                                         w2b[hc][:],
                                         start=(eidx == 0 and hc == 0),
                                         stop=(eidx == NE - 1 and hc == 7)),
                                after=g_mm if hc == 0 else (),
                                mark=(f"w2f{grp}_{dg}_{t128}" if (eidx == NE - 1 and hc == 7)
                                      else None))
                    P.mark_now(f"w2mmprev", P.tt, P.tt_p)
                    if grp == 0:
                        for hc in range(8):
                            P.mark_now(f"w2r0_{eidx}_{hc}", P.tt, P.tt_p)
                gdi = grp * 4 + dg
                for t128 in range(4):
                    j = grp * 4 + t128
                    par = t128 % 2
                    g_xl = []
                    if gdi >= 1:
                        g_xl.append(f"cmb{gdi - 1}_{t128}")
                    elif t128 >= 2:
                        g_xl.append(f"cmb0_{t128 - 2}")
                    P.y(lambda sy, j=j, dg=dg, par=par: sy.dma_start(
                        out=xres[par][:], in_=xm_d[j * 128:(j + 1) * 128, dg * 512:(dg + 1) * 512]),
                        dXR[par], after=g_xl, mark=f"dx{gdi}_{t128}")
                    P.y(lambda sy, j=j, dg=dg, par=par: sy.dma_start(
                        out=sres[par][:], in_=s_d[j * 128:(j + 1) * 128, dg * 512:(dg + 1) * 512]),
                        dSR[par], after=["sp_all"] + g_xl, mark=f"ds{gdi}_{t128}")
                    g_sil = [f"w2f{grp}_{dg}_{t128}"]
                    if t128 >= 1:
                        g_sil.append(f"cmbv{gdi}_{t128 - 1}")
                    elif gdi >= 1:
                        g_sil.append(f"cmbv{gdi - 1}_3")
                    P.s(lambda s, t128=t128: s.activation(gsil[:], pW2[t128][:, 0:512], AF.Silu),
                        after=g_sil, mark=f"csil{gdi}_{t128}")
                    g_cv = [f"csil{gdi}_{t128}", (dSR[par], dSR[par].np_)]
                    P.v(lambda v, par=par: v.tensor_mul(gmul[:], gsil[:], sres[par][:]),
                        after=g_cv, mark=f"cmbv{gdi}_{t128}")
                    g_cmb = [(dXR[par], dXR[par].np_)]
                    if dX1.np_ > 0:
                        g_cmb.append((dX1, dX1.np_))
                    P.v(lambda v, par=par: v.tensor_add(x1t[par][:], gmul[:], xres[par][:]),
                        after=g_cmb, mark=f"cmb{gdi}_{t128}")
                    P.y(lambda sy, j=j, dg=dg, par=par: sy.dma_start(
                        out=x1_d[j * 128:(j + 1) * 128, dg * 512:(dg + 1) * 512],
                        in_=x1t[par][:]), dX1, after=[f"cmb{gdi}_{t128}"])

        P.emit()

        def fin(sy):
            sy.wait_ge(dX1.h, dX1.n)
        ctx.sysec(fin)

    es_ph.close()
    ctx.cur, ctx.prefix = oc, op
    return ctx


def w2bf_slice(w2_d, eidx, hc, dg):
    return w2_d[eidx, hc * 128:(hc + 1) * 128, dg * 512:(dg + 1) * 512]


def _l3_inputs(stage, inp):
    w13 = []
    for e in range(8):
        w13.append(np.asarray(inp["L0_ffn_w1"][e]).T)
        w13.append(np.asarray(inp["L0_ffn_w3"][e]).T)
    w13.append(np.asarray(inp["L0_ffn_shared_w1"]).T)
    w13.append(np.asarray(inp["L0_ffn_shared_w3"]).T)
    w13 = np.ascontiguousarray(np.stack(w13).astype(np.float32))
    w2t = np.ascontiguousarray(np.stack(
        [np.asarray(inp["L0_ffn_w2"][e]).T for e in range(8)]
        + [np.asarray(inp["L0_ffn_shared_w2"]).T]).astype(np.float32))
    e8 = np.zeros((8, 8 * 128), dtype=BF)
    for e in range(8):
        e8[e, e * 128:(e + 1) * 128] = 1
    onesd = np.ones((1, 128), np.float32)
    iden = np.eye(128, dtype=np.float32)
    ebias = np.ascontiguousarray(np.asarray(inp["L0_ffn_expert_bias"], np.float32)[None, :])
    Wo = np.ascontiguousarray(np.asarray(inp["L0_attn_Wo"], np.float32))
    x = np.asarray(inp["L0_attn_x"], np.float32)[0]
    og_tm = stage["og_fm"].T
    maps = []
    for c in range(NC):
        sl = slice(c * TC, (c + 1) * TC)
        ogx = np.ascontiguousarray(np.concatenate(
            [og_tm[sl, 2 * r * 128:(2 * r + 2) * 128] for r in range(NC)], axis=0))
        maps.append({
            "ogx": ogx,
            "Wo": Wo,
            "xnfm": np.ascontiguousarray(stage["xnfm"][:, sl]),
            "gateL": np.ascontiguousarray(stage["gateL"][:, sl]),
            "ebias": ebias, "e8d": e8, "onesd": onesd, "iden": iden,
            "w13": w13, "w2t": w2t,
            "xm": np.ascontiguousarray(x[sl]),
        })
    return maps


def run_l3(stage, inp, **kw):
    ctx = build_l3()
    maps = _l3_inputs(stage, inp)
    res = run_bass_kernel_spmd(ctx.nc, maps, core_ids=list(range(NC)), **kw)
    return res.results


def drams_l4(ctx, inner):
    mk = ctx.dint if inner else ctx.dout
    d = {}
    d["Wqkv"] = ctx.din("Wqkv", [3, D, D], F32)
    d["bcols"] = ctx.din("bcols", [128, 48], F32)
    d["gatew"] = ctx.din("gatew", [128, 8 * 16], F32)
    d["cosd"] = ctx.din("cosd", [128, TC], F32)
    d["ssind"] = ctx.din("ssind", [128, TC], F32)
    d["xn1fm"] = mk("xn1fm", [D, TC], BF16)
    d["q1_d"] = mk("q1_d", [D, TC], BF16)
    d["k1_d"] = mk("k1_d", [D, TC], BF16)
    d["v1tm"] = mk("v1tm", [TC, D], BF16)
    d["gate1L"] = mk("gate1L", [8, TC], F32)
    return d


def build_l4():
    ctx = Ctx()
    dr = drams_l4(ctx, False)
    dr["x1"] = ctx.din("x1", [TC, D], F32)
    dr["iden"] = ctx.din("iden", [128, 128], F32)
    phase_l4(ctx, dr)
    return ctx


def phase_l4(ctx, dr):
    x1_d, Wqkv_d, bcols_d, gatew_d = dr["x1"], dr["Wqkv"], dr["bcols"], dr["gatew"]
    cos_d, ssin_d, iden_d = dr["cosd"], dr["ssind"], dr["iden"]
    xnfm1_d, q1_d, k1_d, v1tm_d, gate1L_d = (dr["xn1fm"], dr["q1_d"], dr["k1_d"],
                                             dr["v1tm"], dr["gate1L"])
    es_ph = ExitStack()
    oc, op = ctx.cur, ctx.prefix
    ctx.cur, ctx.prefix = es_ph, "L4_"

    proj_d = [q1_d, k1_d, None]

    if True:
        xnbf = [ctx.sbuf(f"xnbf{c}", [128, TC], BF16) for c in range(16)]
        idn = ctx.sbuf("idn", [128, 128], F32)
        gws = ctx.sbuf("gws", [128, 8 * 16], F32)
        bcs = ctx.sbuf("bcs", [128, 48], F32)
        glt = ctx.sbuf("glt", [8, TC], F32)
        lnc = ctx.sbuf("lnc", [128, D], F32)
        col = ctx.sbuf("col", [128, 12], F32)
        konst = ctx.sbuf("konst", [128, 2], F32)
        cosb = ctx.sbuf("cosb", [128, TC], F32)
        ssinb = ctx.sbuf("ssinb", [128, TC], F32)
        qpre = [ctx.sbuf(f"qpre{i}", [128, TC], F32) for i in range(2)]
        shuf = ctx.sbuf("shuf", [128, TC], F32)
        rt1 = ctx.sbuf("rt1", [128, TC], F32)
        rt2s = ctx.sbuf("rt2s", [128, TC], F32)
        obf = [ctx.sbuf(f"obf{i}", [128, TC], BF16) for i in range(2)]
        vtb = [ctx.sbuf(f"vtb{i}", [128, 128], BF16) for i in range(2)]
        es_t = ExitStack()
        xt = [ctx.sbuf(f"xt{i}", [128, D], F32, es=es_t) for i in range(3)]
        xnfm = [ctx.sbuf(f"xnfm{c}", [128, TC], F32, es=es_t) for c in range(16)]

        pA = [ctx.psum(f"pA{i}", [128, 512], F32) for i in range(2)]
        pB = [ctx.psum(f"pB{i}", [128, 512], F32) for i in range(2)]
        pT = [ctx.psum(f"pT{i}", [128, 512], F32) for i in range(2)]

        dC = ctx.sem("dC")
        dXT = [ctx.sem(f"dXT{i}") for i in range(3)]
        dW = ctx.sem("dW")
        dS = ctx.sem("dS")
        dQ = [ctx.sem("dQ0"), ctx.sem("dQ1")]
        dQV = [ctx.sem("dQV0"), ctx.sem("dQV1")]

        ctx.start()
        P = Prog(ctx)

        P.y(lambda sy: sy.dma_start(out=idn[:], in_=iden_d[:, :]), dC)
        P.y(lambda sy: sy.dma_start(out=gws[:], in_=gatew_d[:, :]), dC)
        P.y(lambda sy: sy.dma_start(out=bcs[:], in_=bcols_d[:, :]), dC)
        P.y(lambda sy: sy.dma_start(out=cosb[:], in_=cos_d[:, :]), dC)
        P.y(lambda sy: sy.dma_start(out=ssinb[:], in_=ssin_d[:, :]), dC)
        P.mark_now("dC_all", dC, dC.np_)
        P.v(lambda v: v.memset(konst[:, 0:1], EPS), mark="konst")

        for i in range(3):
            P.y(lambda sy, i=i: sy.dma_start(out=xt[i % 3][:], in_=x1_d[i * 128:(i + 1) * 128, :]),
                dXT[i % 3], mark=f"xl{i}")
        for i in range(8):
            o = 6 * (i % 2)
            tile = xt[i % 3]
            P.v(lambda v, i=i, o=o, tile=tile: v.reduce_sum(col[:, o:o + 1], tile[:], axis=AX.X),
                after=[(dXT[i % 3], 16 * (i // 3 + 1))] + ([f"xn{i - 2}"] if i >= 2 else []))
            P.v(lambda v, o=o: v.tensor_scalar(col[:, o + 1:o + 2], col[:, o:o + 1], 1.0 / D,
                                               None, ALU.mult))
            P.v(lambda v, o=o, tile=tile: v.tensor_scalar(lnc[:], tile[:], col[:, o + 1:o + 2],
                                                          None, ALU.subtract))
            P.v(lambda v, tile=tile: v.tensor_mul(tile[:], lnc[:], lnc[:]))
            P.v(lambda v, o=o, tile=tile: v.reduce_sum(col[:, o + 2:o + 3], tile[:], axis=AX.X),
                mark=f"sumsq{i}")
            P.s(lambda s, o=o: s.activation(col[:, o + 3:o + 4], col[:, o + 2:o + 3], AF.Ln,
                                            bias=konst[:, 0:1], scale=1.0 / D),
                after=[f"sumsq{i}", "konst"])
            P.s(lambda s, o=o: s.activation(col[:, o + 4:o + 5], col[:, o + 3:o + 4], AF.Exp,
                                            scale=-0.5), mark=f"rstd{i}")
            P.v(lambda v, o=o, tile=tile: v.tensor_scalar(tile[:], lnc[:], col[:, o + 4:o + 5],
                                                          None, ALU.mult),
                after=[f"rstd{i}"], mark=f"xn{i}")
            for c in range(16):
                gx = i * 16 + c
                par = gx % 2
                g_t = [f"xn{i}", "dC_all"]
                if gx >= 2:
                    g_t.append(f"te{gx - 2}")
                P.t(lambda t, i=i, c=c, par=par, tile=tile: t.transpose(
                    pT[par][0:128, 0:128], tile[:, c * 128:(c + 1) * 128], idn[:]),
                    after=g_t, mark=(f"tT{i}" if c == 15 else f"tt{gx}"))
                P.s(lambda s, i=i, c=c, par=par: s.copy(xnfm[c][:, i * 128:(i + 1) * 128],
                                                        pT[par][0:128, 0:128]),
                    after=[f"tT{i}" if c == 15 else f"tt{gx}"], mark=f"te{gx}")
            if i + 3 < 8:
                P.y(lambda sy, i=i: sy.dma_start(out=xt[(i + 3) % 3][:],
                                                 in_=x1_d[(i + 3) * 128:(i + 4) * 128, :]),
                    dXT[(i + 3) % 3], after=[f"tT{i}"], mark=f"xl{i + 3}")
        P.mark_now("ln_done", ctx.ss, P.ss_p)

        for gi in range(2):
            last = None
            for c in range(16):
                P.t(lambda t, gi=gi, c=c: t.matmul(pA[gi][0:8, 0:512], gws[:, c * 8:(c + 1) * 8],
                                                   xnfm[c][:, gi * 512:(gi + 1) * 512],
                                                   start=(c == 0), stop=(c == 15)),
                    after=["ln_done", "dC_all"] if c == 0 else (), mark=f"tg{gi}" if c == 15 else None)
            P.s(lambda s, gi=gi: s.copy(glt[0:8, gi * 512:(gi + 1) * 512], pA[gi][0:8, 0:512]),
                after=[f"tg{gi}"], mark=f"ge{gi}")
        P.y(lambda sy: sy.dma_start(out=gate1L_d[:, :], in_=glt[:, :]), dS,
            after=["ge0", "ge1"])

        for c in range(16):
            P.v(lambda v, c=c: v.tensor_copy(xnbf[c][:], xnfm[c][:]), after=["ln_done"],
                mark=f"cb{c}")
            P.y(lambda sy, c=c: sy.dma_start(out=xnfm1_d[c * 128:(c + 1) * 128, :],
                                             in_=xnbf[c][:]), dS, after=[f"cb{c}"])
        P.mark_now("cast_done", ctx.vv, P.vv_p)

        es_t.close()
        wbuf = [ctx.sbuf(f"wbuf{c}", [128, D], BF16) for c in range(16)]

        for m in range(3):
            g_w = ["cast_done", "ge0", "ge1"]
            if m >= 1:
                g_w.append(f"pm{m - 1}")
            for c in range(16):
                P.gd(lambda g, m=m, c=c: g.dma_start(out=wbuf[c][:],
                                                     in_=Wqkv_d[m, c * 128:(c + 1) * 128, :]),
                     dW, after=(g_w if c == 0 else ()))
            P.mark_now(f"wl{m}", dW, dW.np_)
            for j in range(16):
                gx = m * 16 + j
                par = gx % 2
                for gi in range(2):
                    g_mm = [f"wl{m}"]
                    if gx >= 2:
                        g_mm.append(f"ev{gx - 2}_{gi}")
                    last = None
                    for c in range(16):
                        P.t(lambda t, c=c, j=j, gi=gi, par=par: t.matmul(
                            pB[par][:, 0:512] if gi == 0 else pA[par][:, 0:512],
                            wbuf[c][:, j * 128:(j + 1) * 128],
                            xnbf[c][:, gi * 512:(gi + 1) * 512],
                            start=(c == 0), stop=(c == 15)),
                            after=g_mm if c == 0 else (),
                            mark=f"pm{gx}_{gi}" if c == 15 else None)
                    g_ev = [f"pm{gx}_{gi}"]
                    if gx >= 2:
                        g_ev.append(f"rope{gx - 2}" if (gx - 2) % 48 < 32 or True else "")
                    P.s(lambda s, gi=gi, par=par, m=m, j=j: s.activation(
                        qpre[par][:, gi * 512:(gi + 1) * 512],
                        (pB[par][:, 0:512] if gi == 0 else pA[par][:, 0:512]),
                        AF.Identity, bias=bcs[:, m * 16 + j:m * 16 + j + 1]),
                        after=g_ev, mark=f"ev{gx}_{gi}")
                if m < 2:
                    g_r = [f"ev{gx}_1", "dC_all"]
                    if gx >= 2:
                        g_r.append((dQ[par], 16 * ((gx - par) // 2)))
                    P.v(lambda v, par=par: v.stream_shuffle(shuf[:], qpre[par][:],
                                                            [i ^ 1 for i in range(32)]),
                        after=g_r)
                    P.v(lambda v, par=par: v.tensor_mul(rt1[:], qpre[par][:], cosb[:]))
                    P.v(lambda v: v.tensor_mul(rt2s[:], shuf[:], ssinb[:]))
                    P.v(lambda v, par=par: v.tensor_add(obf[par][:], rt1[:], rt2s[:]),
                        mark=f"rope{gx}")
                    P.y(lambda sy, m=m, j=j, par=par: sy.dma_start(
                        out=proj_d[m][j * 128:(j + 1) * 128, :], in_=obf[par][:]),
                        dQ[par], after=[f"rope{gx}"])
                else:
                    for t8 in range(8):
                        vgx = j * 8 + t8
                        g_t = [f"ev{gx}_1"]
                        if vgx >= 2:
                            g_t.append(f"vte{vgx - 2}")
                        P.t(lambda t, t8=t8, par=par, vp=vgx % 2: t.transpose(
                            pT[vp][0:128, 0:128], qpre[par][:, t8 * 128:(t8 + 1) * 128],
                            idn[:]), after=g_t, mark=f"vtr{vgx}")
                        g_e = [f"vtr{vgx}"]
                        if vgx >= 2:
                            g_e.append((dQV[vgx % 2], 16 * ((vgx - vgx % 2) // 2)))
                        P.s(lambda s, vp=vgx % 2: s.copy(vtb[vp][:], pT[vp][0:128, 0:128]),
                            after=g_e, mark=f"vte{vgx}")
                        P.y(lambda sy, j=j, t8=t8, vp=vgx % 2: sy.dma_start(
                            out=v1tm_d[t8 * 128:(t8 + 1) * 128, j * 128:(j + 1) * 128],
                            in_=vtb[vp][:]), dQV[vgx % 2], after=[f"vte{vgx}"])
                    P.mark_now(f"rope{gx}", ctx.ss, P.ss_p)
            P.mark_now(f"pm{m}", P.tt, P.tt_p)

        P.emit()

        def fin(sy):
            for sem in (dS, dQ[0], dQ[1], dQV[0], dQV[1]):
                sy.wait_ge(sem.h, sem.n)
        ctx.sysec(fin)

    es_ph.close()
    ctx.cur, ctx.prefix = oc, op
    return ctx


def _l4_inputs(inp, x1_full):
    Wqkv = np.ascontiguousarray(np.stack([inp["L1_attn_Wq"], inp["L1_attn_Wk"],
                                          inp["L1_attn_Wv"]]).astype(np.float32))
    bcols = np.stack([np.asarray(inp["L1_attn_bq"]), np.asarray(inp["L1_attn_bk"]),
                      np.asarray(inp["L1_attn_bv"])]).astype(np.float32)
    bcols = np.ascontiguousarray(bcols.reshape(3, 16, 128).transpose(2, 0, 1).reshape(128, 48))
    gatew = np.asarray(inp["L1_ffn_gate_w"], np.float32)
    gatew = np.ascontiguousarray(gatew.reshape(16, 128, 8).transpose(1, 0, 2).reshape(128, 128))
    iden = np.eye(128, dtype=np.float32)
    hd = 128
    jj = np.arange(hd // 2)
    inv_freq = 10000.0 ** (-2.0 * jj / hd)
    pos = np.arange(T)
    ang = pos[:, None] * inv_freq[None, :]
    sin2 = np.stack([np.sin(ang), np.sin(ang)], -1).reshape(T, hd)
    cos2 = np.stack([np.cos(ang), np.cos(ang)], -1).reshape(T, hd)
    sign = np.where(np.arange(hd) % 2 == 0, -1.0, 1.0)
    ssin = sin2 * sign[None, :]
    cosT = np.ascontiguousarray(cos2.T.astype(np.float32))
    ssinT = np.ascontiguousarray(ssin.T.astype(np.float32))
    maps = []
    for c in range(NC):
        sl = slice(c * TC, (c + 1) * TC)
        maps.append({
            "x1": np.ascontiguousarray(x1_full[sl]),
            "Wqkv": Wqkv, "bcols": bcols, "gatew": gatew,
            "cosd": np.ascontiguousarray(cosT[:, sl]),
            "ssind": np.ascontiguousarray(ssinT[:, sl]),
            "iden": iden,
        })
    return maps


def run_l4(inp, x1_full, **kw):
    ctx = build_l4()
    maps = _l4_inputs(inp, x1_full)
    res = run_bass_kernel_spmd(ctx.nc, maps, core_ids=list(range(NC)), **kw)
    return res.results


def drams_l5(ctx, inner):
    d = {}
    d["k1ag"] = (ctx.dint if inner else ctx.din)("k1ag", [NC * D, TC], BF16)
    d["v1t"] = (ctx.dint if inner else ctx.din)("v1t", [T, D], BF16)
    d["Wo1"] = ctx.din("Wo1", [D, D], F32)
    d["bo1"] = ctx.din("bo1", [1, D], F32)
    d["s1"] = (ctx.dint if inner else ctx.dout)("s1", [TC, D], F32)
    return d


def build_l5():
    ctx = Ctx()
    dr = drams_l5(ctx, False)
    dr["q1_d"] = ctx.din("q1", [D, TC], BF16)
    dr["onesd"] = ctx.din("onesd", [1, 128], F32)
    phase_l5(ctx, dr)
    return ctx


def phase_l5(ctx, dr):
    q1_d, k1ag_d, v1_d = dr["q1_d"], dr["k1ag"], dr["v1t"]
    Wo_d, bo_d, ones_d, s1_d = dr["Wo1"], dr["bo1"], dr["onesd"], dr["s1"]
    es_ph = ExitStack()
    oc, op = ctx.cur, ctx.prefix
    ctx.cur, ctx.prefix = es_ph, "L5_"

    QB = 512
    NKT = T // 128

    if True:
        qfm = [ctx.sbuf(f"qfm{c}", [128, TC], BF16) for c in range(16)]
        kh = [ctx.sbuf(f"kh{i}", [128, T], BF16) for i in range(2)]
        vh = [ctx.sbuf(f"vh{i}", [128, T], BF16) for i in range(2)]
        wos = [ctx.sbuf(f"wos{c}", [128, D], BF16) for c in range(16)]
        PTa = ctx.sbuf("PTa", [128, 2048], BF16)
        PTb = ctx.sbuf("PTb", [128, 1024], BF16)
        On = [ctx.sbuf(f"On{h}", [128, QB], BF16) for h in range(16)]
        ones1 = ctx.sbuf("ones1", [1, 128], F32)
        onesc = ctx.sbuf("onesc", [128, 1], F32)
        onescb = ctx.sbuf("onescb", [128, 1], BF16)
        qsq = ctx.sbuf("qsq", [128, QB], F32)
        kcol = ctx.sbuf("kcol", [128, 1], F32)
        kcsq = ctx.sbuf("kcsq", [128, 1], F32)
        lnkb = ctx.sbuf("lnkb", [1, 16], F32)
        qn2 = ctx.sbuf("qn2", [1, QB], F32)
        sc1 = ctx.sbuf("sc1", [1, 4], F32)
        mhl = ctx.sbuf("mhl", [1, 1], F32)
        negMc = ctx.sbuf("negMc", [128, 1], F32)
        dsb = ctx.sbuf("dsb", [1, QB], F32)
        drc = ctx.sbuf("drc", [1, QB], F32)
        drb = ctx.sbuf("drb", [128, QB], F32)
        bob = [ctx.sbuf(f"bob{dg}", [128, 512], F32) for dg in range(4)]
        bor = ctx.sbuf("bor", [1, 512], F32)
        s1t = [ctx.sbuf(f"s1t{i}", [128, 512], F32) for i in range(2)]

        pSa = ctx.psum("pSa", [128, 2048], F32)
        pSb = ctx.psum("pSb", [128, 1024], F32)
        pO = ctx.psum("pO", [128, 512], F32)
        pDEN = ctx.psum("pDEN", [128, 512], F32)

        dQ = ctx.sem("dQ")
        dK = [ctx.sem("dK0"), ctx.sem("dK1")]
        dV = [ctx.sem("dV0"), ctx.sem("dV1")]
        dWO = ctx.sem("dWO")
        dC = ctx.sem("dC")
        dBOL = ctx.sem("dBOL")
        dS1 = ctx.sem("dS1")

        ctx.start()
        P = Prog(ctx)

        for c in range(16):
            P.y(lambda sy, c=c: sy.dma_start(out=qfm[c][:], in_=q1_d[c * 128:(c + 1) * 128, :]), dQ)
            P.gd(lambda g, c=c: g.dma_start(out=wos[c][:], in_=Wo_d[c * 128:(c + 1) * 128, :]), dWO)
        P.y(lambda sy: sy.dma_start(out=ones1[:], in_=ones_d[:, :]), dC)
        P.mark_now("q_all", dQ, dQ.np_)
        P.mark_now("wo_all", dWO, dWO.np_)
        P.mark_now("c_all", dC, dC.np_)
        P.v(lambda v: v.memset(onesc[:], 1.0), mark="onesc")
        P.v(lambda v: v.memset(onescb[:], 1.0), mark="onescb")
        P.v(lambda v: v.memset(mhl[:], -0.5 * np.log(128.0)), mark="mhl")
        for dg in range(4):
            P.y(lambda sy, dg=dg: sy.dma_start(out=bor[:], in_=bo_d[0:1, dg * 512:(dg + 1) * 512]),
                dBOL, after=[f"bo{dg - 1}"] if dg else [], mark=f"bol{dg}")
            P.t(lambda t, dg=dg: t.matmul(pO[:, 0:512], ones1[:], bor[:],
                                          start=True, stop=True),
                after=["c_all", f"bol{dg}"] + ([f"bo{dg - 1}"] if dg else []), mark=f"tbo{dg}")
            P.s(lambda s, dg=dg: s.copy(bob[dg][:], pO[:, 0:512]), after=[f"tbo{dg}"],
                mark=f"bo{dg}")

        sched = []
        kt = 0
        while kt < NKT:
            n = 4 if (len(sched) % 2 == 0) else 2
            n = min(n, NKT - kt)
            sched.append((kt, n))
            kt += n

        for qb in range(2):
            qsl = slice(qb * QB, (qb + 1) * QB)
            for h in range(16):
                hi = qb * 16 + h
                par = hi % 2
                nk = (hi - par) // 2 + 1
                g_k = [f"att{hi - 2}"] if hi >= 2 else []
                P.y(lambda sy, h=h, par=par: sy.dma_start(
                    out=kh[par][:],
                    in_=k1ag_d.rearrange("(r d) t -> d r t", d=D)[h * 128:(h + 1) * 128]),
                    dK[par], after=g_k)
                P.y(lambda sy, h=h, par=par: sy.dma_start(
                    out=vh[par][:].rearrange("p (kt c) -> p kt c", c=128),
                    in_=v1_d.rearrange("(kt p) (hh c) -> hh p kt c", p=128, c=128)[h]),
                    dV[par], after=g_k)
                KH, VH = kh[par], vh[par]
                kld = (dK[par], 16 * nk)
                vld = (dV[par], 16 * nk)
                if qb == 0:
                    P.v(lambda v, KH=KH: v.reduce_max(kcol[:], KH[:], axis=AX.X,
                                                      apply_absolute_value=True),
                        after=[kld] + ([f"m_lnkb{h - 2}"] if h >= 2 else []), mark=f"kc{h}")
                    P.s(lambda s: s.square(kcsq[:], kcol[:]), after=[f"kc{h}"], mark=f"kcs{h}")
                    g_kb = [f"kcs{h}", "onesc"]
                    if hi >= 1:
                        g_kb.append(f"m_expM{hi - 1}")
                    P.t(lambda t: t.matmul(pDEN[0:1, 0:1], kcsq[:], onesc[:],
                                           start=True, stop=True), after=g_kb, mark=f"m_kb{h}")
                    P.s(lambda s, h=h: s.activation(lnkb[0:1, h:h + 1], pDEN[0:1, 0:1], AF.Ln),
                        after=[f"m_kb{h}"], mark=f"m_lnkb{h}")
                g_qs = ["q_all"]
                if hi >= 1:
                    g_qs.append(f"m_negM{hi - 1}")
                P.s(lambda s, h=h, qsl=qsl: s.square(qsq[:], qfm[h][:, qsl]), after=g_qs,
                    mark=f"m_qsq{hi}")
                g_q2 = [f"m_qsq{hi}", "onesc"]
                g_q2.append(f"m_lnkb{h}" if qb == 0 else "m_lnkb15")
                P.t(lambda t: t.matmul(pDEN[0:1, 0:QB], onesc[:], qsq[:], start=True, stop=True),
                    after=g_q2, mark=f"m_q2{hi}")
                P.s(lambda s: s.copy(qn2[:], pDEN[0:1, 0:QB]), after=[f"m_q2{hi}"], mark=f"m_qc{hi}")
                P.v(lambda v: v.reduce_max(sc1[0:1, 0:1], qn2[:], axis=AX.X), after=[f"m_qc{hi}"],
                    mark=f"m_qm{hi}")
                P.s(lambda s: s.activation(sc1[0:1, 1:2], sc1[0:1, 0:1], AF.Ln),
                    after=[f"m_qm{hi}"], mark=f"m_lnq{hi}")
                P.v(lambda v, h=h: v.tensor_add(sc1[0:1, 2:3], sc1[0:1, 1:2], lnkb[0:1, h:h + 1]),
                    after=[f"m_lnq{hi}"], mark=f"m_sum{hi}")
                P.s(lambda s: s.activation(sc1[0:1, 3:4], sc1[0:1, 2:3], AF.Exp, scale=0.5,
                                           bias=mhl[0:1, 0:1]), after=[f"m_sum{hi}", "mhl"],
                    mark=f"m_expM{hi}")
                P.t(lambda t: t.matmul(pDEN[0:128, 0:1], ones1[:], sc1[0:1, 3:4],
                                       start=True, stop=True), after=[f"m_expM{hi}"],
                    mark=f"m_bc{hi}")
                P.s(lambda s: s.activation(negMc[:], pDEN[0:128, 0:1], AF.Identity, scale=-1.0),
                    after=[f"m_bc{hi}"], mark=f"m_negM{hi}")

                for bi, (kt0, n) in enumerate(sched):
                    bank = pSa if bi % 2 == 0 else pSb
                    pt = PTa if bi % 2 == 0 else PTb
                    for i in range(n):
                        ktc = kt0 + i
                        g_mm = []
                        if i == 0:
                            g_mm = [kld, f"m_negM{hi}"]
                            if bi >= 2:
                                g_mm.append(f"ex{hi}_{bi - 2}")
                            elif hi >= 1:
                                g_mm.append(f"ex{hi - 1}_{last_bi - 2 + bi}" if False else
                                            f"ex{hi - 1}_{len(sched) - 2 + bi}")
                        P.t(lambda t, ktc=ktc, i=i, bank=bank, KH=KH, h=h, qsl=qsl: t.matmul(
                            bank[:, i * 512:(i + 1) * 512], KH[:, ktc * 128:(ktc + 1) * 128],
                            qfm[h][:, qsl], start=True, stop=True),
                            after=g_mm, mark=f"st{hi}_{bi}" if i == n - 1 else None)
                    g_ex = [f"st{hi}_{bi}"]
                    if bi >= 2:
                        g_ex.append(f"dn{hi}_{bi - 2}")
                    elif hi >= 1:
                        g_ex.append(f"dn{hi - 1}_{len(sched) - 2 + bi}")
                    P.s(lambda s, bank=bank, pt=pt, n=n: s.activation(
                        pt[:, 0:n * 512], bank[:, 0:n * 512], AF.Exp,
                        bias=negMc[:, 0:1], scale=RDK), after=g_ex, mark=f"ex{hi}_{bi}")
                    for i in range(n):
                        ktc = kt0 + i
                        g_dn = [f"ex{hi}_{bi}"]
                        if bi == 0 and i == 0:
                            g_dn.append(f"m_negM{hi}")
                            g_dn.append(vld)
                            if hi >= 1:
                                g_dn.append(f"on{hi - 1}")
                        P.t(lambda t, ktc=ktc, i=i, pt=pt, first=(bi == 0 and i == 0): t.matmul(
                            pDEN[0:1, 0:QB], onescb[:], pt[0:128, i * 512:(i + 1) * 512],
                            start=first, stop=(ktc == NKT - 1)),
                            after=g_dn + (["onescb"] if (bi == 0 and i == 0) else []),
                            mark=f"dnf{hi}" if ktc == NKT - 1 else None)
                        P.t(lambda t, ktc=ktc, i=i, pt=pt, VH=VH, first=(bi == 0 and i == 0): t.matmul(
                            pO[:, 0:QB], VH[:, ktc * 128:(ktc + 1) * 128],
                            pt[0:128, i * 512:(i + 1) * 512],
                            start=first, stop=(ktc == NKT - 1)),
                            mark=f"dn{hi}_{bi}" if i == n - 1 else None)
                P.s(lambda s: s.copy(dsb[:], pDEN[0:1, 0:QB]), after=[f"dnf{hi}", f"dn{hi}_{len(sched) - 1}"],
                    mark=f"de{hi}")
                P.v(lambda v: v.reciprocal(drc[:], dsb[:]), after=[f"de{hi}"], mark=f"dr{hi}")
                P.t(lambda t: t.matmul(pDEN[0:128, 0:QB], ones1[:], drc[:], start=True, stop=True),
                    after=[f"dr{hi}"], mark=f"drb{hi}")
                P.s(lambda s: s.copy(drb[:], pDEN[0:128, 0:QB]), after=[f"drb{hi}"], mark=f"drs{hi}")
                g_on = [f"drs{hi}", f"dn{hi}_{len(sched) - 1}"]
                if qb == 1:
                    g_on.append(f"woe{0}_{h}" if False else f"wo_done0")
                P.v(lambda v, h=h: v.tensor_mul(On[h][:], pO[:, 0:QB], drb[:]), after=g_on,
                    mark=f"on{hi}")
                P.mark_now(f"att{hi}", ctx.vv, P.vv_p)

            for t128 in range(4):
                for dg in range(4):
                    wi = t128 * 4 + dg
                    par = wi % 2
                    g_w = ["wo_all", f"on{qb * 16 + 15}"]
                    if wi >= 2:
                        g_w.append(f"woe{qb}_{wi - 2}")
                    for h in range(16):
                        P.t(lambda t, h=h, t128=t128, dg=dg, par=par: t.matmul(
                            (pSa if par == 0 else pSb)[:, 0:512],
                            On[h][:, t128 * 128:(t128 + 1) * 128],
                            wos[h][:, dg * 512:(dg + 1) * 512],
                            start=(h == 0), stop=(h == 15)),
                            after=g_w if h == 0 else (),
                            mark=f"wom{qb}_{wi}" if h == 15 else None)
                    g_e = [f"wom{qb}_{wi}", f"bo{dg}"]
                    if qb * 16 + wi >= 2:
                        g_e.append((dS1, 16 * (qb * 16 + wi)))
                    P.v(lambda v, dg=dg, par=par: v.tensor_add(s1t[par][:],
                                                               (pSa if par == 0 else pSb)[:, 0:512],
                                                               bob[dg][:]),
                        after=g_e, mark=f"woe{qb}_{wi}")
                    P.y(lambda sy, qb=qb, t128=t128, dg=dg, par=par: sy.dma_start(
                        out=s1_d[qb * 512 + t128 * 128:qb * 512 + (t128 + 1) * 128,
                                 dg * 512:(dg + 1) * 512], in_=s1t[par][:]),
                        dS1, after=[f"woe{qb}_{wi}"])
            P.mark_now(f"wo_done{qb}", P.tt, P.tt_p)

        P.emit()

        def fin(sy):
            sy.wait_ge(dS1.h, dS1.n)
        ctx.sysec(fin)

    es_ph.close()
    ctx.cur, ctx.prefix = oc, op
    return ctx


def _l5_inputs(inp, stage):
    Wo = np.ascontiguousarray(np.asarray(inp["L1_attn_Wo"], np.float32))
    bo = np.ascontiguousarray(np.asarray(inp["L1_attn_bo"], np.float32)[None, :])
    onesd = np.ones((1, 128), np.float32)
    k1ag = np.ascontiguousarray(np.concatenate(
        [stage["k1_full"][:, r * TC:(r + 1) * TC] for r in range(NC)], axis=0))
    maps = []
    for c in range(NC):
        sl = slice(c * TC, (c + 1) * TC)
        maps.append({
            "q1": np.ascontiguousarray(stage["q1_full"][:, sl]),
            "k1ag": k1ag, "v1t": stage["v1_tm"],
            "Wo1": Wo, "bo1": bo, "onesd": onesd,
        })
    return maps


def run_l5(inp, stage, **kw):
    ctx = build_l5()
    maps = _l5_inputs(inp, stage)
    res = run_bass_kernel_spmd(ctx.nc, maps, core_ids=list(range(NC)), **kw)
    return res.results


def drams_l6(ctx):
    d = {}
    d["upw"] = ctx.din("upw", [D, T], F32)
    d["dwnw"] = ctx.din("dwnw", [T, D], F32)
    d["eww"] = ctx.din("eww", [D, D], F32)
    d["gam"] = ctx.din("gam", [1, D], F32)
    d["bet"] = ctx.din("bet", [1, D], F32)
    d["outp"] = ctx.dout("outp", [TC, D], F32)
    return d


def build_l6():
    ctx = Ctx()
    dr = drams_l6(ctx)
    dr["xn1fm"] = ctx.din("xn1fm", [D, TC], BF16)
    dr["gate1L"] = ctx.din("gate1L", [8, TC], F32)
    dr["x1"] = ctx.din("x1", [TC, D], F32)
    dr["s1"] = ctx.din("s1", [TC, D], F32)
    dr["onesd"] = ctx.din("onesd", [1, 128], F32)
    dr["iden"] = ctx.din("iden", [128, 128], F32)
    phase_l6(ctx, dr)
    return ctx


def phase_l6(ctx, dr):
    xnfm1_d, gate1L_d = dr["xn1fm"], dr["gate1L"]
    up_d, down_d, ew_d = dr["upw"], dr["dwnw"], dr["eww"]
    gam_d, bet_d = dr["gam"], dr["bet"]
    x1_d, s1_d = dr["x1"], dr["s1"]
    ones_d, iden_d = dr["onesd"], dr["iden"]
    out_d = dr["outp"]
    es_ph = ExitStack()
    oc, op = ctx.cur, ctx.prefix
    ctx.cur, ctx.prefix = es_ph, "L6_"

    NIC = 64

    if True:
        xnfm = [ctx.sbuf(f"xnfm{c}", [128, TC], BF16) for c in range(16)]
        h1 = [ctx.sbuf(f"h1_{i}", [128, 512], BF16) for i in range(NIC)]
        upb = [ctx.sbuf(f"upb{i}", [128, 512], BF16) for i in range(2)]
        dwnb = [ctx.sbuf("dwnb0", [128, 4096], BF16)]
        ewb = [ctx.sbuf("ewb0", [128, D], BF16)]
        relu = ctx.sbuf("relu", [128, 512], F32)
        scl = ctx.sbuf("scl", [128, 8], F32)
        sbcast = ctx.sbuf("sbcast", [128, TC], F32)
        idn = ctx.sbuf("idn", [128, 128], F32)
        ones1 = ctx.sbuf("ones1", [1, 128], F32)
        grow = ctx.sbuf("grow", [1, 512], F32)
        gbc = [ctx.sbuf(f"gbc{dg}", [128, 512], F32) for dg in range(4)]
        bbc = [ctx.sbuf(f"bbc{dg}", [128, 512], F32) for dg in range(4)]
        ytm = [ctx.sbuf(f"ytm{i}", [128, D], F32) for i in range(4)]
        konst = ctx.sbuf("konst", [128, 2], F32)
        col = ctx.sbuf("col", [128, 8], F32)
        lnc = ctx.sbuf("lnc", [128, D], F32)
        lsq = ctx.sbuf("lsq", [128, D], F32)

        es_b = ExitStack()
        gl = ctx.sbuf("gl", [8, TC], F32, es=es_b)
        logt = ctx.sbuf("logt", [128, 8], F32, es=es_b)
        rE = ctx.sbuf("rE", [128, 8], F32, es=es_b)
        rQ = ctx.sbuf("rQ", [128, 8], F32, es=es_b)
        rM = ctx.sbuf("rM", [128, 8], F32, es=es_b)
        rcol = ctx.sbuf("rcol", [128, 4], F32, es=es_b)
        srow = ctx.sbuf("srow", [1, TC], F32, es=es_b)

        pU = [ctx.psum(f"pU{i}", [128, 512], F32) for i in range(4)]
        pY = [ctx.psum(f"pY{i}", [128, 512], F32) for i in range(2)]
        pT2 = [ctx.psum(f"pT2{i}", [128, 512], F32) for i in range(2)]

        dXN = ctx.sem("dXN")
        dGL = ctx.sem("dGL")
        dUP = [ctx.sem("dUP0"), ctx.sem("dUP1")]
        dDW = [ctx.sem("dDW0"), ctx.sem("dDW1")]
        dEW = [ctx.sem("dEW0"), ctx.sem("dEW1")]
        dGB = ctx.sem("dGB")
        dXS = [ctx.sem("dXS0"), ctx.sem("dXS1")]
        dOU = ctx.sem("dOU")
        dOU.np_ = 0

        ctx.start()
        P = Prog(ctx)

        for c in range(16):
            P.y(lambda sy, c=c: sy.dma_start(out=xnfm[c][:], in_=xnfm1_d[c * 128:(c + 1) * 128, :]), dXN)
        P.y(lambda sy: sy.dma_start(out=gl[:], in_=gate1L_d[:, :]), dGL)
        P.y(lambda sy: sy.dma_start(out=idn[:], in_=iden_d[:, :]), dGL)
        P.y(lambda sy: sy.dma_start(out=ones1[:], in_=ones_d[:, :]), dGL)
        P.mark_now("xn_all", dXN, dXN.np_)
        P.mark_now("gl_all", dGL, dGL.np_)
        P.v(lambda v: v.memset(konst[:, 0:1], EPS), mark="konst")

        for j in range(8):
            g_tr = ["gl_all"]
            if j >= 1:
                g_tr.append(f"sc_sr{j - 1}")
            P.t(lambda t, j=j: t.transpose(pT2[0][0:128, 0:8], gl[0:8, j * 128:(j + 1) * 128],
                                           idn[0:8, 0:8]), after=g_tr, mark=f"sc_tr{j}")
            P.s(lambda s: s.copy(logt[:], pT2[0][0:128, 0:8]), after=[f"sc_tr{j}"], mark=f"sc_lg{j}")
            P.v(lambda v: v.reduce_max(rcol[:, 0:1], logt[:], axis=AX.X), after=[f"sc_lg{j}"])
            P.v(lambda v: v.tensor_scalar(rE[:], logt[:], rcol[:, 0:1], None, ALU.is_equal))
            P.v(lambda v: v.tensor_tensor_scan(rQ[:], rE[:], rE[:], 0.0, ALU.add, ALU.bypass))
            P.v(lambda v: v.tensor_scalar(rQ[:], rQ[:], 1.0, None, ALU.is_equal))
            P.v(lambda v: v.tensor_mul(rE[:], rE[:], rQ[:]))
            P.v(lambda v: v.scalar_tensor_tensor(rM[:], rE[:], -1e30, logt[:], ALU.mult, ALU.add))
            P.v(lambda v: v.reduce_max(rcol[:, 1:2], rM[:], axis=AX.X))
            P.v(lambda v, j=j: v.tensor_add(scl[:, j:j + 1], rcol[:, 0:1], rcol[:, 1:2]),
                mark=f"sc_sc{j}")
            P.t(lambda t, j=j: t.transpose(pT2[0][0:1, 0:128], scl[:, j:j + 1], idn[:]),
                after=[f"sc_sc{j}", f"sc_lg{j}"], mark=f"sc_tv{j}")
            P.s(lambda s, j=j: s.copy(srow[0:1, j * 128:(j + 1) * 128], pT2[0][0:1, 0:128]),
                after=[f"sc_tv{j}"], mark=f"sc_sr{j}")
        for gi in range(2):
            P.t(lambda t, gi=gi: t.matmul(pT2[1][:, 0:512], ones1[:],
                                          srow[0:1, gi * 512:(gi + 1) * 512], start=True, stop=True),
                after=["sc_sr7"] + ([f"sc_bc{gi - 1}"] if gi else []), mark=f"sc_tb{gi}")
            P.s(lambda s, gi=gi: s.copy(sbcast[:, gi * 512:(gi + 1) * 512], pT2[1][:, 0:512]),
                after=[f"sc_tb{gi}"], mark=f"sc_bc{gi}")

        for dg in range(4):
            P.y(lambda sy, dg=dg: sy.dma_start(out=grow[:], in_=gam_d[0:1, dg * 512:(dg + 1) * 512]),
                dGB, after=[f"lb_g{dg - 1}"] if dg else [], mark=f"lb_lg{dg}")
            P.t(lambda t, dg=dg: t.matmul(pT2[1][:, 0:512], ones1[:], grow[:], start=True, stop=True),
                after=[f"lb_lg{dg}", "sc_bc1"], mark=f"lb_tg{dg}")
            P.s(lambda s, dg=dg: s.copy(gbc[dg][:], pT2[1][:, 0:512]), after=[f"lb_tg{dg}"],
                mark=f"lb_g{dg}")
        for dg in range(4):
            P.y(lambda sy, dg=dg: sy.dma_start(out=grow[:], in_=bet_d[0:1, dg * 512:(dg + 1) * 512]),
                dGB, after=[f"lb_g3"] + ([f"lb_b{dg - 1}"] if dg else []), mark=f"lb_lb{dg}")
            P.t(lambda t, dg=dg: t.matmul(pT2[1][:, 0:512], ones1[:], grow[:], start=True, stop=True),
                after=[f"lb_lb{dg}", f"lb_g{dg}" if dg == 0 else f"lb_b{dg - 1}"], mark=f"lb_tb{dg}")
            P.s(lambda s, dg=dg: s.copy(bbc[dg][:], pT2[1][:, 0:512]), after=[f"lb_tb{dg}"],
                mark=f"lb_b{dg}")

        es_b.close()
        xct = [ctx.sbuf("xct0", [128, D], F32)]
        sct = [ctx.sbuf("sct0", [128, D], F32)]

        for half in range(2):
            hsl = slice(half * 512, (half + 1) * 512)
            for icg in range(16):
                for dc in range(16):
                    di = half * 256 + icg * 16 + dc
                    par = di % 2
                    g_l = ["xn_all"]
                    if di >= 2:
                        g_l.append(f"up_mm{di - 2}")
                    P.gd(lambda g, icg=icg, dc=dc, par=par: g.dma_start(
                        out=upb[par][:],
                        in_=up_d[dc * 128:(dc + 1) * 128, icg * 512:(icg + 1) * 512]),
                        dUP[par], after=g_l)
                    for i4 in range(4):
                        g_m = [(dUP[par], dUP[par].np_)]
                        if dc == 0:
                            if icg >= 1:
                                g_m.append(f"up_ev{half}_{icg - 1}")
                            elif half == 1:
                                g_m.append(f"up_ev0_15")
                                g_m.append("dn_mm15")
                        P.t(lambda t, i4=i4, dc=dc, par=par, hsl=hsl: t.matmul(
                            pU[i4][:, 0:512], upb[par][:, i4 * 128:(i4 + 1) * 128],
                            xnfm[dc][:, hsl], start=(dc == 0), stop=(dc == 15)),
                            after=g_m if i4 == 0 else (),
                            mark=f"up_mm{di}" if i4 == 3 else None)
                for i4 in range(4):
                    ic = icg * 4 + i4
                    g_r = [f"up_mm{half * 256 + icg * 16 + 15}"]
                    if half == 1:
                        g_r.append("dn_mm15")
                    if ic >= 1:
                        g_r.append(f"up_sq{half}_{ic - 1}")
                    elif half == 1:
                        g_r.append("up_sq0_63")
                    P.v(lambda v, i4=i4: v.tensor_scalar_max(relu[:], pU[i4][:, 0:512], 0.0),
                        after=g_r, mark=f"up_rl{half}_{ic}")
                    P.s(lambda s, ic=ic: s.square(h1[ic][:], relu[:]),
                        after=[f"up_rl{half}_{ic}"], mark=f"up_sq{half}_{ic}")
                P.mark_now(f"up_ev{half}_{icg}", ctx.ss, P.ss_p)
            for c in range(16):
                P.v(lambda v, c=c, hsl=hsl: v.tensor_mul(xnfm[c][:, hsl], xnfm[c][:, hsl],
                                                         sbcast[:, hsl]),
                    after=([f"up_mm{half * 256 + 255}", "sc_bc1"] if c == 0 else ()))
            P.mark_now(f"xs_done{half}", ctx.vv, P.vv_p)
            for dcol in range(16):
                gdi = half * 16 + dcol
                par = gdi % 2
                g_ld = []
                if gdi >= 1:
                    g_ld = [f"dn_mm{gdi - 1}"]
                P.gd(lambda g, dcol=dcol: g.dma_start(
                    out=dwnb[0][:].rearrange("p (ic c) -> p ic c", c=128),
                    in_=down_d.rearrange("(ic p) d -> p ic d", p=128)[:, 0:32, dcol * 128:(dcol + 1) * 128]),
                    dDW[0], after=g_ld, mark=f"dn_ld{gdi}")
                P.gd(lambda g, dcol=dcol: g.dma_start(
                    out=ewb[0][:].rearrange("p (dc c) -> p dc c", c=128),
                    in_=ew_d.rearrange("(dc p) d -> p dc d", p=128)[:, :, dcol * 128:(dcol + 1) * 128]),
                    dEW[0], after=[f"dn_mm{gdi - 1}"] if gdi >= 1 else [],
                    mark=f"ew_ld{gdi}")
                g_mm0 = [f"dn_ld{gdi}", f"up_sq{half}_63", f"xs_done{half}"]
                if gdi >= 2:
                    g_mm0.append(f"yse{gdi - 2}_3")
                for ic in range(32):
                    P.t(lambda t, ic=ic, par=par: t.matmul(
                        pY[par][:, 0:512], dwnb[0][:, ic * 128:(ic + 1) * 128],
                        h1[ic][:], start=(ic == 0), stop=False),
                        after=g_mm0 if ic == 0 else ())
                P.mark_now(f"dn_hm{gdi}", P.tt, P.tt_p)
                P.gd(lambda g, dcol=dcol: g.dma_start(
                    out=dwnb[0][:].rearrange("p (ic c) -> p ic c", c=128),
                    in_=down_d.rearrange("(ic p) d -> p ic d", p=128)[:, 32:64, dcol * 128:(dcol + 1) * 128]),
                    dDW[0], after=[f"dn_hm{gdi}"], mark=f"dn_ld2{gdi}")
                for ic in range(32):
                    P.t(lambda t, ic=ic, par=par: t.matmul(
                        pY[par][:, 0:512], dwnb[0][:, ic * 128:(ic + 1) * 128],
                        h1[32 + ic][:], start=False, stop=False),
                        after=[f"dn_ld2{gdi}"] if ic == 0 else ())
                for dc in range(16):
                    P.t(lambda t, dc=dc, par=par, hsl=hsl: t.matmul(
                        pY[par][:, 0:512], ewb[0][:, dc * 128:(dc + 1) * 128],
                        xnfm[dc][:, hsl], start=False, stop=(dc == 15)),
                        after=[f"ew_ld{gdi}"] if dc == 0 else (),
                        mark=f"dn_mm{gdi}" if dc == 15 else None)
                for t4 in range(4):
                    g_tp = [f"dn_mm{gdi}"]
                    if gdi >= 1:
                        g_tp.append(f"ytr_e{gdi - 1}_{t4}")
                    P.s(lambda s, t4=t4, par=par: s.copy(lsq[:, t4 * 128:(t4 + 1) * 128],
                                                         pY[par][:, t4 * 128:(t4 + 1) * 128]),
                        after=g_tp, mark=f"yse{gdi}_{t4}")
                    P.t(lambda t, t4=t4, par=par: t.transpose(
                        pT2[par][:, 0:128], lsq[:, t4 * 128:(t4 + 1) * 128], idn[:]),
                        after=[f"yse{gdi}_{t4}"] + ([f"ytr_e{gdi - 2}_{t4}"] if gdi >= 2 else []),
                        mark=f"ytr{gdi}_{t4}")
                    P.s(lambda s, t4=t4, par=par, dcol=dcol: s.copy(
                        ytm[t4][:, dcol * 128:(dcol + 1) * 128], pT2[par][:, 0:128]),
                        after=[f"ytr{gdi}_{t4}"], mark=f"ytr_e{gdi}_{t4}")
            for t4 in range(4):
                tj = half * 4 + t4
                g_ln = [f"ytr_e{half * 16 + 15}_3", f"ytr{half * 16 + 15}_3", "konst"]
                if tj >= 1:
                    g_ln.append(f"cmb{tj - 1}")
                P.v(lambda v, t4=t4: v.reduce_sum(col[:, 0:1], ytm[t4][:], axis=AX.X), after=g_ln)
                P.v(lambda v: v.tensor_scalar(col[:, 1:2], col[:, 0:1], 1.0 / D, None, ALU.mult))
                P.v(lambda v, t4=t4: v.tensor_scalar(lnc[:], ytm[t4][:], col[:, 1:2], None,
                                                     ALU.subtract))
                P.v(lambda v: v.tensor_mul(lsq[:], lnc[:], lnc[:]))
                P.v(lambda v: v.reduce_sum(col[:, 2:3], lsq[:], axis=AX.X), mark=f"lvar{tj}")
                P.s(lambda s: s.activation(col[:, 3:4], col[:, 2:3], AF.Ln, bias=konst[:, 0:1],
                                           scale=1.0 / D), after=[f"lvar{tj}"])
                P.s(lambda s: s.activation(col[:, 4:5], col[:, 3:4], AF.Exp, scale=-0.5),
                    mark=f"lrstd{tj}")
                P.v(lambda v: v.tensor_scalar(lsq[:], lnc[:], col[:, 4:5], None, ALU.mult),
                    after=[f"lrstd{tj}"], mark=f"lnn{tj}")
                for dg in range(4):
                    P.v(lambda v, dg=dg: v.tensor_mul(lnc[:, dg * 512:(dg + 1) * 512],
                                                      lsq[:, dg * 512:(dg + 1) * 512], gbc[dg][:]),
                        after=[f"lnn{tj}", "lb_g3"] if dg == 0 else ())
                    P.v(lambda v, dg=dg: v.tensor_add(lnc[:, dg * 512:(dg + 1) * 512],
                                                      lnc[:, dg * 512:(dg + 1) * 512], bbc[dg][:]),
                        after=["lb_b3"] if dg == 0 else (), mark=f"lgb{tj}" if dg == 3 else None)
                P.s(lambda s: s.activation(lsq[:], lnc[:], AF.Silu), after=[f"lgb{tj}"],
                    mark=f"lsil{tj}")
                g_cl = []
                if tj >= 1:
                    g_cl.append(f"cmb{tj - 1}")
                    g_cl.append((dOU, dOU.np_))
                P.y(lambda sy, tj=tj: sy.dma_start(out=xct[0][:],
                                                   in_=x1_d[tj * 128:(tj + 1) * 128, :]),
                    dXS[0], after=g_cl)
                P.y(lambda sy, tj=tj: sy.dma_start(out=sct[0][:],
                                                   in_=s1_d[tj * 128:(tj + 1) * 128, :]),
                    dXS[0])
                g_c = [f"lsil{tj}", (dXS[0], dXS[0].np_)]
                if dOU.np_ > 0:
                    g_c.append((dOU, dOU.np_))
                P.v(lambda v: v.tensor_mul(sct[0][:], lsq[:], sct[0][:]), after=g_c)
                P.v(lambda v: v.tensor_add(xct[0][:], xct[0][:], sct[0][:]),
                    mark=f"cmb{tj}")
                P.y(lambda sy, tj=tj: sy.dma_start(
                    out=out_d[tj * 128:(tj + 1) * 128, :], in_=xct[0][:]), dOU,
                    after=[f"cmb{tj}"])

        P.emit()

        def fin(sy):
            sy.wait_ge(dOU.h, dOU.n)
        ctx.sysec(fin)

    es_ph.close()
    ctx.cur, ctx.prefix = oc, op
    return ctx


def _l6_inputs(inp, stage):
    onesd = np.ones((1, 128), np.float32)
    iden = np.eye(128, dtype=np.float32)
    up = np.ascontiguousarray(np.asarray(inp["L1_ffn_up_w"], np.float32))
    dwn = np.ascontiguousarray(np.asarray(inp["L1_ffn_down_w"], np.float32))
    ew = np.ascontiguousarray(np.asarray(inp["L1_ffn_expert_weight"], np.float32))
    gam = np.ascontiguousarray(np.asarray(inp["L1_ffn_ln_gamma"], np.float32)[None, :])
    bet = np.ascontiguousarray(np.asarray(inp["L1_ffn_ln_beta"], np.float32)[None, :])
    maps = []
    for c in range(NC):
        sl = slice(c * TC, (c + 1) * TC)
        maps.append({
            "xn1fm": np.ascontiguousarray(stage["xnfm1"][:, sl]),
            "gate1L": np.ascontiguousarray(stage["gate1L"][:, sl]),
            "upw": up, "dwnw": dwn, "eww": ew, "gam": gam, "bet": bet,
            "x1": np.ascontiguousarray(stage["x1_full"][sl]),
            "s1": np.ascontiguousarray(stage["s1_full"][sl]),
            "onesd": onesd, "iden": iden,
        })
    return maps


def run_l6(inp, stage, **kw):
    ctx = build_l6()
    maps = _l6_inputs(inp, stage)
    res = run_bass_kernel_spmd(ctx.nc, maps, core_ids=list(range(NC)), **kw)
    return res.results


def _old_kernel_multi(**inputs):
    inp = {k: np.asarray(v) for k, v in inputs.items()}

    res1 = run_l1(inp)
    q_full = np.concatenate([r["q_d"] for r in res1], axis=1)
    k_full = np.concatenate([r["k_d"] for r in res1], axis=1)
    v_full = np.concatenate([r["v_d"] for r in res1], axis=1)
    g_full = np.concatenate([r["g_d"] for r in res1], axis=1)
    small = np.concatenate([r["smallL"] for r in res1], axis=1)
    xnfm_full = np.concatenate([r["xnfm"] for r in res1], axis=1)
    stage2 = {
        "q_full": q_full, "k_full": k_full, "v_full": v_full, "g_full": g_full,
        "alphaL": np.ascontiguousarray(small[0:16]),
        "betaL": np.ascontiguousarray(small[16:32]),
    }
    res2 = run_l2(stage2, inp)
    og_tm = np.concatenate([r["og_d"] for r in res2], axis=1)
    stage3 = {
        "og_fm": np.ascontiguousarray(og_tm.T),
        "xnfm": xnfm_full,
        "gateL": np.ascontiguousarray(small[32:40]),
    }
    res3 = run_l3(stage3, inp)
    x1_full = np.ascontiguousarray(np.concatenate([r["x1"] for r in res3], axis=0))

    res4 = run_l4(inp, x1_full)
    q1_full = np.concatenate([r["q1_d"] for r in res4], axis=1)
    k1_full = np.ascontiguousarray(np.concatenate([r["k1_d"] for r in res4], axis=1))
    v1_tm = np.concatenate([r["v1tm"] for r in res4], axis=0)
    xnfm1_full = np.concatenate([r["xn1fm"] for r in res4], axis=1)
    gate1L_full = np.ascontiguousarray(np.concatenate([r["gate1L"] for r in res4], axis=1))
    stage5 = {
        "q1_full": q1_full,
        "k1_full": k1_full,
        "v1_tm": np.ascontiguousarray(v1_tm),
    }
    res5 = run_l5(inp, stage5)
    s1_full = np.ascontiguousarray(np.concatenate([r["s1"] for r in res5], axis=0))

    stage6 = {
        "xnfm1": xnfm1_full, "gate1L": gate1L_full,
        "x1_full": x1_full, "s1_full": s1_full,
    }
    res6 = run_l6(inp, stage6)
    out = np.concatenate([r["outp"] for r in res6], axis=0).astype(np.float32)
    return out.reshape(1, T, D)


def build_fused():
    ctx = Ctx()
    d1 = drams_l1(ctx, True)
    d2 = drams_l2(ctx, True)
    d3 = drams_l3(ctx, True)
    d4 = drams_l4(ctx, True)
    d5 = drams_l5(ctx, True)
    d6 = drams_l6(ctx)
    d3["xnfm"] = d1["xnfm"]
    d3["gateL"] = d1["smallL"][32:40, :]
    d3["iden"] = d1["iden"]
    d3["xm"] = d1["xm"]
    d4["x1"] = d3["x1"]
    d4["iden"] = d1["iden"]
    d5["q1_d"] = d4["q1_d"]
    d5["onesd"] = d3["onesd"]
    d6["xn1fm"] = d4["xn1fm"]
    d6["gate1L"] = d4["gate1L"]
    d6["x1"] = d3["x1"]
    d6["s1"] = d5["s1"]
    d6["onesd"] = d3["onesd"]
    d6["iden"] = d1["iden"]

    rg = [list(range(NC))]
    qag = ctx.dint("qag", [NC * D, TC], BF16)
    kag = ctx.dint("kag", [NC * D, TC], BF16)
    vag = ctx.dint("vag", [NC * D, TC], BF16)
    gag = ctx.dint("gag", [NC * D, TC], BF16)
    smag = ctx.dint("smag", [NC * 32, TC], F32)
    ogag = ctx.dint("ogag", [NC * T, 256], BF16)

    with ctx.es:
        ccs = ctx.sem("ccs")
        dEX = ctx.sem("dEX")
        ctx.start()

        def collect(pairs):
            def sec(g):
                for (i, o) in pairs:
                    inc(g.collective_compute("AllGather", ALU.bypass, replica_groups=rg,
                                             ins=[i], outs=[o]), ccs, 1)
                g.wait_ge(ccs.h, ccs.n)
            ctx.gsec(sec)

        phase_l1(ctx, d1)
        ctx.barrier()
        collect([(d1["proj"][0][:, :], qag[:, :]),
                 (d1["proj"][1][:, :], kag[:, :]),
                 (d1["proj"][2][:, :], vag[:, :]),
                 (d1["proj"][3][:, :], gag[:, :]),
                 (d1["smallL"][0:32, :], smag[:, :])])
        ctx.barrier()

        def extract1(sy):
            pid = sy.partition_id()
            for c in sy.Switch(pid, NC):
                for (ag, dst) in ((qag, d2["qx"]), (kag, d2["kx"]),
                                  (vag, d2["vx"]), (gag, d2["gx"])):
                    inc(sy.dma_start(
                        out=dst.rearrange("(r dd) t -> r dd t", dd=256),
                        in_=ag.rearrange("(r d) t -> r d t", d=D)[:, 2 * c * 128:(2 * c + 2) * 128, :]),
                        dEX, 16)
                    dEX.n -= 16
                inc(sy.dma_start(
                    out=d2["smx"].rearrange("(r q) t -> r q t", q=4)[:, 0:2, :],
                    in_=smag.rearrange("(r q) t -> r q t", q=32)[:, 2 * c:2 * c + 2, :]),
                    dEX, 16)
                dEX.n -= 16
                inc(sy.dma_start(
                    out=d2["smx"].rearrange("(r q) t -> r q t", q=4)[:, 2:4, :],
                    in_=smag.rearrange("(r q) t -> r q t", q=32)[:, 16 + 2 * c:18 + 2 * c, :]),
                    dEX, 16)
                dEX.n -= 16
            dEX.n += 96
            sy.wait_ge(dEX.h, dEX.n)
        ctx.sysec(extract1)
        ctx.barrier()
        phase_l2(ctx, d2)
        ctx.barrier()
        collect([(d2["og"][:, :], ogag[:, :])])
        ctx.barrier()

        def extract2(sy):
            pid = sy.partition_id()
            for c in sy.Switch(pid, NC):
                inc(sy.dma_start(
                    out=d3["ogx"].rearrange("(r t) h -> r t h", t=TC),
                    in_=ogag.rearrange("(r t) h -> r t h", t=T)[:, c * TC:(c + 1) * TC, :]),
                    dEX, 16)
                dEX.n -= 16
            dEX.n += 16
            sy.wait_ge(dEX.h, dEX.n)
        ctx.sysec(extract2)
        ctx.barrier()
        phase_l3(ctx, d3)
        ctx.barrier()
        phase_l4(ctx, d4)
        ctx.barrier()
        collect([(d4["k1_d"][:, :], d5["k1ag"][:, :]),
                 (d4["v1tm"][:, :], d5["v1t"][:, :])])
        ctx.barrier()
        phase_l5(ctx, d5)
        ctx.barrier()
        phase_l6(ctx, d6)

    return ctx


def _fused_inputs(inp):
    m1 = _l1_inputs(inp)
    Wo = np.ascontiguousarray(np.asarray(inp["L0_attn_Wo"], np.float32))
    w13 = []
    for e in range(8):
        w13.append(np.asarray(inp["L0_ffn_w1"][e]).T)
        w13.append(np.asarray(inp["L0_ffn_w3"][e]).T)
    w13.append(np.asarray(inp["L0_ffn_shared_w1"]).T)
    w13.append(np.asarray(inp["L0_ffn_shared_w3"]).T)
    w13 = np.ascontiguousarray(np.stack(w13).astype(np.float32))
    w2t = np.ascontiguousarray(np.stack(
        [np.asarray(inp["L0_ffn_w2"][e]).T for e in range(8)]
        + [np.asarray(inp["L0_ffn_shared_w2"]).T]).astype(np.float32))
    e8 = np.zeros((8, 8 * 128), dtype=BF)
    for e in range(8):
        e8[e, e * 128:(e + 1) * 128] = 1
    onesd = np.ones((1, 128), np.float32)
    ebias = np.ascontiguousarray(np.asarray(inp["L0_ffn_expert_bias"], np.float32)[None, :])
    lnw = np.asarray(inp["L0_attn_ln_w"], np.float32)[None, :]
    lnb = np.asarray(inp["L0_attn_ln_b"], np.float32)[None, :]
    masks = np.stack([
        np.tril(np.ones((CH, CH), np.float32), -1),
        np.triu(np.ones((CH, CH), np.float32), 1),
        np.triu(np.ones((CH, CH), np.float32), 0),
    ])
    idenf = np.eye(128, dtype=np.float32)
    idenb = np.eye(128, dtype=np.float32).astype(BF)
    Wqkv = np.ascontiguousarray(np.stack([inp["L1_attn_Wq"], inp["L1_attn_Wk"],
                                          inp["L1_attn_Wv"]]).astype(np.float32))
    bcols = np.stack([np.asarray(inp["L1_attn_bq"]), np.asarray(inp["L1_attn_bk"]),
                      np.asarray(inp["L1_attn_bv"])]).astype(np.float32)
    bcols = np.ascontiguousarray(bcols.reshape(3, 16, 128).transpose(2, 0, 1).reshape(128, 48))
    gatew = np.asarray(inp["L1_ffn_gate_w"], np.float32)
    gatew = np.ascontiguousarray(gatew.reshape(16, 128, 8).transpose(1, 0, 2).reshape(128, 128))
    hd = 128
    jj = np.arange(hd // 2)
    inv_freq = 10000.0 ** (-2.0 * jj / hd)
    pos = np.arange(T)
    ang = pos[:, None] * inv_freq[None, :]
    sin2 = np.stack([np.sin(ang), np.sin(ang)], -1).reshape(T, hd)
    cos2 = np.stack([np.cos(ang), np.cos(ang)], -1).reshape(T, hd)
    sign = np.where(np.arange(hd) % 2 == 0, -1.0, 1.0)
    ssin = sin2 * sign[None, :]
    cosT = np.ascontiguousarray(cos2.T.astype(np.float32))
    ssinT = np.ascontiguousarray(ssin.T.astype(np.float32))
    Wo1 = np.ascontiguousarray(np.asarray(inp["L1_attn_Wo"], np.float32))
    bo1 = np.ascontiguousarray(np.asarray(inp["L1_attn_bo"], np.float32)[None, :])
    up = np.ascontiguousarray(np.asarray(inp["L1_ffn_up_w"], np.float32))
    dwn = np.ascontiguousarray(np.asarray(inp["L1_ffn_down_w"], np.float32))
    ew = np.ascontiguousarray(np.asarray(inp["L1_ffn_expert_weight"], np.float32))
    gam = np.ascontiguousarray(np.asarray(inp["L1_ffn_ln_gamma"], np.float32)[None, :])
    bet = np.ascontiguousarray(np.asarray(inp["L1_ffn_ln_beta"], np.float32)[None, :])
    maps = []
    for c in range(NC):
        sl = slice(c * TC, (c + 1) * TC)
        m = dict(m1[c])
        m.update({
            "Wo": Wo, "ebias": ebias, "e8d": e8, "onesd": onesd,
            "w13": w13, "w2t": w2t,
            "lnw": lnw, "lnb": lnb, "masks": masks, "idenf": idenf, "idenb": idenb,
            "Wqkv": Wqkv, "bcols": bcols, "gatew": gatew,
            "cosd": np.ascontiguousarray(cosT[:, sl]),
            "ssind": np.ascontiguousarray(ssinT[:, sl]),
            "Wo1": Wo1, "bo1": bo1,
            "upw": up, "dwnw": dwn, "eww": ew, "gam": gam, "bet": bet,
        })
        maps.append(m)
    return maps


def kernel(**inputs):
    inp = {k: np.asarray(v) for k, v in inputs.items()}
    ctx = build_fused()
    maps = _fused_inputs(inp)
    res = run_bass_kernel_spmd(ctx.nc, maps, core_ids=list(range(NC)))
    out = np.concatenate([r["outp"] for r in res.results], axis=0).astype(np.float32)
    return out.reshape(1, T, D)




# version 06
# speedup vs baseline: 1.1768x; 1.1158x over previous
import numpy as np
import ml_dtypes
from contextlib import ExitStack

import concourse.bass as bass
import concourse.mybir as mybir
from concourse.bass_utils import run_bass_kernel_spmd

F32 = mybir.dt.float32
BF16 = mybir.dt.bfloat16
AF = mybir.ActivationFunctionType
ALU = mybir.AluOpType
AX = mybir.AxisListType
BF = ml_dtypes.bfloat16

NC = 8
T = 8192
TC = T // NC
D = 2048
H = 16
DK = 128
CH = 128
FW = TC + 3
EPS = 1e-5
RDK = float(DK) ** -0.5

_WRAP = {
    "memset", "memzero", "tensor_copy", "tensor_tensor", "tensor_add", "tensor_sub",
    "tensor_mul", "tensor_max", "tensor_scalar", "tensor_single_scalar",
    "tensor_scalar_add", "tensor_scalar_sub", "tensor_scalar_mul", "tensor_scalar_max",
    "tensor_scalar_min", "tensor_relu", "tensor_reduce", "reduce_max", "reduce_sum",
    "scalar_tensor_tensor", "tensor_tensor_scan", "select", "copy_predicated",
    "transpose", "reciprocal", "stream_shuffle", "bn_stats", "bn_aggr", "max",
    "max_index", "pool", "pool_avg", "pool_max", "match_replace",
    "activation", "sign", "square", "sqrt", "mul", "copy", "add", "iota",
    "affine_select",
}


class Sem:
    __slots__ = ("h", "n", "np_")

    def __init__(self, h):
        self.h = h
        self.n = 0


def inc(instr, sem, k=1):
    instr.then_inc(sem.h, k)
    sem.n += k


class W:

    def __init__(self, eng, sem):
        self._e = eng
        self.s = sem

    @property
    def n(self):
        return self.s.n

    def wait(self, sem, val):
        self._e.wait_ge(sem.h, val)

    def __getattr__(self, name):
        attr = getattr(self._e, name)
        if name in _WRAP:
            def f(*a, **k):
                self._e.wait_ge(self.s.h, self.s.n)
                ins = attr(*a, **k)
                inc(ins, self.s, 1)
                return ins
            return f
        return attr


class Ctx:
    def __init__(self):
        self.nc = bass.Bass("TRN2", target_bir_lowering=False)
        self.es = ExitStack()
        self.block = None
        self.bar = None
        self.barn = 0
        self.vv = None
        self.ss = None
        self.gg = None
        self.cur = None
        self.prefix = ""

    def sem(self, name):
        return Sem(self.es.enter_context(self.nc.semaphore(self.prefix + name)))

    def sbuf(self, name, shape, dt, es=None):
        tgt = es or self.cur or self.es
        return tgt.enter_context(self.nc.sbuf_tensor(self.prefix + name, shape, dt))

    def psum(self, name, shape, dt=F32):
        tgt = self.cur or self.es
        return tgt.enter_context(self.nc.psum_tensor(self.prefix + name, shape, dt))

    def din(self, name, shape, dt):
        return self.nc.dram_tensor(name, shape, dt, kind="ExternalInput").ap()

    def dout(self, name, shape, dt):
        return self.nc.dram_tensor(name, shape, dt, kind="ExternalOutput").ap()

    def dint(self, name, shape, dt):
        return self.nc.dram_tensor(name, shape, dt, kind="Internal").ap()

    def start(self):
        if self.block is not None:
            return self.block
        self.bar = self.sem("bar")
        self.vv = self.sem("vv")
        self.ss = self.sem("ss")
        self.gg = self.sem("gg")
        self.block = self.es.enter_context(self.nc.Block())
        return self.block

    def vsec(self, fn):
        @self.block.vector
        def _(v):
            fn(W(v, self.vv))

    def ssec(self, fn):
        @self.block.scalar
        def _(s):
            fn(W(s, self.ss))

    def gsec(self, fn):
        @self.block.gpsimd
        def _(g):
            fn(W(g, self.gg))

    def tsec(self, fn):
        @self.block.tensor
        def _(t):
            fn(t)

    def sysec(self, fn):
        @self.block.sync
        def _(sy):
            fn(sy)

    def barrier(self):
        self.barn += 1
        tgt = 5 * self.barn
        bar = self.bar

        def arr(e):
            e._e.wait_ge(e.s.h, e.s.n)
            e._e.sem_inc(bar.h, 1)
            e._e.wait_ge(bar.h, tgt)

        self.vsec(arr)
        self.ssec(arr)
        self.gsec(arr)

        @self.block.tensor
        def _(t):
            t.sem_inc(bar.h, 1)
            t.wait_ge(bar.h, tgt)

        @self.block.sync
        def _(sy):
            sy.sem_inc(bar.h, 1)
            sy.wait_ge(bar.h, tgt)
        bar.n = tgt


def drams_l1(ctx, inner):
    mk = ctx.dint if inner else ctx.dout
    d = {}
    d["xm"] = ctx.din("xm", [TC, D], F32)
    d["xh"] = ctx.din("xh", [3, D], F32)
    d["Wqkvg"] = ctx.din("Wqkvg", [4, D, D], F32)
    d["Wab"] = ctx.din("Wab", [128, 40 * 16], F32)
    d["bab"] = ctx.din("bab", [40, 1], F32)
    d["convw"] = ctx.din("convw", [128, 192], F32)
    d["convb"] = ctx.din("convb", [128, 48], F32)
    d["iden"] = ctx.din("iden", [128, 128], F32)
    d["xnfm"] = mk("xnfm", [D, TC], BF16)
    d["proj"] = [mk(nm, [D, TC], BF16) for nm in ("q_d", "k_d", "v_d", "g_d")]
    d["smallL"] = mk("smallL", [40, TC], F32)
    return d


def build_l1():
    ctx = Ctx()
    dr = drams_l1(ctx, False)
    phase_l1(ctx, dr)
    return ctx


def phase_l1(ctx, dr):
    xm, xh, Wqkvg, Wab, bab = dr["xm"], dr["xh"], dr["Wqkvg"], dr["Wab"], dr["bab"]
    convw, convb, iden = dr["convw"], dr["convb"], dr["iden"]
    xnfm_d, proj_d, smallL_d = dr["xnfm"], dr["proj"], dr["smallL"]
    es_ph = ExitStack()
    oc, op = ctx.cur, ctx.prefix
    ctx.cur, ctx.prefix = es_ph, "L1_"

    grps = [(0, 512), (512, 512), (1024, 3)]

    if True:
        xt = [ctx.sbuf(f"xt{i}", [128, D], F32) for i in range(3)]
        lnc = ctx.sbuf("lnc", [128, D], F32)
        col = ctx.sbuf("col", [128, 12], F32)
        konst = ctx.sbuf("konst", [128, 2], F32)
        xnbf = [ctx.sbuf(f"xnbf{c}", [128, FW], BF16) for c in range(16)]
        idn = ctx.sbuf("idn", [128, 128], F32)
        wabs = ctx.sbuf("wabs", [128, 40 * 16], F32)
        babs = ctx.sbuf("babs", [40, 1], F32)
        smalls = ctx.sbuf("smalls", [40, FW], F32)
        cw = ctx.sbuf("cw", [128, 192], F32)
        cb = ctx.sbuf("cb", [128, 48], F32)
        qpre = [ctx.sbuf(f"qpre{i}", [128, FW], F32) for i in range(2)]
        cvo = [ctx.sbuf(f"cvo{i}", [128, TC], F32) for i in range(4)]
        obf = [ctx.sbuf(f"obf{i}", [128, TC], BF16) for i in range(2)]
        okbf = ctx.sbuf("okbf", [128, TC], BF16)
        gbf = [ctx.sbuf(f"gbf{i}", [128, FW], BF16) for i in range(2)]

        pA = [ctx.psum(f"pA{i}", [128, 512], F32) for i in range(2)]
        pB = [ctx.psum(f"pB{i}", [128, 512], F32) for i in range(2)]
        pC = [ctx.psum(f"pC{i}", [128, 512], F32) for i in range(2)]
        pT = [ctx.psum(f"pT{i}", [128, 512], F32) for i in range(2)]

        dC = ctx.sem("dC")
        dXT = [ctx.sem(f"dXT{i}") for i in range(3)]
        dW = ctx.sem("dW")
        dS = ctx.sem("dS")
        tT = ctx.sem("tT")
        tG = ctx.sem("tG")
        dOBq = ctx.sem("dOBq")
        dOBv = ctx.sem("dOBv")
        dGB = [ctx.sem("dGB0"), ctx.sem("dGB1")]
        dOK = ctx.sem("dOK")

        b = ctx.start()

        es_fm = ExitStack()
        xnfm = [ctx.sbuf(f"xnfm{c}", [128, FW], F32, es=es_fm) for c in range(16)]

        def p0(sy):
            inc(sy.dma_start(out=idn[:], in_=iden[:, :]), dC, 16)
            inc(sy.dma_start(out=babs[:], in_=bab[:, :]), dC, 16)
            inc(sy.dma_start(out=cw[:], in_=convw[:, :]), dC, 16)
            inc(sy.dma_start(out=cb[:], in_=convb[:, :]), dC, 16)
            inc(sy.dma_start(out=wabs[:], in_=Wab[:, :]), dC, 16)
        ctx.sysec(p0)
        const_loads = dC.n

        nrows = [3] + [128] * 8
        coff = [0] + [3 + 128 * i for i in range(8)]

        def p1_loads(sy):
            for i in range(9):
                if i >= 3:
                    sy.wait_ge(tT.h, 16 * (i - 2))
                if i == 0:
                    inc(sy.dma_start(out=xt[0][0:3, :], in_=xh[:, :]), dXT[0], 16)
                else:
                    inc(sy.dma_start(out=xt[i % 3][:, :], in_=xm[(i - 1) * 128:i * 128, :]),
                        dXT[i % 3], 16)
        ctx.sysec(p1_loads)

        vv0 = ctx.vv.n
        ss0 = ctx.ss.n
        vmark = {}

        def rstd_end_fwd(i):
            return ss0 + (2 * (i + 1) if i <= 2 else 24 + 18 * (i - 3))

        def p1_v(v):
            v.memset(konst[:, 0:1], EPS)
            for i in range(9):
                r = nrows[i]
                tile = xt[i % 3]
                o = 6 * (i % 2)
                v.wait(dXT[i % 3], 16 * (i // 3 + 1))
                v.reduce_sum(col[0:r, o:o + 1], tile[0:r, :], axis=AX.X)
                v.tensor_scalar(col[0:r, o + 1:o + 2], col[0:r, o:o + 1], 1.0 / D, None, ALU.mult)
                v.tensor_scalar(lnc[0:r, :], tile[0:r, :], col[0:r, o + 1:o + 2], None, ALU.subtract)
                v.tensor_mul(tile[0:r, :], lnc[0:r, :], lnc[0:r, :])
                v.reduce_sum(col[0:r, o + 2:o + 3], tile[0:r, :], axis=AX.X)
                vmark[("sumsq", i)] = ctx.vv.n
                v.wait(ctx.ss, rstd_end_fwd(i))
                v.tensor_scalar(tile[0:r, :], lnc[0:r, :], col[0:r, o + 4:o + 5], None, ALU.mult)
                vmark[("xn", i)] = ctx.vv.n
        ctx.vsec(p1_v)

        se_mark = {}

        def rstd_end(i):
            return ss0 + (2 * (i + 1) if i <= 2 else 24 + 18 * (i - 3))

        def p1_s(s):
            s.wait(dC, const_loads)

            def rstd(i):
                r = nrows[i]
                o = 6 * (i % 2)
                s.wait(ctx.vv, vmark[("sumsq", i)])
                s.activation(col[0:r, o + 3:o + 4], col[0:r, o + 2:o + 3], AF.Ln,
                             bias=konst[0:r, 0:1], scale=1.0 / D)
                s.activation(col[0:r, o + 4:o + 5], col[0:r, o + 3:o + 4], AF.Exp, scale=-0.5)
                assert ctx.ss.n == rstd_end(i), (i, ctx.ss.n)

            def evacs(i):
                r = nrows[i]
                for c in range(16):
                    gx = i * 16 + c
                    par = gx % 2
                    s.wait(tT, gx + 1)
                    s.copy(xnfm[c][:, coff[i]:coff[i] + r], pT[par][0:128, 0:r])
                    se_mark[gx] = ctx.ss.n

            for i in range(3):
                rstd(i)
            for i in range(3, 9):
                evacs(i - 3)
                rstd(i)
            for i in range(6, 9):
                evacs(i)
        ctx.ssec(p1_s)
        assert ctx.ss.n == ss0 + 18 + 144

        def p1_t(t):
            t.wait_ge(dC.h, const_loads)
            for i in range(9):
                r = nrows[i]
                tile = xt[i % 3]
                t.wait_ge(ctx.vv.h, vmark[("xn", i)])
                for c in range(16):
                    gx = i * 16 + c
                    par = gx % 2
                    if gx >= 2:
                        t.wait_ge(ctx.ss.h, se_mark[gx - 2])
                    inc(t.transpose(pT[par][0:128, 0:r], tile[0:r, c * 128:(c + 1) * 128],
                                    idn[0:r, 0:r]), tT, 1)
        ctx.tsec(p1_t)

        ctx.barrier()

        tG0 = tG.n
        ss1 = ctx.ss.n

        def p2_t(t):
            t.wait_ge(dC.h, const_loads)
            for gi, (go, gn) in enumerate(grps):
                dst = [pA[0], pB[0], pC[0]][gi]
                last = None
                for c in range(16):
                    last = t.matmul(dst[0:40, 0:gn], wabs[:, c * 40:(c + 1) * 40],
                                    xnfm[c][:, go:go + gn], start=(c == 0), stop=(c == 15))
                inc(last, tG, 1)
        ctx.tsec(p2_t)

        def p2_s(s):
            s.wait(dC, const_loads)
            for gi, (go, gn) in enumerate(grps):
                src = [pA[0], pB[0], pC[0]][gi]
                s.wait(tG, tG0 + gi + 1)
                s.activation(smalls[0:40, go:go + gn], src[0:40, 0:gn], AF.Identity,
                             bias=babs[0:40, 0:1])
        ctx.ssec(p2_s)

        def p2_st(sy):
            sy.wait_ge(ctx.ss.h, ss1 + 3)
            inc(sy.dma_start(out=smallL_d[:, :], in_=smalls[0:40, 3:FW]), dS, 16)
        ctx.sysec(p2_st)

        vv1 = ctx.vv.n

        def p3_v(v):
            for c in range(16):
                v.tensor_copy(xnbf[c][:], xnfm[c][:])
        ctx.vsec(p3_v)

        def p3_st(sy):
            for c in range(16):
                sy.wait_ge(ctx.vv.h, vv1 + c + 1)
                inc(sy.dma_start(out=xnfm_d[c * 128:(c + 1) * 128, :], in_=xnbf[c][:, 3:FW]), dS, 16)
        ctx.sysec(p3_st)

        ctx.barrier()
        es_fm.close()

        wbuf = [ctx.sbuf(f"wbuf{c}", [128, D], BF16) for c in range(16)]

        tG1 = tG.n
        dW0 = dW.n
        ss2 = ctx.ss.n
        vv2 = ctx.vv.n

        def s_evac_n(gx, gi):
            if gx < 48:
                return ss2 + 4 * gx + gi + 1
            return ss2 + 192 + 3 * (gx - 48) + gi + 1

        def s_silu_n(gx):
            return ss2 + 4 * gx + 4

        def v_conv_n(gx):
            ksc = min(max(gx - 16, 0), 16)
            return vv2 + 4 * gx + ksc + 4

        def v_kscale_n(j):
            return v_conv_n(16 + j) + 1

        def p4_g(g):
            for m in range(4):
                if m >= 1:
                    g.wait_ge(tG.h, tG1 + 48 * m)
                for c in range(16):
                    inc(g.dma_start(out=wbuf[c][:], in_=Wqkvg[m, c * 128:(c + 1) * 128, :]), dW, 16)
        ctx.gsec(p4_g)

        def p4_t(t):
            for m in range(4):
                t.wait_ge(dW.h, dW0 + 256 * (m + 1))
                for j in range(16):
                    gx = m * 16 + j
                    if gx >= 2:
                        t.wait_ge(ctx.ss.h, s_evac_n(gx - 2, 2))
                    par = gx % 2
                    for gi, (go, gn) in enumerate(grps):
                        dst = [pA[par], pB[par], pC[par]][gi]
                        last = None
                        for c in range(16):
                            last = t.matmul(dst[0:128, 0:gn], wbuf[c][:, j * 128:(j + 1) * 128],
                                            xnbf[c][:, go:go + gn], start=(c == 0), stop=(c == 15))
                        inc(last, tG, 1)
        ctx.tsec(p4_t)

        def p4_s(s):
            for m in range(4):
                for j in range(16):
                    gx = m * 16 + j
                    par = gx % 2
                    if m < 3 and gx >= 2:
                        s.wait(ctx.vv, v_conv_n(gx - 2))
                    if m == 3 and j >= 2:
                        s.wait(dGB[j % 2], 16 * ((j - j % 2) // 2))
                    for gi, (go, gn) in enumerate(grps):
                        src = [pA[par], pB[par], pC[par]][gi]
                        s.wait(tG, tG1 + 3 * gx + gi + 1)
                        if m < 3:
                            s.activation(qpre[par][:, go:go + gn], src[0:128, 0:gn], AF.Identity)
                        else:
                            s.activation(gbf[j % 2][:, go:go + gn], src[0:128, 0:gn], AF.Sigmoid)
                    if m < 3:
                        s.wait(ctx.vv, v_conv_n(gx))
                        if m == 0 and j >= 2:
                            s.wait(dOBq, 16 * j)
                        elif m == 1:
                            if j < 2:
                                s.wait(dOBq, 16 * 16)
                            else:
                                s.wait(ctx.vv, v_kscale_n(j - 2))
                        elif m == 2:
                            if j < 2:
                                s.wait(ctx.vv, v_kscale_n(15))
                            else:
                                s.wait(dOBv, 16 * j)
                        s.activation(obf[par][:], cvo[2 * par + 1][:], AF.Silu,
                                     bias=cb[0:128, m * 16 + j:m * 16 + j + 1])
        ctx.ssec(p4_s)
        assert ctx.ss.n == ss2 + 240, ctx.ss.n - ss2

        def p4_v(v):
            v.wait(dC, const_loads)
            for m in range(3):
                for j in range(16):
                    gx = m * 16 + j
                    par = gx % 2
                    v.wait(ctx.ss, s_evac_n(gx, 2))
                    src = qpre[par]
                    ping, pong = cvo[2 * par], cvo[2 * par + 1]
                    v.tensor_scalar(ping[:], src[:, 0:TC],
                                    cw[:, m * 64 + 0 * 16 + j:m * 64 + 0 * 16 + j + 1], None, ALU.mult)
                    cur, oth = ping, pong
                    for jj in (1, 2, 3):
                        v.scalar_tensor_tensor(oth[:], src[:, jj:jj + TC],
                                               cw[:, m * 64 + jj * 16 + j:m * 64 + jj * 16 + j + 1],
                                               cur[:], ALU.mult, ALU.add)
                        cur, oth = oth, cur
                    assert ctx.vv.n == v_conv_n(gx), (gx, ctx.vv.n - vv2)
                    if m == 1:
                        v.wait(ctx.ss, s_silu_n(gx))
                        if j >= 1:
                            v.wait(dOK, 16 * j)
                        v.tensor_scalar(okbf[:], obf[par][:], RDK, None, ALU.mult)
        ctx.vsec(p4_v)

        def p4_st(sy):
            for m in range(4):
                for j in range(16):
                    gx = m * 16 + j
                    par = gx % 2
                    if m == 3:
                        sy.wait_ge(ctx.ss.h, s_evac_n(gx, 2))
                        inc(sy.dma_start(out=proj_d[3][j * 128:(j + 1) * 128, :],
                                         in_=gbf[j % 2][:, 3:FW]), dGB[j % 2], 16)
                    elif m == 1:
                        sy.wait_ge(ctx.vv.h, v_kscale_n(j))
                        inc(sy.dma_start(out=proj_d[1][j * 128:(j + 1) * 128, :],
                                         in_=okbf[:]), dOK, 16)
                    else:
                        sy.wait_ge(ctx.ss.h, s_silu_n(gx))
                        sem = dOBq if m == 0 else dOBv
                        inc(sy.dma_start(out=proj_d[m][j * 128:(j + 1) * 128, :],
                                         in_=obf[par][:]), sem, 16)
        ctx.sysec(p4_st)

        def fin(sy):
            for sem in (dS, dOBq, dOBv, dGB[0], dGB[1], dOK):
                sy.wait_ge(sem.h, sem.n)
        ctx.sysec(fin)

    es_ph.close()
    ctx.cur, ctx.prefix = oc, op
    return ctx


def _l1_inputs(inp):
    x = np.asarray(inp["L0_attn_x"], np.float32)[0]
    Wqkvg = np.ascontiguousarray(np.stack([inp["L0_attn_Wq"], inp["L0_attn_Wk"],
                                           inp["L0_attn_Wv"], inp["L0_attn_Wg"]]).astype(np.float32))
    Wab = np.concatenate(
        [inp["L0_attn_Wa"], inp["L0_attn_Wb"], np.asarray(inp["L0_ffn_gate"]).T], axis=1).astype(np.float32)
    Wab = np.ascontiguousarray(Wab.reshape(16, 128, 40).transpose(1, 0, 2).reshape(128, 640))
    bab = np.ascontiguousarray(np.concatenate(
        [inp["L0_attn_ba"], inp["L0_attn_bb"], np.zeros(8, np.float32)]).astype(np.float32)[:, None])
    convw = np.stack([inp["L0_attn_qcw"], inp["L0_attn_kcw"],
                      inp["L0_attn_vcw"]]).astype(np.float32)
    convw = np.ascontiguousarray(convw.reshape(3, 16, 128, 4).transpose(2, 0, 3, 1).reshape(128, 192))
    convb = np.stack([inp["L0_attn_qcb"], inp["L0_attn_kcb"],
                      inp["L0_attn_vcb"]]).astype(np.float32)
    convb = np.ascontiguousarray(convb.reshape(3, 16, 128).transpose(2, 0, 1).reshape(128, 48))
    iden = np.eye(128, dtype=np.float32)
    maps = []
    for c in range(NC):
        s = c * TC
        xhc = x[s - 3:s] if c > 0 else np.zeros((3, D), np.float32)
        maps.append({
            "xm": np.ascontiguousarray(x[s:s + TC]),
            "xh": np.ascontiguousarray(xhc),
            "Wqkvg": Wqkvg, "Wab": Wab, "bab": bab,
            "convw": convw, "convb": convb, "iden": iden,
        })
    return maps


def run_l1(inp, **kw):
    ctx = build_l1()
    maps = _l1_inputs(inp)
    res = run_bass_kernel_spmd(ctx.nc, maps, core_ids=list(range(NC)), **kw)
    return res.results


class Prog:
    def __init__(self, ctx):
        self.ctx = ctx
        self.q = {"t": [], "v": [], "s": [], "g": [], "y": []}
        self.marks = {}
        self.tt = ctx.sem("tt")
        self.tt_p = self.tt.n
        self.vv_p = ctx.vv.n
        self.ss_p = ctx.ss.n
        self.gg_p = ctx.gg.n

    def _resolve(self, after):
        out = []
        for a in after or ():
            if isinstance(a, tuple):
                out.append(a)
            else:
                out.append(self.marks[a])
        return out

    def t(self, emit, after=(), mark=None):
        w = self._resolve(after)
        self.q["t"].append((emit, w, self.tt))
        self.tt_p += 1
        if mark:
            self.marks[mark] = (self.tt, self.tt_p)

    def v(self, emit, after=(), mark=None):
        w = self._resolve(after)
        self.q["v"].append((emit, w, None))
        self.vv_p += 1
        if mark:
            self.marks[mark] = (self.ctx.vv, self.vv_p)

    def s(self, emit, after=(), mark=None):
        w = self._resolve(after)
        self.q["s"].append((emit, w, None))
        self.ss_p += 1
        if mark:
            self.marks[mark] = (self.ctx.ss, self.ss_p)

    def g(self, emit, after=(), mark=None):
        w = self._resolve(after)
        self.q["g"].append((emit, w, None))
        self.gg_p += 1
        if mark:
            self.marks[mark] = (self.ctx.gg, self.gg_p)

    def y(self, emit, dma_sem, after=(), mark=None):
        w = self._resolve(after)
        self.q["y"].append((emit, w, dma_sem))
        if not hasattr(dma_sem, "np_"):
            dma_sem.np_ = dma_sem.n
        dma_sem.np_ += 16
        if mark:
            self.marks[mark] = (dma_sem, dma_sem.np_)

    def gd(self, emit, dma_sem, after=(), mark=None):
        w = self._resolve(after)
        self.q["g"].append(("dma", emit, w, dma_sem))
        if not hasattr(dma_sem, "np_"):
            dma_sem.np_ = dma_sem.n
        dma_sem.np_ += 16
        if mark:
            self.marks[mark] = (dma_sem, dma_sem.np_)

    def mark_now(self, name, sem, val):
        self.marks[name] = (sem, val)

    def emit(self):
        ctx = self.ctx

        def run_t(t):
            for emit, waits, sem in self.q["t"]:
                for (s, v) in waits:
                    t.wait_ge(s.h, v)
                ins = emit(t)
                inc(ins, sem, 1)

        def run_w(w, ops):
            for entry in ops:
                if entry[0] == "dma":
                    _, emit, waits, dsem = entry
                    for (s, v) in waits:
                        w.wait(s, v)
                    ins = emit(w)
                    inc(ins, dsem, 16)
                else:
                    emit, waits, _ = entry
                    for (s, v) in waits:
                        w.wait(s, v)
                    emit(w)

        def run_y(sy):
            for emit, waits, dsem in self.q["y"]:
                for (s, v) in waits:
                    sy.wait_ge(s.h, v)
                ins = emit(sy)
                inc(ins, dsem, 16)

        ctx.tsec(run_t)
        ctx.vsec(lambda v: run_w(v, self.q["v"]))
        ctx.ssec(lambda s: run_w(s, self.q["s"]))
        if self.q["g"]:
            ctx.gsec(lambda g: run_w(g, self.q["g"]))
        ctx.sysec(run_y)
        assert ctx.vv.n == self.vv_p and ctx.ss.n == self.ss_p and ctx.gg.n == self.gg_p


def drams_l2(ctx, inner):
    mk = ctx.dint if inner else ctx.din
    d = {}
    for nm in ("qx", "kx", "vx", "gx"):
        d[nm] = mk(nm, [D, TC], BF16)
    d["smx"] = mk("smx", [32, TC], F32)
    d["lnw"] = ctx.din("lnw", [1, DK], F32)
    d["lnb"] = ctx.din("lnb", [1, DK], F32)
    d["masks"] = ctx.din("masks", [3, CH, CH], F32)
    d["idenf"] = ctx.din("idenf", [128, 128], F32)
    d["idenb"] = ctx.din("idenb", [128, 128], BF16)
    d["scr"] = ctx.dint("scr_l2", [3, 64, CH], F32)
    d["og"] = (ctx.dint if inner else ctx.dout)("og_d", [T, 256], BF16)
    return d


def build_l2():
    ctx = Ctx()
    dr = drams_l2(ctx, False)
    phase_l2(ctx, dr)
    return ctx


def phase_l2(ctx, dr):
    qx, kx, vx, gx2 = dr["qx"], dr["kx"], dr["vx"], dr["gx"]
    smx = dr["smx"]
    lnw_d, lnb_d = dr["lnw"], dr["lnb"]
    masks_d, idenf_d, idenb_d = dr["masks"], dr["idenf"], dr["idenb"]
    scr_d = dr["scr"]
    og_d = dr["og"]
    es_ph = ExitStack()
    oc, op = ctx.cur, ctx.prefix
    ctx.cur, ctx.prefix = es_ph, "L2_"

    NCH = 64
    GRP = 8

    if True:
        Kfm = [ctx.sbuf(f"Kfm{h}", [128, T], BF16) for h in range(2)]
        Qfm = [ctx.sbuf(f"Qfm{h}", [128, T], BF16) for h in range(2)]
        Ktm = [ctx.sbuf(f"Ktm{i}", [128, 128], BF16) for i in range(2)]
        Vtm = [ctx.sbuf(f"Vtm{i}", [128, 128], BF16) for i in range(2)]
        Gtm = [ctx.sbuf(f"Gtm{i}", [128, 128], BF16) for i in range(2)]
        ogb = [ctx.sbuf(f"ogb{i}", [128, 128], BF16) for i in range(2)]
        FRN = [ctx.sbuf(f"FRN{i}", [1, GRP * CH], F32) for i in range(2)]
        FRP = [ctx.sbuf(f"FRP{i}", [1, GRP * CH], F32) for i in range(2)]
        FRC = [ctx.sbuf(f"FRC{i}", [1, GRP * CH], F32) for i in range(2)]
        Alg = ctx.sbuf("Alg", [64, CH], F32)
        Blg = ctx.sbuf("Blg", [64, CH], F32)
        asig = ctx.sbuf("asig", [64, CH], F32)
        la = ctx.sbuf("la", [64, CH], F32)
        bsig = ctx.sbuf("bsig", [64, CH], F32)
        lb = ctx.sbuf("lb", [64, CH], F32)
        CUM = ctx.sbuf("CUM", [64, CH + 2], F32)
        cplnb = ctx.sbuf("cplnb", [64, CH], F32)
        negc = ctx.sbuf("negc", [64, CH], F32)
        Ccol = ctx.sbuf("Ccol", [128, 64], F32)
        CPcol = ctx.sbuf("CPcol", [128, 64], F32)
        bcol = ctx.sbuf("bcol", [128, 64], F32)
        LBcol = ctx.sbuf("LBcol", [128, 64], F32)
        cplnbcol = ctx.sbuf("cplnbcol", [128, 64], F32)
        negccol = ctx.sbuf("negccol", [128, 64], F32)
        ccrow = ctx.sbuf("ccrow", [1, 64], F32)
        CBsb = ctx.sbuf("CBsb", [128, 64], F32)
        ECcol = ctx.sbuf("ECcol", [128, 64], F32)
        ecp = ctx.sbuf("ecp", [128, 64], F32)
        bec = ctx.sbuf("bec", [128, 64], F32)
        cbmc = ctx.sbuf("cbmc", [128, 64], F32)
        KSC = ctx.sbuf("KSC", [128, 64], F32)
        ACB = ctx.sbuf("ACB", [128, 64], F32)
        ones1 = ctx.sbuf("ones1", [1, 128], F32)
        msk = [ctx.sbuf(f"msk{i}", [128, 128], F32) for i in range(3)]
        idnf = ctx.sbuf("idnf", [128, 128], F32)
        idnb = ctx.sbuf("idnb", [128, 128], BF16)
        lnwr = ctx.sbuf("lnwr", [1, DK], F32)
        lnbr = ctx.sbuf("lnbr", [1, DK], F32)
        LNWbc = ctx.sbuf("LNWbc", [128, DK], F32)
        LNBbc = ctx.sbuf("LNBbc", [128, DK], F32)
        konst = ctx.sbuf("konst", [128, 1], F32)
        eN = ctx.sbuf("eN", [128, 128], F32)
        eNT = ctx.sbuf("eNT", [128, 128], F32)
        eP = ctx.sbuf("eP", [128, 128], F32)
        kkSL = ctx.sbuf("kkSL", [128, 128], F32)
        kkSU = ctx.sbuf("kkSU", [128, 128], F32)
        qkUI = ctx.sbuf("qkUI", [128, 128], F32)
        Nbf = ctx.sbuf("Nbf", [128, 128], BF16)
        NTbf = ctx.sbuf("NTbf", [128, 128], BF16)
        Mqkbf = ctx.sbuf("Mqkbf", [128, 128], BF16)
        NP = [ctx.sbuf(f"NP{l}", [128, 256], BF16) for l in range(1, 7)]
        Xb = [ctx.sbuf(f"Xb{i}", [128, 256], BF16) for i in range(2)]
        Wfm = ctx.sbuf("Wfm", [128, 128], BF16)
        Rbf = ctx.sbuf("Rbf", [128, 128], BF16)
        Kpbf = ctx.sbuf("Kpbf", [128, 128], BF16)
        O1 = ctx.sbuf("O1", [128, 128], F32)
        Of = ctx.sbuf("Of", [128, 128], F32)
        Oc = ctx.sbuf("Oc", [128, 128], F32)
        gsq = ctx.sbuf("gsq", [128, 128], F32)
        gapp = ctx.sbuf("gapp", [128, 128], F32)
        gcol = ctx.sbuf("gcol", [128, 8], F32)
        Zf = ctx.sbuf("Zf", [128, 128], F32)
        Zbf = ctx.sbuf("Zbf", [128, 128], BF16)

        pKDQ = ctx.psum("pKDQ", [128, 512], F32)
        pQS = ctx.psum("pQS", [128, 512], F32)
        pSOL = ctx.psum("pSOL", [128, 512], F32)
        pWZ = ctx.psum("pWZ", [128, 512], F32)
        pQZ = ctx.psum("pQZ", [128, 512], F32)
        pMK = ctx.psum("pMK", [128, 512], F32)
        pKP = ctx.psum("pKP", [128, 512], F32)
        pTB = ctx.psum("pTB", [128, 1024], BF16)
        pST = pSOL

        dC = ctx.sem("dC")
        dKQ = ctx.sem("dKQ")
        dAB = ctx.sem("dAB")
        dBNw = ctx.sem("dBNw")
        dFR = [ctx.sem("dFR0"), ctx.sem("dFR1")]
        dKT = [ctx.sem("dKT0"), ctx.sem("dKT1")]
        dVT = [ctx.sem("dVT0"), ctx.sem("dVT1")]
        dGT = [ctx.sem("dGT0"), ctx.sem("dGT1")]
        dOG = [ctx.sem("dOG0"), ctx.sem("dOG1")]

        ctx.start()
        P = Prog(ctx)

        P.y(lambda sy: sy.dma_start(out=msk[0][:], in_=masks_d[0]), dC)
        P.y(lambda sy: sy.dma_start(out=msk[1][:], in_=masks_d[1]), dC)
        P.y(lambda sy: sy.dma_start(out=msk[2][:], in_=masks_d[2]), dC)
        P.y(lambda sy: sy.dma_start(out=idnf[:], in_=idenf_d[:, :]), dC)
        P.y(lambda sy: sy.dma_start(out=idnb[:], in_=idenb_d[:, :]), dC)
        P.y(lambda sy: sy.dma_start(out=lnwr[:], in_=lnw_d[:, :]), dC)
        P.y(lambda sy: sy.dma_start(out=lnbr[:], in_=lnb_d[:, :]), dC)
        kx4 = kx.rearrange("(r dd) t -> dd r t", dd=256)
        qx4 = qx.rearrange("(r dd) t -> dd r t", dd=256)
        for h in range(2):
            P.y(lambda sy, h=h: sy.dma_start(out=Kfm[h][:], in_=kx4[h * 128:(h + 1) * 128]), dKQ)
            P.y(lambda sy, h=h: sy.dma_start(out=Qfm[h][:], in_=qx4[h * 128:(h + 1) * 128]), dKQ)
        P.mark_now("dC_all", dC, dC.np_)
        P.mark_now("dKQ_all", dKQ, dKQ.np_)

        P.v(lambda v: v.memset(ones1[:], 1.0), mark="ones")
        P.v(lambda v: v.memset(konst[:], EPS), mark="konst")
        P.t(lambda t: t.matmul(pST[:, 0:DK], ones1[:], lnwr[:], start=True, stop=True),
            after=["ones", "dC_all"], mark="t_lnw")
        P.s(lambda s: s.copy(LNWbc[:], pST[:, 0:DK]), after=["t_lnw"], mark="s_lnw")
        P.t(lambda t: t.matmul(pST[:, 0:DK], ones1[:], lnbr[:], start=True, stop=True),
            after=["s_lnw"], mark="t_lnb")
        P.s(lambda s: s.copy(LNBbc[:], pST[:, 0:DK]), after=["t_lnb"], mark="s_lnb")

        def setup(h, guard):
            hp = f"h{h}"
            smx4 = smx.rearrange("(r q) (lc t) -> q r lc t", q=4, t=CH)
            P.y(lambda sy, h=h: sy.dma_start(out=Alg[:], in_=smx4[h]), dAB, after=guard)
            P.y(lambda sy, h=h: sy.dma_start(out=Blg[:], in_=smx4[2 + h]), dAB)
            P.mark_now(f"{hp}_ab", dAB, dAB.np_)
            P.s(lambda s: s.activation(asig[:], Alg[:], AF.Sigmoid), after=[f"{hp}_ab"] + guard)
            P.s(lambda s: s.activation(la[:], asig[:], AF.Ln), mark=f"{hp}_la")
            P.s(lambda s: s.activation(bsig[:], Blg[:], AF.Sigmoid), mark=f"{hp}_bsig")
            P.s(lambda s: s.activation(lb[:], bsig[:], AF.Ln), mark=f"{hp}_lb")
            P.v(lambda v: v.memset(CUM[:, 0:1], 0.0), after=guard)
            P.v(lambda v: v.tensor_tensor_scan(CUM[:, 1:CH + 1], la[:], la[:], 0.0,
                                               ALU.add, ALU.bypass), after=[f"{hp}_lb"],
                mark=f"{hp}_cum")
            P.v(lambda v: v.tensor_add(cplnb[:], CUM[:, 0:CH], lb[:]), mark=f"{hp}_cplnb")
            P.v(lambda v: v.tensor_scalar(negc[:], CUM[:, 1:CH + 1], -1.0, None, ALU.mult),
                mark=f"{hp}_negc")
            P.y(lambda sy: sy.dma_start(out=scr_d[0], in_=cplnb[:]), dBNw,
                after=[f"{hp}_cplnb"] + guard)
            P.y(lambda sy: sy.dma_start(out=scr_d[1], in_=negc[:]), dBNw, after=[f"{hp}_negc"])
            P.y(lambda sy: sy.dma_start(out=scr_d[2], in_=CUM[:, 1:CH + 1]), dBNw)
            P.mark_now(f"{hp}_bnw", dBNw, dBNw.np_)
            P.t(lambda t: t.transpose(pST[0:128, 0:64], CUM[:, 1:CH + 1], idnf[0:64, 0:64]),
                after=[f"{hp}_negc", "dC_all", "s_lnb"], mark=f"{hp}_tc")
            P.s(lambda s: s.copy(Ccol[:], pST[0:128, 0:64]), after=[f"{hp}_tc"], mark=f"{hp}_ccol")
            P.t(lambda t: t.transpose(pST[0:128, 0:64], CUM[:, 0:CH], idnf[0:64, 0:64]),
                after=[f"{hp}_ccol"], mark=f"{hp}_tcp")
            P.s(lambda s: s.copy(CPcol[:], pST[0:128, 0:64]), after=[f"{hp}_tcp"], mark=f"{hp}_cpcol")
            P.t(lambda t: t.transpose(pST[0:128, 0:64], bsig[:], idnf[0:64, 0:64]),
                after=[f"{hp}_cpcol"], mark=f"{hp}_tb")
            P.s(lambda s: s.copy(bcol[:], pST[0:128, 0:64]), after=[f"{hp}_tb"], mark=f"{hp}_bcol")
            P.t(lambda t: t.transpose(pST[0:1, 0:64], CUM[:, CH:CH + 1], idnf[0:64, 0:64]),
                after=[f"{hp}_bcol"], mark=f"{hp}_tcc")
            P.s(lambda s: s.copy(ccrow[:], pST[0:1, 0:64]), after=[f"{hp}_tcc"], mark=f"{hp}_ccrow")
            P.t(lambda t: t.matmul(pST[0:128, 0:64], ones1[:], ccrow[:], start=True, stop=True),
                after=[f"{hp}_ccrow", "ones"], mark=f"{hp}_tcb")
            P.s(lambda s: s.copy(CBsb[:], pST[0:128, 0:64]), after=[f"{hp}_tcb"], mark=f"{hp}_cb")
            P.s(lambda s: s.activation(LBcol[:], bcol[:], AF.Ln), after=[f"{hp}_bcol"],
                mark=f"{hp}_lbcol")
            P.v(lambda v: v.tensor_add(cplnbcol[:], CPcol[:], LBcol[:]),
                after=[f"{hp}_lbcol", f"{hp}_cpcol"], mark=f"{hp}_cpc")
            P.v(lambda v: v.tensor_scalar(negccol[:], Ccol[:], -1.0, None, ALU.mult),
                after=[f"{hp}_ccol"], mark=f"{hp}_ngc")
            P.s(lambda s: s.activation(ECcol[:], Ccol[:], AF.Exp), mark=f"{hp}_ec")
            P.s(lambda s: s.activation(ecp[:], CPcol[:], AF.Exp), mark=f"{hp}_ecp")
            P.v(lambda v: v.tensor_mul(bec[:], bcol[:], ecp[:]), after=[f"{hp}_ecp", f"{hp}_bcol"],
                mark=f"{hp}_bec")
            P.v(lambda v: v.tensor_sub(cbmc[:], CBsb[:], Ccol[:]), after=[f"{hp}_cb"],
                mark=f"{hp}_cbmc")
            P.s(lambda s: s.activation(KSC[:], cbmc[:], AF.Exp), after=[f"{hp}_cbmc"], mark=f"{hp}_ksc")
            P.s(lambda s: s.activation(ACB[:], CBsb[:], AF.Exp), mark=f"{hp}_acb")
            P.v(lambda v: v.memset(Zf[:], 0.0), after=guard)
            P.v(lambda v: v.memset(Zbf[:], 0.0), mark=f"{hp}_zbf_init")

        def refill(h, g, gg):
            par2 = gg % 2
            rs = slice(g * GRP, (g + 1) * GRP)
            guard = [f"h{h}_bnw"]
            if gg >= 2:
                guard.append(f"g{(gg - 2) * GRP + GRP - 1}_dp")
            P.y(lambda sy: sy.dma_start(out=FRP[par2][:],
                                        in_=scr_d[0:1, rs].rearrange("x a b -> x (a b)")),
                dFR[par2], after=guard)
            P.y(lambda sy: sy.dma_start(out=FRN[par2][:],
                                        in_=scr_d[1:2, rs].rearrange("x a b -> x (a b)")),
                dFR[par2])
            P.y(lambda sy: sy.dma_start(out=FRC[par2][:],
                                        in_=scr_d[2:3, rs].rearrange("x a b -> x (a b)")),
                dFR[par2])
            P.mark_now(f"fr{gg}", dFR[par2], dFR[par2].np_)

        def chunk(h, c, gc):
            hp = f"h{h}"
            p = f"g{gc}"
            pm = f"g{gc - 1}"
            csl = slice(c * CH, (c + 1) * CH)
            KfmH, QfmH = Kfm[h], Qfm[h]
            par = gc % 2
            gg = gc // GRP
            rsl = slice((c % GRP) * CH, (c % GRP + 1) * CH)

            nload = (gc - par) // 2 + 1
            rr, lc = c // 8, c % 8
            rows = slice(rr * 256 + h * 128, rr * 256 + (h + 1) * 128)
            lsl = slice(lc * 128, (lc + 1) * 128)
            ld_guard = [f"g{gc - 2}_kp"] if gc >= 2 else []
            P.y(lambda sy, rows=rows, lsl=lsl: sy.dma_start(
                out=Ktm[par][:], in_=kx[rows, lsl], transpose=True), dKT[par], after=ld_guard)
            ld_guard = [f"g{gc - 2}_x0b"] if gc >= 2 else []
            P.y(lambda sy, rows=rows, lsl=lsl: sy.dma_start(
                out=Vtm[par][:], in_=vx[rows, lsl], transpose=True), dVT[par], after=ld_guard)
            ld_guard = [f"g{gc - 2}_og"] if gc >= 2 else []
            P.y(lambda sy, rows=rows, lsl=lsl: sy.dma_start(
                out=Gtm[par][:], in_=gx2[rows, lsl], transpose=True), dGT[par], after=ld_guard)

            guard1 = ["dKQ_all", f"fr{gg}"]
            if gc > 0:
                guard1 += [f"{pm}_qkui"]
            P.t(lambda t: t.matmul(pKDQ[:, 0:128], KfmH[:, csl], KfmH[:, csl],
                                   start=True, stop=True), after=guard1, mark=f"{p}_kk")
            P.t(lambda t: t.matmul(pKDQ[:, 128:256], ones1[:], FRN[gg % 2][0:1, rsl],
                                   start=True, stop=True), mark=f"{p}_dn")
            P.t(lambda t: t.matmul(pKDQ[:, 256:384], ones1[:], FRP[gg % 2][0:1, rsl],
                                   start=True, stop=True), mark=f"{p}_dnt")
            P.t(lambda t: t.matmul(pKDQ[:, 384:512], ones1[:], FRC[gg % 2][0:1, rsl],
                                   start=True, stop=True), mark=f"{p}_dp")
            P.t(lambda t: t.matmul(pQS[:, 0:128], KfmH[:, csl], QfmH[:, csl],
                                   start=True, stop=True), mark=f"{p}_qk")

            g_eN = [f"{p}_dp", f"{hp}_cpc"] + ([f"{pm}_nbf"] if gc > 0 else [])
            P.v(lambda v: v.tensor_scalar(eN[:], pKDQ[:, 128:256], cplnbcol[:, c:c + 1],
                                          0.0, ALU.add, ALU.min), after=g_eN, mark=f"{p}_ena")
            P.s(lambda s: s.activation(eN[:], eN[:], AF.Exp), after=[f"{p}_ena"], mark=f"{p}_en")
            P.v(lambda v: v.tensor_scalar(eNT[:], pKDQ[:, 256:384], negccol[:, c:c + 1],
                                          0.0, ALU.add, ALU.min),
                after=[f"{p}_dp", f"{hp}_ngc"] + ([f"{pm}_ntbf"] if gc > 0 else []),
                mark=f"{p}_enta")
            P.s(lambda s: s.activation(eNT[:], eNT[:], AF.Exp), after=[f"{p}_enta"],
                mark=f"{p}_ent")
            P.v(lambda v: v.tensor_scalar(eP[:], pKDQ[:, 384:512], negccol[:, c:c + 1],
                                          0.0, ALU.add, ALU.min),
                after=[f"{p}_dp"] + ([f"{pm}_mqkbf"] if gc > 0 else []), mark=f"{p}_epa")
            P.s(lambda s: s.activation(eP[:], eP[:], AF.Exp), after=[f"{p}_epa"], mark=f"{p}_ep")
            g_sl = [f"{p}_dp", "dC_all"] + ([f"{pm}_nbf"] if gc > 0 else [])
            P.v(lambda v: v.tensor_mul(kkSL[:], pKDQ[:, 0:128], msk[0][:]), after=g_sl,
                mark=f"{p}_kksl")
            P.v(lambda v: v.tensor_mul(kkSU[:], pKDQ[:, 0:128], msk[1][:]),
                after=([f"{pm}_ntbf"] if gc > 0 else []), mark=f"{p}_kksu")
            P.v(lambda v: v.tensor_mul(qkUI[:], pQS[:, 0:128], msk[2][:]),
                after=[f"{p}_qk"] + ([f"{pm}_mqkbf"] if gc > 0 else []), mark=f"{p}_qkui")
            g_nb = [f"{p}_en"]
            if gc > 0:
                g_nb.append(f"{pm}_tsol7")
            P.v(lambda v: v.tensor_mul(Nbf[:], eN[:], kkSL[:]), after=g_nb, mark=f"{p}_nbf")
            P.v(lambda v: v.tensor_mul(NTbf[:], eNT[:], kkSU[:]), after=[f"{p}_ent"],
                mark=f"{p}_ntbf")
            g_mq = [f"{p}_ep"]
            if gc > 0:
                g_mq.append(f"{pm}_tmqkr")
            P.v(lambda v: v.tensor_mul(Mqkbf[:], eP[:], qkUI[:]), after=g_mq, mark=f"{p}_mqkbf")

            for l in range(1, 7):
                if l == 1:
                    lA = NTbf[:]
                    lB = Nbf[:]
                else:
                    lA = NP[l - 2][:, 128:256]
                    lB = NP[l - 2][:, 0:128]
                g_sq = [f"{p}_ntbf"] if l == 1 else [f"{p}_sqe{l - 1}"]
                if gc > 0:
                    g_sq.append(f"{pm}_tsol{7 - l}")
                P.t(lambda t, lA=lA, lB=lB: t.matmul(pQS[:, 128:256], lA, lB,
                                                     start=True, stop=True),
                    after=g_sq, mark=f"{p}_sqa{l}")
                P.t(lambda t, lA=lA, lB=lB: t.matmul(pQS[:, 256:384], lB, lA,
                                                     start=True, stop=True), mark=f"{p}_sqb{l}")
                P.s(lambda s, l=l: s.copy(NP[l - 1][:], pQS[:, 128:384]), after=[f"{p}_sqb{l}"],
                    mark=f"{p}_sqe{l}")

            g_x0 = [f"{hp}_bec", (dKT[par], 16 * nload)]
            if gc >= 1:
                g_x0.append(f"{pm}_tsol7")
            P.v(lambda v: v.tensor_scalar(Xb[0][:, 0:128], Ktm[par][:], bec[:, c:c + 1],
                                          None, ALU.mult), after=g_x0, mark=f"{p}_x0a")
            P.v(lambda v: v.tensor_scalar(Xb[0][:, 128:256], Vtm[par][:], bcol[:, c:c + 1],
                                          None, ALU.mult),
                after=[f"{hp}_bcol", (dVT[par], 16 * nload)], mark=f"{p}_x0b")
            P.v(lambda v: v.tensor_scalar(Kpbf[:], Ktm[par][:], KSC[:, c:c + 1], None, ALU.mult),
                after=[f"{hp}_ksc"] + ([f"{pm}_tkpr"] if gc > 0 else []), mark=f"{p}_kp")

            xp = 0
            for j in range(1, 8):
                if j < 7:
                    g_s = [f"{p}_sqe{7 - j}", f"{p}_x0b" if j == 1 else f"{p}_x{j - 1}"]
                else:
                    g_s = [f"{p}_x{j - 1}"]
                if j == 1 and gc > 0:
                    g_s.append(f"{pm}_x7")
                if j < 7:
                    lhs = NP[6 - j]
                    P.t(lambda t, lhs=lhs, xp=xp: t.matmul(pSOL[:, 0:256], lhs[:, 128:256],
                                                           Xb[xp][:], start=True, stop=True),
                        after=g_s, mark=f"{p}_tsol{j}")
                else:
                    P.t(lambda t, xp=xp: t.matmul(pSOL[:, 0:256], NTbf[:], Xb[xp][:],
                                                  start=True, stop=True),
                        after=g_s, mark=f"{p}_tsol{j}")
                g_v = [f"{p}_tsol{j}"]
                if j == 2 and gc > 0:
                    g_v.append(f"{pm}_twtr")
                if j < 7:
                    P.v(lambda v, xp=xp: v.tensor_add(Xb[1 - xp][:], Xb[xp][:], pSOL[:, 0:256]),
                        after=g_v, mark=f"{p}_x{j}")
                else:
                    P.v(lambda v, xp=xp: v.tensor_sub(Xb[1 - xp][:], Xb[xp][:], pSOL[:, 0:256]),
                        after=g_v, mark=f"{p}_x{j}")
                xp = 1 - xp
            xf = xp

            P.t(lambda t, xf=xf: t.transpose(pTB[:, 0:128], Xb[xf][:, 0:128], idnb[:]),
                after=[f"{p}_x7"] + ([f"{pm}_twz"] if gc > 0 else []), mark=f"{p}_twtr")
            P.s(lambda s: s.copy(Wfm[:], pTB[:, 0:128]), after=[f"{p}_twtr"], mark=f"{p}_wfm")

            g_wz = [f"{p}_wfm", f"{hp}_zbf_init" if c == 0 else f"{pm}_zbf"]
            if gc > 0:
                g_wz.append(f"{pm}_r")
            P.t(lambda t: t.matmul(pWZ[:, 0:128], Wfm[:], Zbf[:], start=True, stop=True),
                after=g_wz, mark=f"{p}_twz")
            g_qz = [f"{pm}_o1"] if gc > 0 else []
            P.t(lambda t: t.matmul(pQZ[:, 0:128], QfmH[:, csl], Zbf[:], start=True, stop=True),
                after=g_qz, mark=f"{p}_tqz")
            P.v(lambda v, xf=xf: v.tensor_sub(Rbf[:], Xb[xf][:, 128:256], pWZ[:, 0:128]),
                after=[f"{p}_twz"], mark=f"{p}_r")
            P.t(lambda t: t.matmul(pMK[:, 0:128], Mqkbf[:], Rbf[:], start=True, stop=True),
                after=[f"{p}_r", f"{p}_mqkbf"] + ([f"{pm}_oadd"] if gc > 0 else []),
                mark=f"{p}_tmqkr")
            P.t(lambda t: t.matmul(pKP[:, 0:128], Kpbf[:], Rbf[:], start=True, stop=True),
                after=[f"{p}_kp"] + ([f"{pm}_zstt"] if gc > 0 else []), mark=f"{p}_tkpr")
            g_o1 = [f"{p}_tqz", f"{hp}_ec"]
            if gc > 0:
                g_o1.append(f"{pm}_oadd")
            P.s(lambda s: s.activation(O1[:], pQZ[:, 0:128], AF.Identity,
                                       scale=ECcol[:, c:c + 1]), after=g_o1, mark=f"{p}_o1")
            P.v(lambda v: v.tensor_add(Of[:], O1[:], pMK[:, 0:128]),
                after=[f"{p}_o1", f"{p}_tmqkr"], mark=f"{p}_oadd")
            P.v(lambda v: v.scalar_tensor_tensor(Zf[:], Zf[:], ACB[:, c:c + 1], pKP[:, 0:128],
                                                 ALU.mult, ALU.add),
                after=[f"{p}_tkpr", f"{hp}_acb"], mark=f"{p}_zstt")
            P.v(lambda v: v.tensor_copy(Zbf[:], Zf[:]), mark=f"{p}_zbf")

            P.v(lambda v: v.reduce_sum(gcol[:, 0:1], Of[:], axis=AX.X), after=[f"{p}_oadd"])
            P.v(lambda v: v.tensor_scalar(gcol[:, 1:2], gcol[:, 0:1], 1.0 / DK, None, ALU.mult))
            P.v(lambda v: v.tensor_scalar(Oc[:], Of[:], gcol[:, 1:2], None, ALU.subtract))
            P.v(lambda v: v.tensor_mul(gsq[:], Oc[:], Oc[:]))
            P.v(lambda v: v.reduce_sum(gcol[:, 2:3], gsq[:], axis=AX.X), mark=f"{p}_gvar")
            P.s(lambda s: s.activation(gcol[:, 3:4], gcol[:, 2:3], AF.Ln,
                                       bias=konst[:, 0:1], scale=1.0 / DK),
                after=[f"{p}_gvar", "konst"])
            P.s(lambda s: s.activation(gcol[:, 4:5], gcol[:, 3:4], AF.Exp, scale=-0.5),
                mark=f"{p}_grstd")
            P.v(lambda v: v.scalar_tensor_tensor(gapp[:], Oc[:], gcol[:, 4:5], LNWbc[:],
                                                 ALU.mult, ALU.mult),
                after=[f"{p}_grstd", "s_lnw"])
            P.v(lambda v: v.tensor_add(gapp[:], gapp[:], LNBbc[:]), after=["s_lnb"])
            g_og = [(dGT[par], 16 * nload)]
            if gc >= 2:
                g_og.append((dOG[par], 16 * ((gc - par) // 2)))
            P.v(lambda v: v.tensor_mul(ogb[par][:], gapp[:], Gtm[par][:]), after=g_og,
                mark=f"{p}_og")
            P.y(lambda sy: sy.dma_start(out=og_d[c * CH:(c + 1) * CH, h * 128:(h + 1) * 128],
                                        in_=ogb[par][:]), dOG[par], after=[f"{p}_og"])

        NG = NCH // GRP
        setup(0, ["dC_all"])
        for c in range(NCH):
            if c % GRP == 0:
                refill(0, c // GRP, c // GRP)
            chunk(0, c, c)
        gl = NCH - 1
        g1 = [f"g{gl}_dp", f"g{gl}_tqz", f"g{gl}_x0a", f"g{gl}_zstt",
              f"g{gl}_o1", f"g{gl}_kp", f"g{gl}_grstd", f"g{gl}_og"]
        setup(1, g1)
        for c in range(NCH):
            if c % GRP == 0:
                refill(1, c // GRP, NG + c // GRP)
            chunk(1, c, NCH + c)

        P.emit()

        def fin(sy):
            for sem in (dOG[0], dOG[1]):
                sy.wait_ge(sem.h, sem.n)
        ctx.sysec(fin)

    es_ph.close()
    ctx.cur, ctx.prefix = oc, op
    return ctx


def _l2_inputs(stage, inp):
    masks = np.stack([
        np.tril(np.ones((CH, CH), np.float32), -1),
        np.triu(np.ones((CH, CH), np.float32), 1),
        np.triu(np.ones((CH, CH), np.float32), 0),
    ])
    idenf = np.eye(128, dtype=np.float32)
    idenb = np.eye(128, dtype=np.float32).astype(BF)
    lnw = np.asarray(inp["L0_attn_ln_w"], np.float32)[None, :]
    lnb = np.asarray(inp["L0_attn_ln_b"], np.float32)[None, :]

    def a2a(full, c):
        return np.ascontiguousarray(np.concatenate(
            [full[2 * c * 128:(2 * c + 2) * 128, r * TC:(r + 1) * TC] for r in range(NC)],
            axis=0))

    maps = []
    for c in range(NC):
        h0, h1 = 2 * c, 2 * c + 1
        smx = np.ascontiguousarray(np.concatenate(
            [np.stack([stage["alphaL"][h0, r * TC:(r + 1) * TC],
                       stage["alphaL"][h1, r * TC:(r + 1) * TC],
                       stage["betaL"][h0, r * TC:(r + 1) * TC],
                       stage["betaL"][h1, r * TC:(r + 1) * TC]]) for r in range(NC)],
            axis=0).astype(np.float32))
        maps.append({
            "qx": a2a(stage["q_full"], c),
            "kx": a2a(stage["k_full"], c),
            "vx": a2a(stage["v_full"], c),
            "gx": a2a(stage["g_full"], c),
            "smx": smx, "lnw": lnw, "lnb": lnb,
            "masks": masks, "idenf": idenf, "idenb": idenb,
        })
    return maps


def run_l2(stage, inp, **kw):
    ctx = build_l2()
    maps = _l2_inputs(stage, inp)
    res = run_bass_kernel_spmd(ctx.nc, maps, core_ids=list(range(NC)), **kw)
    return res.results


def drams_l3(ctx, inner):
    d = {}
    d["ogx"] = (ctx.dint if inner else ctx.din)("ogx", [T, 256], BF16)
    d["Wo"] = ctx.din("Wo", [D, D], F32)
    d["ebias"] = ctx.din("ebias", [1, 8], F32)
    d["e8d"] = ctx.din("e8d", [8, 8 * 128], BF16)
    d["onesd"] = ctx.din("onesd", [1, 128], F32)
    d["w13"] = ctx.din("w13", [18, D, 1024], F32)
    d["w2t"] = ctx.din("w2t", [9, 1024, D], F32)
    d["s_spill"] = ctx.dint("s_spill", [TC, D], F32)
    d["x1"] = (ctx.dint if inner else ctx.dout)("x1", [TC, D], F32)
    return d


def build_l3():
    ctx = Ctx()
    dr = drams_l3(ctx, False)
    dr["xnfm"] = ctx.din("xnfm", [D, TC], BF16)
    dr["gateL"] = ctx.din("gateL", [8, TC], F32)
    dr["iden"] = ctx.din("iden", [128, 128], F32)
    dr["xm"] = ctx.din("xm", [TC, D], F32)
    phase_l3(ctx, dr)
    return ctx


def phase_l3(ctx, dr):
    ogx_d = dr["ogx"]
    Wo_d, xnfm_d, gateL_d = dr["Wo"], dr["xnfm"], dr["gateL"]
    ebias_d, e8_d, ones_d, iden_d = dr["ebias"], dr["e8d"], dr["onesd"], dr["iden"]
    w13_d, w2_d, xm_d, s_d, x1_d = dr["w13"], dr["w2t"], dr["xm"], dr["s_spill"], dr["x1"]
    es_ph = ExitStack()
    oc, op = ctx.cur, ctx.prefix
    ctx.cur, ctx.prefix = es_ph, "L3_"

    NE = 9

    if True:
        xnfm = [ctx.sbuf(f"xnfm{c}", [128, TC], BF16) for c in range(16)]
        stile = [ctx.sbuf(f"stile{i}", [128, 512], F32) for i in range(2)]
        gl = ctx.sbuf("gl", [8, TC], F32)
        sselfm = ctx.sbuf("sselfm", [8, TC], BF16)
        ones1 = ctx.sbuf("ones1", [1, 128], F32)
        e8s = ctx.sbuf("e8s", [8, 8 * 128], BF16)
        ebs = ctx.sbuf("ebs", [1, 8], F32)
        bbc = ctx.sbuf("bbc", [128, 8], F32)
        idn = ctx.sbuf("idn", [128, 128], F32)
        rT = ctx.sbuf("rT", [128, 8], F32)
        rB = ctx.sbuf("rB", [128, 8], F32)
        rE = ctx.sbuf("rE", [128, 8], F32)
        rQ = ctx.sbuf("rQ", [128, 8], F32)
        rS = ctx.sbuf("rS", [128, 8], F32)
        rcol = ctx.sbuf("rcol", [128, 4], F32)
        sval = ctx.sbuf("sval", [128, 8], F32)
        sz = ctx.sbuf("sz", [128, 512], BF16)
        sbc = [ctx.sbuf(f"sbc{i}", [128, 512], F32) for i in range(2)]
        xres = [ctx.sbuf(f"xres{i}", [128, 512], F32) for i in range(2)]
        sres = [ctx.sbuf(f"sres{i}", [128, 512], F32) for i in range(2)]
        gsil = ctx.sbuf("gsil", [128, 512], F32)
        gmul = ctx.sbuf("gmul", [128, 512], F32)
        x1t = [ctx.sbuf(f"x1t{i}", [128, 512], F32) for i in range(2)]
        es_a = ExitStack()
        ogfm = [ctx.sbuf(f"ogfm{c}", [128, TC], BF16, es=es_a) for c in range(16)]
        wob = [ctx.sbuf(f"wob{c}", [128, D], BF16, es=es_a) for c in range(16)]

        pZ = [ctx.psum(f"pZ{i}", [128, 512], F32) for i in range(2)]
        pW2 = [ctx.psum(f"pW2{i}", [128, 512], F32) for i in range(4)]
        pMISC = ctx.psum("pMISC", [128, 512], F32)
        pBC = ctx.psum("pBC", [128, 512], F32)

        dOG = ctx.sem("dOG")
        dWO = ctx.sem("dWO")
        dXN = ctx.sem("dXN")
        dGL = ctx.sem("dGL")
        dSP = ctx.sem("dSP")
        dW1 = [ctx.sem("dW1a"), ctx.sem("dW1b")]
        dW2 = ctx.sem("dW2")
        dXR = [ctx.sem("dXR0"), ctx.sem("dXR1")]
        dSR = [ctx.sem("dSR0"), ctx.sem("dSR1")]
        dX1 = ctx.sem("dX1")
        dX1.np_ = 0

        ctx.start()
        P = Prog(ctx)

        for c in range(16):
            for j in range(8):
                rr, hh = c // 2, c % 2
                P.y(lambda sy, c=c, j=j, rr=rr, hh=hh: sy.dma_start(
                    out=ogfm[c][:, j * 128:(j + 1) * 128],
                    in_=ogx_d[rr * TC + j * 128:rr * TC + (j + 1) * 128,
                              hh * 128:(hh + 1) * 128], transpose=True), dOG)
            P.y(lambda sy, c=c: sy.dma_start(out=xnfm[c][:], in_=xnfm_d[c * 128:(c + 1) * 128, :]), dXN)
            P.gd(lambda g, c=c: g.dma_start(out=wob[c][:], in_=Wo_d[c * 128:(c + 1) * 128, :]), dWO)
        P.mark_now("og_all", dOG, dOG.np_)
        P.mark_now("xn_all", dXN, dXN.np_)
        P.mark_now("wo_all", dWO, dWO.np_)

        P.y(lambda sy: sy.dma_start(out=gl[:], in_=gateL_d[:, :]), dGL)
        P.y(lambda sy: sy.dma_start(out=ones1[:], in_=ones_d[:, :]), dGL)
        P.y(lambda sy: sy.dma_start(out=e8s[:], in_=e8_d[:, :]), dGL)
        P.y(lambda sy: sy.dma_start(out=ebs[:], in_=ebias_d[:, :]), dGL)
        P.y(lambda sy: sy.dma_start(out=idn[:], in_=iden_d[:, :]), dGL)
        P.mark_now("gl_all", dGL, dGL.np_)

        for j in range(8):
            for dg in range(4):
                gi = j * 4 + dg
                par = gi % 2
                guard = ["og_all", "wo_all"]
                if gi >= 2:
                    guard.append(f"sA{gi - 2}")
                for kc in range(16):
                    P.t(lambda t, kc=kc, j=j, dg=dg, par=par: t.matmul(
                        pZ[par][:, 0:512], ogfm[kc][:, j * 128:(j + 1) * 128],
                        wob[kc][:, dg * 512:(dg + 1) * 512],
                        start=(kc == 0), stop=(kc == 15)),
                        after=guard if kc == 0 else (), mark=f"tA{gi}" if kc == 15 else None)
                g_e = [f"tA{gi}"]
                if gi >= 2:
                    g_e.append((dSP, 16 * gi))
                P.s(lambda s, par=par: s.copy(stile[par][:], pZ[par][:, 0:512]),
                    after=g_e, mark=f"sA{gi}")
                P.y(lambda sy, j=j, dg=dg, par=par: sy.dma_start(
                    out=s_d[j * 128:(j + 1) * 128, dg * 512:(dg + 1) * 512], in_=stile[par][:]),
                    dSP, after=[f"sA{gi}"])
        P.mark_now("sp_all", dSP, dSP.np_)

        P.t(lambda t: t.matmul(pBC[0:128, 0:8], ones1[:], ebs[:], start=True, stop=True),
            after=["gl_all"], mark="t_bbc")
        P.s(lambda s: s.copy(bbc[:], pBC[0:128, 0:8]), after=["t_bbc"], mark="s_bbc")
        for j in range(8):
            g_tr = ["gl_all"]
            if j >= 1:
                g_tr.append(f"r_ss{j - 1}")
            P.t(lambda t, j=j: t.transpose(pMISC[0:128, 0:8], gl[0:8, j * 128:(j + 1) * 128],
                                           idn[0:8, 0:8]), after=g_tr, mark=f"r_tr{j}")
            g_st = [f"r_tr{j}"]
            if j >= 1:
                g_st.append(f"r_sv{j - 1}")
            P.s(lambda s: s.activation(rT[:], pMISC[0:128, 0:8], AF.Sigmoid),
                after=g_st, mark=f"r_st{j}")
            P.v(lambda v: v.tensor_add(rB[:], rT[:], bbc[:]), after=[f"r_st{j}", "s_bbc"])
            P.v(lambda v: v.reduce_max(rcol[:, 0:1], rB[:], axis=AX.X))
            P.v(lambda v: v.tensor_scalar(rE[:], rB[:], rcol[:, 0:1], None, ALU.is_equal))
            P.v(lambda v: v.tensor_tensor_scan(rQ[:], rE[:], rE[:], 0.0, ALU.add, ALU.bypass))
            P.v(lambda v: v.tensor_scalar(rQ[:], rQ[:], 1.0, None, ALU.is_equal))
            P.v(lambda v: v.tensor_mul(rS[:], rE[:], rQ[:]))
            P.v(lambda v: v.scalar_tensor_tensor(rB[:], rS[:], -1e30, rB[:], ALU.mult, ALU.add))
            P.v(lambda v: v.reduce_max(rcol[:, 1:2], rB[:], axis=AX.X))
            P.v(lambda v: v.tensor_scalar(rE[:], rB[:], rcol[:, 1:2], None, ALU.is_equal))
            P.v(lambda v: v.tensor_tensor_scan(rQ[:], rE[:], rE[:], 0.0, ALU.add, ALU.bypass))
            P.v(lambda v: v.tensor_scalar(rQ[:], rQ[:], 1.0, None, ALU.is_equal))
            P.v(lambda v: v.tensor_mul(rE[:], rE[:], rQ[:]))
            P.v(lambda v: v.tensor_add(rS[:], rS[:], rE[:]))
            P.v(lambda v: v.tensor_mul(sval[:], rT[:], rS[:]), mark=f"r_sv{j}")
            g_tv = [f"r_sv{j}", f"r_st{j}"]
            P.t(lambda t, j=j: t.transpose(pMISC[0:8, 0:128], sval[:], idn[:]),
                after=g_tv, mark=f"r_tv{j}")
            P.s(lambda s, j=j: s.copy(sselfm[0:8, j * 128:(j + 1) * 128], pMISC[0:8, 0:128]),
                after=[f"r_tv{j}"], mark=f"r_ss{j}")

        es_a.close()
        wm = [[ctx.sbuf(f"wm0_{c}", [128, 1024], BF16) for c in range(16)]]
        hti = [[ctx.sbuf(f"ht{e}_{hc}", [128, 512], BF16) for hc in range(8)] for e in range(NE)]
        xe = [ctx.sbuf(f"xe{c}", [128, 512], BF16) for c in range(16)]
        w2b = [ctx.sbuf(f"w2b{hc}", [128, 512], BF16) for hc in range(8)]
        _hsz = [ctx.sbuf(f"hsz{hc}", [128, 512], BF16) for hc in range(8)]
        hsz = [_hsz for _ in range(NE)]
        widx = 0

        def load_mat(eidx, m):
            nonlocal widx
            guard = ["wo_all", "tA31"]
            if widx >= 1:
                guard.append(f"zmm_done{widx - 1}")
            for c in range(16):
                P.gd(lambda g, eidx=eidx, m=m, c=c: g.dma_start(
                    out=wm[0][c][:], in_=w13_d[2 * eidx + m, c * 128:(c + 1) * 128, :]),
                    dW1[0], after=(guard if c == 0 else ()))
            P.mark_now(f"w{widx}", dW1[0], dW1[0].np_)
            widx += 1

        for grp in range(2):
            gsl = slice(grp * 512, (grp + 1) * 512)
            for eidx in range(NE):
                ge = grp * NE + eidx
                epar = ge % 2
                if eidx < 8:
                    g_b = ["r_ss7"]
                    if ge >= 2:
                        g_b.append(f"xe_done{ge - 2}")
                    P.t(lambda t, eidx=eidx, gsl=gsl: t.matmul(
                        pBC[:, 0:512], e8s[:, eidx * 128:(eidx + 1) * 128], sselfm[0:8, gsl],
                        start=True, stop=True), after=g_b, mark=f"t_bc{ge}")
                    P.s(lambda s, epar=epar: s.copy(sbc[epar][:], pBC[:, 0:512]),
                        after=[f"t_bc{ge}"], mark=f"s_bc{ge}")
                    for c in range(16):
                        g_xe = [f"s_bc{ge}", "xn_all"]
                        if ge >= 1:
                            g_xe.append(f"zmm_done{2 * ge - 1}")
                        P.v(lambda v, c=c, epar=epar, gsl=gsl: v.tensor_mul(
                            xe[c][:], xnfm[c][:, gsl], sbc[epar][:]),
                            after=g_xe if c == 0 else ())
                    P.mark_now(f"xe_done{ge}", ctx.vv, P.vv_p)
                else:
                    P.mark_now(f"xe_done{ge}", ctx.vv, P.vv_p)
                load_mat(eidx, 0)
                w1i = widx - 1
                for hc in range(8):
                    par = hc % 2
                    g_m = [f"w{w1i}"]
                    if eidx < 8:
                        g_m.append(f"xe_done{ge}")
                    if hc >= 2:
                        g_m.append(f"sz{ge}_{hc - 2}")
                    elif ge >= 1:
                        g_m.append(f"h{ge - 1}_{6 + hc}")
                    for c in range(16):
                        P.t(lambda t, c=c, hc=hc, par=par, eidx=eidx, gsl=gsl:
                            t.matmul(pZ[par][:, 0:512],
                                     wm[0][c][:, hc * 128:(hc + 1) * 128],
                                     (xe[c][:] if eidx < 8 else xnfm[c][:, gsl]),
                                     start=(c == 0), stop=(c == 15)),
                            after=g_m if c == 0 else (),
                            mark=f"z1mm{ge}_{hc}" if c == 15 else None)
                    g_sz = [f"z1mm{ge}_{hc}"]
                    if hc >= 1:
                        g_sz.append(f"h{ge}_{hc - 1}_free" if False else f"sz{ge}_{hc - 1}")
                    P.s(lambda s, par=par, hc=hc, eidx=eidx: s.activation(
                        hsz[eidx][hc][:], pZ[par][:, 0:512], AF.Silu),
                        after=g_sz, mark=f"sz{ge}_{hc}")
                P.mark_now(f"zmm_done{w1i}", P.tt, P.tt_p)
                load_mat(eidx, 1)
                w3i = widx - 1
                for hc in range(8):
                    par = hc % 2
                    g_m = [f"w{w3i}", f"sz{ge}_{hc}"]
                    if hc >= 2:
                        g_m.append(f"h{ge}_{hc - 2}")
                    else:
                        g_m.append(f"sz{ge}_{6 + hc}")
                    for c in range(16):
                        P.t(lambda t, c=c, hc=hc, par=par, eidx=eidx, gsl=gsl:
                            t.matmul(pZ[par][:, 0:512],
                                     wm[0][c][:, hc * 128:(hc + 1) * 128],
                                     (xe[c][:] if eidx < 8 else xnfm[c][:, gsl]),
                                     start=(c == 0), stop=(c == 15)),
                            after=g_m if c == 0 else (),
                            mark=f"z3mm{ge}_{hc}" if c == 15 else None)
                    g_h = [f"z3mm{ge}_{hc}"]
                    if grp == 1:
                        g_h.append(f"w2r0_{eidx}_{hc}")
                    P.v(lambda v, hc=hc, par=par, eidx=eidx: v.tensor_mul(
                        hti[eidx][hc][:], pZ[par][:, 0:512], hsz[eidx][hc][:]),
                        after=g_h, mark=f"h{ge}_{hc}")
                P.mark_now(f"zmm_done{w3i}", P.tt, P.tt_p)

            for dg in range(4):
                for eidx in range(NE):
                    for hc in range(8):
                        g_l = []
                        if hc == 0:
                            g_l = [f"h{grp * NE + eidx}_7"]
                            if grp > 0 or dg > 0 or eidx > 0:
                                g_l.append(f"w2mmprev")
                        P.gd(lambda g, eidx=eidx, dg=dg, hc=hc: g.dma_start(
                            out=w2b[hc][:], in_=w2bf_slice(w2_d, eidx, hc, dg)), dW2,
                            after=g_l)
                    P.mark_now(f"w2l{grp}_{dg}_{eidx}", dW2, dW2.np_)
                    for t128 in range(4):
                        g_mm = [f"w2l{grp}_{dg}_{eidx}"]
                        if eidx == 0 and hc == 7:
                            pass
                        if eidx == 0:
                            g_mm.append(f"h{grp * NE + NE - 1}_7")
                            if grp > 0 or dg > 0:
                                g_mm.append(f"csil{grp * 4 + dg - 1}_{t128}")
                        for hc in range(8):
                            P.t(lambda t, t128=t128, hc=hc, eidx=eidx:
                                t.matmul(pW2[t128][:, 0:512],
                                         hti[eidx][hc][:, t128 * 128:(t128 + 1) * 128],
                                         w2b[hc][:],
                                         start=(eidx == 0 and hc == 0),
                                         stop=(eidx == NE - 1 and hc == 7)),
                                after=g_mm if hc == 0 else (),
                                mark=(f"w2f{grp}_{dg}_{t128}" if (eidx == NE - 1 and hc == 7)
                                      else None))
                    P.mark_now(f"w2mmprev", P.tt, P.tt_p)
                    if grp == 0:
                        for hc in range(8):
                            P.mark_now(f"w2r0_{eidx}_{hc}", P.tt, P.tt_p)
                gdi = grp * 4 + dg
                for t128 in range(4):
                    j = grp * 4 + t128
                    par = t128 % 2
                    g_xl = []
                    if gdi >= 1:
                        g_xl.append(f"cmb{gdi - 1}_{t128}")
                    elif t128 >= 2:
                        g_xl.append(f"cmb0_{t128 - 2}")
                    P.y(lambda sy, j=j, dg=dg, par=par: sy.dma_start(
                        out=xres[par][:], in_=xm_d[j * 128:(j + 1) * 128, dg * 512:(dg + 1) * 512]),
                        dXR[par], after=g_xl, mark=f"dx{gdi}_{t128}")
                    P.y(lambda sy, j=j, dg=dg, par=par: sy.dma_start(
                        out=sres[par][:], in_=s_d[j * 128:(j + 1) * 128, dg * 512:(dg + 1) * 512]),
                        dSR[par], after=["sp_all"] + g_xl, mark=f"ds{gdi}_{t128}")
                    g_sil = [f"w2f{grp}_{dg}_{t128}"]
                    if t128 >= 1:
                        g_sil.append(f"cmbv{gdi}_{t128 - 1}")
                    elif gdi >= 1:
                        g_sil.append(f"cmbv{gdi - 1}_3")
                    P.s(lambda s, t128=t128: s.activation(gsil[:], pW2[t128][:, 0:512], AF.Silu),
                        after=g_sil, mark=f"csil{gdi}_{t128}")
                    g_cv = [f"csil{gdi}_{t128}", (dSR[par], dSR[par].np_)]
                    P.v(lambda v, par=par: v.tensor_mul(gmul[:], gsil[:], sres[par][:]),
                        after=g_cv, mark=f"cmbv{gdi}_{t128}")
                    g_cmb = [(dXR[par], dXR[par].np_)]
                    if dX1.np_ > 0:
                        g_cmb.append((dX1, dX1.np_))
                    P.v(lambda v, par=par: v.tensor_add(x1t[par][:], gmul[:], xres[par][:]),
                        after=g_cmb, mark=f"cmb{gdi}_{t128}")
                    P.y(lambda sy, j=j, dg=dg, par=par: sy.dma_start(
                        out=x1_d[j * 128:(j + 1) * 128, dg * 512:(dg + 1) * 512],
                        in_=x1t[par][:]), dX1, after=[f"cmb{gdi}_{t128}"])

        P.emit()

        def fin(sy):
            sy.wait_ge(dX1.h, dX1.n)
        ctx.sysec(fin)

    es_ph.close()
    ctx.cur, ctx.prefix = oc, op
    return ctx


def w2bf_slice(w2_d, eidx, hc, dg):
    return w2_d[eidx, hc * 128:(hc + 1) * 128, dg * 512:(dg + 1) * 512]


def _l3_inputs(stage, inp):
    w13 = []
    for e in range(8):
        w13.append(np.asarray(inp["L0_ffn_w1"][e]).T)
        w13.append(np.asarray(inp["L0_ffn_w3"][e]).T)
    w13.append(np.asarray(inp["L0_ffn_shared_w1"]).T)
    w13.append(np.asarray(inp["L0_ffn_shared_w3"]).T)
    w13 = np.ascontiguousarray(np.stack(w13).astype(np.float32))
    w2t = np.ascontiguousarray(np.stack(
        [np.asarray(inp["L0_ffn_w2"][e]).T for e in range(8)]
        + [np.asarray(inp["L0_ffn_shared_w2"]).T]).astype(np.float32))
    e8 = np.zeros((8, 8 * 128), dtype=BF)
    for e in range(8):
        e8[e, e * 128:(e + 1) * 128] = 1
    onesd = np.ones((1, 128), np.float32)
    iden = np.eye(128, dtype=np.float32)
    ebias = np.ascontiguousarray(np.asarray(inp["L0_ffn_expert_bias"], np.float32)[None, :])
    Wo = np.ascontiguousarray(np.asarray(inp["L0_attn_Wo"], np.float32))
    x = np.asarray(inp["L0_attn_x"], np.float32)[0]
    og_tm = stage["og_fm"].T
    maps = []
    for c in range(NC):
        sl = slice(c * TC, (c + 1) * TC)
        ogx = np.ascontiguousarray(np.concatenate(
            [og_tm[sl, 2 * r * 128:(2 * r + 2) * 128] for r in range(NC)], axis=0))
        maps.append({
            "ogx": ogx,
            "Wo": Wo,
            "xnfm": np.ascontiguousarray(stage["xnfm"][:, sl]),
            "gateL": np.ascontiguousarray(stage["gateL"][:, sl]),
            "ebias": ebias, "e8d": e8, "onesd": onesd, "iden": iden,
            "w13": w13, "w2t": w2t,
            "xm": np.ascontiguousarray(x[sl]),
        })
    return maps


def run_l3(stage, inp, **kw):
    ctx = build_l3()
    maps = _l3_inputs(stage, inp)
    res = run_bass_kernel_spmd(ctx.nc, maps, core_ids=list(range(NC)), **kw)
    return res.results


def drams_l4(ctx, inner):
    mk = ctx.dint if inner else ctx.dout
    d = {}
    d["Wqkv"] = ctx.din("Wqkv", [3, D, D], F32)
    d["bcols"] = ctx.din("bcols", [128, 48], F32)
    d["gatew"] = ctx.din("gatew", [128, 8 * 16], F32)
    d["cosd"] = ctx.din("cosd", [128, TC], F32)
    d["ssind"] = ctx.din("ssind", [128, TC], F32)
    d["xn1fm"] = mk("xn1fm", [D, TC], BF16)
    d["q1_d"] = mk("q1_d", [D, TC], BF16)
    d["k1_d"] = mk("k1_d", [D, TC], BF16)
    d["v1tm"] = mk("v1tm", [TC, D], BF16)
    d["gate1L"] = mk("gate1L", [8, TC], F32)
    return d


def build_l4():
    ctx = Ctx()
    dr = drams_l4(ctx, False)
    dr["x1"] = ctx.din("x1", [TC, D], F32)
    dr["iden"] = ctx.din("iden", [128, 128], F32)
    phase_l4(ctx, dr)
    return ctx


def phase_l4(ctx, dr):
    x1_d, Wqkv_d, bcols_d, gatew_d = dr["x1"], dr["Wqkv"], dr["bcols"], dr["gatew"]
    cos_d, ssin_d, iden_d = dr["cosd"], dr["ssind"], dr["iden"]
    xnfm1_d, q1_d, k1_d, v1tm_d, gate1L_d = (dr["xn1fm"], dr["q1_d"], dr["k1_d"],
                                             dr["v1tm"], dr["gate1L"])
    es_ph = ExitStack()
    oc, op = ctx.cur, ctx.prefix
    ctx.cur, ctx.prefix = es_ph, "L4_"

    proj_d = [q1_d, k1_d, None]

    if True:
        xnbf = [ctx.sbuf(f"xnbf{c}", [128, TC], BF16) for c in range(16)]
        idn = ctx.sbuf("idn", [128, 128], F32)
        gws = ctx.sbuf("gws", [128, 8 * 16], F32)
        bcs = ctx.sbuf("bcs", [128, 48], F32)
        glt = ctx.sbuf("glt", [8, TC], F32)
        lnc = ctx.sbuf("lnc", [128, D], F32)
        col = ctx.sbuf("col", [128, 12], F32)
        konst = ctx.sbuf("konst", [128, 2], F32)
        cosb = ctx.sbuf("cosb", [128, TC], F32)
        ssinb = ctx.sbuf("ssinb", [128, TC], F32)
        qpre = [ctx.sbuf(f"qpre{i}", [128, TC], F32) for i in range(2)]
        shuf = ctx.sbuf("shuf", [128, TC], F32)
        rt1 = ctx.sbuf("rt1", [128, TC], F32)
        rt2s = ctx.sbuf("rt2s", [128, TC], F32)
        obf = [ctx.sbuf(f"obf{i}", [128, TC], BF16) for i in range(2)]
        vtb = [ctx.sbuf(f"vtb{i}", [128, 128], BF16) for i in range(2)]
        es_t = ExitStack()
        xt = [ctx.sbuf(f"xt{i}", [128, D], F32, es=es_t) for i in range(3)]
        xnfm = [ctx.sbuf(f"xnfm{c}", [128, TC], F32, es=es_t) for c in range(16)]

        pA = [ctx.psum(f"pA{i}", [128, 512], F32) for i in range(2)]
        pB = [ctx.psum(f"pB{i}", [128, 512], F32) for i in range(2)]
        pT = [ctx.psum(f"pT{i}", [128, 512], F32) for i in range(2)]

        dC = ctx.sem("dC")
        dXT = [ctx.sem(f"dXT{i}") for i in range(3)]
        dW = ctx.sem("dW")
        dS = ctx.sem("dS")
        dQ = [ctx.sem("dQ0"), ctx.sem("dQ1")]
        dQV = [ctx.sem("dQV0"), ctx.sem("dQV1")]

        ctx.start()
        P = Prog(ctx)

        P.y(lambda sy: sy.dma_start(out=idn[:], in_=iden_d[:, :]), dC)
        P.y(lambda sy: sy.dma_start(out=gws[:], in_=gatew_d[:, :]), dC)
        P.y(lambda sy: sy.dma_start(out=bcs[:], in_=bcols_d[:, :]), dC)
        P.y(lambda sy: sy.dma_start(out=cosb[:], in_=cos_d[:, :]), dC)
        P.y(lambda sy: sy.dma_start(out=ssinb[:], in_=ssin_d[:, :]), dC)
        P.mark_now("dC_all", dC, dC.np_)
        P.v(lambda v: v.memset(konst[:, 0:1], EPS), mark="konst")

        for i in range(3):
            P.y(lambda sy, i=i: sy.dma_start(out=xt[i % 3][:], in_=x1_d[i * 128:(i + 1) * 128, :]),
                dXT[i % 3], mark=f"xl{i}")
        for i in range(8):
            o = 6 * (i % 2)
            tile = xt[i % 3]
            P.v(lambda v, i=i, o=o, tile=tile: v.reduce_sum(col[:, o:o + 1], tile[:], axis=AX.X),
                after=[(dXT[i % 3], 16 * (i // 3 + 1))] + ([f"xn{i - 2}"] if i >= 2 else []))
            P.v(lambda v, o=o: v.tensor_scalar(col[:, o + 1:o + 2], col[:, o:o + 1], 1.0 / D,
                                               None, ALU.mult))
            P.v(lambda v, o=o, tile=tile: v.tensor_scalar(lnc[:], tile[:], col[:, o + 1:o + 2],
                                                          None, ALU.subtract))
            P.v(lambda v, tile=tile: v.tensor_mul(tile[:], lnc[:], lnc[:]))
            P.v(lambda v, o=o, tile=tile: v.reduce_sum(col[:, o + 2:o + 3], tile[:], axis=AX.X),
                mark=f"sumsq{i}")
            P.s(lambda s, o=o: s.activation(col[:, o + 3:o + 4], col[:, o + 2:o + 3], AF.Ln,
                                            bias=konst[:, 0:1], scale=1.0 / D),
                after=[f"sumsq{i}", "konst"])
            P.s(lambda s, o=o: s.activation(col[:, o + 4:o + 5], col[:, o + 3:o + 4], AF.Exp,
                                            scale=-0.5), mark=f"rstd{i}")
            P.v(lambda v, o=o, tile=tile: v.tensor_scalar(tile[:], lnc[:], col[:, o + 4:o + 5],
                                                          None, ALU.mult),
                after=[f"rstd{i}"], mark=f"xn{i}")
            for c in range(16):
                gx = i * 16 + c
                par = gx % 2
                g_t = [f"xn{i}", "dC_all"]
                if gx >= 2:
                    g_t.append(f"te{gx - 2}")
                P.t(lambda t, i=i, c=c, par=par, tile=tile: t.transpose(
                    pT[par][0:128, 0:128], tile[:, c * 128:(c + 1) * 128], idn[:]),
                    after=g_t, mark=(f"tT{i}" if c == 15 else f"tt{gx}"))
                P.s(lambda s, i=i, c=c, par=par: s.copy(xnfm[c][:, i * 128:(i + 1) * 128],
                                                        pT[par][0:128, 0:128]),
                    after=[f"tT{i}" if c == 15 else f"tt{gx}"], mark=f"te{gx}")
            if i + 3 < 8:
                P.y(lambda sy, i=i: sy.dma_start(out=xt[(i + 3) % 3][:],
                                                 in_=x1_d[(i + 3) * 128:(i + 4) * 128, :]),
                    dXT[(i + 3) % 3], after=[f"tT{i}"], mark=f"xl{i + 3}")
        P.mark_now("ln_done", ctx.ss, P.ss_p)

        for gi in range(2):
            last = None
            for c in range(16):
                P.t(lambda t, gi=gi, c=c: t.matmul(pA[gi][0:8, 0:512], gws[:, c * 8:(c + 1) * 8],
                                                   xnfm[c][:, gi * 512:(gi + 1) * 512],
                                                   start=(c == 0), stop=(c == 15)),
                    after=["ln_done", "dC_all"] if c == 0 else (), mark=f"tg{gi}" if c == 15 else None)
            P.s(lambda s, gi=gi: s.copy(glt[0:8, gi * 512:(gi + 1) * 512], pA[gi][0:8, 0:512]),
                after=[f"tg{gi}"], mark=f"ge{gi}")
        P.y(lambda sy: sy.dma_start(out=gate1L_d[:, :], in_=glt[:, :]), dS,
            after=["ge0", "ge1"])

        for c in range(16):
            P.v(lambda v, c=c: v.tensor_copy(xnbf[c][:], xnfm[c][:]), after=["ln_done"],
                mark=f"cb{c}")
            P.y(lambda sy, c=c: sy.dma_start(out=xnfm1_d[c * 128:(c + 1) * 128, :],
                                             in_=xnbf[c][:]), dS, after=[f"cb{c}"])
        P.mark_now("cast_done", ctx.vv, P.vv_p)

        es_t.close()
        wbuf = [ctx.sbuf(f"wbuf{c}", [128, D], BF16) for c in range(16)]

        for m in range(3):
            g_w = ["cast_done", "ge0", "ge1"]
            if m >= 1:
                g_w.append(f"pm{m - 1}")
            for c in range(16):
                P.gd(lambda g, m=m, c=c: g.dma_start(out=wbuf[c][:],
                                                     in_=Wqkv_d[m, c * 128:(c + 1) * 128, :]),
                     dW, after=(g_w if c == 0 else ()))
            P.mark_now(f"wl{m}", dW, dW.np_)
            for j in range(16):
                gx = m * 16 + j
                par = gx % 2
                for gi in range(2):
                    g_mm = [f"wl{m}"]
                    if gx >= 2:
                        g_mm.append(f"ev{gx - 2}_{gi}")
                    last = None
                    for c in range(16):
                        P.t(lambda t, c=c, j=j, gi=gi, par=par: t.matmul(
                            pB[par][:, 0:512] if gi == 0 else pA[par][:, 0:512],
                            wbuf[c][:, j * 128:(j + 1) * 128],
                            xnbf[c][:, gi * 512:(gi + 1) * 512],
                            start=(c == 0), stop=(c == 15)),
                            after=g_mm if c == 0 else (),
                            mark=f"pm{gx}_{gi}" if c == 15 else None)
                    g_ev = [f"pm{gx}_{gi}"]
                    if gx >= 2:
                        g_ev.append(f"rope{gx - 2}" if (gx - 2) % 48 < 32 or True else "")
                    P.s(lambda s, gi=gi, par=par, m=m, j=j: s.activation(
                        qpre[par][:, gi * 512:(gi + 1) * 512],
                        (pB[par][:, 0:512] if gi == 0 else pA[par][:, 0:512]),
                        AF.Identity, bias=bcs[:, m * 16 + j:m * 16 + j + 1]),
                        after=g_ev, mark=f"ev{gx}_{gi}")
                if m < 2:
                    g_r = [f"ev{gx}_1", "dC_all"]
                    if gx >= 2:
                        g_r.append((dQ[par], 16 * ((gx - par) // 2)))
                    P.v(lambda v, par=par: v.stream_shuffle(shuf[:], qpre[par][:],
                                                            [i ^ 1 for i in range(32)]),
                        after=g_r)
                    P.v(lambda v, par=par: v.tensor_mul(rt1[:], qpre[par][:], cosb[:]))
                    P.v(lambda v: v.tensor_mul(rt2s[:], shuf[:], ssinb[:]))
                    P.v(lambda v, par=par: v.tensor_add(obf[par][:], rt1[:], rt2s[:]),
                        mark=f"rope{gx}")
                    P.y(lambda sy, m=m, j=j, par=par: sy.dma_start(
                        out=proj_d[m][j * 128:(j + 1) * 128, :], in_=obf[par][:]),
                        dQ[par], after=[f"rope{gx}"])
                else:
                    for t8 in range(8):
                        vgx = j * 8 + t8
                        g_t = [f"ev{gx}_1"]
                        if vgx >= 2:
                            g_t.append(f"vte{vgx - 2}")
                        P.t(lambda t, t8=t8, par=par, vp=vgx % 2: t.transpose(
                            pT[vp][0:128, 0:128], qpre[par][:, t8 * 128:(t8 + 1) * 128],
                            idn[:]), after=g_t, mark=f"vtr{vgx}")
                        g_e = [f"vtr{vgx}"]
                        if vgx >= 2:
                            g_e.append((dQV[vgx % 2], 16 * ((vgx - vgx % 2) // 2)))
                        P.s(lambda s, vp=vgx % 2: s.copy(vtb[vp][:], pT[vp][0:128, 0:128]),
                            after=g_e, mark=f"vte{vgx}")
                        P.y(lambda sy, j=j, t8=t8, vp=vgx % 2: sy.dma_start(
                            out=v1tm_d[t8 * 128:(t8 + 1) * 128, j * 128:(j + 1) * 128],
                            in_=vtb[vp][:]), dQV[vgx % 2], after=[f"vte{vgx}"])
                    P.mark_now(f"rope{gx}", ctx.ss, P.ss_p)
            P.mark_now(f"pm{m}", P.tt, P.tt_p)

        P.emit()

        def fin(sy):
            for sem in (dS, dQ[0], dQ[1], dQV[0], dQV[1]):
                sy.wait_ge(sem.h, sem.n)
        ctx.sysec(fin)

    es_ph.close()
    ctx.cur, ctx.prefix = oc, op
    return ctx


def _l4_inputs(inp, x1_full):
    Wqkv = np.ascontiguousarray(np.stack([inp["L1_attn_Wq"], inp["L1_attn_Wk"],
                                          inp["L1_attn_Wv"]]).astype(np.float32))
    bcols = np.stack([np.asarray(inp["L1_attn_bq"]), np.asarray(inp["L1_attn_bk"]),
                      np.asarray(inp["L1_attn_bv"])]).astype(np.float32)
    bcols = np.ascontiguousarray(bcols.reshape(3, 16, 128).transpose(2, 0, 1).reshape(128, 48))
    gatew = np.asarray(inp["L1_ffn_gate_w"], np.float32)
    gatew = np.ascontiguousarray(gatew.reshape(16, 128, 8).transpose(1, 0, 2).reshape(128, 128))
    iden = np.eye(128, dtype=np.float32)
    hd = 128
    jj = np.arange(hd // 2)
    inv_freq = 10000.0 ** (-2.0 * jj / hd)
    pos = np.arange(T)
    ang = pos[:, None] * inv_freq[None, :]
    sin2 = np.stack([np.sin(ang), np.sin(ang)], -1).reshape(T, hd)
    cos2 = np.stack([np.cos(ang), np.cos(ang)], -1).reshape(T, hd)
    sign = np.where(np.arange(hd) % 2 == 0, -1.0, 1.0)
    ssin = sin2 * sign[None, :]
    cosT = np.ascontiguousarray(cos2.T.astype(np.float32))
    ssinT = np.ascontiguousarray(ssin.T.astype(np.float32))
    maps = []
    for c in range(NC):
        sl = slice(c * TC, (c + 1) * TC)
        maps.append({
            "x1": np.ascontiguousarray(x1_full[sl]),
            "Wqkv": Wqkv, "bcols": bcols, "gatew": gatew,
            "cosd": np.ascontiguousarray(cosT[:, sl]),
            "ssind": np.ascontiguousarray(ssinT[:, sl]),
            "iden": iden,
        })
    return maps


def run_l4(inp, x1_full, **kw):
    ctx = build_l4()
    maps = _l4_inputs(inp, x1_full)
    res = run_bass_kernel_spmd(ctx.nc, maps, core_ids=list(range(NC)), **kw)
    return res.results


def drams_l5(ctx, inner):
    d = {}
    d["k1ag"] = (ctx.dint if inner else ctx.din)("k1ag", [NC * D, TC], BF16)
    d["v1t"] = (ctx.dint if inner else ctx.din)("v1t", [T, D], BF16)
    d["Wo1"] = ctx.din("Wo1", [D, D], F32)
    d["bo1"] = ctx.din("bo1", [1, D], F32)
    d["s1"] = (ctx.dint if inner else ctx.dout)("s1", [TC, D], F32)
    return d


def build_l5():
    ctx = Ctx()
    dr = drams_l5(ctx, False)
    dr["q1_d"] = ctx.din("q1", [D, TC], BF16)
    dr["onesd"] = ctx.din("onesd", [1, 128], F32)
    phase_l5(ctx, dr)
    return ctx


def phase_l5(ctx, dr):
    q1_d, k1ag_d, v1_d = dr["q1_d"], dr["k1ag"], dr["v1t"]
    Wo_d, bo_d, ones_d, s1_d = dr["Wo1"], dr["bo1"], dr["onesd"], dr["s1"]
    es_ph = ExitStack()
    oc, op = ctx.cur, ctx.prefix
    ctx.cur, ctx.prefix = es_ph, "L5_"

    QB = 512
    NKT = T // 128

    if True:
        qfm = [ctx.sbuf(f"qfm{c}", [128, TC], BF16) for c in range(16)]
        kh = [ctx.sbuf(f"kh{i}", [128, T], BF16) for i in range(2)]
        vh = [ctx.sbuf(f"vh{i}", [128, T], BF16) for i in range(2)]
        wos = [ctx.sbuf(f"wos{c}", [128, D], BF16) for c in range(16)]
        PTa = ctx.sbuf("PTa", [128, 1536], BF16)
        PTb = ctx.sbuf("PTb", [128, 1024], BF16)
        On = [ctx.sbuf(f"On{h}", [128, QB], BF16) for h in range(16)]
        ones1 = ctx.sbuf("ones1", [1, 128], F32)
        onesc = ctx.sbuf("onesc", [128, 1], F32)
        onescb = ctx.sbuf("onescb", [128, 1], BF16)
        qsq = ctx.sbuf("qsq", [128, 512], F32)
        kcol = ctx.sbuf("kcol", [128, 1], F32)
        kcsq = ctx.sbuf("kcsq", [128, 1], F32)
        lnkb = ctx.sbuf("lnkb", [1, 16], F32)
        lnq16 = ctx.sbuf("lnq16", [1, 16], F32)
        qmax16 = ctx.sbuf("qmax16", [1, 16], F32)
        sc1 = ctx.sbuf("sc1", [1, 4], F32)
        mhl = ctx.sbuf("mhl", [1, 1], F32)
        negM16 = ctx.sbuf("negM16", [128, 16], F32)
        drc = ctx.sbuf("drc", [1, QB], F32)
        drb = ctx.sbuf("drb", [128, QB], F32)
        bob = [ctx.sbuf(f"bob{dg}", [128, 512], F32) for dg in range(4)]
        bor = ctx.sbuf("bor", [1, 512], F32)
        s1t = [ctx.sbuf(f"s1t{i}", [128, 512], F32) for i in range(2)]

        pSa = ctx.psum("pSa", [128, 1536], F32)
        pSb = ctx.psum("pSb", [128, 1024], F32)
        pO = ctx.psum("pO", [128, 512], F32)
        pDEN = ctx.psum("pDEN", [128, 512], F32)
        pM = ctx.psum("pM", [128, 512], F32)

        dQ = ctx.sem("dQ")
        dK = [ctx.sem("dK0"), ctx.sem("dK1")]
        dV = [ctx.sem("dV0"), ctx.sem("dV1")]
        dWO = ctx.sem("dWO")
        dC = ctx.sem("dC")
        dBOL = ctx.sem("dBOL")
        dS1 = ctx.sem("dS1")

        ctx.start()
        P = Prog(ctx)

        def kvload(hi):
            h = hi % 16
            par = hi % 2
            g_k = [f"att{hi - 2}"] if hi >= 2 else []
            P.y(lambda sy, h=h, par=par: sy.dma_start(
                out=kh[par][:],
                in_=k1ag_d.rearrange("(r d) t -> d r t", d=D)[h * 128:(h + 1) * 128]),
                dK[par], after=g_k)
            P.y(lambda sy, h=h, par=par: sy.dma_start(
                out=vh[par][:].rearrange("p (kt c) -> p kt c", c=128),
                in_=v1_d.rearrange("(kt p) (hh c) -> hh p kt c", p=128, c=128)[h]),
                dV[par], after=g_k)

        kvload(0)
        for c in range(16):
            P.y(lambda sy, c=c: sy.dma_start(out=qfm[c][:], in_=q1_d[c * 128:(c + 1) * 128, :]), dQ)
            P.gd(lambda g, c=c: g.dma_start(out=wos[c][:], in_=Wo_d[c * 128:(c + 1) * 128, :]), dWO)
        P.y(lambda sy: sy.dma_start(out=ones1[:], in_=ones_d[:, :]), dC)
        P.mark_now("q_all", dQ, dQ.np_)
        P.mark_now("wo_all", dWO, dWO.np_)
        P.mark_now("c_all", dC, dC.np_)
        P.v(lambda v: v.memset(onesc[:], 1.0), mark="onesc")
        P.v(lambda v: v.memset(onescb[:], 1.0), mark="onescb")
        P.v(lambda v: v.memset(mhl[:], -0.5 * np.log(128.0)), mark="mhl")
        for dg in range(4):
            P.y(lambda sy, dg=dg: sy.dma_start(out=bor[:], in_=bo_d[0:1, dg * 512:(dg + 1) * 512]),
                dBOL, after=[f"bo{dg - 1}"] if dg else [], mark=f"bol{dg}")
            P.t(lambda t, dg=dg: t.matmul(pO[:, 0:512], ones1[:], bor[:],
                                          start=True, stop=True),
                after=["c_all", f"bol{dg}"] + ([f"bo{dg - 1}"] if dg else []), mark=f"tbo{dg}")
            P.s(lambda s, dg=dg: s.copy(bob[dg][:], pO[:, 0:512]), after=[f"tbo{dg}"],
                mark=f"bo{dg}")

        for c in range(16):
            g_sq = ["q_all"] + ([f"p_q2b{c - 1}"] if c else [])
            P.v(lambda v, c=c: v.tensor_mul(qsq[:], qfm[c][:, 0:512], qfm[c][:, 0:512]),
                after=g_sq, mark=f"p_sq{c}")
            P.t(lambda t: t.matmul(pM[0:1, 0:512], onesc[:], qsq[:], start=True, stop=True),
                after=[f"p_sq{c}", "onesc"] + ([f"p_qmb{c - 1}"] if c else []), mark=f"p_q2a{c}")
            P.v(lambda v, c=c: v.reduce_max(qmax16[0:1, c:c + 1], pM[0:1, 0:512], axis=AX.X),
                after=[f"p_q2a{c}"], mark=f"p_qma{c}")
            P.v(lambda v, c=c: v.tensor_mul(qsq[:], qfm[c][:, 512:1024], qfm[c][:, 512:1024]),
                after=[f"p_q2a{c}"], mark=f"p_sqb{c}")
            P.t(lambda t: t.matmul(pM[0:1, 0:512], onesc[:], qsq[:], start=True, stop=True),
                after=[f"p_sqb{c}", f"p_qma{c}"], mark=f"p_q2b{c}")
            P.v(lambda v: v.reduce_max(sc1[0:1, 0:1], pM[0:1, 0:512], axis=AX.X),
                after=[f"p_q2b{c}"], mark=f"p_qmb{c}")
            P.v(lambda v, c=c: v.tensor_max(qmax16[0:1, c:c + 1], qmax16[0:1, c:c + 1],
                                            sc1[0:1, 0:1]), after=[f"p_qmb{c}"],
                mark=f"p_qm{c}")
        P.s(lambda s: s.activation(lnq16[:], qmax16[:], AF.Ln), after=["p_qm15"], mark="p_lnq")

        def chain_seg(h2, seg):
            kpar = h2 % 2
            nk2 = (h2 - kpar) // 2 + 1
            if seg == 0:
                g_kc = [(dK[kpar], 16 * nk2)]
                if h2 >= 1:
                    g_kc.append(f"m_kcs{h2 - 1}")
                P.v(lambda v, kpar=kpar: v.reduce_max(kcol[:], kh[kpar][:], axis=AX.X,
                                                      apply_absolute_value=True),
                    after=g_kc, mark=f"m_kc{h2}")
            elif seg == 1:
                g_ks = [f"m_kc{h2}"]
                if h2 >= 1:
                    g_ks.append(f"m_kb{h2 - 1}")
                P.s(lambda s: s.square(kcsq[:], kcol[:]), after=g_ks, mark=f"m_kcs{h2}")
                g_kb = [f"m_kcs{h2}", "onesc"]
                g_kb.append(f"m_negM{h2 - 1}" if h2 >= 1 else "p_lnq")
                P.t(lambda t: t.matmul(pM[0:1, 0:1], kcsq[:], onesc[:], start=True, stop=True),
                    after=g_kb, mark=f"m_kb{h2}")
            elif seg == 2:
                P.s(lambda s, h2=h2: s.activation(lnkb[0:1, h2:h2 + 1], pM[0:1, 0:1], AF.Ln),
                    after=[f"m_kb{h2}"], mark=f"m_lnkb{h2}")
                g_sm = [f"m_lnkb{h2}"]
                if h2 >= 1:
                    g_sm.append(f"m_expM{h2 - 1}")
                P.v(lambda v, h2=h2: v.tensor_add(sc1[0:1, 2:3], lnq16[0:1, h2:h2 + 1],
                                                  lnkb[0:1, h2:h2 + 1]), after=g_sm,
                    mark=f"m_sum{h2}")
                P.s(lambda s: s.activation(sc1[0:1, 3:4], sc1[0:1, 2:3], AF.Exp, scale=0.5,
                                           bias=mhl[0:1, 0:1]), after=[f"m_sum{h2}", "mhl"],
                    mark=f"m_expM{h2}")
            else:
                P.t(lambda t: t.matmul(pM[0:128, 0:1], ones1[:], sc1[0:1, 3:4],
                                       start=True, stop=True),
                    after=[f"m_expM{h2}", f"m_lnkb{h2}", "c_all"], mark=f"m_bc{h2}")
                P.s(lambda s, h2=h2: s.activation(negM16[:, h2:h2 + 1], pM[0:128, 0:1],
                                                  AF.Identity, scale=-1.0),
                    after=[f"m_bc{h2}"], mark=f"m_negM{h2}")

        for seg in range(4):
            chain_seg(0, seg)

        sched = []
        kt = 0
        while kt < NKT:
            n = 3 if (len(sched) % 2 == 0) else 2
            n = min(n, NKT - kt)
            sched.append((kt, n))
            kt += n
        NB = len(sched)

        for qb in range(2):
            qsl = slice(qb * QB, (qb + 1) * QB)
            for h in range(16):
                hi = qb * 16 + h
                par = hi % 2
                nk = (hi - par) // 2 + 1
                KH, VH = kh[par], vh[par]
                kld = (dK[par], 16 * nk)
                vld = (dV[par], 16 * nk)
                if hi < 31:
                    kvload(hi + 1)
                do_chain = (qb == 0 and h < 15)

                def emit_den(pend):
                    bi, kt0, n = pend
                    pt = PTa if bi % 2 == 0 else PTb
                    for i in range(n):
                        ktc = kt0 + i
                        first = (bi == 0 and i == 0)
                        g_dn = [f"ex{hi}_{bi}"] if i == 0 else []
                        if first:
                            g_dn += [vld, "onescb"]
                            if hi >= 1:
                                g_dn.append(f"dr{hi - 1}")
                        P.t(lambda t, ktc=ktc, i=i, pt=pt, first=first: t.matmul(
                            pDEN[0:1, 0:QB], onescb[:], pt[0:128, i * 512:(i + 1) * 512],
                            start=first, stop=(ktc == NKT - 1)),
                            after=g_dn, mark=f"dnf{hi}" if ktc == NKT - 1 else None)
                        g_v = []
                        if first:
                            g_v = [f"on{hi - 1}"] if hi >= 1 else ["bo3"]
                        P.t(lambda t, ktc=ktc, i=i, pt=pt, VH=VH, first=first: t.matmul(
                            pO[:, 0:QB], VH[:, ktc * 128:(ktc + 1) * 128],
                            pt[0:128, i * 512:(i + 1) * 512],
                            start=first, stop=(ktc == NKT - 1)),
                            after=g_v, mark=f"dn{hi}_{bi}" if i == n - 1 else None)

                pend = None
                for bi, (kt0, n) in enumerate(sched):
                    bank = pSa if bi % 2 == 0 else pSb
                    pt = PTa if bi % 2 == 0 else PTb
                    for i in range(n):
                        ktc = kt0 + i
                        g_mm = []
                        if i == 0:
                            g_mm = [kld]
                            if bi >= 2:
                                g_mm.append(f"ex{hi}_{bi - 2}")
                            elif hi >= 1:
                                g_mm.append(f"ex{hi - 1}_{NB - 2 + bi}")
                            if hi == 16 and bi < 2:
                                g_mm.append(f"woe0_{14 + bi}")
                        P.t(lambda t, ktc=ktc, i=i, bank=bank, KH=KH, h=h, qsl=qsl: t.matmul(
                            bank[:, i * 512:(i + 1) * 512], KH[:, ktc * 128:(ktc + 1) * 128],
                            qfm[h][:, qsl], start=True, stop=True),
                            after=g_mm, mark=f"st{hi}_{bi}" if i == n - 1 else None)
                    g_ex = [f"st{hi}_{bi}"]
                    if bi == 0:
                        g_ex.append(f"m_negM{h}")
                    if bi >= 2:
                        g_ex.append(f"dn{hi}_{bi - 2}")
                    elif hi >= 1:
                        g_ex.append(f"dn{hi - 1}_{NB - 2 + bi}")
                    P.s(lambda s, bank=bank, pt=pt, n=n, h=h: s.activation(
                        pt[:, 0:n * 512], bank[:, 0:n * 512], AF.Exp,
                        bias=negM16[:, h:h + 1], scale=RDK), after=g_ex, mark=f"ex{hi}_{bi}")
                    if pend is not None:
                        emit_den(pend)
                    pend = (bi, kt0, n)
                    if do_chain:
                        if bi == 4:
                            chain_seg(h + 1, 0)
                        elif bi == 14:
                            chain_seg(h + 1, 1)
                        elif bi == 18:
                            chain_seg(h + 1, 2)
                        elif bi == 21:
                            chain_seg(h + 1, 3)
                emit_den(pend)
                P.v(lambda v: v.reciprocal(drc[:], pDEN[0:1, 0:QB]),
                    after=[f"dnf{hi}", f"dn{hi}_{NB - 1}"], mark=f"dr{hi}")
                g_db = [f"dr{hi}"]
                if hi >= 1:
                    g_db.append(f"drs{hi - 1}")
                else:
                    g_db.append("c_all")
                if do_chain:
                    g_db.append(f"m_negM{h + 1}")
                P.t(lambda t: t.matmul(pM[0:128, 0:QB], ones1[:], drc[:], start=True, stop=True),
                    after=g_db, mark=f"drb{hi}")
                P.s(lambda s: s.copy(drb[:], pM[0:128, 0:QB]), after=[f"drb{hi}"],
                    mark=f"drs{hi}")
                g_on = [f"drs{hi}", f"dn{hi}_{NB - 1}"]
                if qb == 1:
                    g_on.append("wo_done0")
                P.v(lambda v, h=h: v.tensor_mul(On[h][:], pO[:, 0:QB], drb[:]), after=g_on,
                    mark=f"on{hi}")
                P.mark_now(f"att{hi}", ctx.vv, P.vv_p)

            for t128 in range(4):
                for dg in range(4):
                    wi = t128 * 4 + dg
                    par = wi % 2
                    g_w = ["wo_all", f"on{qb * 16 + 15}"]
                    if wi >= 2:
                        g_w.append(f"woe{qb}_{wi - 2}")
                    for h in range(16):
                        P.t(lambda t, h=h, t128=t128, dg=dg, par=par: t.matmul(
                            (pSa if par == 0 else pSb)[:, 0:512],
                            On[h][:, t128 * 128:(t128 + 1) * 128],
                            wos[h][:, dg * 512:(dg + 1) * 512],
                            start=(h == 0), stop=(h == 15)),
                            after=g_w if h == 0 else (),
                            mark=f"wom{qb}_{wi}" if h == 15 else None)
                    g_e = [f"wom{qb}_{wi}", f"bo{dg}"]
                    if qb * 16 + wi >= 2:
                        g_e.append((dS1, 16 * (qb * 16 + wi)))
                    P.v(lambda v, dg=dg, par=par: v.tensor_add(s1t[par][:],
                                                               (pSa if par == 0 else pSb)[:, 0:512],
                                                               bob[dg][:]),
                        after=g_e, mark=f"woe{qb}_{wi}")
                    P.y(lambda sy, qb=qb, t128=t128, dg=dg, par=par: sy.dma_start(
                        out=s1_d[qb * 512 + t128 * 128:qb * 512 + (t128 + 1) * 128,
                                 dg * 512:(dg + 1) * 512], in_=s1t[par][:]),
                        dS1, after=[f"woe{qb}_{wi}"])
            P.mark_now(f"wo_done{qb}", P.tt, P.tt_p)

        P.emit()

        def fin(sy):
            sy.wait_ge(dS1.h, dS1.n)
        ctx.sysec(fin)

    es_ph.close()
    ctx.cur, ctx.prefix = oc, op
    return ctx


def _l5_inputs(inp, stage):
    Wo = np.ascontiguousarray(np.asarray(inp["L1_attn_Wo"], np.float32))
    bo = np.ascontiguousarray(np.asarray(inp["L1_attn_bo"], np.float32)[None, :])
    onesd = np.ones((1, 128), np.float32)
    k1ag = np.ascontiguousarray(np.concatenate(
        [stage["k1_full"][:, r * TC:(r + 1) * TC] for r in range(NC)], axis=0))
    maps = []
    for c in range(NC):
        sl = slice(c * TC, (c + 1) * TC)
        maps.append({
            "q1": np.ascontiguousarray(stage["q1_full"][:, sl]),
            "k1ag": k1ag, "v1t": stage["v1_tm"],
            "Wo1": Wo, "bo1": bo, "onesd": onesd,
        })
    return maps


def run_l5(inp, stage, **kw):
    ctx = build_l5()
    maps = _l5_inputs(inp, stage)
    res = run_bass_kernel_spmd(ctx.nc, maps, core_ids=list(range(NC)), **kw)
    return res.results


def drams_l6(ctx):
    d = {}
    d["upw"] = ctx.din("upw", [D, T], F32)
    d["dwnw"] = ctx.din("dwnw", [T, D], F32)
    d["eww"] = ctx.din("eww", [D, D], F32)
    d["gam"] = ctx.din("gam", [1, D], F32)
    d["bet"] = ctx.din("bet", [1, D], F32)
    d["outp"] = ctx.dout("outp", [TC, D], F32)
    return d


def build_l6():
    ctx = Ctx()
    dr = drams_l6(ctx)
    dr["xn1fm"] = ctx.din("xn1fm", [D, TC], BF16)
    dr["gate1L"] = ctx.din("gate1L", [8, TC], F32)
    dr["x1"] = ctx.din("x1", [TC, D], F32)
    dr["s1"] = ctx.din("s1", [TC, D], F32)
    dr["onesd"] = ctx.din("onesd", [1, 128], F32)
    dr["iden"] = ctx.din("iden", [128, 128], F32)
    phase_l6(ctx, dr)
    return ctx


def phase_l6(ctx, dr):
    xnfm1_d, gate1L_d = dr["xn1fm"], dr["gate1L"]
    up_d, down_d, ew_d = dr["upw"], dr["dwnw"], dr["eww"]
    gam_d, bet_d = dr["gam"], dr["bet"]
    x1_d, s1_d = dr["x1"], dr["s1"]
    ones_d, iden_d = dr["onesd"], dr["iden"]
    out_d = dr["outp"]
    es_ph = ExitStack()
    oc, op = ctx.cur, ctx.prefix
    ctx.cur, ctx.prefix = es_ph, "L6_"

    NIC = 64

    if True:
        xnfm = [ctx.sbuf(f"xnfm{c}", [128, TC], BF16) for c in range(16)]
        h1 = [ctx.sbuf(f"h1_{i}", [128, 512], BF16) for i in range(NIC)]
        upb = [ctx.sbuf(f"upb{i}", [128, 512], BF16) for i in range(2)]
        dwnb = [ctx.sbuf("dwnb0", [128, 4096], BF16)]
        ewb = [ctx.sbuf("ewb0", [128, D], BF16)]
        relu = ctx.sbuf("relu", [128, 512], F32)
        scl = ctx.sbuf("scl", [128, 8], F32)
        sbcast = ctx.sbuf("sbcast", [128, TC], F32)
        idn = ctx.sbuf("idn", [128, 128], F32)
        ones1 = ctx.sbuf("ones1", [1, 128], F32)
        grow = ctx.sbuf("grow", [1, 512], F32)
        gbc = [ctx.sbuf(f"gbc{dg}", [128, 512], F32) for dg in range(4)]
        bbc = [ctx.sbuf(f"bbc{dg}", [128, 512], F32) for dg in range(4)]
        ytm = [ctx.sbuf(f"ytm{i}", [128, D], F32) for i in range(4)]
        konst = ctx.sbuf("konst", [128, 2], F32)
        col = ctx.sbuf("col", [128, 8], F32)
        lnc = ctx.sbuf("lnc", [128, D], F32)
        lsq = ctx.sbuf("lsq", [128, D], F32)

        es_b = ExitStack()
        gl = ctx.sbuf("gl", [8, TC], F32, es=es_b)
        logt = ctx.sbuf("logt", [128, 8], F32, es=es_b)
        rE = ctx.sbuf("rE", [128, 8], F32, es=es_b)
        rQ = ctx.sbuf("rQ", [128, 8], F32, es=es_b)
        rM = ctx.sbuf("rM", [128, 8], F32, es=es_b)
        rcol = ctx.sbuf("rcol", [128, 4], F32, es=es_b)
        srow = ctx.sbuf("srow", [1, TC], F32, es=es_b)

        pU = [ctx.psum(f"pU{i}", [128, 512], F32) for i in range(4)]
        pY = [ctx.psum(f"pY{i}", [128, 512], F32) for i in range(2)]
        pT2 = [ctx.psum(f"pT2{i}", [128, 512], F32) for i in range(2)]

        dXN = ctx.sem("dXN")
        dGL = ctx.sem("dGL")
        dUP = [ctx.sem("dUP0"), ctx.sem("dUP1")]
        dDW = [ctx.sem("dDW0"), ctx.sem("dDW1")]
        dEW = [ctx.sem("dEW0"), ctx.sem("dEW1")]
        dGB = ctx.sem("dGB")
        dXS = [ctx.sem("dXS0"), ctx.sem("dXS1")]
        dOU = ctx.sem("dOU")
        dOU.np_ = 0

        ctx.start()
        P = Prog(ctx)

        for c in range(16):
            P.y(lambda sy, c=c: sy.dma_start(out=xnfm[c][:], in_=xnfm1_d[c * 128:(c + 1) * 128, :]), dXN)
        P.y(lambda sy: sy.dma_start(out=gl[:], in_=gate1L_d[:, :]), dGL)
        P.y(lambda sy: sy.dma_start(out=idn[:], in_=iden_d[:, :]), dGL)
        P.y(lambda sy: sy.dma_start(out=ones1[:], in_=ones_d[:, :]), dGL)
        P.mark_now("xn_all", dXN, dXN.np_)
        P.mark_now("gl_all", dGL, dGL.np_)
        P.v(lambda v: v.memset(konst[:, 0:1], EPS), mark="konst")

        for j in range(8):
            g_tr = ["gl_all"]
            if j >= 1:
                g_tr.append(f"sc_sr{j - 1}")
            P.t(lambda t, j=j: t.transpose(pT2[0][0:128, 0:8], gl[0:8, j * 128:(j + 1) * 128],
                                           idn[0:8, 0:8]), after=g_tr, mark=f"sc_tr{j}")
            P.s(lambda s: s.copy(logt[:], pT2[0][0:128, 0:8]), after=[f"sc_tr{j}"], mark=f"sc_lg{j}")
            P.v(lambda v: v.reduce_max(rcol[:, 0:1], logt[:], axis=AX.X), after=[f"sc_lg{j}"])
            P.v(lambda v: v.tensor_scalar(rE[:], logt[:], rcol[:, 0:1], None, ALU.is_equal))
            P.v(lambda v: v.tensor_tensor_scan(rQ[:], rE[:], rE[:], 0.0, ALU.add, ALU.bypass))
            P.v(lambda v: v.tensor_scalar(rQ[:], rQ[:], 1.0, None, ALU.is_equal))
            P.v(lambda v: v.tensor_mul(rE[:], rE[:], rQ[:]))
            P.v(lambda v: v.scalar_tensor_tensor(rM[:], rE[:], -1e30, logt[:], ALU.mult, ALU.add))
            P.v(lambda v: v.reduce_max(rcol[:, 1:2], rM[:], axis=AX.X))
            P.v(lambda v, j=j: v.tensor_add(scl[:, j:j + 1], rcol[:, 0:1], rcol[:, 1:2]),
                mark=f"sc_sc{j}")
            P.t(lambda t, j=j: t.transpose(pT2[0][0:1, 0:128], scl[:, j:j + 1], idn[:]),
                after=[f"sc_sc{j}", f"sc_lg{j}"], mark=f"sc_tv{j}")
            P.s(lambda s, j=j: s.copy(srow[0:1, j * 128:(j + 1) * 128], pT2[0][0:1, 0:128]),
                after=[f"sc_tv{j}"], mark=f"sc_sr{j}")
        for gi in range(2):
            P.t(lambda t, gi=gi: t.matmul(pT2[1][:, 0:512], ones1[:],
                                          srow[0:1, gi * 512:(gi + 1) * 512], start=True, stop=True),
                after=["sc_sr7"] + ([f"sc_bc{gi - 1}"] if gi else []), mark=f"sc_tb{gi}")
            P.s(lambda s, gi=gi: s.copy(sbcast[:, gi * 512:(gi + 1) * 512], pT2[1][:, 0:512]),
                after=[f"sc_tb{gi}"], mark=f"sc_bc{gi}")

        for dg in range(4):
            P.y(lambda sy, dg=dg: sy.dma_start(out=grow[:], in_=gam_d[0:1, dg * 512:(dg + 1) * 512]),
                dGB, after=[f"lb_g{dg - 1}"] if dg else [], mark=f"lb_lg{dg}")
            P.t(lambda t, dg=dg: t.matmul(pT2[1][:, 0:512], ones1[:], grow[:], start=True, stop=True),
                after=[f"lb_lg{dg}", "sc_bc1"], mark=f"lb_tg{dg}")
            P.s(lambda s, dg=dg: s.copy(gbc[dg][:], pT2[1][:, 0:512]), after=[f"lb_tg{dg}"],
                mark=f"lb_g{dg}")
        for dg in range(4):
            P.y(lambda sy, dg=dg: sy.dma_start(out=grow[:], in_=bet_d[0:1, dg * 512:(dg + 1) * 512]),
                dGB, after=[f"lb_g3"] + ([f"lb_b{dg - 1}"] if dg else []), mark=f"lb_lb{dg}")
            P.t(lambda t, dg=dg: t.matmul(pT2[1][:, 0:512], ones1[:], grow[:], start=True, stop=True),
                after=[f"lb_lb{dg}", f"lb_g{dg}" if dg == 0 else f"lb_b{dg - 1}"], mark=f"lb_tb{dg}")
            P.s(lambda s, dg=dg: s.copy(bbc[dg][:], pT2[1][:, 0:512]), after=[f"lb_tb{dg}"],
                mark=f"lb_b{dg}")

        es_b.close()
        xct = [ctx.sbuf("xct0", [128, D], F32)]
        sct = [ctx.sbuf("sct0", [128, D], F32)]

        for half in range(2):
            hsl = slice(half * 512, (half + 1) * 512)
            for icg in range(16):
                for dc in range(16):
                    di = half * 256 + icg * 16 + dc
                    par = di % 2
                    g_l = ["xn_all"]
                    if di >= 2:
                        g_l.append(f"up_mm{di - 2}")
                    P.gd(lambda g, icg=icg, dc=dc, par=par: g.dma_start(
                        out=upb[par][:],
                        in_=up_d[dc * 128:(dc + 1) * 128, icg * 512:(icg + 1) * 512]),
                        dUP[par], after=g_l)
                    for i4 in range(4):
                        g_m = [(dUP[par], dUP[par].np_)]
                        if dc == 0:
                            if icg >= 1:
                                g_m.append(f"up_ev{half}_{icg - 1}")
                            elif half == 1:
                                g_m.append(f"up_ev0_15")
                                g_m.append("dn_mm15")
                        P.t(lambda t, i4=i4, dc=dc, par=par, hsl=hsl: t.matmul(
                            pU[i4][:, 0:512], upb[par][:, i4 * 128:(i4 + 1) * 128],
                            xnfm[dc][:, hsl], start=(dc == 0), stop=(dc == 15)),
                            after=g_m if i4 == 0 else (),
                            mark=f"up_mm{di}" if i4 == 3 else None)
                for i4 in range(4):
                    ic = icg * 4 + i4
                    g_r = [f"up_mm{half * 256 + icg * 16 + 15}"]
                    if half == 1:
                        g_r.append("dn_mm15")
                    if ic >= 1:
                        g_r.append(f"up_sq{half}_{ic - 1}")
                    elif half == 1:
                        g_r.append("up_sq0_63")
                    P.v(lambda v, i4=i4: v.tensor_scalar_max(relu[:], pU[i4][:, 0:512], 0.0),
                        after=g_r, mark=f"up_rl{half}_{ic}")
                    P.s(lambda s, ic=ic: s.square(h1[ic][:], relu[:]),
                        after=[f"up_rl{half}_{ic}"], mark=f"up_sq{half}_{ic}")
                P.mark_now(f"up_ev{half}_{icg}", ctx.ss, P.ss_p)
            for c in range(16):
                P.v(lambda v, c=c, hsl=hsl: v.tensor_mul(xnfm[c][:, hsl], xnfm[c][:, hsl],
                                                         sbcast[:, hsl]),
                    after=([f"up_mm{half * 256 + 255}", "sc_bc1"] if c == 0 else ()))
            P.mark_now(f"xs_done{half}", ctx.vv, P.vv_p)
            for dcol in range(16):
                gdi = half * 16 + dcol
                par = gdi % 2
                g_ld = []
                if gdi >= 1:
                    g_ld = [f"dn_mm{gdi - 1}"]
                P.gd(lambda g, dcol=dcol: g.dma_start(
                    out=dwnb[0][:].rearrange("p (ic c) -> p ic c", c=128),
                    in_=down_d.rearrange("(ic p) d -> p ic d", p=128)[:, 0:32, dcol * 128:(dcol + 1) * 128]),
                    dDW[0], after=g_ld, mark=f"dn_ld{gdi}")
                P.gd(lambda g, dcol=dcol: g.dma_start(
                    out=ewb[0][:].rearrange("p (dc c) -> p dc c", c=128),
                    in_=ew_d.rearrange("(dc p) d -> p dc d", p=128)[:, :, dcol * 128:(dcol + 1) * 128]),
                    dEW[0], after=[f"dn_mm{gdi - 1}"] if gdi >= 1 else [],
                    mark=f"ew_ld{gdi}")
                g_mm0 = [f"dn_ld{gdi}", f"up_sq{half}_63", f"xs_done{half}"]
                if gdi >= 2:
                    g_mm0.append(f"yse{gdi - 2}_3")
                for ic in range(32):
                    P.t(lambda t, ic=ic, par=par: t.matmul(
                        pY[par][:, 0:512], dwnb[0][:, ic * 128:(ic + 1) * 128],
                        h1[ic][:], start=(ic == 0), stop=False),
                        after=g_mm0 if ic == 0 else ())
                P.mark_now(f"dn_hm{gdi}", P.tt, P.tt_p)
                P.gd(lambda g, dcol=dcol: g.dma_start(
                    out=dwnb[0][:].rearrange("p (ic c) -> p ic c", c=128),
                    in_=down_d.rearrange("(ic p) d -> p ic d", p=128)[:, 32:64, dcol * 128:(dcol + 1) * 128]),
                    dDW[0], after=[f"dn_hm{gdi}"], mark=f"dn_ld2{gdi}")
                for ic in range(32):
                    P.t(lambda t, ic=ic, par=par: t.matmul(
                        pY[par][:, 0:512], dwnb[0][:, ic * 128:(ic + 1) * 128],
                        h1[32 + ic][:], start=False, stop=False),
                        after=[f"dn_ld2{gdi}"] if ic == 0 else ())
                for dc in range(16):
                    P.t(lambda t, dc=dc, par=par, hsl=hsl: t.matmul(
                        pY[par][:, 0:512], ewb[0][:, dc * 128:(dc + 1) * 128],
                        xnfm[dc][:, hsl], start=False, stop=(dc == 15)),
                        after=[f"ew_ld{gdi}"] if dc == 0 else (),
                        mark=f"dn_mm{gdi}" if dc == 15 else None)
                for t4 in range(4):
                    g_tp = [f"dn_mm{gdi}"]
                    if gdi >= 1:
                        g_tp.append(f"ytr_e{gdi - 1}_{t4}")
                    P.s(lambda s, t4=t4, par=par: s.copy(lsq[:, t4 * 128:(t4 + 1) * 128],
                                                         pY[par][:, t4 * 128:(t4 + 1) * 128]),
                        after=g_tp, mark=f"yse{gdi}_{t4}")
                    P.t(lambda t, t4=t4, par=par: t.transpose(
                        pT2[par][:, 0:128], lsq[:, t4 * 128:(t4 + 1) * 128], idn[:]),
                        after=[f"yse{gdi}_{t4}"] + ([f"ytr_e{gdi - 2}_{t4}"] if gdi >= 2 else []),
                        mark=f"ytr{gdi}_{t4}")
                    P.s(lambda s, t4=t4, par=par, dcol=dcol: s.copy(
                        ytm[t4][:, dcol * 128:(dcol + 1) * 128], pT2[par][:, 0:128]),
                        after=[f"ytr{gdi}_{t4}"], mark=f"ytr_e{gdi}_{t4}")
            for t4 in range(4):
                tj = half * 4 + t4
                g_ln = [f"ytr_e{half * 16 + 15}_3", f"ytr{half * 16 + 15}_3", "konst"]
                if tj >= 1:
                    g_ln.append(f"cmb{tj - 1}")
                P.v(lambda v, t4=t4: v.reduce_sum(col[:, 0:1], ytm[t4][:], axis=AX.X), after=g_ln)
                P.v(lambda v: v.tensor_scalar(col[:, 1:2], col[:, 0:1], 1.0 / D, None, ALU.mult))
                P.v(lambda v, t4=t4: v.tensor_scalar(lnc[:], ytm[t4][:], col[:, 1:2], None,
                                                     ALU.subtract))
                P.v(lambda v: v.tensor_mul(lsq[:], lnc[:], lnc[:]))
                P.v(lambda v: v.reduce_sum(col[:, 2:3], lsq[:], axis=AX.X), mark=f"lvar{tj}")
                P.s(lambda s: s.activation(col[:, 3:4], col[:, 2:3], AF.Ln, bias=konst[:, 0:1],
                                           scale=1.0 / D), after=[f"lvar{tj}"])
                P.s(lambda s: s.activation(col[:, 4:5], col[:, 3:4], AF.Exp, scale=-0.5),
                    mark=f"lrstd{tj}")
                P.v(lambda v: v.tensor_scalar(lsq[:], lnc[:], col[:, 4:5], None, ALU.mult),
                    after=[f"lrstd{tj}"], mark=f"lnn{tj}")
                for dg in range(4):
                    P.v(lambda v, dg=dg: v.tensor_mul(lnc[:, dg * 512:(dg + 1) * 512],
                                                      lsq[:, dg * 512:(dg + 1) * 512], gbc[dg][:]),
                        after=[f"lnn{tj}", "lb_g3"] if dg == 0 else ())
                    P.v(lambda v, dg=dg: v.tensor_add(lnc[:, dg * 512:(dg + 1) * 512],
                                                      lnc[:, dg * 512:(dg + 1) * 512], bbc[dg][:]),
                        after=["lb_b3"] if dg == 0 else (), mark=f"lgb{tj}" if dg == 3 else None)
                P.s(lambda s: s.activation(lsq[:], lnc[:], AF.Silu), after=[f"lgb{tj}"],
                    mark=f"lsil{tj}")
                g_cl = []
                if tj >= 1:
                    g_cl.append(f"cmb{tj - 1}")
                    g_cl.append((dOU, dOU.np_))
                P.y(lambda sy, tj=tj: sy.dma_start(out=xct[0][:],
                                                   in_=x1_d[tj * 128:(tj + 1) * 128, :]),
                    dXS[0], after=g_cl)
                P.y(lambda sy, tj=tj: sy.dma_start(out=sct[0][:],
                                                   in_=s1_d[tj * 128:(tj + 1) * 128, :]),
                    dXS[0])
                g_c = [f"lsil{tj}", (dXS[0], dXS[0].np_)]
                if dOU.np_ > 0:
                    g_c.append((dOU, dOU.np_))
                P.v(lambda v: v.tensor_mul(sct[0][:], lsq[:], sct[0][:]), after=g_c)
                P.v(lambda v: v.tensor_add(xct[0][:], xct[0][:], sct[0][:]),
                    mark=f"cmb{tj}")
                P.y(lambda sy, tj=tj: sy.dma_start(
                    out=out_d[tj * 128:(tj + 1) * 128, :], in_=xct[0][:]), dOU,
                    after=[f"cmb{tj}"])

        P.emit()

        def fin(sy):
            sy.wait_ge(dOU.h, dOU.n)
        ctx.sysec(fin)

    es_ph.close()
    ctx.cur, ctx.prefix = oc, op
    return ctx


def _l6_inputs(inp, stage):
    onesd = np.ones((1, 128), np.float32)
    iden = np.eye(128, dtype=np.float32)
    up = np.ascontiguousarray(np.asarray(inp["L1_ffn_up_w"], np.float32))
    dwn = np.ascontiguousarray(np.asarray(inp["L1_ffn_down_w"], np.float32))
    ew = np.ascontiguousarray(np.asarray(inp["L1_ffn_expert_weight"], np.float32))
    gam = np.ascontiguousarray(np.asarray(inp["L1_ffn_ln_gamma"], np.float32)[None, :])
    bet = np.ascontiguousarray(np.asarray(inp["L1_ffn_ln_beta"], np.float32)[None, :])
    maps = []
    for c in range(NC):
        sl = slice(c * TC, (c + 1) * TC)
        maps.append({
            "xn1fm": np.ascontiguousarray(stage["xnfm1"][:, sl]),
            "gate1L": np.ascontiguousarray(stage["gate1L"][:, sl]),
            "upw": up, "dwnw": dwn, "eww": ew, "gam": gam, "bet": bet,
            "x1": np.ascontiguousarray(stage["x1_full"][sl]),
            "s1": np.ascontiguousarray(stage["s1_full"][sl]),
            "onesd": onesd, "iden": iden,
        })
    return maps


def run_l6(inp, stage, **kw):
    ctx = build_l6()
    maps = _l6_inputs(inp, stage)
    res = run_bass_kernel_spmd(ctx.nc, maps, core_ids=list(range(NC)), **kw)
    return res.results


def _old_kernel_multi(**inputs):
    inp = {k: np.asarray(v) for k, v in inputs.items()}

    res1 = run_l1(inp)
    q_full = np.concatenate([r["q_d"] for r in res1], axis=1)
    k_full = np.concatenate([r["k_d"] for r in res1], axis=1)
    v_full = np.concatenate([r["v_d"] for r in res1], axis=1)
    g_full = np.concatenate([r["g_d"] for r in res1], axis=1)
    small = np.concatenate([r["smallL"] for r in res1], axis=1)
    xnfm_full = np.concatenate([r["xnfm"] for r in res1], axis=1)
    stage2 = {
        "q_full": q_full, "k_full": k_full, "v_full": v_full, "g_full": g_full,
        "alphaL": np.ascontiguousarray(small[0:16]),
        "betaL": np.ascontiguousarray(small[16:32]),
    }
    res2 = run_l2(stage2, inp)
    og_tm = np.concatenate([r["og_d"] for r in res2], axis=1)
    stage3 = {
        "og_fm": np.ascontiguousarray(og_tm.T),
        "xnfm": xnfm_full,
        "gateL": np.ascontiguousarray(small[32:40]),
    }
    res3 = run_l3(stage3, inp)
    x1_full = np.ascontiguousarray(np.concatenate([r["x1"] for r in res3], axis=0))

    res4 = run_l4(inp, x1_full)
    q1_full = np.concatenate([r["q1_d"] for r in res4], axis=1)
    k1_full = np.ascontiguousarray(np.concatenate([r["k1_d"] for r in res4], axis=1))
    v1_tm = np.concatenate([r["v1tm"] for r in res4], axis=0)
    xnfm1_full = np.concatenate([r["xn1fm"] for r in res4], axis=1)
    gate1L_full = np.ascontiguousarray(np.concatenate([r["gate1L"] for r in res4], axis=1))
    stage5 = {
        "q1_full": q1_full,
        "k1_full": k1_full,
        "v1_tm": np.ascontiguousarray(v1_tm),
    }
    res5 = run_l5(inp, stage5)
    s1_full = np.ascontiguousarray(np.concatenate([r["s1"] for r in res5], axis=0))

    stage6 = {
        "xnfm1": xnfm1_full, "gate1L": gate1L_full,
        "x1_full": x1_full, "s1_full": s1_full,
    }
    res6 = run_l6(inp, stage6)
    out = np.concatenate([r["outp"] for r in res6], axis=0).astype(np.float32)
    return out.reshape(1, T, D)


def build_fused():
    ctx = Ctx()
    d1 = drams_l1(ctx, True)
    d2 = drams_l2(ctx, True)
    d3 = drams_l3(ctx, True)
    d4 = drams_l4(ctx, True)
    d5 = drams_l5(ctx, True)
    d6 = drams_l6(ctx)
    d3["xnfm"] = d1["xnfm"]
    d3["gateL"] = d1["smallL"][32:40, :]
    d3["iden"] = d1["iden"]
    d3["xm"] = d1["xm"]
    d4["x1"] = d3["x1"]
    d4["iden"] = d1["iden"]
    d5["q1_d"] = d4["q1_d"]
    d5["onesd"] = d3["onesd"]
    d6["xn1fm"] = d4["xn1fm"]
    d6["gate1L"] = d4["gate1L"]
    d6["x1"] = d3["x1"]
    d6["s1"] = d5["s1"]
    d6["onesd"] = d3["onesd"]
    d6["iden"] = d1["iden"]

    rg = [list(range(NC))]
    qag = ctx.dint("qag", [NC * D, TC], BF16)
    kag = ctx.dint("kag", [NC * D, TC], BF16)
    vag = ctx.dint("vag", [NC * D, TC], BF16)
    gag = ctx.dint("gag", [NC * D, TC], BF16)
    smag = ctx.dint("smag", [NC * 32, TC], F32)
    ogag = ctx.dint("ogag", [NC * T, 256], BF16)

    with ctx.es:
        ccs = ctx.sem("ccs")
        dEX = ctx.sem("dEX")
        ctx.start()

        def collect(pairs):
            def sec(g):
                for (i, o) in pairs:
                    inc(g.collective_compute("AllGather", ALU.bypass, replica_groups=rg,
                                             ins=[i], outs=[o]), ccs, 1)
                g.wait_ge(ccs.h, ccs.n)
            ctx.gsec(sec)

        phase_l1(ctx, d1)
        ctx.barrier()
        collect([(d1["proj"][0][:, :], qag[:, :]),
                 (d1["proj"][1][:, :], kag[:, :]),
                 (d1["proj"][2][:, :], vag[:, :]),
                 (d1["proj"][3][:, :], gag[:, :]),
                 (d1["smallL"][0:32, :], smag[:, :])])
        ctx.barrier()

        def extract1(sy):
            pid = sy.partition_id()
            for c in sy.Switch(pid, NC):
                for (ag, dst) in ((qag, d2["qx"]), (kag, d2["kx"]),
                                  (vag, d2["vx"]), (gag, d2["gx"])):
                    inc(sy.dma_start(
                        out=dst.rearrange("(r dd) t -> r dd t", dd=256),
                        in_=ag.rearrange("(r d) t -> r d t", d=D)[:, 2 * c * 128:(2 * c + 2) * 128, :]),
                        dEX, 16)
                    dEX.n -= 16
                inc(sy.dma_start(
                    out=d2["smx"].rearrange("(r q) t -> r q t", q=4)[:, 0:2, :],
                    in_=smag.rearrange("(r q) t -> r q t", q=32)[:, 2 * c:2 * c + 2, :]),
                    dEX, 16)
                dEX.n -= 16
                inc(sy.dma_start(
                    out=d2["smx"].rearrange("(r q) t -> r q t", q=4)[:, 2:4, :],
                    in_=smag.rearrange("(r q) t -> r q t", q=32)[:, 16 + 2 * c:18 + 2 * c, :]),
                    dEX, 16)
                dEX.n -= 16
            dEX.n += 96
            sy.wait_ge(dEX.h, dEX.n)
        ctx.sysec(extract1)
        ctx.barrier()
        phase_l2(ctx, d2)
        ctx.barrier()
        collect([(d2["og"][:, :], ogag[:, :])])
        ctx.barrier()

        def extract2(sy):
            pid = sy.partition_id()
            for c in sy.Switch(pid, NC):
                inc(sy.dma_start(
                    out=d3["ogx"].rearrange("(r t) h -> r t h", t=TC),
                    in_=ogag.rearrange("(r t) h -> r t h", t=T)[:, c * TC:(c + 1) * TC, :]),
                    dEX, 16)
                dEX.n -= 16
            dEX.n += 16
            sy.wait_ge(dEX.h, dEX.n)
        ctx.sysec(extract2)
        ctx.barrier()
        phase_l3(ctx, d3)
        ctx.barrier()
        phase_l4(ctx, d4)
        ctx.barrier()
        collect([(d4["k1_d"][:, :], d5["k1ag"][:, :]),
                 (d4["v1tm"][:, :], d5["v1t"][:, :])])
        ctx.barrier()
        phase_l5(ctx, d5)
        ctx.barrier()
        phase_l6(ctx, d6)

    return ctx


def _fused_inputs(inp):
    m1 = _l1_inputs(inp)
    Wo = np.ascontiguousarray(np.asarray(inp["L0_attn_Wo"], np.float32))
    w13 = []
    for e in range(8):
        w13.append(np.asarray(inp["L0_ffn_w1"][e]).T)
        w13.append(np.asarray(inp["L0_ffn_w3"][e]).T)
    w13.append(np.asarray(inp["L0_ffn_shared_w1"]).T)
    w13.append(np.asarray(inp["L0_ffn_shared_w3"]).T)
    w13 = np.ascontiguousarray(np.stack(w13).astype(np.float32))
    w2t = np.ascontiguousarray(np.stack(
        [np.asarray(inp["L0_ffn_w2"][e]).T for e in range(8)]
        + [np.asarray(inp["L0_ffn_shared_w2"]).T]).astype(np.float32))
    e8 = np.zeros((8, 8 * 128), dtype=BF)
    for e in range(8):
        e8[e, e * 128:(e + 1) * 128] = 1
    onesd = np.ones((1, 128), np.float32)
    ebias = np.ascontiguousarray(np.asarray(inp["L0_ffn_expert_bias"], np.float32)[None, :])
    lnw = np.asarray(inp["L0_attn_ln_w"], np.float32)[None, :]
    lnb = np.asarray(inp["L0_attn_ln_b"], np.float32)[None, :]
    masks = np.stack([
        np.tril(np.ones((CH, CH), np.float32), -1),
        np.triu(np.ones((CH, CH), np.float32), 1),
        np.triu(np.ones((CH, CH), np.float32), 0),
    ])
    idenf = np.eye(128, dtype=np.float32)
    idenb = np.eye(128, dtype=np.float32).astype(BF)
    Wqkv = np.ascontiguousarray(np.stack([inp["L1_attn_Wq"], inp["L1_attn_Wk"],
                                          inp["L1_attn_Wv"]]).astype(np.float32))
    bcols = np.stack([np.asarray(inp["L1_attn_bq"]), np.asarray(inp["L1_attn_bk"]),
                      np.asarray(inp["L1_attn_bv"])]).astype(np.float32)
    bcols = np.ascontiguousarray(bcols.reshape(3, 16, 128).transpose(2, 0, 1).reshape(128, 48))
    gatew = np.asarray(inp["L1_ffn_gate_w"], np.float32)
    gatew = np.ascontiguousarray(gatew.reshape(16, 128, 8).transpose(1, 0, 2).reshape(128, 128))
    hd = 128
    jj = np.arange(hd // 2)
    inv_freq = 10000.0 ** (-2.0 * jj / hd)
    pos = np.arange(T)
    ang = pos[:, None] * inv_freq[None, :]
    sin2 = np.stack([np.sin(ang), np.sin(ang)], -1).reshape(T, hd)
    cos2 = np.stack([np.cos(ang), np.cos(ang)], -1).reshape(T, hd)
    sign = np.where(np.arange(hd) % 2 == 0, -1.0, 1.0)
    ssin = sin2 * sign[None, :]
    cosT = np.ascontiguousarray(cos2.T.astype(np.float32))
    ssinT = np.ascontiguousarray(ssin.T.astype(np.float32))
    Wo1 = np.ascontiguousarray(np.asarray(inp["L1_attn_Wo"], np.float32))
    bo1 = np.ascontiguousarray(np.asarray(inp["L1_attn_bo"], np.float32)[None, :])
    up = np.ascontiguousarray(np.asarray(inp["L1_ffn_up_w"], np.float32))
    dwn = np.ascontiguousarray(np.asarray(inp["L1_ffn_down_w"], np.float32))
    ew = np.ascontiguousarray(np.asarray(inp["L1_ffn_expert_weight"], np.float32))
    gam = np.ascontiguousarray(np.asarray(inp["L1_ffn_ln_gamma"], np.float32)[None, :])
    bet = np.ascontiguousarray(np.asarray(inp["L1_ffn_ln_beta"], np.float32)[None, :])
    maps = []
    for c in range(NC):
        sl = slice(c * TC, (c + 1) * TC)
        m = dict(m1[c])
        m.update({
            "Wo": Wo, "ebias": ebias, "e8d": e8, "onesd": onesd,
            "w13": w13, "w2t": w2t,
            "lnw": lnw, "lnb": lnb, "masks": masks, "idenf": idenf, "idenb": idenb,
            "Wqkv": Wqkv, "bcols": bcols, "gatew": gatew,
            "cosd": np.ascontiguousarray(cosT[:, sl]),
            "ssind": np.ascontiguousarray(ssinT[:, sl]),
            "Wo1": Wo1, "bo1": bo1,
            "upw": up, "dwnw": dwn, "eww": ew, "gam": gam, "bet": bet,
        })
        maps.append(m)
    return maps


def kernel(**inputs):
    inp = {k: np.asarray(v) for k, v in inputs.items()}
    ctx = build_fused()
    maps = _fused_inputs(inp)
    res = run_bass_kernel_spmd(ctx.nc, maps, core_ids=list(range(NC)))
    out = np.concatenate([r["outp"] for r in res.results], axis=0).astype(np.float32)
    return out.reshape(1, T, D)


